# version 05
# baseline (speedup 1.0000x reference)
import contextlib
import numpy as np

import concourse.bass as bass
import concourse.mybir as mybir
from concourse.bass_utils import run_bass_kernel_spmd

F32 = mybir.dt.float32
BF16 = mybir.dt.bfloat16
AF = mybir.ActivationFunctionType
ALU = mybir.AluOpType
AX = mybir.AxisListType

NC = 8
T = 8192
D = 2048
TL = 1024
NB = 8
DC = 16
FF = 8192
FC = 64
DFF1 = 1408
LN_EPS = 1e-5

ENGINES = ("tn", "act", "dve", "gp", "sy")
_ENG_ATTR = {"tn": "tensor", "act": "scalar", "dve": "vector", "gp": "gpsimd", "sy": "sync"}



def _tile_w(w_in_out):
    n_in, n_out = w_in_out.shape
    ic, oc = n_in // 128, (n_out + 127) // 128
    if n_out % 128:
        w_in_out = np.concatenate(
            [w_in_out, np.zeros((n_in, oc * 128 - n_out), w_in_out.dtype)], 1)
    w4 = w_in_out.reshape(ic, 128, oc, 128)
    return np.ascontiguousarray(w4.transpose(2, 1, 0, 3).reshape(oc, 128, ic * 128))


def _vec_pc(v):
    return np.ascontiguousarray(np.asarray(v, np.float32).reshape(-1, 128).T)


def _core_tokens(i):
    idx = []
    for j in range(NB):
        g = i + 8 * j
        idx.extend(range(g * 128, (g + 1) * 128))
    return np.array(idx)



class Sem:
    __slots__ = ("h", "n", "name", "dma")

    def __init__(self, h, name, dma=False):
        self.h, self.n, self.name, self.dma = h, 0, name, dma


class Ctx:
    def __init__(self, nc, es):
        self.nc = nc
        self.es = es
        self.ops = {e: [] for e in ENGINES}
        self.sems = {}

    def sem(self, name, dma=False):
        s = Sem(self.es.enter_context(self.nc.semaphore(name)), name, dma)
        self.sems[name] = s
        return s

    def emit(self, eng, fn, waits=(), inc=None, inck=1):
        w = [(s.h, int(v)) for s, v in waits if v > 0]
        if inc is not None and inc.dma and inc.n > 0:
            w.append((inc.h, inc.n))
        tgt = None
        if inc is not None:
            inc.n += inck
            tgt = (inc.h, inck)
        self.ops[eng].append((w, fn, tgt))
        return inc.n if inc is not None else None

    def barrier(self):
        cur = [(s, s.n) for s in self.sems.values() if s.n > 0]
        for e in ENGINES:
            for s, v in cur:
                self.ops[e].append(([(s.h, v)], None, None))

    def finalize(self):
        block = self.es.enter_context(self.nc.Block())
        for name in ENGINES:
            ops = self.ops[name]

            def make(ops):
                def body(eng):
                    for w, fn, tgt in ops:
                        for h, v in w:
                            eng.wait_ge(h, v)
                        if fn is None:
                            continue
                        ins = fn(eng)
                        if tgt is not None:
                            ins.then_inc(tgt[0], tgt[1])
                return body
            getattr(block, _ENG_ATTR[name])(make(ops))


def _now(*sems):
    return [(s, s.n) for s in sems]



def build_launch1():
    nc = bass.Bass("TRN2", target_bir_lowering=False, debug=False)
    ein = lambda n, sh, dt: nc.dram_tensor(n, sh, dt, kind="ExternalInput").ap()
    eout = lambda n, sh, dt: nc.dram_tensor(n, sh, dt, kind="ExternalOutput").ap()
    xfm = ein("xfm", [DC, 128, TL], F32)
    xbnd = ein("xbnd", [NB, D], F32)
    ident = ein("ident", [128, 128], F32)
    wk = ein("wk", [DC, 128, D], F32)
    wv = ein("wv", [DC, 128, D], F32)
    wr = ein("wr", [DC, 128, D], F32)
    gw = ein("gw", [128, 32], F32)
    wg = ein("wg", [2, FC, 128, D], F32)
    wu = ein("wu", [2, FC, 128, D], F32)
    wd = ein("wd", [2, DC, 128, FF], F32)
    td = ein("td", [128, DC], F32)
    tmk = ein("tmk", [128, DC], F32)
    tmv = ein("tmv", [128, DC], F32)
    tmr = ein("tmr", [128, DC], F32)
    a_out = eout("a_out", [DC, 128, TL], F32)
    b_out = eout("b_out", [DC, 128, TL], F32)
    r_out = eout("r_out", [DC, 128, TL], BF16)
    g0_out = eout("g0_out", [DC, 128, TL], F32)
    zacc = nc.dram_tensor("zacc", [DC, 128, TL], F32).ap()

    es = contextlib.ExitStack()
    with es:
        ctx = Ctx(nc, es)
        sb = lambda n, sh, dt: es.enter_context(nc.sbuf_tensor(n, sh, dt))
        xn = sb("xn", [128, DC * TL], BF16)
        ones1 = sb("ones1", [128, 1], F32)
        epsc = sb("epsc", [128, 1], F32)
        onesrow = sb("onesrow", [1, 128], F32)
        identb = sb("identb", [128, 128], BF16)
        w0bc = sb("w0bc", [128, TL], F32)
        w1bc = sb("w1bc", [128, TL], F32)
        tmm = sb("tmm", [128, 3 * DC], F32)
        gwt = sb("gwt", [128, 32], BF16)
        xnbf = sb("xnbf", [128, DC * NB], BF16)
        xtile = [sb(f"xt{i}", [128, TL], F32) for i in range(2)]
        pha = contextlib.ExitStack()
        vr = pha.enter_context(nc.sbuf_tensor("vr", [1, 8 * TL], F32))
        brow = pha.enter_context(nc.sbuf_tensor("brow", [8, 16], F32))
        g2 = pha.enter_context(nc.sbuf_tensor("g2", [2, TL], F32))
        xbt = pha.enter_context(nc.sbuf_tensor("xbt", [NB, D], F32))
        xbsq = pha.enter_context(nc.sbuf_tensor("xbsq", [NB, D], F32))
        sq = pha.enter_context(nc.sbuf_tensor("sq", [128, TL], F32))
        mubc = pha.enter_context(nc.sbuf_tensor("mubc", [128, TL], F32))
        rbc = pha.enter_context(nc.sbuf_tensor("rbc", [128, TL], F32))
        P = [es.enter_context(nc.psum_tensor(f"pb{i}", [128, 512], F32)) for i in range(8)]

        s_tn, s_act, s_dve = ctx.sem("s_tn"), ctx.sem("s_act"), ctx.sem("s_dve")
        d_cn = ctx.sem("d_cn", dma=True)
        d_ld = ctx.sem("d_ld", dma=True)
        d_st = ctx.sem("d_st", dma=True)
        d_g0, d_g1 = ctx.sem("d_g0", dma=True), ctx.sem("d_g1", dma=True)
        d_gs = ctx.sem("d_gs", dma=True)
        d_ac = ctx.sem("d_ac", dma=True)
        dws = [d_g0, d_g1]
        H2 = (slice(0, 512), slice(512, 1024))

        def bcast_row(row_ap, dst, bank, wait_sems):
            for h in range(2):
                sl = H2[h]
                n = ctx.emit("tn", lambda t, sl=sl: t.matmul(
                    P[bank][:, :], onesrow[:], row_ap[:, sl], start=True, stop=True),
                    waits=list(wait_sems) + _now(s_dve), inc=s_tn)
                ctx.emit("dve", lambda v, sl=sl: v.tensor_copy(dst[:, sl], P[bank][:, :]),
                         waits=[(s_tn, n)], inc=s_dve)

        ctx.emit("dve", lambda v: v.memset(ones1[:], 1.0), inc=s_dve)
        ctx.emit("dve", lambda v: v.memset(epsc[:], LN_EPS), inc=s_dve)
        ctx.emit("dve", lambda v: v.memset(onesrow[:], 1.0), inc=s_dve)
        n_const = s_dve.n
        ctx.emit("sy", lambda s: s.dma_start(out=xbt[:], in_=xbnd[:, :]), inc=d_cn, inck=16)
        ctx.emit("sy", lambda s: s.dma_start(out=tmm[:, 0:DC], in_=td[:, :]), inc=d_cn, inck=16)
        ctx.emit("sy", lambda s: s.dma_start(out=tmm[:, 0:DC], in_=tmk[:, :]), inc=d_cn, inck=16)
        ctx.emit("sy", lambda s: s.dma_start(out=tmm[:, DC:2 * DC], in_=tmv[:, :]), inc=d_cn, inck=16)
        ctx.emit("sy", lambda s: s.dma_start(out=tmm[:, 2 * DC:3 * DC], in_=tmr[:, :]), inc=d_cn, inck=16)
        ctx.emit("gp", lambda g: g.dma_start(out=gwt[:], in_=gw[:, :]), inc=d_gs, inck=16)
        ctx.emit("gp", lambda g: g.dma_start(out=identb[:], in_=ident[:, :]), inc=d_gs, inck=16)
        n_tm = ctx.emit("dve", lambda v: v.tensor_scalar(tmm[:], tmm[:], -1.0, None, op0=ALU.add),
                        waits=[(d_cn, d_cn.n)], inc=s_dve)

        sq_marks = []
        for c in range(DC):
            buf = xtile[c % 2]
            wts = [(s_tn, sq_marks[c - 2][1])] if c >= 2 else []
            n = ctx.emit("sy", lambda s, c=c, buf=buf: s.dma_start(out=buf[:], in_=xfm[c, :, :]),
                         waits=wts, inc=d_ld, inck=16)
            na = ctx.emit("act", lambda a, buf=buf: a.activation(sq[:], buf[:], AF.Square),
                          waits=[(d_ld, n)] + ([(s_tn, sq_marks[c - 1][1])] if c >= 1 else []),
                          inc=s_act)
            last = None
            for h in range(2):
                ctx.emit("tn", lambda t, buf=buf, h=h, c=c: t.matmul(
                    P[h][0:1, :], ones1[:], buf[:, H2[h]], start=(c == 0), stop=(c == DC - 1)),
                    waits=[(s_act, na)] + ([(s_dve, n_const)] if c == 0 else []))
                last = ctx.emit("tn", lambda t, h=h, c=c: t.matmul(
                    P[2 + h][0:1, :], ones1[:], sq[:, H2[h]], start=(c == 0), stop=(c == DC - 1)),
                    waits=[(s_act, na)], inc=s_tn)
            sq_marks.append((n, last))
        V = lambda i: vr[0:1, i * TL:(i + 1) * TL]
        Vh = lambda i, h: vr[0:1, i * TL + H2[h].start:i * TL + H2[h].stop]
        for h in range(2):
            ctx.emit("dve", lambda v, h=h: v.tensor_copy(Vh(0, h), P[h][0:1, :]),
                     waits=_now(s_tn), inc=s_dve)
            ctx.emit("dve", lambda v, h=h: v.tensor_copy(Vh(1, h), P[2 + h][0:1, :]),
                     waits=_now(s_tn), inc=s_dve)
        ctx.emit("dve", lambda v: v.tensor_scalar(V(2), V(0), 1.0 / D, None, op0=ALU.mult),
                 waits=_now(s_dve), inc=s_dve)
        ctx.emit("dve", lambda v: v.tensor_scalar(V(1), V(1), 1.0 / D, None, op0=ALU.mult),
                 waits=_now(s_dve), inc=s_dve)
        ctx.emit("dve", lambda v: v.tensor_tensor(V(4), V(2), V(2), op=ALU.mult),
                 waits=_now(s_dve), inc=s_dve)
        ctx.emit("dve", lambda v: v.tensor_tensor(V(5), V(1), V(4), op=ALU.subtract),
                 waits=_now(s_dve), inc=s_dve)
        ctx.emit("act", lambda a: a.activation(V(6), V(5), AF.Ln, bias=epsc[0:1, 0:1]),
                 waits=_now(s_dve), inc=s_act)
        ctx.emit("act", lambda a: a.activation(V(3), V(6), AF.Exp, scale=-0.5),
                 waits=_now(s_act), inc=s_act)
        bcast_row(V(2), mubc, 4, _now(s_act))
        bcast_row(V(3), rbc, 4, _now(s_act))

        ap_marks = []
        for c in range(DC):
            buf = xtile[c % 2]
            wts = [(s_dve, ap_marks[c - 2])] if c >= 2 else _now(s_act)
            n = ctx.emit("sy", lambda s, c=c, buf=buf: s.dma_start(out=buf[:], in_=xfm[c, :, :]),
                         waits=wts, inc=d_ld, inck=16)
            ctx.emit("dve", lambda v, buf=buf: v.tensor_tensor(sq[:], buf[:], mubc[:], op=ALU.subtract),
                     waits=[(d_ld, n)] + _now(s_dve), inc=s_dve)
            m = ctx.emit("dve", lambda v, c=c: v.tensor_tensor(
                xn[:, c * TL:(c + 1) * TL], sq[:], rbc[:], op=ALU.mult),
                waits=_now(s_dve), inc=s_dve)
            ap_marks.append(m)

        ctx.emit("dve", lambda v: v.tensor_reduce(brow[0:8, 0:1], xbt[:], axis=AX.X, op=ALU.add),
                 waits=[(d_cn, 16)], inc=s_dve)
        ctx.emit("dve", lambda v: v.tensor_scalar(brow[0:8, 1:2], brow[0:8, 0:1], 1.0 / D, None, op0=ALU.mult),
                 waits=_now(s_dve), inc=s_dve)
        ctx.emit("act", lambda a: a.activation(xbsq[:], xbt[:], AF.Square, accum_out=brow[0:8, 2:3]),
                 waits=[(d_cn, 16)], inc=s_act)
        ctx.emit("dve", lambda v: v.tensor_scalar(brow[0:8, 3:4], brow[0:8, 2:3], 1.0 / D, None, op0=ALU.mult),
                 waits=_now(s_act), inc=s_dve)
        ctx.emit("dve", lambda v: v.tensor_tensor(brow[0:8, 4:5], brow[0:8, 1:2], brow[0:8, 1:2], op=ALU.mult),
                 waits=_now(s_dve), inc=s_dve)
        ctx.emit("dve", lambda v: v.tensor_tensor(brow[0:8, 5:6], brow[0:8, 3:4], brow[0:8, 4:5], op=ALU.subtract),
                 waits=_now(s_dve), inc=s_dve)
        ctx.emit("act", lambda a: a.activation(brow[0:8, 6:7], brow[0:8, 5:6], AF.Ln, bias=epsc[0:8, 0:1]),
                 waits=_now(s_dve), inc=s_act)
        ctx.emit("act", lambda a: a.activation(brow[0:8, 7:8], brow[0:8, 6:7], AF.Exp, scale=-0.5),
                 waits=_now(s_act), inc=s_act)
        ctx.emit("dve", lambda v: v.tensor_scalar(xbsq[:], xbt[:], brow[0:8, 1:2], None, op0=ALU.subtract),
                 waits=_now(s_act), inc=s_dve)
        xbt_bf = xbt.bitcast(BF16)
        ctx.emit("dve", lambda v: v.tensor_scalar(xbt_bf[:, 0:D], xbsq[:], brow[0:8, 7:8], None, op0=ALU.mult),
                 waits=_now(s_dve), inc=s_dve)
        for c in range(DC):
            n = ctx.emit("tn", lambda t, c=c: t.transpose(
                P[5].bitcast(BF16)[:, 0:8], xbt_bf[0:8, c * 128:(c + 1) * 128], identb[0:8, 0:8]),
                waits=_now(s_dve, d_gs), inc=s_tn)
            ctx.emit("dve", lambda v, c=c: v.tensor_copy(
                xnbf[:, c * NB:(c + 1) * NB], P[5].bitcast(BF16)[:, 0:8]),
                waits=[(s_tn, n)], inc=s_dve)

        for c in range(DC):
            for h in range(2):
                ctx.emit("tn", lambda t, c=c, h=h: t.matmul(
                    P[6 + h][0:2, :], gwt[:, 2 * c:2 * c + 2], xn[:, c * TL + H2[h].start:c * TL + H2[h].stop],
                    start=(c == 0), stop=(c == DC - 1)),
                    waits=(_now(s_dve, d_gs) if c == 0 else []),
                    inc=(s_tn if c == DC - 1 and h == 1 else None))
        for h in range(2):
            ctx.emit("dve", lambda v, h=h: v.tensor_copy(g2[0:2, H2[h]], P[6 + h][0:2, :]),
                     waits=_now(s_tn), inc=s_dve)
        ctx.emit("sy", lambda s: s.dma_start(out=V(1), in_=g2[1:2, 0:TL]),
                 waits=_now(s_dve), inc=d_cn, inck=16)
        ctx.emit("dve", lambda v: v.tensor_tensor(V(3), g2[0:1, 0:TL], V(1), op=ALU.subtract),
                 waits=[(d_cn, d_cn.n)], inc=s_dve)
        ctx.emit("act", lambda a: a.activation(V(4), V(3), AF.Abs),
                 waits=_now(s_dve), inc=s_act)
        ctx.emit("act", lambda a: a.activation(V(5), V(4), AF.Exp, scale=-1.0),
                 waits=_now(s_act), inc=s_act)
        ctx.emit("dve", lambda v: v.tensor_scalar(V(6), V(5), 1.0, None, op0=ALU.add),
                 waits=_now(s_act), inc=s_dve)
        ctx.emit("dve", lambda v: v.reciprocal(V(5), V(6)),
                 waits=_now(s_dve), inc=s_dve)
        ctx.emit("dve", lambda v: v.tensor_scalar(V(6), V(3), 0.0, None, op0=ALU.is_ge),
                 waits=_now(s_dve), inc=s_dve)
        ctx.emit("dve", lambda v: v.tensor_tensor(V(7), V(5), V(6), op=ALU.mult),
                 waits=_now(s_dve), inc=s_dve)
        ctx.emit("dve", lambda v: v.tensor_tensor(V(6), V(5), V(7), op=ALU.subtract),
                 waits=_now(s_dve), inc=s_dve)
        bcast_row(V(7), w0bc, 6, _now(s_dve))
        bcast_row(V(6), w1bc, 6, _now(s_dve))

        ctx.barrier()
        pha.close()

        with contextlib.ExitStack() as ph:
            hbuf = ph.enter_context(nc.sbuf_tensor("hbuf", [128, FC * TL], BF16))
            warena = ph.enter_context(nc.sbuf_tensor("warena", [128, 8192], BF16))
            hs = ph.enter_context(nc.sbuf_tensor("hs", [128, 512], F32))
            zst = ph.enter_context(nc.sbuf_tensor("zst", [128, TL], F32))
            wgb = [warena[:, 0:2048], warena[:, 2048:4096]]
            wub = [warena[:, 4096:6144], warena[:, 6144:8192]]
            wdb = [warena[:, 0:4096], warena[:, 4096:8192]]
            for e in range(2):
                gu_marks = []
                for fc in range(FC):
                    b = fc % 2
                    wts = [(s_tn, gu_marks[fc - 2])] if fc >= 2 else []
                    ctx.emit("gp", lambda g, e=e, fc=fc, b=b: g.dma_start(
                        out=wgb[b][:], in_=wg[e, fc, :, :]), waits=wts, inc=dws[b], inck=16)
                    nu = ctx.emit("gp", lambda g, e=e, fc=fc, b=b: g.dma_start(
                        out=wub[b][:], in_=wu[e, fc, :, :]), inc=dws[b], inck=16)
                    mlast = None
                    for th in range(2):
                        pg, pu = P[2 * th], P[2 * th + 1]
                        pw = [(dws[b], nu)] + _now(s_act, s_dve)
                        for dc in range(DC):
                            ctx.emit("tn", lambda t, b=b, dc=dc, th=th, pg=pg: t.matmul(
                                pg[:, :], wgb[b][:, dc * 128:(dc + 1) * 128],
                                xn[:, dc * TL + H2[th].start:dc * TL + H2[th].stop],
                                start=(dc == 0), stop=(dc == DC - 1)),
                                waits=(pw if dc == 0 else []))
                        for dc in range(DC):
                            mlast = ctx.emit("tn", lambda t, b=b, dc=dc, th=th, pu=pu: t.matmul(
                                pu[:, :], wub[b][:, dc * 128:(dc + 1) * 128],
                                xn[:, dc * TL + H2[th].start:dc * TL + H2[th].stop],
                                start=(dc == 0), stop=(dc == DC - 1)),
                                inc=(s_tn if dc == DC - 1 else None))
                        na = ctx.emit("act", lambda a, pg=pg: a.activation(hs[:], pg[:, :], AF.Silu),
                                      waits=[(s_tn, mlast)] + _now(s_dve), inc=s_act)
                        ctx.emit("dve", lambda v, fc=fc, th=th, pu=pu: v.tensor_tensor(
                            hbuf[:, fc * TL + H2[th].start:fc * TL + H2[th].stop], hs[:], pu[:, :], op=ALU.mult),
                            waits=[(s_act, na)], inc=s_dve)
                    gu_marks.append(mlast)
                ctx.barrier()
                wbc = w0bc if e == 0 else w1bc
                y_marks = []
                for do in range(DC):
                    mlast = None
                    for fh in range(2):
                        b = fh
                        wts = [(s_tn, y_marks[do - 1])] if do >= 1 else []
                        nw = ctx.emit("gp", lambda g, e=e, do=do, fh=fh, b=b: g.dma_start(
                            out=wdb[b][:], in_=wd[e, do, :, fh * 4096:(fh + 1) * 4096]),
                            waits=wts, inc=dws[b], inck=16)
                        for th in range(2):
                            py = P[4 + th]
                            pw = [(dws[b], nw), (d_ac, d_ac.n)] + _now(s_dve)
                            for fi in range(32):
                                fc = fh * 32 + fi
                                mlast = ctx.emit("tn", lambda t, b=b, fi=fi, fc=fc, th=th, py=py: t.matmul(
                                    py[:, :], wdb[b][:, fi * 128:(fi + 1) * 128],
                                    hbuf[:, fc * TL + H2[th].start:fc * TL + H2[th].stop],
                                    start=(fc == 0), stop=(fc == FC - 1)),
                                    waits=(pw if fi == 0 else []),
                                    inc=(s_tn if (fh == 1 and fi == 31) else None))
                    y_marks.append(mlast)
                    for th in range(2):
                        py = P[4 + th]
                        ctx.emit("dve", lambda v, th=th, py=py, wbc=wbc: v.tensor_tensor(
                            zst[:, H2[th]], py[:, :], wbc[:, H2[th]], op=ALU.mult),
                            waits=[(s_tn, mlast), (d_ac, d_ac.n)], inc=s_dve)
                    ctx.emit("gp", lambda g, do=do, e=e: (
                        g.dma_start(out=zacc[do, :, :], in_=zst[:]) if e == 0 else
                        g.dma_start(out=zacc[do, :, :], in_=zst[:], accum_op=ALU.add)),
                        waits=_now(s_dve) + [(d_ac, d_ac.n)], inc=d_ac, inck=16)
                ctx.barrier()
            g0_marks = []
            for do in range(DC):
                buf = xtile[do % 2]
                wts = [(d_ac, d_ac.n)] + ([(s_act, g0_marks[do - 2])] if do >= 2 else [])
                n = ctx.emit("sy", lambda s, do=do, buf=buf: s.dma_start(out=buf[:], in_=zacc[do, :, :]),
                             waits=wts, inc=d_ld, inck=16)
                na = ctx.emit("act", lambda a, buf=buf: a.activation(zst[:, 0:TL], buf[:], AF.Silu),
                              waits=[(d_ld, n), (d_st, d_st.n)], inc=s_act)
                g0_marks.append(na)
                ctx.emit("sy", lambda s, do=do: s.dma_start(out=g0_out[do, :, :], in_=zst[:, 0:TL]),
                         waits=[(s_act, na)], inc=d_st, inck=16)
            ctx.barrier()

        with contextlib.ExitStack() as ph:
            xk = ph.enter_context(nc.sbuf_tensor("xk", [128, DC * TL], BF16))
            xv = ph.enter_context(nc.sbuf_tensor("xv", [128, DC * TL], BF16))
            xr = ph.enter_context(nc.sbuf_tensor("xr", [128, DC * TL], BF16))
            dtile = ph.enter_context(nc.sbuf_tensor("dtile", [128, TL], BF16))
            warena = ph.enter_context(nc.sbuf_tensor("warena2", [128, 8192], BF16))
            ct = ph.enter_context(nc.sbuf_tensor("ct", [128, TL], F32))
            cvt = ph.enter_context(nc.sbuf_tensor("cvt", [128, TL], F32))
            at = ph.enter_context(nc.sbuf_tensor("at", [128, TL], F32))
            bt = ph.enter_context(nc.sbuf_tensor("bt", [128, TL], F32))
            onesT = ph.enter_context(nc.sbuf_tensor("onesT", [128, TL], F32))
            rt = ph.enter_context(nc.sbuf_tensor("rt", [128, TL], BF16))
            ctx.emit("dve", lambda v: v.memset(onesT[:], 1.0), inc=s_dve)

            for c in range(DC):
                xn_c = xn[:, c * TL:(c + 1) * TL]
                ctx.emit("dve", lambda v, xn_c=xn_c: v.tensor_tensor(
                    dtile[:, 1:TL], xn_c[:, 1:TL], xn_c[:, 0:TL - 1], op=ALU.subtract),
                    waits=_now(s_dve), inc=s_dve)
                bsl = dtile[:].rearrange("p (j r) -> p j r", r=128)[:, :, 0:1].squeeze(2)
                xcol = xn_c.rearrange("p (j r) -> p j r", r=128)[:, :, 0:1].squeeze(2)
                ctx.emit("dve", lambda v, c=c, bsl=bsl, xcol=xcol: v.tensor_tensor(
                    bsl, xcol, xnbf[:, c * NB:(c + 1) * NB], op=ALU.subtract),
                    waits=_now(s_dve), inc=s_dve)
                for j, dst in enumerate((xk, xv, xr)):
                    ctx.emit("dve", lambda v, c=c, j=j, dst=dst, xn_c=xn_c: v.scalar_tensor_tensor(
                        dst[:, c * TL:(c + 1) * TL], dtile[:, 0:TL], tmm[:, j * DC + c:j * DC + c + 1],
                        xn_c, op0=ALU.mult, op1=ALU.add),
                        waits=_now(s_dve), inc=s_dve)
            ctx.barrier()

            wrb = [warena[:, 0:2048], warena[:, 2048:4096]]
            r_marks = []
            for c in range(DC):
                b = c % 2
                wts = [(s_tn, r_marks[c - 2])] if c >= 2 else []
                nw = ctx.emit("gp", lambda g, c=c, b=b: g.dma_start(out=wrb[b][:], in_=wr[c, :, :]),
                              waits=wts, inc=dws[b], inck=16)
                mlast = None
                for th in range(2):
                    pr = P[th]
                    for dc in range(DC):
                        mlast = ctx.emit("tn", lambda t, b=b, dc=dc, th=th, pr=pr: t.matmul(
                            pr[:, :], wrb[b][:, dc * 128:(dc + 1) * 128],
                            xr[:, dc * TL + H2[th].start:dc * TL + H2[th].stop],
                            start=(dc == 0), stop=(dc == DC - 1)),
                            waits=([(dws[b], nw)] + _now(s_act) if dc == 0 else []),
                            inc=(s_tn if dc == DC - 1 else None))
                    ctx.emit("act", lambda a, th=th, pr=pr: a.activation(rt[:, H2[th]], pr[:, :], AF.Sigmoid),
                             waits=[(s_tn, mlast), (d_st, d_st.n)], inc=s_act)
                r_marks.append(mlast)
                ctx.emit("sy", lambda s, c=c: s.dma_start(out=r_out[c, :, :], in_=rt[:]),
                         waits=_now(s_act), inc=d_st, inck=16)
            ctx.barrier()

            wkb = [warena[:, 0:2048], warena[:, 2048:4096]]
            wvb = [warena[:, 4096:6144], warena[:, 6144:8192]]
            kv_marks = []
            for c in range(DC):
                b = c % 2
                wts = [(s_tn, kv_marks[c - 2])] if c >= 2 else []
                ctx.emit("gp", lambda g, c=c, b=b: g.dma_start(out=wkb[b][:], in_=wk[c, :, :]),
                         waits=wts, inc=dws[b], inck=16)
                nv = ctx.emit("gp", lambda g, c=c, b=b: g.dma_start(out=wvb[b][:], in_=wv[c, :, :]),
                              inc=dws[b], inck=16)
                mlast = None
                for th in range(2):
                    pk, pv = P[2 * th], P[2 * th + 1]
                    for dc in range(DC):
                        ctx.emit("tn", lambda t, b=b, dc=dc, th=th, pk=pk: t.matmul(
                            pk[:, :], wkb[b][:, dc * 128:(dc + 1) * 128],
                            xk[:, dc * TL + H2[th].start:dc * TL + H2[th].stop],
                            start=(dc == 0), stop=(dc == DC - 1)),
                            waits=([(dws[b], nv)] + _now(s_act, s_dve) if dc == 0 else []))
                    for dc in range(DC):
                        mlast = ctx.emit("tn", lambda t, b=b, dc=dc, th=th, pv=pv: t.matmul(
                            pv[:, :], wvb[b][:, dc * 128:(dc + 1) * 128],
                            xv[:, dc * TL + H2[th].start:dc * TL + H2[th].stop],
                            start=(dc == 0), stop=(dc == DC - 1)),
                            inc=(s_tn if dc == DC - 1 else None))
                    na = ctx.emit("act", lambda a, th=th, pk=pk: a.activation(ct[:, H2[th]], pk[:, :], AF.Exp),
                                  waits=[(s_tn, mlast)] + _now(s_dve), inc=s_act)
                    ctx.emit("dve", lambda v, th=th, pv=pv: v.tensor_tensor(
                        cvt[:, H2[th]], ct[:, H2[th]], pv[:, :], op=ALU.mult),
                        waits=[(s_act, na)] + _now(s_dve), inc=s_dve)
                kv_marks.append(mlast)
                ctx.emit("dve", lambda v: v.tensor_tensor_scan(at[:], onesT[:], cvt[:], 0.0, op0=ALU.mult, op1=ALU.add),
                         waits=_now(s_dve) + [(d_st, d_st.n)], inc=s_dve)
                ctx.emit("dve", lambda v: v.tensor_tensor_scan(bt[:], onesT[:], ct[:], 0.0, op0=ALU.mult, op1=ALU.add),
                         waits=_now(s_dve), inc=s_dve)
                ctx.emit("sy", lambda s, c=c: s.dma_start(out=a_out[c, :, :], in_=at[:]),
                         waits=_now(s_dve), inc=d_st, inck=16)
                ctx.emit("sy", lambda s, c=c: s.dma_start(out=b_out[c, :, :], in_=bt[:]),
                         waits=_now(s_dve), inc=d_st, inck=16)

        for e in ENGINES:
            ctx.ops[e].append(([(d_st.h, d_st.n), (d_ac.h, d_ac.n)], None, None))
        ctx.finalize()
    return nc



def build_launch2():
    nc = bass.Bass("TRN2", target_bir_lowering=False, debug=False)
    ein = lambda n, sh, dt: nc.dram_tensor(n, sh, dt, kind="ExternalInput").ap()
    eout = lambda n, sh, dt: nc.dram_tensor(n, sh, dt, kind="ExternalOutput").ap()
    xfm = ein("xfm", [DC, 128, TL], F32)
    a_in = ein("a_in", [DC, 128, TL], F32)
    b_in = ein("b_in", [DC, 128, TL], F32)
    r_in = ein("r_in", [DC, 128, TL], BF16)
    g0_in = ein("g0_in", [DC, 128, TL], F32)
    epta = ein("epta", [DC, 128, 64], F32)
    eptb = ein("eptb", [DC, 128, 64], F32)
    offw = ein("offw", [NB, 128, 64], F32)
    tf = ein("tf", [128, DC], F32)
    lng = ein("lng", [128, DC], F32)
    lnb = ein("lnb", [128, DC], F32)
    wo = ein("wo", [DC, 128, D], F32)
    wks = ein("wks", [128, D], F32)
    wkg = ein("wkg", [128, D], F32)
    wv1 = ein("wv1", [2, 128, D], F32)
    swp = ein("swp", [128, 128], F32)
    ident = ein("ident", [128, 128], F32)
    cosk = ein("cosk", [128, TL], F32)
    sink = ein("sink", [128, TL], F32)
    poolm = ein("poolm", [128, 2], F32)
    sw1 = ein("sw1", [2, 11, 128, D], F32)
    sw2 = ein("sw2", [2, DC, 128, DFF1], F32)
    rw1 = ein("rw1", [16, 11, 128, D], F32)
    rw2 = ein("rw2", [16, DC, 128, DFF1], F32)
    b1all = ein("b1all", [18, 128, 11], F32)
    b2t = ein("b2t", [DC, 18, 128], F32)
    cent = ein("cent", [128, DC * 16], F32)
    cbias = ein("cbias", [128, 16], F32)
    x1o = eout("x1o", [DC, 128, TL], F32)
    xn1o = eout("xn1o", [DC, 128, TL], BF16)
    g1o = eout("g1o", [DC, 128, TL], F32)
    ks2o = eout("ks2o", [128, 14], F32)
    kg2o = eout("kg2o", [128, 14], F32)
    v2o = eout("v2o", [14, 256], F32)
    ksro = eout("ksro", [128, 128], F32)
    kgro = eout("kgro", [128, 128], F32)
    vro = eout("vro", [128, 256], F32)
    macc = nc.dram_tensor("macc", [DC, 128, TL], F32).ap()

    GROUPS = [[0, 1, 2], [3, 4, 5], [6, 7, 8], [9, 10, 11], [12, 13, 14], [15, 16, 17]]

    es = contextlib.ExitStack()
    with es:
        ctx = Ctx(nc, es)
        sb = lambda n, sh, dt: es.enter_context(nc.sbuf_tensor(n, sh, dt))
        xn1 = sb("xn1", [128, DC * TL], BF16)
        ones1 = sb("ones1", [128, 1], F32)
        onesrow = sb("onesrow", [1, 128], F32)
        epsc = sb("epsc", [128, 1], F32)
        identb = sb("identb", [128, 128], BF16)
        swpb = sb("swpb", [128, 128], BF16)
        euc = sb("euc", [128, 4 * DC], F32)
        poolt = sb("poolt", [128, 2], BF16)
        cbt = sb("cbt", [128, 16], F32)
        G18 = sb("G18", [18, TL], BF16)
        onesrow_bf = sb("onesrow_bf", [1, 128], BF16)
        ones1b = sb("ones1b", [128, 1], BF16)
        onestl2 = sb("onestl2", [2, TL], BF16)
        identf = sb("identf", [128, 128], F32)
        centb = sb("centb", [128, DC * 16], BF16)
        sq = sb("sq", [128, TL], F32)
        abuf = [sb(f"ab{i}", [128, TL], F32) for i in range(2)]
        bbuf = [sb(f"bb{i}", [128, TL], F32) for i in range(2)]
        P = [es.enter_context(nc.psum_tensor(f"pb{i}", [128, 512], F32)) for i in range(8)]

        s_tn, s_act, s_dve = ctx.sem("s_tn"), ctx.sem("s_act"), ctx.sem("s_dve")
        d_cn = ctx.sem("d_cn", dma=True)
        d_ld = ctx.sem("d_ld", dma=True)
        d_l2 = ctx.sem("d_l2", dma=True)
        d_st = ctx.sem("d_st", dma=True)
        d_g0, d_g1 = ctx.sem("d_g0", dma=True), ctx.sem("d_g1", dma=True)
        d_gs = ctx.sem("d_gs", dma=True)
        d_ac = ctx.sem("d_ac", dma=True)
        dws = [d_g0, d_g1]
        H2 = (slice(0, 512), slice(512, 1024))

        def bcast_row(row_ap, dst, bank, wait_sems):
            for h in range(2):
                sl = H2[h]
                n = ctx.emit("tn", lambda t, sl=sl: t.matmul(
                    P[bank][:, :], onesrow[:], row_ap[:, sl], start=True, stop=True),
                    waits=list(wait_sems) + _now(s_dve), inc=s_tn)
                ctx.emit("dve", lambda v, sl=sl: v.tensor_copy(dst[:, sl], P[bank][:, :]),
                         waits=[(s_tn, n)], inc=s_dve)

        ctx.emit("dve", lambda v: v.memset(ones1[:], 1.0), inc=s_dve)
        ctx.emit("dve", lambda v: v.memset(onesrow[:], 1.0), inc=s_dve)
        ctx.emit("dve", lambda v: v.memset(epsc[:], LN_EPS), inc=s_dve)
        ctx.emit("dve", lambda v: v.memset(onesrow_bf[:], 1.0), inc=s_dve)
        ctx.emit("dve", lambda v: v.memset(ones1b[:], 1.0), inc=s_dve)
        ctx.emit("dve", lambda v: v.memset(onestl2[:], 1.0), inc=s_dve)
        n_const = s_dve.n
        ctx.emit("sy", lambda s: s.dma_start(out=euc[:, 0:DC], in_=tf[:, :]), inc=d_cn, inck=16)
        ctx.emit("sy", lambda s: s.dma_start(out=euc[:, 2 * DC:3 * DC], in_=lng[:, :]), inc=d_cn, inck=16)
        ctx.emit("sy", lambda s: s.dma_start(out=euc[:, 3 * DC:4 * DC], in_=lnb[:, :]), inc=d_cn, inck=16)
        ctx.emit("sy", lambda s: s.dma_start(out=cbt[:], in_=cbias[:, :]), inc=d_cn, inck=16)
        ctx.emit("gp", lambda g: g.dma_start(out=identb[:], in_=ident[:, :]), inc=d_gs, inck=16)
        ctx.emit("gp", lambda g: g.dma_start(out=centb[:], in_=cent[:, :]), inc=d_gs, inck=16)
        ctx.emit("sy", lambda s: s.dma_start(out=identf[:], in_=ident[:, :]), inc=d_cn, inck=16)
        ctx.emit("gp", lambda g: g.dma_start(out=swpb[:], in_=swp[:, :]), inc=d_gs, inck=16)
        ctx.emit("gp", lambda g: g.dma_start(out=poolt[:], in_=poolm[:, :]), inc=d_gs, inck=16)
        ctx.emit("act", lambda a: a.activation(euc[:, 0:DC], euc[:, 0:DC], AF.Exp),
                 waits=[(d_cn, 16)], inc=s_act)
        ctx.emit("dve", lambda v: v.tensor_scalar(euc[:, DC:2 * DC], euc[:, 0:DC], -1.0, None, op0=ALU.mult),
                 waits=_now(s_act), inc=s_dve)
        ctx.emit("dve", lambda v: v.tensor_scalar(euc[:, DC:2 * DC], euc[:, DC:2 * DC], 1.0, None, op0=ALU.add),
                 waits=_now(s_dve), inc=s_dve)

        with contextlib.ExitStack() as ph:
            vr = ph.enter_context(nc.sbuf_tensor("vr", [1, 7 * TL], F32))
            mubc = ph.enter_context(nc.sbuf_tensor("mubc", [128, TL], F32))
            rbc = ph.enter_context(nc.sbuf_tensor("rbc", [128, TL], F32))
            eptA = ph.enter_context(nc.sbuf_tensor("eptA", [128, DC * 64], F32))
            eptB = ph.enter_context(nc.sbuf_tensor("eptB", [128, DC * 64], F32))
            offwt = ph.enter_context(nc.sbuf_tensor("offwt", [128, NB * 64], F32))
            rwkvb = ph.enter_context(nc.sbuf_tensor("rwkvb", [128, DC * TL], BF16))
            wkvs = ph.enter_context(nc.sbuf_tensor("wkvs", [128, DC * TL], BF16))
            rbuf = ph.enter_context(nc.sbuf_tensor("rbuf", [128, TL], BF16))
            numt = ph.enter_context(nc.sbuf_tensor("numt", [128, TL], F32))
            dent = ph.enter_context(nc.sbuf_tensor("dent", [128, TL], F32))
            offc = ph.enter_context(nc.sbuf_tensor("offc", [128, 2 * NB], F32))
            tmpo = ph.enter_context(nc.sbuf_tensor("tmpo", [128, 64], F32))
            V = lambda i: vr[0:1, i * TL:(i + 1) * TL]
            Vh = lambda i, h: vr[0:1, i * TL + H2[h].start:i * TL + H2[h].stop]
            for c in range(DC):
                ctx.emit("sy", lambda s, c=c: s.dma_start(out=eptA[:, c * 64:(c + 1) * 64], in_=epta[c, :, :]), inc=d_cn, inck=16)
                ctx.emit("sy", lambda s, c=c: s.dma_start(out=eptB[:, c * 64:(c + 1) * 64], in_=eptb[c, :, :]), inc=d_cn, inck=16)
            for j in range(NB):
                ctx.emit("sy", lambda s, j=j: s.dma_start(out=offwt[:, j * 64:(j + 1) * 64], in_=offw[j, :, :]), inc=d_cn, inck=16)
            wk_marks = []
            for c in range(DC):
                ba = abuf[c % 2]
                bb = bbuf[c % 2]
                wts = ([(s_dve, wk_marks[c - 2])] if c >= 2 else [])
                na = ctx.emit("sy", lambda s, c=c, ba=ba: s.dma_start(out=ba[:], in_=a_in[c, :, :]),
                              waits=wts, inc=d_ld, inck=16)
                nb = ctx.emit("sy", lambda s, c=c, bb=bb: s.dma_start(out=bb[:], in_=b_in[c, :, :]),
                              inc=d_ld, inck=16)
                for j in range(NB):
                    ctx.emit("dve", lambda v, c=c, j=j: v.tensor_tensor(
                        tmpo[:], eptA[:, c * 64:(c + 1) * 64], offwt[:, j * 64:(j + 1) * 64], op=ALU.mult),
                        waits=[(d_cn, d_cn.n)] + _now(s_dve), inc=s_dve)
                    ctx.emit("dve", lambda v, j=j: v.tensor_reduce(
                        offc[:, j:j + 1], tmpo[:], axis=AX.X, op=ALU.add),
                        waits=_now(s_dve), inc=s_dve)
                    ctx.emit("dve", lambda v, c=c, j=j: v.tensor_tensor(
                        tmpo[:], eptB[:, c * 64:(c + 1) * 64], offwt[:, j * 64:(j + 1) * 64], op=ALU.mult),
                        waits=_now(s_dve), inc=s_dve)
                    ctx.emit("dve", lambda v, j=j: v.tensor_reduce(
                        offc[:, NB + j:NB + j + 1], tmpo[:], axis=AX.X, op=ALU.add),
                        waits=_now(s_dve), inc=s_dve)
                eu_c = euc[:, c:c + 1]
                ieu_c = euc[:, DC + c:DC + c + 1]
                ctx.emit("dve", lambda v, ba=ba, eu_c=eu_c: v.tensor_scalar(
                    numt[:], ba[:], eu_c, None, op0=ALU.mult),
                    waits=[(d_ld, na)] + _now(s_dve, s_act), inc=s_dve)
                ctx.emit("dve", lambda v, ba=ba, ieu_c=ieu_c: v.scalar_tensor_tensor(
                    numt[:, 1:TL], ba[:, 0:TL - 1], ieu_c, numt[:, 1:TL], op0=ALU.mult, op1=ALU.add),
                    waits=_now(s_dve), inc=s_dve)
                ctx.emit("dve", lambda v, bb=bb, eu_c=eu_c: v.tensor_scalar(
                    dent[:], bb[:], eu_c, None, op0=ALU.mult),
                    waits=[(d_ld, nb)] + _now(s_dve), inc=s_dve)
                ctx.emit("dve", lambda v, bb=bb, ieu_c=ieu_c: v.scalar_tensor_tensor(
                    dent[:, 1:TL], bb[:, 0:TL - 1], ieu_c, dent[:, 1:TL], op0=ALU.mult, op1=ALU.add),
                    waits=_now(s_dve), inc=s_dve)
                offa3 = offc[:, 0:NB].unsqueeze(2).broadcast_to([128, NB, 128])
                offb3 = offc[:, NB:2 * NB].unsqueeze(2).broadcast_to([128, NB, 128])
                ctx.emit("dve", lambda v, offa3=offa3: v.tensor_tensor(
                    numt[:].rearrange("p (j r) -> p j r", r=128), numt[:].rearrange("p (j r) -> p j r", r=128),
                    offa3, op=ALU.add), waits=_now(s_dve), inc=s_dve)
                ctx.emit("dve", lambda v, offb3=offb3: v.tensor_tensor(
                    dent[:].rearrange("p (j r) -> p j r", r=128), dent[:].rearrange("p (j r) -> p j r", r=128),
                    offb3, op=ALU.add), waits=_now(s_dve), inc=s_dve)
                ctx.emit("act", lambda a: a.activation(sq[:], dent[:], AF.Ln),
                         waits=_now(s_dve, s_tn), inc=s_act)
                ctx.emit("act", lambda a: a.activation(sq[:], sq[:], AF.Exp, scale=-1.0),
                         waits=_now(s_act), inc=s_act)
                mw = ctx.emit("dve", lambda v, c=c: v.tensor_tensor(
                    wkvs[:, c * TL:(c + 1) * TL], numt[:], sq[:], op=ALU.mult),
                    waits=_now(s_act), inc=s_dve)
                wk_marks.append(mw)
                na2 = ctx.emit("act", lambda a, c=c: a.activation(sq[:], wkvs[:, c * TL:(c + 1) * TL], AF.Square),
                               waits=[(s_dve, mw)] + _now(s_tn), inc=s_act)
                for h in range(2):
                    ctx.emit("tn", lambda t, c=c, h=h: t.matmul(
                        P[h][0:1, :], ones1b[:], wkvs[:, c * TL + H2[h].start:c * TL + H2[h].stop],
                        start=(c == 0), stop=(c == DC - 1)),
                        waits=[(s_act, na2)] + ([(s_dve, n_const)] if c == 0 else []))
                    ctx.emit("tn", lambda t, c=c, h=h: t.matmul(
                        P[2 + h][0:1, :], ones1[:], sq[:, H2[h]],
                        start=(c == 0), stop=(c == DC - 1)),
                        waits=[(s_act, na2)], inc=(s_tn if h == 1 else None))
            for h in range(2):
                ctx.emit("dve", lambda v, h=h: v.tensor_copy(Vh(0, h), P[h][0:1, :]),
                         waits=_now(s_tn), inc=s_dve)
                ctx.emit("dve", lambda v, h=h: v.tensor_copy(Vh(1, h), P[2 + h][0:1, :]),
                         waits=_now(s_tn), inc=s_dve)
            ctx.emit("dve", lambda v: v.tensor_scalar(V(2), V(0), 1.0 / D, None, op0=ALU.mult),
                     waits=_now(s_dve), inc=s_dve)
            ctx.emit("dve", lambda v: v.tensor_scalar(V(1), V(1), 1.0 / D, None, op0=ALU.mult),
                     waits=_now(s_dve), inc=s_dve)
            ctx.emit("dve", lambda v: v.tensor_tensor(V(4), V(2), V(2), op=ALU.mult),
                     waits=_now(s_dve), inc=s_dve)
            ctx.emit("dve", lambda v: v.tensor_tensor(V(5), V(1), V(4), op=ALU.subtract),
                     waits=_now(s_dve), inc=s_dve)
            ctx.emit("act", lambda a: a.activation(V(6), V(5), AF.Ln, bias=epsc[0:1, 0:1]),
                     waits=_now(s_dve), inc=s_act)
            ctx.emit("act", lambda a: a.activation(V(3), V(6), AF.Exp, scale=-0.5),
                     waits=_now(s_act), inc=s_act)
            bcast_row(V(2), mubc, 4, _now(s_act))
            bcast_row(V(3), rbc, 4, _now(s_act))

            for c in range(DC):
                rb = rbuf
                nr = ctx.emit("sy", lambda s, c=c: s.dma_start(out=rbuf[:], in_=r_in[c, :, :]),
                              waits=(_now(s_dve) if c > 0 else _now(s_dve)), inc=d_ld, inck=16)
                ctx.emit("dve", lambda v, c=c: v.tensor_tensor(
                    sq[:], wkvs[:, c * TL:(c + 1) * TL], mubc[:], op=ALU.subtract),
                    waits=_now(s_dve) + [(d_ld, nr)], inc=s_dve)
                ctx.emit("dve", lambda v: v.tensor_tensor(sq[:], sq[:], rbc[:], op=ALU.mult),
                         waits=_now(s_dve), inc=s_dve)
                ctx.emit("dve", lambda v, c=c: v.tensor_scalar(
                    sq[:], sq[:], euc[:, 2 * DC + c:2 * DC + c + 1], euc[:, 3 * DC + c:3 * DC + c + 1],
                    op0=ALU.mult, op1=ALU.add),
                    waits=_now(s_dve), inc=s_dve)
                ctx.emit("dve", lambda v, c=c: v.tensor_tensor(
                    rwkvb[:, c * TL:(c + 1) * TL], sq[:], rbuf[:], op=ALU.mult),
                    waits=_now(s_dve), inc=s_dve)
            ctx.barrier()
            warena = ph.enter_context(nc.sbuf_tensor("warena", [128, 4096], BF16))
            wob = [warena[:, 0:2048], warena[:, 2048:4096]]
            x1t = ph.enter_context(nc.sbuf_tensor("x1t", [128, TL], F32))
            gbuf = ph.enter_context(nc.sbuf_tensor("gbuf", [128, TL], F32))
            s0_marks = []
            for co in range(DC):
                b = co % 2
                wts = [(s_tn, s0_marks[co - 2])] if co >= 2 else []
                nw = ctx.emit("gp", lambda g, co=co, b=b: g.dma_start(out=wob[b][:], in_=wo[co, :, :]),
                              waits=wts, inc=dws[b], inck=16)
                nx = ctx.emit("sy", lambda s, co=co: s.dma_start(out=abuf[0][:], in_=xfm[co, :, :]),
                              waits=(_now(s_dve) if co > 0 else []), inc=d_ld, inck=16)
                ng = ctx.emit("sy", lambda s, co=co: s.dma_start(out=gbuf[:], in_=g0_in[co, :, :]),
                              waits=(_now(s_dve) if co > 0 else []), inc=d_l2, inck=16)
                mlast = None
                for th in range(2):
                    po = P[4 + th]
                    for dc in range(DC):
                        mlast = ctx.emit("tn", lambda t, b=b, dc=dc, th=th, po=po: t.matmul(
                            po[:, :], wob[b][:, dc * 128:(dc + 1) * 128],
                            rwkvb[:, dc * TL + H2[th].start:dc * TL + H2[th].stop],
                            start=(dc == 0), stop=(dc == DC - 1)),
                            waits=([(dws[b], nw)] + _now(s_dve) if dc == 0 else []),
                            inc=(s_tn if dc == DC - 1 else None))
                    ctx.emit("dve", lambda v, th=th, po=po: v.tensor_tensor(
                        x1t[:, H2[th]], po[:, :], gbuf[:, H2[th]], op=ALU.mult),
                        waits=[(s_tn, mlast), (d_l2, ng), (d_ld, nx), (d_st, d_st.n)] + _now(s_act), inc=s_dve)
                    ctx.emit("dve", lambda v, th=th: v.tensor_tensor(
                        x1t[:, H2[th]], x1t[:, H2[th]], abuf[0][:, H2[th]], op=ALU.add),
                        waits=_now(s_dve), inc=s_dve)
                s0_marks.append(mlast)
                ctx.emit("sy", lambda s, co=co: s.dma_start(out=x1o[co, :, :], in_=x1t[:]),
                         waits=_now(s_dve), inc=d_st, inck=16)
                na2 = ctx.emit("act", lambda a: a.activation(sq[:], x1t[:], AF.Square),
                               waits=_now(s_dve, s_tn), inc=s_act)
                for h in range(2):
                    ctx.emit("tn", lambda t, co=co, h=h: t.matmul(
                        P[h][0:1, :], ones1[:], x1t[:, H2[h]], start=(co == 0), stop=(co == DC - 1)),
                        waits=[(s_act, na2)] + ([(s_dve, n_const)] if co == 0 else []))
                    ctx.emit("tn", lambda t, co=co, h=h: t.matmul(
                        P[2 + h][0:1, :], ones1[:], sq[:, H2[h]], start=(co == 0), stop=(co == DC - 1)),
                        waits=[(s_act, na2)], inc=(s_tn if h == 1 else None))
            for h in range(2):
                ctx.emit("dve", lambda v, h=h: v.tensor_copy(Vh(0, h), P[h][0:1, :]),
                         waits=_now(s_tn), inc=s_dve)
                ctx.emit("dve", lambda v, h=h: v.tensor_copy(Vh(1, h), P[2 + h][0:1, :]),
                         waits=_now(s_tn), inc=s_dve)
            ctx.emit("dve", lambda v: v.tensor_scalar(V(2), V(0), 1.0 / D, None, op0=ALU.mult),
                     waits=_now(s_dve), inc=s_dve)
            ctx.emit("dve", lambda v: v.tensor_scalar(V(1), V(1), 1.0 / D, None, op0=ALU.mult),
                     waits=_now(s_dve), inc=s_dve)
            ctx.emit("dve", lambda v: v.tensor_tensor(V(4), V(2), V(2), op=ALU.mult),
                     waits=_now(s_dve), inc=s_dve)
            ctx.emit("dve", lambda v: v.tensor_tensor(V(5), V(1), V(4), op=ALU.subtract),
                     waits=_now(s_dve), inc=s_dve)
            ctx.emit("act", lambda a: a.activation(V(6), V(5), AF.Ln, bias=epsc[0:1, 0:1]),
                     waits=_now(s_dve), inc=s_act)
            ctx.emit("act", lambda a: a.activation(V(3), V(6), AF.Exp, scale=-0.5),
                     waits=_now(s_act), inc=s_act)
            bcast_row(V(2), mubc, 4, _now(s_act))
            bcast_row(V(3), rbc, 4, _now(s_act))
            ctx.barrier()
            xa_marks = []
            for c in range(DC):
                buf = abuf[c % 2]
                wts = [(s_dve, xa_marks[c - 2])] if c >= 2 else []
                n = ctx.emit("sy", lambda s, c=c, buf=buf: s.dma_start(out=buf[:], in_=x1o[c, :, :]),
                             waits=wts, inc=d_ld, inck=16)
                ctx.emit("dve", lambda v, buf=buf: v.tensor_tensor(sq[:], buf[:], mubc[:], op=ALU.subtract),
                         waits=[(d_ld, n)] + _now(s_dve), inc=s_dve)
                m = ctx.emit("dve", lambda v, c=c: v.tensor_tensor(
                    xn1[:, c * TL:(c + 1) * TL], sq[:], rbc[:], op=ALU.mult),
                    waits=_now(s_dve), inc=s_dve)
                xa_marks.append(m)
                ctx.emit("sy", lambda s, c=c: s.dma_start(
                    out=xn1o[c, :, :], in_=xn1[:, c * TL:(c + 1) * TL]),
                    waits=[(s_dve, m)], inc=d_st, inck=16)
            ctx.barrier()
        with contextlib.ExitStack() as ph:
            cost = ph.enter_context(nc.sbuf_tensor("cost", [128, TL], F32))
            sint = ph.enter_context(nc.sbuf_tensor("sint", [128, TL], F32))
            wsl = ph.enter_context(nc.sbuf_tensor("wsl", [128, 2048], BF16))
            ksf = ph.enter_context(nc.sbuf_tensor("ksf", [128, TL], F32))
            kgf = ph.enter_context(nc.sbuf_tensor("kgf", [128, TL], F32))
            kgb = ph.enter_context(nc.sbuf_tensor("kgb", [128, TL], BF16))
            kgr = ph.enter_context(nc.sbuf_tensor("kgr", [128, TL], F32))
            p16 = ph.enter_context(nc.sbuf_tensor("p16", [128, 16], F32))
            vtm = ph.enter_context(nc.sbuf_tensor("vtm", [128, 256], F32))
            vtmb = ph.enter_context(nc.sbuf_tensor("vtmb", [128, 256], BF16))
            v2s = ph.enter_context(nc.sbuf_tensor("v2s", [2, 256], F32))
            ctx.emit("sy", lambda s: s.dma_start(out=cost[:], in_=cosk[:, :]), inc=d_cn, inck=16)
            ctx.emit("sy", lambda s: s.dma_start(out=sint[:], in_=sink[:, :]), inc=d_cn, inck=16)
            nw = ctx.emit("gp", lambda g: g.dma_start(out=wsl[:], in_=wks[:, :]), inc=d_g0, inck=16)
            ml = None
            for th in range(2):
                for dc in range(DC):
                    ml = ctx.emit("tn", lambda t, dc=dc, th=th: t.matmul(
                        P[th][:, :], wsl[:, dc * 128:(dc + 1) * 128],
                        xn1[:, dc * TL + H2[th].start:dc * TL + H2[th].stop],
                        start=(dc == 0), stop=(dc == DC - 1)),
                        waits=([(d_g0, nw)] if dc == 0 else []), inc=(s_tn if dc == DC - 1 else None))
                ctx.emit("dve", lambda v, th=th: v.tensor_copy(ksf[:, H2[th]], P[th][:, :]),
                         waits=[(s_tn, ml)], inc=s_dve)
            ctx.emit("dve", lambda v: v.tensor_reduce(
                p16[:], ksf[:].rearrange("p (j r) -> p j r", r=64), axis=AX.X, op=ALU.add),
                waits=_now(s_dve), inc=s_dve)
            ctx.emit("dve", lambda v: v.tensor_scalar(p16[:], p16[:], 1.0 / 64, None, op0=ALU.mult),
                     waits=_now(s_dve), inc=s_dve)
            ctx.emit("sy", lambda s: s.dma_start(out=ks2o[:, :], in_=p16[:, 0:14]),
                     waits=_now(s_dve), inc=d_st, inck=16)
            ctx.emit("sy", lambda s: s.dma_start(out=ksro[:, :], in_=ksf[:, 896:1024]),
                     waits=_now(s_dve), inc=d_st, inck=16)
            nw = ctx.emit("gp", lambda g: g.dma_start(out=wsl[:], in_=wkg[:, :]),
                          waits=_now(s_tn), inc=d_g0, inck=16)
            for th in range(2):
                for dc in range(DC):
                    ml = ctx.emit("tn", lambda t, dc=dc, th=th: t.matmul(
                        P[th][:, :], wsl[:, dc * 128:(dc + 1) * 128],
                        xn1[:, dc * TL + H2[th].start:dc * TL + H2[th].stop],
                        start=(dc == 0), stop=(dc == DC - 1)),
                        waits=([(d_g0, nw), (d_st, d_st.n)] + _now(s_dve) if dc == 0 else []),
                        inc=(s_tn if dc == DC - 1 else None))
                ctx.emit("dve", lambda v, th=th: v.tensor_copy(kgf[:, H2[th]], P[th][:, :]),
                         waits=[(s_tn, ml)], inc=s_dve)
                ctx.emit("dve", lambda v, th=th: v.tensor_copy(kgb[:, H2[th]], kgf[:, H2[th]]),
                         waits=_now(s_dve), inc=s_dve)
                ns = ctx.emit("tn", lambda t, th=th: t.matmul(
                    P[2 + th][:, :], swpb[:], kgb[:, H2[th]], start=True, stop=True),
                    waits=_now(s_dve) + [(d_gs, d_gs.n)], inc=s_tn)
                ctx.emit("dve", lambda v, th=th: v.tensor_tensor(
                    kgr[:, H2[th]], kgf[:, H2[th]], cost[:, H2[th]], op=ALU.mult),
                    waits=[(s_tn, ns), (d_cn, d_cn.n)], inc=s_dve)
                ctx.emit("dve", lambda v, th=th: v.tensor_tensor(
                    sq[:, H2[th]], P[2 + th][:, :], sint[:, H2[th]], op=ALU.mult),
                    waits=[(s_tn, ns)], inc=s_dve)
                ctx.emit("dve", lambda v, th=th: v.tensor_tensor(
                    kgr[:, H2[th]], kgr[:, H2[th]], sq[:, H2[th]], op=ALU.add),
                    waits=_now(s_dve), inc=s_dve)
            ctx.emit("dve", lambda v: v.tensor_reduce(
                p16[:], kgr[:].rearrange("p (j r) -> p j r", r=64), axis=AX.X, op=ALU.add),
                waits=_now(s_dve), inc=s_dve)
            ctx.emit("dve", lambda v: v.tensor_scalar(p16[:], p16[:], 1.0 / 64, None, op0=ALU.mult),
                     waits=_now(s_dve), inc=s_dve)
            ctx.emit("sy", lambda s: s.dma_start(out=kg2o[:, :], in_=p16[:, 0:14]),
                     waits=_now(s_dve), inc=d_st, inck=16)
            ctx.emit("sy", lambda s: s.dma_start(out=kgro[:, :], in_=kgr[:, 896:1024]),
                     waits=_now(s_dve), inc=d_st, inck=16)
            nw = ctx.emit("gp", lambda g: g.dma_start(out=wsl[:, 0:2048], in_=wv1[0, :, :]),
                          waits=_now(s_tn), inc=d_g0, inck=16)
            nw2 = ctx.emit("gp", lambda g: g.dma_start(out=kgf[:].bitcast(BF16)[:, 0:2048], in_=wv1[1, :, :]),
                           waits=_now(s_dve, s_tn) + [(d_st, d_st.n)], inc=d_g1, inck=16)
            for tc in range(8):
                ml = None
                for oc in range(2):
                    wv_sl = wsl if oc == 0 else kgf.bitcast(BF16)[:, 0:2048]
                    for dc in range(DC):
                        ml = ctx.emit("tn", lambda t, tc=tc, oc=oc, dc=dc, wv_sl=wv_sl: t.matmul(
                            P[4 + oc][:, 0:128], xn1[:, dc * TL + tc * 128:dc * TL + (tc + 1) * 128],
                            wv_sl[:, dc * 128:(dc + 1) * 128],
                            start=(dc == 0), stop=(dc == DC - 1)),
                            waits=([(d_g0, nw), (d_g1, nw2)] + _now(s_dve) if dc == 0 else []),
                            inc=(s_tn if dc == DC - 1 else None))
                for oc in range(2):
                    ctx.emit("dve", lambda v, oc=oc: v.tensor_copy(vtm[:, oc * 128:(oc + 1) * 128], P[4 + oc][:, 0:128]),
                             waits=[(s_tn, ml)] + _now(s_dve), inc=s_dve)
                ctx.emit("dve", lambda v: v.tensor_copy(vtmb[:], vtm[:, 0:256]),
                         waits=_now(s_dve), inc=s_dve)
                if tc < 7:
                    np_ = ctx.emit("tn", lambda t: t.matmul(
                        P[6][0:2, 0:256], poolt[:], vtmb[:], start=True, stop=True),
                        waits=_now(s_dve) + [(d_gs, d_gs.n)], inc=s_tn)
                    ctx.emit("dve", lambda v: v.tensor_copy(v2s[:, :], P[6][0:2, 0:256]),
                             waits=[(s_tn, np_), (d_st, d_st.n)], inc=s_dve)
                    ctx.emit("sy", lambda s, tc=tc: s.dma_start(out=v2o[2 * tc:2 * tc + 2, :], in_=v2s[:, :]),
                             waits=_now(s_dve), inc=d_st, inck=16)
                else:
                    ctx.emit("sy", lambda s: s.dma_start(out=vro[:, :], in_=vtm[:, 0:256]),
                             waits=_now(s_dve), inc=d_st, inck=16)
            ctx.barrier()

        with contextlib.ExitStack() as ph:
            Gflat = ph.enter_context(nc.sbuf_tensor("Gflat", [1, 18 * TL], BF16))
            gm2 = ph.enter_context(nc.sbuf_tensor("gm2", [128, 16], F32))
            hg = ph.enter_context(nc.sbuf_tensor("hg", [128, 3 * 11 * TL], BF16))
            w1b = ph.enter_context(nc.sbuf_tensor("w1b", [128, 4096], BF16))
            w2b = ph.enter_context(nc.sbuf_tensor("w2b", [128, 2816 + 36], BF16))
            b1t = ph.enter_context(nc.sbuf_tensor("b1t", [128, 18 * 11], F32))
            hs1 = ph.enter_context(nc.sbuf_tensor("hs1", [128, 512], F32))
            gbc = ph.enter_context(nc.sbuf_tensor("gbc", [128, TL], BF16))
            g16 = ph.enter_context(nc.sbuf_tensor("g16", [128, 16], F32))
            gm = ph.enter_context(nc.sbuf_tensor("gm", [128, 16], F32))
            gv = ph.enter_context(nc.sbuf_tensor("gv", [128, 3], F32))
            zst = ph.enter_context(nc.sbuf_tensor("zst2", [128, TL], F32))
            w1bb = [w1b[:, 0:2048], w1b[:, 2048:4096]]
            w2bb = [w2b[:, 0:1408], w2b[:, 1408:2816]]
            b2sb = ph.enter_context(nc.sbuf_tensor("b2sb", [18, 128], BF16))
            for e in range(18):
                ctx.emit("sy", lambda s, e=e: s.dma_start(out=b1t[:, e * 11:(e + 1) * 11], in_=b1all[e, :, :]),
                         inc=d_cn, inck=16)
            n_b1 = d_cn.n
            for tc in range(8):
                ml = None
                for dc in range(DC):
                    ml = ctx.emit("tn", lambda t, tc=tc, dc=dc: t.matmul(
                        P[6][:, 0:16], xn1[:, dc * TL + tc * 128:dc * TL + (tc + 1) * 128],
                        centb[:, dc * 16:(dc + 1) * 16],
                        start=(dc == 0), stop=(dc == DC - 1)),
                        waits=(_now(s_dve) + [(d_gs, d_gs.n)] if dc == 0 else []),
                        inc=(s_tn if dc == DC - 1 else None))
                na = ctx.emit("act", lambda a: a.activation(g16[:], P[6][:, 0:16], AF.Sigmoid),
                              waits=[(s_tn, ml)] + _now(s_dve), inc=s_act)
                ctx.emit("dve", lambda v: v.tensor_tensor(g16[:], g16[:], cbt[:], op=ALU.add),
                         waits=[(s_act, na), (d_cn, d_cn.n)], inc=s_dve)
                ctx.emit("dve", lambda v: v.tensor_copy(gm[:], g16[:]), waits=_now(s_dve), inc=s_dve)
                for it in range(4):
                    ctx.emit("dve", lambda v: v.reduce_max(gv[:, 0:1], gm[:], axis=AX.X),
                             waits=_now(s_dve), inc=s_dve)
                    if it < 3:
                        ctx.emit("dve", lambda v: v.tensor_scalar(gm2[:], gm[:], gv[:, 0:1], None, op0=ALU.is_ge),
                                 waits=_now(s_dve), inc=s_dve)
                        ctx.emit("dve", lambda v: v.scalar_tensor_tensor(gm[:], gm2[:], -1e30, gm[:], op0=ALU.mult, op1=ALU.add),
                                 waits=_now(s_dve), inc=s_dve)
                ctx.emit("dve", lambda v: v.tensor_scalar(gm[:], g16[:], gv[:, 0:1], None, op0=ALU.is_ge),
                         waits=_now(s_dve), inc=s_dve)
                ctx.emit("dve", lambda v: v.tensor_tensor(gm[:], gm[:], g16[:], op=ALU.mult),
                         waits=_now(s_dve), inc=s_dve)
                ctx.emit("dve", lambda v: v.tensor_reduce(gv[:, 1:2], gm[:], axis=AX.X, op=ALU.add),
                         waits=_now(s_dve), inc=s_dve)
                ctx.emit("dve", lambda v: v.tensor_scalar(gv[:, 1:2], gv[:, 1:2], 1e-8, None, op0=ALU.max),
                         waits=_now(s_dve), inc=s_dve)
                ctx.emit("dve", lambda v: v.reciprocal(gv[:, 2:3], gv[:, 1:2]),
                         waits=_now(s_dve), inc=s_dve)
                mg = ctx.emit("dve", lambda v: v.tensor_scalar(gm[:], gm[:], gv[:, 2:3], None, op0=ALU.mult),
                              waits=_now(s_dve), inc=s_dve)
                nt = ctx.emit("tn", lambda t: t.transpose(
                    P[7][0:16, 0:128], gm[:, 0:16], identf[:]),
                    waits=[(s_dve, mg), (d_cn, d_cn.n)] + _now(s_act), inc=s_tn)
                ctx.emit("dve", lambda v, tc=tc: v.tensor_copy(
                    G18[0:16, tc * 128:(tc + 1) * 128], P[7][0:16, 0:128]),
                    waits=[(s_tn, nt)], inc=s_dve)
            ctx.emit("sy", lambda s: s.dma_start(out=G18[16:18, :], in_=onestl2[:, :]),
                     waits=_now(s_dve), inc=d_cn, inck=16)
            n_g18 = ctx.emit("sy", lambda s: s.dma_start(
                out=Gflat[0:1, :].rearrange("o (e t) -> o e t", t=TL), in_=G18[:, :]),
                waits=[(d_cn, d_cn.n)] + _now(s_dve), inc=d_cn, inck=16)
            ctx.barrier()
            first_grp = True
            for gi, grp in enumerate(GROUPS):
                for ei, e in enumerate(grp):
                    routed = e < 16
                    if routed:
                        for h in range(2):
                            nb_ = ctx.emit("tn", lambda t, e=e, h=h: t.matmul(
                                P[6][:, :], onesrow_bf[:], Gflat[0:1, e * TL + H2[h].start:e * TL + H2[h].stop],
                                start=True, stop=True),
                                waits=[(d_cn, n_g18)] + _now(s_dve, s_act), inc=s_tn)
                            ctx.emit("dve", lambda v, h=h: v.tensor_copy(gbc[:, H2[h]], P[6][:, :]),
                                     waits=[(s_tn, nb_)], inc=s_dve)
                    for fc in range(11):
                        b = (ei * 11 + fc) % 2
                        wts = _now(s_tn) if (gi > 0 or ei > 0 or fc >= 2) else []
                        nw = ctx.emit("gp", lambda g, e=e, fc=fc, b=b, routed=routed: g.dma_start(
                            out=w1bb[b][:], in_=(rw1[e, fc, :, :] if routed else sw1[e - 16, fc, :, :])),
                            waits=wts, inc=dws[b], inck=16)
                        ml = None
                        for th in range(2):
                            for dc in range(DC):
                                ml = ctx.emit("tn", lambda t, b=b, dc=dc, th=th: t.matmul(
                                    P[th][:, :], w1bb[b][:, dc * 128:(dc + 1) * 128],
                                    xn1[:, dc * TL + H2[th].start:dc * TL + H2[th].stop],
                                    start=(dc == 0), stop=(dc == DC - 1)),
                                    waits=([(dws[b], nw)] + _now(s_act, s_dve) if dc == 0 else []),
                                    inc=(s_tn if dc == DC - 1 else None))
                            if routed:
                                na = ctx.emit("act", lambda a, th=th, e=e, fc=fc: a.activation(
                                    hs1[:], P[th][:, :], AF.Relu, bias=b1t[:, e * 11 + fc:e * 11 + fc + 1]),
                                    waits=[(s_tn, ml), (d_cn, n_b1)] + _now(s_dve), inc=s_act)
                                ctx.emit("dve", lambda v, ei=ei, fc=fc, th=th: v.tensor_tensor(
                                    hg[:, (ei * 11 + fc) * TL + H2[th].start:(ei * 11 + fc) * TL + H2[th].stop],
                                    hs1[:], gbc[:, H2[th]], op=ALU.mult),
                                    waits=[(s_act, na)], inc=s_dve)
                            else:
                                ctx.emit("act", lambda a, ei=ei, fc=fc, th=th, e=e: a.activation(
                                    hg[:, (ei * 11 + fc) * TL + H2[th].start:(ei * 11 + fc) * TL + H2[th].stop],
                                    P[th][:, :], AF.Relu, bias=b1t[:, e * 11 + fc:e * 11 + fc + 1]),
                                    waits=[(s_tn, ml), (d_cn, n_b1)] + _now(s_dve), inc=s_act)
                ctx.barrier()
                last = (gi == len(GROUPS) - 1)
                ym = []
                for do in range(DC):
                    ml = None
                    for ei, e in enumerate(grp):
                        routed = e < 16
                        b = ei % 2
                        nw = ctx.emit("gp", lambda g, e=e, do=do, b=b, routed=routed: g.dma_start(
                            out=w2bb[b][:], in_=(rw2[e, do, :, :] if routed else sw2[e - 16, do, :, :])),
                            waits=_now(s_tn), inc=dws[b], inck=16)
                        for th in range(2):
                            py = P[4 + th]
                            for fc in range(11):
                                is_stop = (not last) and (ei == len(grp) - 1) and (fc == 10)
                                ml = ctx.emit("tn", lambda t, b=b, ei=ei, fc=fc, th=th, py=py, is_stop=is_stop: t.matmul(
                                    py[:, :], w2bb[b][:, fc * 128:(fc + 1) * 128],
                                    hg[:, (ei * 11 + fc) * TL + H2[th].start:(ei * 11 + fc) * TL + H2[th].stop],
                                    start=(ei == 0 and fc == 0), stop=is_stop),
                                    waits=([(dws[b], nw), (d_ac, d_ac.n)] + _now(s_dve) if fc == 0 else []),
                                    inc=(s_tn if (fc == 10 and th == 1) else None))
                    if last:
                        nb2 = ctx.emit("gp", lambda g, do=do: g.dma_start(out=b2sb[:, :], in_=b2t[do, :, :]),
                                       waits=_now(s_tn), inc=d_gs, inck=16)
                        for th in range(2):
                            py = P[4 + th]
                            ctx.emit("tn", lambda t, th=th, py=py: t.matmul(
                                py[:, :], b2sb[:, :], G18[:, H2[th]], start=False, stop=False),
                                waits=[(d_gs, nb2), (d_cn, n_g18)])
                            ml = ctx.emit("tn", lambda t, do=do, th=th, py=py: t.matmul(
                                py[:, :], identb[:], xn1[:, do * TL + H2[th].start:do * TL + H2[th].stop],
                                start=False, stop=True),
                                waits=[], inc=s_tn)
                    ym.append(ml)
                    for th in range(2):
                        py = P[4 + th]
                        ctx.emit("dve", lambda v, th=th, py=py: v.tensor_copy(zst[:, H2[th]], py[:, :]),
                                 waits=[(s_tn, s_tn.n), (d_ac, d_ac.n)], inc=s_dve)
                    ctx.emit("gp", lambda g, do=do, first_grp=first_grp: (
                        g.dma_start(out=macc[do, :, :], in_=zst[:]) if first_grp else
                        g.dma_start(out=macc[do, :, :], in_=zst[:], accum_op=ALU.add)),
                        waits=_now(s_dve) + [(d_ac, d_ac.n)], inc=d_ac, inck=16)
                first_grp = False
                ctx.barrier()
            for do in range(DC):
                buf = abuf[do % 2]
                n = ctx.emit("sy", lambda s, do=do, buf=buf: s.dma_start(out=buf[:], in_=macc[do, :, :]),
                             waits=[(d_ac, d_ac.n)] + (_now(s_act) if do >= 2 else []), inc=d_ld, inck=16)
                na = ctx.emit("act", lambda a, buf=buf: a.activation(zst[:, 0:TL], buf[:], AF.Silu),
                              waits=[(d_ld, n), (d_st, d_st.n)], inc=s_act)
                ctx.emit("sy", lambda s, do=do: s.dma_start(out=g1o[do, :, :], in_=zst[:, 0:TL]),
                         waits=[(s_act, na)], inc=d_st, inck=16)
        for e in ENGINES:
            ctx.ops[e].append(([(d_st.h, d_st.n), (d_ac.h, d_ac.n)], None, None))
        ctx.finalize()
    return nc



def _swap_perm():
    P = np.zeros((128, 128), np.float32)
    for m in range(128):
        blk, d = m // 32, m % 32
        P[blk * 32 + ((d + 16) % 32), m] = 1.0
    return P


def _rope_tables(pos):
    invf = 1.0 / (10000.0 ** (np.arange(0, 32, 2, dtype=np.float64) / 32.0))
    ang = pos[None, :].astype(np.float64) * invf[:, None]
    cos = np.cos(ang).astype(np.float32)
    sin = np.sin(ang).astype(np.float32)
    C = np.zeros((128, len(pos)), np.float32)
    S = np.zeros((128, len(pos)), np.float32)
    for r in range(128):
        d = r % 32
        f = d % 16
        C[r] = cos[f]
        S[r] = sin[f] if d >= 16 else -sin[f]
    return C, S


def _prep_shared_l1(inp):
    f = lambda n: np.ascontiguousarray(np.asarray(inp[n], np.float32))
    out = {}
    out["wk"] = _tile_w(f("L0_attn_Wk").T)
    out["wv"] = _tile_w(f("L0_attn_Wv").T)
    out["wr"] = _tile_w(f("L0_attn_Wr").T)
    gwt = _tile_w(f("L0_ffn_gate_w").T)
    gw = np.zeros((128, 32), np.float32)
    for c in range(DC):
        gw[:, 2 * c:2 * c + 2] = gwt[0][:, c * 128:c * 128 + 2]
    out["gw"] = gw
    out["wg"] = np.stack([_tile_w(f("L0_ffn_w_gate")[e].T) for e in range(2)])
    out["wu"] = np.stack([_tile_w(f("L0_ffn_w_up")[e].T) for e in range(2)])
    out["wd"] = np.stack([_tile_w(f("L0_ffn_w_down")[e].T) for e in range(2)])
    out["td"] = _vec_pc(f("L0_attn_time_decay").ravel())
    out["tmk"] = _vec_pc(f("L0_attn_tm_k").ravel())
    out["tmv"] = _vec_pc(f("L0_attn_tm_v").ravel())
    out["tmr"] = _vec_pc(f("L0_attn_tm_r").ravel())
    out["ident"] = np.eye(128, dtype=np.float32)
    return out


def _launch1_inmaps(inp, shared):
    x = np.ascontiguousarray(np.asarray(inp["L0_attn_x"], np.float32)[0])
    maps = []
    for i in range(NC):
        toks = _core_tokens(i)
        xfm = np.ascontiguousarray(x[toks].T.reshape(DC, 128, TL))
        xbnd = np.zeros((NB, D), np.float32)
        for j in range(NB):
            g = i + 8 * j
            if g > 0:
                xbnd[j] = x[g * 128 - 1]
        m = {"xfm": xfm, "xbnd": xbnd}
        m.update(shared)
        maps.append(m)
    return maps


def _prep_shared_l2(inp):
    f = lambda n: np.ascontiguousarray(np.asarray(inp[n], np.float32))
    out = {}
    out["wo"] = _tile_w(f("L0_attn_Wo").T)
    out["wks"] = _tile_w(f("L1_attn_Wk_sem"))[0]
    out["wkg"] = _tile_w(f("L1_attn_Wk_geo"))[0]
    out["wv1"] = _tile_w(f("L1_attn_Wv"))
    out["swp"] = _swap_perm()
    out["ident"] = np.eye(128, dtype=np.float32)
    pm = np.zeros((128, 2), np.float32)
    pm[0:64, 0] = 1.0 / 64
    pm[64:128, 1] = 1.0 / 64
    out["poolm"] = pm
    out["sw1"] = np.stack([_tile_w(f("L1_ffn_shared_w1")[n]) for n in range(2)])
    out["sw2"] = np.stack([_tile_w(f("L1_ffn_shared_w2")[n]) for n in range(2)])
    out["rw1"] = np.stack([_tile_w(f("L1_ffn_routed_w1")[n]) for n in range(16)])
    out["rw2"] = np.stack([_tile_w(f("L1_ffn_routed_w2")[n]) for n in range(16)])
    b1 = np.concatenate([f("L1_ffn_routed_b1"), f("L1_ffn_shared_b1")], 0)
    out["b1all"] = np.ascontiguousarray(
        b1.reshape(18, 11, 128).transpose(0, 2, 1))
    b2 = np.concatenate([f("L1_ffn_routed_b2"), f("L1_ffn_shared_b2")], 0)
    out["b2t"] = np.ascontiguousarray(b2.reshape(18, DC, 128).transpose(1, 0, 2))
    cen = f("L1_ffn_centroids")
    out["cent"] = np.ascontiguousarray(
        cen.reshape(16, DC, 128).transpose(2, 1, 0).reshape(128, DC * 16))
    out["cbias"] = np.ascontiguousarray(np.broadcast_to(f("L1_ffn_bias"), (128, 16)))
    out["tf"] = _vec_pc(f("L0_attn_time_first").ravel())
    out["lng"] = _vec_pc(f("L0_attn_ln_g").ravel())
    out["lnb"] = _vec_pc(f("L0_attn_ln_b").ravel())
    return out


def _launch2_inmaps(inp, shared2, l1res):
    x = np.ascontiguousarray(np.asarray(inp["L0_attn_x"], np.float32)[0])
    epta = np.zeros((DC, 128, 64), np.float32)
    eptb = np.zeros((DC, 128, 64), np.float32)
    for g in range(64):
        c, j = g % 8, g // 8
        epta[:, :, g] = np.asarray(l1res[c]["a_out"])[:, :, j * 128 + 127]
        eptb[:, :, g] = np.asarray(l1res[c]["b_out"])[:, :, j * 128 + 127]
    maps = []
    for i in range(NC):
        toks = _core_tokens(i)
        xfm = np.ascontiguousarray(x[toks].T.reshape(DC, 128, TL))
        offw = np.zeros((NB, 128, 64), np.float32)
        for j in range(NB):
            gj = i + 8 * j
            for cp in range(8):
                J = (gj - 1 - cp) // 8
                if cp < gj and cp != i:
                    offw[j, :, cp + 8 * J] = 1.0
        C, S = _rope_tables(toks)
        m = {
            "xfm": xfm,
            "a_in": np.asarray(l1res[i]["a_out"]),
            "b_in": np.asarray(l1res[i]["b_out"]),
            "r_in": np.asarray(l1res[i]["r_out"]),
            "g0_in": np.asarray(l1res[i]["g0_out"]),
            "epta": epta, "eptb": eptb, "offw": offw,
            "cosk": C, "sink": S,
        }
        m.update(shared2)
        maps.append(m)
    return maps



def build_launch3():
    nc = bass.Bass("TRN2", target_bir_lowering=False, debug=False)
    ein = lambda n, sh, dt: nc.dram_tensor(n, sh, dt, kind="ExternalInput").ap()
    eout = lambda n, sh, dt: nc.dram_tensor(n, sh, dt, kind="ExternalOutput").ap()
    xn1i = ein("xn1i", [DC, 128, TL], BF16)
    x1i = ein("x1i", [DC, 128, TL], F32)
    g1i = ein("g1i", [DC, 128, TL], F32)
    wqs = ein("wqs", [4, 128, D], F32)
    wqg = ein("wqg", [4, 128, D], F32)
    wo1 = ein("wo1", [DC, 128, 1024], F32)
    lsrep = ein("lsrep", [128, 4], F32)
    swp = ein("swp", [128, 128], F32)
    ident = ein("ident", [128, 128], F32)
    cosq = ein("cosq", [128, TL], F32)
    sinq = ein("sinq", [128, TL], F32)
    kst = ein("kst", [4, 128, 112], F32)
    kgt = ein("kgt", [4, 128, 112], F32)
    ksr = ein("ksr", [4, 128, 1024], F32)
    kgr = ein("kgr", [4, 128, 1024], F32)
    vt = ein("vt", [112, 256], F32)
    vrr = ein("vrr", [8, 128, 256], F32)
    pmask = ein("pmask", [NB, 128, 112], F32)
    rmask = ein("rmask", [128, 1024], F32)
    fixm = ein("fixm", [128, 128], F32)
    outp = eout("outp", [DC, 128, TL], F32)

    es = contextlib.ExitStack()
    with es:
        ctx = Ctx(nc, es)
        sb = lambda n, sh, dt: es.enter_context(nc.sbuf_tensor(n, sh, dt))
        xn1 = sb("xn1", [128, DC * TL], BF16)
        identb = sb("identb", [128, 128], BF16)
        swpb = sb("swpb", [128, 128], BF16)
        cost = sb("cost", [128, TL], F32)
        sint = sb("sint", [128, TL], F32)
        sclc = sb("sclc", [128, 4], F32)
        ones1 = sb("ones1", [128, 1], F32)
        ones1b = sb("ones1b", [128, 1], BF16)
        kstb = sb("kstb", [128, 4 * 112], BF16)
        kgtb = sb("kgtb", [128, 4 * 112], BF16)
        ksrb = sb("ksrb", [128, 4 * TL], BF16)
        kgrb = sb("kgrb", [128, 4 * TL], BF16)
        vtb = sb("vtb", [112, 256], BF16)
        vrb = sb("vrb", [128, 8 * 256], BF16)
        pmt = sb("pmt", [128, NB * 112], F32)
        rmt = sb("rmt", [128, 1024], F32)
        fmt = sb("fmt", [128, 128], F32)
        vbarc = sb("vbarc", [128, 4], F32)
        qs = sb("qs", [128, 4 * TL], BF16)
        qg = sb("qg", [128, 4 * TL], BF16)
        qgf = sb("qgf", [128, TL], F32)
        att = sb("att", [128, 8 * TL], BF16)
        zp = sb("zp", [128, 112], F32)
        zr = sb("zr", [128, 1024], F32)
        ep = sb("ep", [128, 112], BF16)
        er = sb("er", [128, 1024], BF16)
        etp = sb("etp", [112, 128], BF16)
        etr = sb("etr", [128, 8 * 128], BF16)
        rowv = sb("rowv", [128, 8], F32)
        wsl = sb("wsl", [128, 2048], BF16)
        x1t = sb("x1t", [128, TL], F32)
        gbuf = sb("gbuf", [128, TL], F32)
        ot = sb("ot", [128, TL], F32)
        P = [es.enter_context(nc.psum_tensor(f"pb{i}", [128, 512], F32)) for i in range(8)]

        s_tn, s_act, s_dve = ctx.sem("s_tn"), ctx.sem("s_act"), ctx.sem("s_dve")
        d_cn = ctx.sem("d_cn", dma=True)
        d_ld = ctx.sem("d_ld", dma=True)
        d_l2 = ctx.sem("d_l2", dma=True)
        d_st = ctx.sem("d_st", dma=True)
        d_g0, d_g1 = ctx.sem("d_g0", dma=True), ctx.sem("d_g1", dma=True)
        d_gs = ctx.sem("d_gs", dma=True)
        dws = [d_g0, d_g1]
        H2 = (slice(0, 512), slice(512, 1024))

        ctx.emit("dve", lambda v: v.memset(ones1[:], 1.0), inc=s_dve)
        ctx.emit("dve", lambda v: v.memset(ones1b[:], 1.0), inc=s_dve)
        n_const = s_dve.n
        ctx.emit("sy", lambda s: s.dma_start(out=cost[:], in_=cosq[:, :]), inc=d_cn, inck=16)
        ctx.emit("sy", lambda s: s.dma_start(out=sint[:], in_=sinq[:, :]), inc=d_cn, inck=16)
        ctx.emit("sy", lambda s: s.dma_start(out=sclc[:], in_=lsrep[:, :]), inc=d_cn, inck=16)
        for j in range(NB):
            ctx.emit("sy", lambda s, j=j: s.dma_start(out=pmt[:, j * 112:(j + 1) * 112], in_=pmask[j, :, :]), inc=d_cn, inck=16)
        ctx.emit("sy", lambda s: s.dma_start(out=rmt[:], in_=rmask[:, :]), inc=d_cn, inck=16)
        ctx.emit("sy", lambda s: s.dma_start(out=fmt[:], in_=fixm[:, :]), inc=d_cn, inck=16)
        ctx.emit("gp", lambda g: g.dma_start(out=identb[:], in_=ident[:, :]), inc=d_gs, inck=16)
        ctx.emit("gp", lambda g: g.dma_start(out=swpb[:], in_=swp[:, :]), inc=d_gs, inck=16)
        for g_ in range(4):
            ctx.emit("gp", lambda g, g_=g_: g.dma_start(out=kstb[:, g_ * 112:(g_ + 1) * 112], in_=kst[g_, :, :]), inc=d_gs, inck=16)
            ctx.emit("gp", lambda g, g_=g_: g.dma_start(out=kgtb[:, g_ * 112:(g_ + 1) * 112], in_=kgt[g_, :, :]), inc=d_gs, inck=16)
            ctx.emit("gp", lambda g, g_=g_: g.dma_start(out=ksrb[:, g_ * TL:(g_ + 1) * TL], in_=ksr[g_, :, :]), inc=d_gs, inck=16)
            ctx.emit("gp", lambda g, g_=g_: g.dma_start(out=kgrb[:, g_ * TL:(g_ + 1) * TL], in_=kgr[g_, :, :]), inc=d_gs, inck=16)
        ctx.emit("gp", lambda g: g.dma_start(out=vtb[:, :], in_=vt[:, :]), inc=d_gs, inck=16)
        for kc in range(8):
            ctx.emit("gp", lambda g, kc=kc: g.dma_start(out=vrb[:, kc * 256:(kc + 1) * 256], in_=vrr[kc, :, :]), inc=d_gs, inck=16)
        for c in range(DC):
            ctx.emit("sy", lambda s, c=c: s.dma_start(out=xn1[:, c * TL:(c + 1) * TL], in_=xn1i[c, :, :]), inc=d_ld, inck=16)
        n_xn1 = d_ld.n
        ctx.emit("act", lambda a: a.activation(sclc[:], sclc[:], AF.Exp), waits=[(d_cn, 48)], inc=s_act)
        ctx.emit("dve", lambda v: v.tensor_scalar(sclc[:], sclc[:], 1.0 / np.sqrt(32.0), None, op0=ALU.mult),
                 waits=_now(s_act), inc=s_dve)
        nv1 = ctx.emit("tn", lambda t: t.matmul(P[0][0:128, 0:1], vtb[:, 0:128], ones1b[0:112, 0:1], start=True, stop=False),
                       waits=[(d_gs, d_gs.n), (s_dve, n_const)])
        ctx.emit("tn", lambda t: t.matmul(P[1][0:128, 0:1], vtb[:, 128:256], ones1b[0:112, 0:1], start=True, stop=False))
        for kc in range(8):
            ctx.emit("tn", lambda t, kc=kc: t.matmul(P[0][0:128, 0:1], vrb[:, kc * 256:kc * 256 + 128],
                                                     ones1b[:, 0:1], start=False, stop=(kc == 7)))
            ctx.emit("tn", lambda t, kc=kc: t.matmul(P[1][0:128, 0:1], vrb[:, kc * 256 + 128:(kc + 1) * 256],
                                                     ones1b[:, 0:1], start=False, stop=(kc == 7)),
                     inc=(s_tn if kc == 7 else None))
        for g_ in range(4):
            src = P[g_ // 2]
            lo = (g_ % 2) * 64
            for half in range(2):
                ctx.emit("dve", lambda v, g_=g_, src=src, lo=lo, half=half: v.tensor_scalar(
                    vbarc[half * 64:(half + 1) * 64, g_:g_ + 1], src[lo:lo + 64, 0:1], 1.0 / 1136.0, None, op0=ALU.mult),
                    waits=_now(s_tn), inc=s_dve)
        ctx.barrier()

        wq_marks = []
        for qt in range(4):
            b = qt % 2
            wts = [(s_tn, wq_marks[qt - 2][0])] if qt >= 2 else []
            nw = ctx.emit("gp", lambda g, qt=qt: g.dma_start(out=wsl[:], in_=wqs[qt, :, :]),
                          waits=(_now(s_tn) if qt > 0 else []), inc=d_g0, inck=16)
            ml = None
            for th in range(2):
                for dc in range(DC):
                    ml = ctx.emit("tn", lambda t, dc=dc, th=th: t.matmul(
                        P[th][:, :], wsl[:, dc * 128:(dc + 1) * 128],
                        xn1[:, dc * TL + H2[th].start:dc * TL + H2[th].stop],
                        start=(dc == 0), stop=(dc == DC - 1)),
                        waits=([(d_g0, nw), (d_ld, n_xn1)] + _now(s_dve, s_act) if dc == 0 else []),
                        inc=(s_tn if dc == DC - 1 else None))
                ctx.emit("dve", lambda v, qt=qt, th=th: v.tensor_scalar(
                    qs[:, qt * TL + H2[th].start:qt * TL + H2[th].stop], P[th][:, :],
                    sclc[:, qt:qt + 1], None, op0=ALU.mult),
                    waits=[(s_tn, ml)], inc=s_dve)
            nw2 = ctx.emit("gp", lambda g, qt=qt: g.dma_start(out=wsl[:], in_=wqg[qt, :, :]),
                           waits=_now(s_tn), inc=d_g0, inck=16)
            for th in range(2):
                for dc in range(DC):
                    ml = ctx.emit("tn", lambda t, dc=dc, th=th: t.matmul(
                        P[2 + th][:, :], wsl[:, dc * 128:(dc + 1) * 128],
                        xn1[:, dc * TL + H2[th].start:dc * TL + H2[th].stop],
                        start=(dc == 0), stop=(dc == DC - 1)),
                        waits=([(d_g0, nw2)] + _now(s_dve) if dc == 0 else []),
                        inc=(s_tn if dc == DC - 1 else None))
                ctx.emit("dve", lambda v, th=th: v.tensor_copy(qgf[:, H2[th]], P[2 + th][:, :]),
                         waits=[(s_tn, ml)], inc=s_dve)
                ctx.emit("dve", lambda v, qt=qt, th=th: v.tensor_copy(
                    qg[:, qt * TL + H2[th].start:qt * TL + H2[th].stop], qgf[:, H2[th]]),
                    waits=_now(s_dve), inc=s_dve)
                ns = ctx.emit("tn", lambda t, qt=qt, th=th: t.matmul(
                    P[4 + th][:, :], swpb[:], qg[:, qt * TL + H2[th].start:qt * TL + H2[th].stop],
                    start=True, stop=True),
                    waits=_now(s_dve) + [(d_gs, 32)], inc=s_tn)
                ctx.emit("dve", lambda v, th=th: v.tensor_tensor(
                    qgf[:, H2[th]], qgf[:, H2[th]], cost[:, H2[th]], op=ALU.mult),
                    waits=[(s_tn, s_tn.n), (d_cn, d_cn.n)], inc=s_dve)
                ctx.emit("dve", lambda v, th=th: v.tensor_tensor(
                    x1t[:, H2[th]], P[4 + th][:, :], sint[:, H2[th]], op=ALU.mult),
                    waits=_now(s_dve), inc=s_dve)
                ctx.emit("dve", lambda v, th=th: v.tensor_tensor(
                    qgf[:, H2[th]], qgf[:, H2[th]], x1t[:, H2[th]], op=ALU.add),
                    waits=_now(s_dve), inc=s_dve)
                ctx.emit("dve", lambda v, qt=qt, th=th: v.tensor_scalar(
                    qg[:, qt * TL + H2[th].start:qt * TL + H2[th].stop], qgf[:, H2[th]],
                    sclc[:, qt:qt + 1], None, op0=ALU.mult),
                    waits=_now(s_dve), inc=s_dve)
            wq_marks.append((ml,))
        ctx.barrier()

        for qt in range(4):
            for m in range(4):
                base = 32 * m
                for qc in range(NB):
                    is_raw = (qc == NB - 1)
                    qsl = slice(qc * 128, (qc + 1) * 128)
                    n1 = ctx.emit("tn", lambda t, qt=qt, m=m, base=base, qsl=qsl: t.matmul(
                        P[0][:, 0:112], qs[base:base + 32, qt * TL + qsl.start:qt * TL + qsl.stop],
                        kstb[base:base + 32, qt * 112:(qt + 1) * 112],
                        start=True, stop=False, tile_position=(base, 0)),
                        waits=_now(s_dve, s_act))
                    n1 = ctx.emit("tn", lambda t, qt=qt, m=m, base=base, qsl=qsl: t.matmul(
                        P[0][:, 0:112], qg[base:base + 32, qt * TL + qsl.start:qt * TL + qsl.stop],
                        kgtb[base:base + 32, qt * 112:(qt + 1) * 112],
                        start=False, stop=True, tile_position=(base, 0)),
                        inc=s_tn)
                    if is_raw:
                        for rh in range(2):
                            ctx.emit("tn", lambda t, qt=qt, base=base, qsl=qsl, rh=rh: t.matmul(
                                P[1 + rh][:, :], qs[base:base + 32, qt * TL + qsl.start:qt * TL + qsl.stop],
                                ksrb[base:base + 32, qt * TL + H2[rh].start:qt * TL + H2[rh].stop],
                                start=True, stop=False, tile_position=(base, 0)))
                            ctx.emit("tn", lambda t, qt=qt, base=base, qsl=qsl, rh=rh: t.matmul(
                                P[1 + rh][:, :], qg[base:base + 32, qt * TL + qsl.start:qt * TL + qsl.stop],
                                kgrb[base:base + 32, qt * TL + H2[rh].start:qt * TL + H2[rh].stop],
                                start=False, stop=True, tile_position=(base, 0)),
                                inc=(s_tn if rh == 1 else None))
                    ctx.emit("dve", lambda v, qc=qc: v.scalar_tensor_tensor(
                        zp[:], P[0][:, 0:112], 30000.0, pmt[:, qc * 112:(qc + 1) * 112],
                        op0=ALU.add, op1=ALU.mult),
                        waits=_now(s_tn) + [(d_cn, d_cn.n), (s_act, s_act.n)], inc=s_dve)
                    ctx.emit("dve", lambda v: v.reduce_max(rowv[:, 0:1], zp[:], axis=AX.X, negate=True),
                             waits=_now(s_dve), inc=s_dve)
                    if is_raw:
                        for rh in range(2):
                            ctx.emit("dve", lambda v, rh=rh: v.scalar_tensor_tensor(
                                zr[:, H2[rh]], P[1 + rh][:, :], 30000.0, rmt[:, H2[rh]],
                                op0=ALU.add, op1=ALU.mult),
                                waits=_now(s_tn), inc=s_dve)
                        ctx.emit("dve", lambda v: v.reduce_max(rowv[:, 1:2], zr[:], axis=AX.X, negate=True),
                                 waits=_now(s_dve), inc=s_dve)
                        ctx.emit("dve", lambda v: v.tensor_tensor(rowv[:, 0:1], rowv[:, 0:1], rowv[:, 1:2], op=ALU.min),
                                 waits=_now(s_dve), inc=s_dve)
                    na = ctx.emit("act", lambda a: a.activation(
                        ep[:], zp[:], AF.Exp, bias=rowv[:, 0:1], accum_out=rowv[:, 2:3]),
                        waits=_now(s_dve), inc=s_act)
                    if is_raw:
                        na = ctx.emit("act", lambda a: a.activation(
                            er[:], zr[:], AF.Exp, bias=rowv[:, 0:1], accum_out=rowv[:, 3:4]),
                            waits=_now(s_act), inc=s_act)
                        ctx.emit("dve", lambda v: v.tensor_tensor(rowv[:, 2:3], rowv[:, 2:3], rowv[:, 3:4], op=ALU.add),
                                 waits=[(s_act, na)], inc=s_dve)
                    ctx.emit("dve", lambda v: v.reciprocal(rowv[:, 4:5], rowv[:, 2:3]),
                             waits=[(s_act, na)] + _now(s_dve), inc=s_dve)
                    ctx.emit("dve", lambda v: v.tensor_scalar(ep[:], ep[:], rowv[:, 4:5], None, op0=ALU.mult),
                             waits=_now(s_dve), inc=s_dve)
                    nt = ctx.emit("tn", lambda t: t.transpose(P[3].bitcast(BF16)[0:112, 0:128], ep[:], identb[:]),
                                  waits=_now(s_dve) + [(d_gs, 32)], inc=s_tn)
                    ctx.emit("dve", lambda v: v.tensor_copy(etp[:, :], P[3].bitcast(BF16)[0:112, 0:128]),
                             waits=[(s_tn, nt)], inc=s_dve)
                    if is_raw:
                        ctx.emit("dve", lambda v: v.tensor_scalar(er[:], er[:], rowv[:, 4:5], None, op0=ALU.mult),
                                 waits=_now(s_dve), inc=s_dve)
                        for kc in range(8):
                            nt = ctx.emit("tn", lambda t, kc=kc: t.transpose(
                                P[3].bitcast(BF16)[:, 0:128], er[:, kc * 128:(kc + 1) * 128], identb[:]),
                                waits=_now(s_dve), inc=s_tn)
                            ctx.emit("dve", lambda v, kc=kc: v.tensor_copy(
                                etr[:, kc * 128:(kc + 1) * 128], P[3].bitcast(BF16)[:, 0:128]),
                                waits=[(s_tn, nt)], inc=s_dve)
                    npv = ctx.emit("tn", lambda t, qt=qt, is_raw=is_raw: t.matmul(
                        P[6][0:64, 0:128], vtb[:, qt * 64:(qt + 1) * 64], etp[:, :],
                        start=True, stop=(not is_raw)),
                        waits=_now(s_dve), inc=(None if is_raw else s_tn))
                    if is_raw:
                        for kc in range(8):
                            npv = ctx.emit("tn", lambda t, qt=qt, kc=kc: t.matmul(
                                P[6][0:64, 0:128], vrb[:, kc * 256 + qt * 64:kc * 256 + (qt + 1) * 64],
                                etr[:, kc * 128:(kc + 1) * 128],
                                start=False, stop=(kc == 7)),
                                inc=(s_tn if kc == 7 else None))
                    h = 4 * qt + m
                    jt, sub = h // 2, h % 2
                    if qc == 0:
                        ctx.emit("dve", lambda v, jt=jt, sub=sub, qsl=qsl: v.tensor_copy(
                            att[sub * 64:(sub + 1) * 64, jt * TL + qsl.start:jt * TL + qsl.stop],
                            P[6][0:64, 0:128]),
                            waits=[(s_tn, s_tn.n)], inc=s_dve)
                        sl2 = slice(sub * 64, (sub + 1) * 64)
                        ctx.emit("dve", lambda v, jt=jt, sl2=sl2, qt=qt: v.tensor_scalar(
                            x1t[sl2, 0:128], att[sl2, jt * TL:jt * TL + 128],
                            vbarc[sl2, qt:qt + 1], None, op0=ALU.subtract),
                            waits=_now(s_dve), inc=s_dve)
                        ctx.emit("dve", lambda v, sl2=sl2: v.tensor_tensor(
                            x1t[sl2, 0:128], x1t[sl2, 0:128], fmt[sl2, 0:128], op=ALU.mult),
                            waits=_now(s_dve) + [(d_cn, d_cn.n)], inc=s_dve)
                        ctx.emit("dve", lambda v, jt=jt, sl2=sl2: v.tensor_tensor(
                            att[sl2, jt * TL:jt * TL + 128],
                            att[sl2, jt * TL:jt * TL + 128],
                            x1t[sl2, 0:128], op=ALU.subtract),
                            waits=_now(s_dve), inc=s_dve)
                    else:
                        ctx.emit("dve", lambda v, jt=jt, sub=sub, qsl=qsl: v.tensor_copy(
                            att[sub * 64:(sub + 1) * 64, jt * TL + qsl.start:jt * TL + qsl.stop],
                            P[6][0:64, 0:128]),
                            waits=[(s_tn, s_tn.n)], inc=s_dve)
        ctx.barrier()

        s1_marks = []
        for co in range(DC):
            nw = ctx.emit("gp", lambda g, co=co: g.dma_start(out=wsl[:, 0:1024], in_=wo1[co, :, :]),
                          waits=(_now(s_tn) if co > 0 else []), inc=d_g0, inck=16)
            nx = ctx.emit("sy", lambda s, co=co: s.dma_start(out=x1t[:], in_=x1i[co, :, :]),
                          waits=(_now(s_dve) if co > 0 else []), inc=d_ld, inck=16)
            ng = ctx.emit("sy", lambda s, co=co: s.dma_start(out=gbuf[:], in_=g1i[co, :, :]),
                          waits=(_now(s_dve) if co > 0 else []), inc=d_l2, inck=16)
            ml = None
            for th in range(2):
                po = P[4 + th]
                for j in range(8):
                    ml = ctx.emit("tn", lambda t, j=j, th=th, po=po: t.matmul(
                        po[:, :], wsl[:, j * 128:(j + 1) * 128],
                        att[:, j * TL + H2[th].start:j * TL + H2[th].stop],
                        start=(j == 0), stop=(j == 7)),
                        waits=([(d_g0, nw)] + _now(s_dve) if j == 0 else []),
                        inc=(s_tn if j == 7 else None))
                ctx.emit("dve", lambda v, th=th, po=po: v.tensor_tensor(
                    ot[:, H2[th]], po[:, :], gbuf[:, H2[th]], op=ALU.mult),
                    waits=[(s_tn, ml), (d_l2, ng), (d_ld, nx), (d_st, d_st.n)], inc=s_dve)
                ctx.emit("dve", lambda v, th=th: v.tensor_tensor(
                    ot[:, H2[th]], ot[:, H2[th]], x1t[:, H2[th]], op=ALU.add),
                    waits=_now(s_dve), inc=s_dve)
            s1_marks.append(ml)
            ctx.emit("sy", lambda s, co=co: s.dma_start(out=outp[co, :, :], in_=ot[:]),
                     waits=_now(s_dve), inc=d_st, inck=16)
        for e in ENGINES:
            ctx.ops[e].append(([(d_st.h, d_st.n)], None, None))
        ctx.finalize()
    return nc


def _prep_shared_l3(inp):
    f = lambda n: np.ascontiguousarray(np.asarray(inp[n], np.float32))
    out = {}
    out["wqs"] = _tile_w(f("L1_attn_Wq_sem"))
    out["wqg"] = _tile_w(f("L1_attn_Wq_geo"))
    out["wo1"] = _tile_w(f("L1_attn_Wo"))
    ls = f("L1_attn_logit_scale")
    lsrep = np.zeros((128, 4), np.float32)
    for qt in range(4):
        for p in range(128):
            lsrep[p, qt] = ls[4 * qt + p // 32]
    out["lsrep"] = lsrep
    out["swp"] = _swap_perm()
    out["ident"] = np.eye(128, dtype=np.float32)
    return out


def _launch3_inmaps(inp, shared3, l2res):
    ks2g = np.zeros((128, 112), np.float32)
    kg2g = np.zeros((128, 112), np.float32)
    v2g = np.zeros((112, 256), np.float32)
    for g in range(56):
        c, j = g % 8, g // 8
        ks2g[:, 2 * g:2 * g + 2] = np.asarray(l2res[c]["ks2o"])[:, 2 * j:2 * j + 2]
        kg2g[:, 2 * g:2 * g + 2] = np.asarray(l2res[c]["kg2o"])[:, 2 * j:2 * j + 2]
        v2g[2 * g:2 * g + 2, :] = np.asarray(l2res[c]["v2o"])[2 * j:2 * j + 2, :]
    ksrg = np.concatenate([np.asarray(l2res[c]["ksro"]) for c in range(NC)], axis=1)
    kgrg = np.concatenate([np.asarray(l2res[c]["kgro"]) for c in range(NC)], axis=1)
    vrg = np.stack([np.asarray(l2res[c]["vro"]) for c in range(NC)])
    kst = np.stack([np.tile(ks2g[32 * g:32 * (g + 1), :], (4, 1)) for g in range(4)])
    kgt = np.stack([np.tile(kg2g[32 * g:32 * (g + 1), :], (4, 1)) for g in range(4)])
    ksr = np.stack([np.tile(ksrg[32 * g:32 * (g + 1), :], (4, 1)) for g in range(4)])
    kgr = np.stack([np.tile(kgrg[32 * g:32 * (g + 1), :], (4, 1)) for g in range(4)])
    k_pos = np.concatenate([np.arange(112) * 64 + 63, np.arange(7168, T)])
    maps = []
    for i in range(NC):
        toks = _core_tokens(i)
        C, S = _rope_tables(toks)
        pmask = np.zeros((NB, 128, 112), np.float32)
        for j in range(NB):
            qpos = (i + 8 * j) * 128 + np.arange(128)
            pmask[j] = (k_pos[None, 0:112] <= qpos[:, None]).astype(np.float32)
        qpos7 = (56 + i) * 128 + np.arange(128)
        rmask = (k_pos[None, 112:] <= qpos7[:, None]).astype(np.float32)
        fixm = np.zeros((128, 128), np.float32)
        if i == 0:
            for r in range(63):
                fixm[:, r] = 1.0
        m = {
            "xn1i": np.asarray(l2res[i]["xn1o"]),
            "x1i": np.asarray(l2res[i]["x1o"]),
            "g1i": np.asarray(l2res[i]["g1o"]),
            "cosq": C, "sinq": S,
            "kst": np.ascontiguousarray(kst), "kgt": np.ascontiguousarray(kgt),
            "ksr": np.ascontiguousarray(ksr), "kgr": np.ascontiguousarray(kgr),
            "vt": np.ascontiguousarray(v2g), "vrr": np.ascontiguousarray(vrg),
            "pmask": pmask, "rmask": np.ascontiguousarray(rmask), "fixm": fixm,
        }
        m.update(shared3)
        maps.append(m)
    return maps


def kernel(**inputs):
    shared1 = _prep_shared_l1(inputs)
    nc1 = build_launch1()
    res1 = run_bass_kernel_spmd(nc1, _launch1_inmaps(inputs, shared1), core_ids=list(range(NC)))
    shared2 = _prep_shared_l2(inputs)
    nc2 = build_launch2()
    res2 = run_bass_kernel_spmd(nc2, _launch2_inmaps(inputs, shared2, res1.results), core_ids=list(range(NC)))
    shared3 = _prep_shared_l3(inputs)
    nc3 = build_launch3()
    res3 = run_bass_kernel_spmd(nc3, _launch3_inmaps(inputs, shared3, res2.results), core_ids=list(range(NC)))
    out = np.zeros((T, D), np.float32)
    for i in range(NC):
        o = np.asarray(res3.results[i]["outp"])
        for j in range(NB):
            g = i + 8 * j
            out[g * 128:(g + 1) * 128, :] = o[:, :, j * 128:(j + 1) * 128].transpose(2, 0, 1).reshape(128, D)
    return out[None]


# version 06
# speedup vs baseline: 1.3053x; 1.3053x over previous
import contextlib
import numpy as np

import concourse.bass as bass
import concourse.mybir as mybir
from concourse.bass_utils import run_bass_kernel_spmd

F32 = mybir.dt.float32
BF16 = mybir.dt.bfloat16
AF = mybir.ActivationFunctionType
ALU = mybir.AluOpType
AX = mybir.AxisListType

NC = 8
T = 8192
D = 2048
TL = 1024
NB = 8
DC = 16
FF = 8192
FC = 64
DFF1 = 1408
LN_EPS = 1e-5

ENGINES = ("tn", "act", "dve", "gp", "sy")
_ENG_ATTR = {"tn": "tensor", "act": "scalar", "dve": "vector", "gp": "gpsimd", "sy": "sync"}



def _tile_w(w_in_out):
    n_in, n_out = w_in_out.shape
    ic, oc = n_in // 128, (n_out + 127) // 128
    if n_out % 128:
        w_in_out = np.concatenate(
            [w_in_out, np.zeros((n_in, oc * 128 - n_out), w_in_out.dtype)], 1)
    w4 = w_in_out.reshape(ic, 128, oc, 128)
    return np.ascontiguousarray(w4.transpose(2, 1, 0, 3).reshape(oc, 128, ic * 128))


def _vec_pc(v):
    return np.ascontiguousarray(np.asarray(v, np.float32).reshape(-1, 128).T)


def _core_tokens(i):
    idx = []
    for j in range(NB):
        g = i + 8 * j
        idx.extend(range(g * 128, (g + 1) * 128))
    return np.array(idx)



class Sem:
    __slots__ = ("h", "n", "name", "dma")

    def __init__(self, h, name, dma=False):
        self.h, self.n, self.name, self.dma = h, 0, name, dma


class Ctx:
    def __init__(self, nc, es):
        self.nc = nc
        self.es = es
        self.ops = {e: [] for e in ENGINES}
        self.sems = {}

    def sem(self, name, dma=False):
        s = Sem(self.es.enter_context(self.nc.semaphore(name)), name, dma)
        self.sems[name] = s
        return s

    def emit(self, eng, fn, waits=(), inc=None, inck=1):
        w = [(s.h, int(v)) for s, v in waits if v > 0]
        if inc is not None and inc.dma and inc.n > 0:
            w.append((inc.h, inc.n))
        tgt = None
        if inc is not None:
            inc.n += inck
            tgt = (inc.h, inck)
        self.ops[eng].append((w, fn, tgt))
        return inc.n if inc is not None else None

    def barrier(self):
        cur = [(s, s.n) for s in self.sems.values() if s.n > 0]
        for e in ENGINES:
            for s, v in cur:
                self.ops[e].append(([(s.h, v)], None, None))

    def finalize(self):
        block = self.es.enter_context(self.nc.Block())
        for name in ENGINES:
            ops = self.ops[name]

            def make(ops):
                def body(eng):
                    for w, fn, tgt in ops:
                        for h, v in w:
                            eng.wait_ge(h, v)
                        if fn is None:
                            continue
                        ins = fn(eng)
                        if tgt is not None:
                            ins.then_inc(tgt[0], tgt[1])
                return body
            getattr(block, _ENG_ATTR[name])(make(ops))


def _now(*sems):
    return [(s, s.n) for s in sems]



def build_launch1():
    nc = bass.Bass("TRN2", target_bir_lowering=False, debug=False)
    ein = lambda n, sh, dt: nc.dram_tensor(n, sh, dt, kind="ExternalInput").ap()
    eout = lambda n, sh, dt: nc.dram_tensor(n, sh, dt, kind="ExternalOutput").ap()
    xfm = ein("xfm", [DC, 128, TL], F32)
    xbnd = ein("xbnd", [NB, D], F32)
    ident = ein("ident", [128, 128], F32)
    wk = ein("wk", [DC, 128, D], F32)
    wv = ein("wv", [DC, 128, D], F32)
    wr = ein("wr", [DC, 128, D], F32)
    gw = ein("gw", [128, 32], F32)
    wg = ein("wg", [2, FC, 128, D], F32)
    wu = ein("wu", [2, FC, 128, D], F32)
    wd = ein("wd", [2, DC, 128, FF], F32)
    td = ein("td", [128, DC], F32)
    tmk = ein("tmk", [128, DC], F32)
    tmv = ein("tmv", [128, DC], F32)
    tmr = ein("tmr", [128, DC], F32)
    a_out = eout("a_out", [DC, 128, TL], F32)
    b_out = eout("b_out", [DC, 128, TL], F32)
    r_out = eout("r_out", [DC, 128, TL], BF16)
    g0_out = eout("g0_out", [DC, 128, TL], F32)
    zacc = nc.dram_tensor("zacc", [DC, 128, TL], F32).ap()

    es = contextlib.ExitStack()
    with es:
        ctx = Ctx(nc, es)
        sb = lambda n, sh, dt: es.enter_context(nc.sbuf_tensor(n, sh, dt))
        xn = sb("xn", [128, DC * TL], BF16)
        ones1 = sb("ones1", [128, 1], F32)
        epsc = sb("epsc", [128, 1], F32)
        onesrow = sb("onesrow", [1, 128], F32)
        identb = sb("identb", [128, 128], BF16)
        w0bc = sb("w0bc", [128, TL], F32)
        w1bc = sb("w1bc", [128, TL], F32)
        tmm = sb("tmm", [128, 3 * DC], F32)
        gwt = sb("gwt", [128, 32], BF16)
        xnbf = sb("xnbf", [128, DC * NB], BF16)
        xtile = [sb(f"xt{i}", [128, TL], F32) for i in range(2)]
        pha = contextlib.ExitStack()
        vr = pha.enter_context(nc.sbuf_tensor("vr", [1, 8 * TL], F32))
        brow = pha.enter_context(nc.sbuf_tensor("brow", [8, 16], F32))
        g2 = pha.enter_context(nc.sbuf_tensor("g2", [2, TL], F32))
        xbt = pha.enter_context(nc.sbuf_tensor("xbt", [NB, D], F32))
        xbsq = pha.enter_context(nc.sbuf_tensor("xbsq", [NB, D], F32))
        sq = pha.enter_context(nc.sbuf_tensor("sq", [128, TL], F32))
        mubc = pha.enter_context(nc.sbuf_tensor("mubc", [128, TL], F32))
        rbc = pha.enter_context(nc.sbuf_tensor("rbc", [128, TL], F32))
        P = [es.enter_context(nc.psum_tensor(f"pb{i}", [128, 512], F32)) for i in range(8)]

        s_tn, s_act, s_dve = ctx.sem("s_tn"), ctx.sem("s_act"), ctx.sem("s_dve")
        d_cn = ctx.sem("d_cn", dma=True)
        d_ld = ctx.sem("d_ld", dma=True)
        d_st = ctx.sem("d_st", dma=True)
        d_g0, d_g1 = ctx.sem("d_g0", dma=True), ctx.sem("d_g1", dma=True)
        d_gs = ctx.sem("d_gs", dma=True)
        d_ac = ctx.sem("d_ac", dma=True)
        dws = [d_g0, d_g1]
        H2 = (slice(0, 512), slice(512, 1024))

        def bcast_row(row_ap, dst, bank, wait_sems):
            for h in range(2):
                sl = H2[h]
                n = ctx.emit("tn", lambda t, sl=sl: t.matmul(
                    P[bank][:, :], onesrow[:], row_ap[:, sl], start=True, stop=True),
                    waits=list(wait_sems) + _now(s_dve), inc=s_tn)
                ctx.emit("dve", lambda v, sl=sl: v.tensor_copy(dst[:, sl], P[bank][:, :]),
                         waits=[(s_tn, n)], inc=s_dve)

        ctx.emit("dve", lambda v: v.memset(ones1[:], 1.0), inc=s_dve)
        ctx.emit("dve", lambda v: v.memset(epsc[:], LN_EPS), inc=s_dve)
        ctx.emit("dve", lambda v: v.memset(onesrow[:], 1.0), inc=s_dve)
        n_const = s_dve.n
        ctx.emit("sy", lambda s: s.dma_start(out=xbt[:], in_=xbnd[:, :]), inc=d_cn, inck=16)
        ctx.emit("sy", lambda s: s.dma_start(out=tmm[:, 0:DC], in_=td[:, :]), inc=d_cn, inck=16)
        ctx.emit("sy", lambda s: s.dma_start(out=tmm[:, 0:DC], in_=tmk[:, :]), inc=d_cn, inck=16)
        ctx.emit("sy", lambda s: s.dma_start(out=tmm[:, DC:2 * DC], in_=tmv[:, :]), inc=d_cn, inck=16)
        ctx.emit("sy", lambda s: s.dma_start(out=tmm[:, 2 * DC:3 * DC], in_=tmr[:, :]), inc=d_cn, inck=16)
        ctx.emit("gp", lambda g: g.dma_start(out=gwt[:], in_=gw[:, :]), inc=d_gs, inck=16)
        ctx.emit("gp", lambda g: g.dma_start(out=identb[:], in_=ident[:, :]), inc=d_gs, inck=16)
        n_tm = ctx.emit("dve", lambda v: v.tensor_scalar(tmm[:], tmm[:], -1.0, None, op0=ALU.add),
                        waits=[(d_cn, d_cn.n)], inc=s_dve)

        sq_marks = []
        for c in range(DC):
            buf = xtile[c % 2]
            wts = [(s_tn, sq_marks[c - 2][1])] if c >= 2 else []
            n = ctx.emit("sy", lambda s, c=c, buf=buf: s.dma_start(out=buf[:], in_=xfm[c, :, :]),
                         waits=wts, inc=d_ld, inck=16)
            na = ctx.emit("act", lambda a, buf=buf: a.activation(sq[:], buf[:], AF.Square),
                          waits=[(d_ld, n)] + ([(s_tn, sq_marks[c - 1][1])] if c >= 1 else []),
                          inc=s_act)
            last = None
            for h in range(2):
                ctx.emit("tn", lambda t, buf=buf, h=h, c=c: t.matmul(
                    P[h][0:1, :], ones1[:], buf[:, H2[h]], start=(c == 0), stop=(c == DC - 1)),
                    waits=[(s_act, na)] + ([(s_dve, n_const)] if c == 0 else []))
                last = ctx.emit("tn", lambda t, h=h, c=c: t.matmul(
                    P[2 + h][0:1, :], ones1[:], sq[:, H2[h]], start=(c == 0), stop=(c == DC - 1)),
                    waits=[(s_act, na)], inc=s_tn)
            sq_marks.append((n, last))
        V = lambda i: vr[0:1, i * TL:(i + 1) * TL]
        Vh = lambda i, h: vr[0:1, i * TL + H2[h].start:i * TL + H2[h].stop]
        for h in range(2):
            ctx.emit("dve", lambda v, h=h: v.tensor_copy(Vh(0, h), P[h][0:1, :]),
                     waits=_now(s_tn), inc=s_dve)
            ctx.emit("dve", lambda v, h=h: v.tensor_copy(Vh(1, h), P[2 + h][0:1, :]),
                     waits=_now(s_tn), inc=s_dve)
        ctx.emit("dve", lambda v: v.tensor_scalar(V(2), V(0), 1.0 / D, None, op0=ALU.mult),
                 waits=_now(s_dve), inc=s_dve)
        ctx.emit("dve", lambda v: v.tensor_scalar(V(1), V(1), 1.0 / D, None, op0=ALU.mult),
                 waits=_now(s_dve), inc=s_dve)
        ctx.emit("dve", lambda v: v.tensor_tensor(V(4), V(2), V(2), op=ALU.mult),
                 waits=_now(s_dve), inc=s_dve)
        ctx.emit("dve", lambda v: v.tensor_tensor(V(5), V(1), V(4), op=ALU.subtract),
                 waits=_now(s_dve), inc=s_dve)
        ctx.emit("act", lambda a: a.activation(V(6), V(5), AF.Ln, bias=epsc[0:1, 0:1]),
                 waits=_now(s_dve), inc=s_act)
        ctx.emit("act", lambda a: a.activation(V(3), V(6), AF.Exp, scale=-0.5),
                 waits=_now(s_act), inc=s_act)
        bcast_row(V(2), mubc, 4, _now(s_act))
        bcast_row(V(3), rbc, 4, _now(s_act))

        ap_marks = []
        for c in range(DC):
            buf = xtile[c % 2]
            wts = [(s_dve, ap_marks[c - 2])] if c >= 2 else _now(s_act)
            n = ctx.emit("sy", lambda s, c=c, buf=buf: s.dma_start(out=buf[:], in_=xfm[c, :, :]),
                         waits=wts, inc=d_ld, inck=16)
            ctx.emit("dve", lambda v, buf=buf: v.tensor_tensor(sq[:], buf[:], mubc[:], op=ALU.subtract),
                     waits=[(d_ld, n)] + _now(s_dve), inc=s_dve)
            m = ctx.emit("dve", lambda v, c=c: v.tensor_tensor(
                xn[:, c * TL:(c + 1) * TL], sq[:], rbc[:], op=ALU.mult),
                waits=_now(s_dve), inc=s_dve)
            ap_marks.append(m)

        ctx.emit("dve", lambda v: v.tensor_reduce(brow[0:8, 0:1], xbt[:], axis=AX.X, op=ALU.add),
                 waits=[(d_cn, 16)], inc=s_dve)
        ctx.emit("dve", lambda v: v.tensor_scalar(brow[0:8, 1:2], brow[0:8, 0:1], 1.0 / D, None, op0=ALU.mult),
                 waits=_now(s_dve), inc=s_dve)
        ctx.emit("act", lambda a: a.activation(xbsq[:], xbt[:], AF.Square, accum_out=brow[0:8, 2:3]),
                 waits=[(d_cn, 16)], inc=s_act)
        ctx.emit("dve", lambda v: v.tensor_scalar(brow[0:8, 3:4], brow[0:8, 2:3], 1.0 / D, None, op0=ALU.mult),
                 waits=_now(s_act), inc=s_dve)
        ctx.emit("dve", lambda v: v.tensor_tensor(brow[0:8, 4:5], brow[0:8, 1:2], brow[0:8, 1:2], op=ALU.mult),
                 waits=_now(s_dve), inc=s_dve)
        ctx.emit("dve", lambda v: v.tensor_tensor(brow[0:8, 5:6], brow[0:8, 3:4], brow[0:8, 4:5], op=ALU.subtract),
                 waits=_now(s_dve), inc=s_dve)
        ctx.emit("act", lambda a: a.activation(brow[0:8, 6:7], brow[0:8, 5:6], AF.Ln, bias=epsc[0:8, 0:1]),
                 waits=_now(s_dve), inc=s_act)
        ctx.emit("act", lambda a: a.activation(brow[0:8, 7:8], brow[0:8, 6:7], AF.Exp, scale=-0.5),
                 waits=_now(s_act), inc=s_act)
        ctx.emit("dve", lambda v: v.tensor_scalar(xbsq[:], xbt[:], brow[0:8, 1:2], None, op0=ALU.subtract),
                 waits=_now(s_act), inc=s_dve)
        xbt_bf = xbt.bitcast(BF16)
        ctx.emit("dve", lambda v: v.tensor_scalar(xbt_bf[:, 0:D], xbsq[:], brow[0:8, 7:8], None, op0=ALU.mult),
                 waits=_now(s_dve), inc=s_dve)
        for c in range(DC):
            n = ctx.emit("tn", lambda t, c=c: t.transpose(
                P[5].bitcast(BF16)[:, 0:8], xbt_bf[0:8, c * 128:(c + 1) * 128], identb[0:8, 0:8]),
                waits=_now(s_dve, d_gs), inc=s_tn)
            ctx.emit("dve", lambda v, c=c: v.tensor_copy(
                xnbf[:, c * NB:(c + 1) * NB], P[5].bitcast(BF16)[:, 0:8]),
                waits=[(s_tn, n)], inc=s_dve)

        for c in range(DC):
            for h in range(2):
                ctx.emit("tn", lambda t, c=c, h=h: t.matmul(
                    P[6 + h][0:2, :], gwt[:, 2 * c:2 * c + 2], xn[:, c * TL + H2[h].start:c * TL + H2[h].stop],
                    start=(c == 0), stop=(c == DC - 1)),
                    waits=(_now(s_dve, d_gs) if c == 0 else []),
                    inc=(s_tn if c == DC - 1 and h == 1 else None))
        for h in range(2):
            ctx.emit("dve", lambda v, h=h: v.tensor_copy(g2[0:2, H2[h]], P[6 + h][0:2, :]),
                     waits=_now(s_tn), inc=s_dve)
        ctx.emit("sy", lambda s: s.dma_start(out=V(1), in_=g2[1:2, 0:TL]),
                 waits=_now(s_dve), inc=d_cn, inck=16)
        ctx.emit("dve", lambda v: v.tensor_tensor(V(3), g2[0:1, 0:TL], V(1), op=ALU.subtract),
                 waits=[(d_cn, d_cn.n)], inc=s_dve)
        ctx.emit("act", lambda a: a.activation(V(4), V(3), AF.Abs),
                 waits=_now(s_dve), inc=s_act)
        ctx.emit("act", lambda a: a.activation(V(5), V(4), AF.Exp, scale=-1.0),
                 waits=_now(s_act), inc=s_act)
        ctx.emit("dve", lambda v: v.tensor_scalar(V(6), V(5), 1.0, None, op0=ALU.add),
                 waits=_now(s_act), inc=s_dve)
        ctx.emit("dve", lambda v: v.reciprocal(V(5), V(6)),
                 waits=_now(s_dve), inc=s_dve)
        ctx.emit("dve", lambda v: v.tensor_scalar(V(6), V(3), 0.0, None, op0=ALU.is_ge),
                 waits=_now(s_dve), inc=s_dve)
        ctx.emit("dve", lambda v: v.tensor_tensor(V(7), V(5), V(6), op=ALU.mult),
                 waits=_now(s_dve), inc=s_dve)
        ctx.emit("dve", lambda v: v.tensor_tensor(V(6), V(5), V(7), op=ALU.subtract),
                 waits=_now(s_dve), inc=s_dve)
        bcast_row(V(7), w0bc, 6, _now(s_dve))
        bcast_row(V(6), w1bc, 6, _now(s_dve))

        ctx.barrier()
        pha.close()

        with contextlib.ExitStack() as ph:
            hbuf = ph.enter_context(nc.sbuf_tensor("hbuf", [128, FC * TL], BF16))
            warena = ph.enter_context(nc.sbuf_tensor("warena", [128, 8192], BF16))
            hs = ph.enter_context(nc.sbuf_tensor("hs", [128, 1024], F32))
            zst = ph.enter_context(nc.sbuf_tensor("zst", [128, TL], F32))
            zstB = ph.enter_context(nc.sbuf_tensor("zstB", [128, TL], F32))
            zbuf1 = [zst, zstB]
            wgb = [warena[:, 0:2048], warena[:, 2048:4096]]
            wub = [warena[:, 4096:6144], warena[:, 6144:8192]]
            wdb = [warena[:, 0:4096], warena[:, 4096:8192]]
            for e in range(2):
                gu_marks = []
                cons_a = [0, 0]
                cons_d = [0, 0]
                for fc in range(FC):
                    b = fc % 2
                    wts = [(s_tn, gu_marks[fc - 2])] if fc >= 2 else []
                    ctx.emit("gp", lambda g, e=e, fc=fc, b=b: g.dma_start(
                        out=wgb[b][:], in_=wg[e, fc, :, :]), waits=wts, inc=dws[b], inck=16)
                    nu = ctx.emit("gp", lambda g, e=e, fc=fc, b=b: g.dma_start(
                        out=wub[b][:], in_=wu[e, fc, :, :]), inc=dws[b], inck=16)
                    mlast = None
                    for th in range(2):
                        pg, pu = P[2 * th], P[2 * th + 1]
                        pw = [(dws[b], nu)]
                        if cons_a[th]:
                            pw.append((s_act, cons_a[th]))
                        if cons_d[th]:
                            pw.append((s_dve, cons_d[th]))
                        for dc in range(DC):
                            ctx.emit("tn", lambda t, b=b, dc=dc, th=th, pg=pg: t.matmul(
                                pg[:, :], wgb[b][:, dc * 128:(dc + 1) * 128],
                                xn[:, dc * TL + H2[th].start:dc * TL + H2[th].stop],
                                start=(dc == 0), stop=(dc == DC - 1)),
                                waits=(pw if dc == 0 else []))
                        for dc in range(DC):
                            mlast = ctx.emit("tn", lambda t, b=b, dc=dc, th=th, pu=pu: t.matmul(
                                pu[:, :], wub[b][:, dc * 128:(dc + 1) * 128],
                                xn[:, dc * TL + H2[th].start:dc * TL + H2[th].stop],
                                start=(dc == 0), stop=(dc == DC - 1)),
                                inc=(s_tn if dc == DC - 1 else None))
                        na = ctx.emit("act", lambda a, pg=pg, th=th: a.activation(hs[:, H2[th]], pg[:, :], AF.Silu),
                                      waits=[(s_tn, mlast)] + ([(s_dve, cons_d[th])] if cons_d[th] else []),
                                      inc=s_act)
                        cons_a[th] = na
                        cons_d[th] = ctx.emit("dve", lambda v, fc=fc, th=th, pu=pu: v.tensor_tensor(
                            hbuf[:, fc * TL + H2[th].start:fc * TL + H2[th].stop], hs[:, H2[th]], pu[:, :], op=ALU.mult),
                            waits=[(s_act, na)], inc=s_dve)
                    gu_marks.append(mlast)
                ctx.barrier()
                wbc = w0bc if e == 0 else w1bc
                y_marks = []
                cons_y = [0, 0]
                pend = []
                for do in range(DC):
                    while len(pend) >= 2:
                        pend.pop(0)()
                    mlast = None
                    for fh in range(2):
                        b = fh
                        wts = [(s_tn, y_marks[do - 1])] if do >= 1 else []
                        nw = ctx.emit("gp", lambda g, e=e, do=do, fh=fh, b=b: g.dma_start(
                            out=wdb[b][:], in_=wd[e, do, :, fh * 4096:(fh + 1) * 4096]),
                            waits=wts, inc=dws[b], inck=16)
                        for th in range(2):
                            py = P[4 + th]
                            pw = [(dws[b], nw)]
                            if fh == 0 and cons_y[th]:
                                pw.append((s_dve, cons_y[th]))
                            for fi in range(32):
                                fc = fh * 32 + fi
                                mlast = ctx.emit("tn", lambda t, b=b, fi=fi, fc=fc, th=th, py=py: t.matmul(
                                    py[:, :], wdb[b][:, fi * 128:(fi + 1) * 128],
                                    hbuf[:, fc * TL + H2[th].start:fc * TL + H2[th].stop],
                                    start=(fc == 0), stop=(fc == FC - 1)),
                                    waits=(pw if fi == 0 else []),
                                    inc=(s_tn if (fh == 1 and fi == 31) else None))
                    y_marks.append(mlast)
                    zb = zbuf1[do % 2]
                    for th in range(2):
                        py = P[4 + th]
                        cons_y[th] = ctx.emit("dve", lambda v, th=th, py=py, zb=zb, wbc=wbc: v.tensor_tensor(
                            zb[:, H2[th]], py[:, :], wbc[:, H2[th]], op=ALU.mult),
                            waits=[(s_tn, mlast), (d_ac, d_ac.n)], inc=s_dve)
                    nd = s_dve.n
                    def mk_acc(do=do, zb=zb, e=e, nd=nd):
                        def go():
                            ctx.emit("gp", lambda g: (
                                g.dma_start(out=zacc[do, :, :], in_=zb[:]) if e == 0 else
                                g.dma_start(out=zacc[do, :, :], in_=zb[:], accum_op=ALU.add)),
                                waits=[(s_dve, nd)], inc=d_ac, inck=16)
                        return go
                    pend.append(mk_acc())
                for p_ in pend:
                    p_()
                ctx.barrier()
            g0_marks = []
            for do in range(DC):
                buf = xtile[do % 2]
                wts = [(d_ac, d_ac.n)] + ([(s_act, g0_marks[do - 2])] if do >= 2 else [])
                n = ctx.emit("sy", lambda s, do=do, buf=buf: s.dma_start(out=buf[:], in_=zacc[do, :, :]),
                             waits=wts, inc=d_ld, inck=16)
                na = ctx.emit("act", lambda a, buf=buf: a.activation(zst[:, 0:TL], buf[:], AF.Silu),
                              waits=[(d_ld, n), (d_st, d_st.n)], inc=s_act)
                g0_marks.append(na)
                ctx.emit("sy", lambda s, do=do: s.dma_start(out=g0_out[do, :, :], in_=zst[:, 0:TL]),
                         waits=[(s_act, na)], inc=d_st, inck=16)
            ctx.barrier()

        with contextlib.ExitStack() as ph:
            xk = ph.enter_context(nc.sbuf_tensor("xk", [128, DC * TL], BF16))
            xv = ph.enter_context(nc.sbuf_tensor("xv", [128, DC * TL], BF16))
            xr = ph.enter_context(nc.sbuf_tensor("xr", [128, DC * TL], BF16))
            dtile = ph.enter_context(nc.sbuf_tensor("dtile", [128, TL], BF16))
            warena = ph.enter_context(nc.sbuf_tensor("warena2", [128, 8192], BF16))
            ct = ph.enter_context(nc.sbuf_tensor("ct", [128, TL], F32))
            cvt = ph.enter_context(nc.sbuf_tensor("cvt", [128, TL], F32))
            at = ph.enter_context(nc.sbuf_tensor("at", [128, TL], F32))
            bt = ph.enter_context(nc.sbuf_tensor("bt", [128, TL], F32))
            onesT = ph.enter_context(nc.sbuf_tensor("onesT", [128, TL], F32))
            rt = ph.enter_context(nc.sbuf_tensor("rt", [128, TL], BF16))
            ctx.emit("dve", lambda v: v.memset(onesT[:], 1.0), inc=s_dve)

            for c in range(DC):
                xn_c = xn[:, c * TL:(c + 1) * TL]
                ctx.emit("dve", lambda v, xn_c=xn_c: v.tensor_tensor(
                    dtile[:, 1:TL], xn_c[:, 1:TL], xn_c[:, 0:TL - 1], op=ALU.subtract),
                    waits=_now(s_dve), inc=s_dve)
                bsl = dtile[:].rearrange("p (j r) -> p j r", r=128)[:, :, 0:1].squeeze(2)
                xcol = xn_c.rearrange("p (j r) -> p j r", r=128)[:, :, 0:1].squeeze(2)
                ctx.emit("dve", lambda v, c=c, bsl=bsl, xcol=xcol: v.tensor_tensor(
                    bsl, xcol, xnbf[:, c * NB:(c + 1) * NB], op=ALU.subtract),
                    waits=_now(s_dve), inc=s_dve)
                for j, dst in enumerate((xk, xv, xr)):
                    ctx.emit("dve", lambda v, c=c, j=j, dst=dst, xn_c=xn_c: v.scalar_tensor_tensor(
                        dst[:, c * TL:(c + 1) * TL], dtile[:, 0:TL], tmm[:, j * DC + c:j * DC + c + 1],
                        xn_c, op0=ALU.mult, op1=ALU.add),
                        waits=_now(s_dve), inc=s_dve)
            ctx.barrier()

            wrb = [warena[:, 0:2048], warena[:, 2048:4096]]
            r_marks = []
            for c in range(DC):
                b = c % 2
                wts = [(s_tn, r_marks[c - 2])] if c >= 2 else []
                nw = ctx.emit("gp", lambda g, c=c, b=b: g.dma_start(out=wrb[b][:], in_=wr[c, :, :]),
                              waits=wts, inc=dws[b], inck=16)
                mlast = None
                for th in range(2):
                    pr = P[th]
                    for dc in range(DC):
                        mlast = ctx.emit("tn", lambda t, b=b, dc=dc, th=th, pr=pr: t.matmul(
                            pr[:, :], wrb[b][:, dc * 128:(dc + 1) * 128],
                            xr[:, dc * TL + H2[th].start:dc * TL + H2[th].stop],
                            start=(dc == 0), stop=(dc == DC - 1)),
                            waits=([(dws[b], nw)] + _now(s_act) if dc == 0 else []),
                            inc=(s_tn if dc == DC - 1 else None))
                    ctx.emit("act", lambda a, th=th, pr=pr: a.activation(rt[:, H2[th]], pr[:, :], AF.Sigmoid),
                             waits=[(s_tn, mlast), (d_st, d_st.n)], inc=s_act)
                r_marks.append(mlast)
                ctx.emit("sy", lambda s, c=c: s.dma_start(out=r_out[c, :, :], in_=rt[:]),
                         waits=_now(s_act), inc=d_st, inck=16)
            ctx.barrier()

            wkb = [warena[:, 0:2048], warena[:, 2048:4096]]
            wvb = [warena[:, 4096:6144], warena[:, 6144:8192]]
            kv_marks = []
            kv_a = [0, 0]
            kv_d = [0, 0]
            for c in range(DC):
                b = c % 2
                wts = [(s_tn, kv_marks[c - 2])] if c >= 2 else []
                ctx.emit("gp", lambda g, c=c, b=b: g.dma_start(out=wkb[b][:], in_=wk[c, :, :]),
                         waits=wts, inc=dws[b], inck=16)
                nv = ctx.emit("gp", lambda g, c=c, b=b: g.dma_start(out=wvb[b][:], in_=wv[c, :, :]),
                              inc=dws[b], inck=16)
                mlast = None
                for th in range(2):
                    pk, pv = P[2 * th], P[2 * th + 1]
                    pw = [(dws[b], nv)]
                    if kv_a[th]:
                        pw.append((s_act, kv_a[th]))
                    if kv_d[th]:
                        pw.append((s_dve, kv_d[th]))
                    for dc in range(DC):
                        ctx.emit("tn", lambda t, b=b, dc=dc, th=th, pk=pk: t.matmul(
                            pk[:, :], wkb[b][:, dc * 128:(dc + 1) * 128],
                            xk[:, dc * TL + H2[th].start:dc * TL + H2[th].stop],
                            start=(dc == 0), stop=(dc == DC - 1)),
                            waits=(pw if dc == 0 else []))
                    for dc in range(DC):
                        mlast = ctx.emit("tn", lambda t, b=b, dc=dc, th=th, pv=pv: t.matmul(
                            pv[:, :], wvb[b][:, dc * 128:(dc + 1) * 128],
                            xv[:, dc * TL + H2[th].start:dc * TL + H2[th].stop],
                            start=(dc == 0), stop=(dc == DC - 1)),
                            inc=(s_tn if dc == DC - 1 else None))
                    na = ctx.emit("act", lambda a, th=th, pk=pk: a.activation(ct[:, H2[th]], pk[:, :], AF.Exp),
                                  waits=[(s_tn, mlast)] + _now(s_dve), inc=s_act)
                    kv_a[th] = na
                    kv_d[th] = ctx.emit("dve", lambda v, th=th, pv=pv: v.tensor_tensor(
                        cvt[:, H2[th]], ct[:, H2[th]], pv[:, :], op=ALU.mult),
                        waits=[(s_act, na)] + _now(s_dve), inc=s_dve)
                kv_marks.append(mlast)
                ctx.emit("dve", lambda v: v.tensor_tensor_scan(at[:], onesT[:], cvt[:], 0.0, op0=ALU.mult, op1=ALU.add),
                         waits=_now(s_dve) + [(d_st, d_st.n)], inc=s_dve)
                ctx.emit("dve", lambda v: v.tensor_tensor_scan(bt[:], onesT[:], ct[:], 0.0, op0=ALU.mult, op1=ALU.add),
                         waits=_now(s_dve), inc=s_dve)
                ctx.emit("sy", lambda s, c=c: s.dma_start(out=a_out[c, :, :], in_=at[:]),
                         waits=_now(s_dve), inc=d_st, inck=16)
                ctx.emit("sy", lambda s, c=c: s.dma_start(out=b_out[c, :, :], in_=bt[:]),
                         waits=_now(s_dve), inc=d_st, inck=16)

        for e in ENGINES:
            ctx.ops[e].append(([(d_st.h, d_st.n), (d_ac.h, d_ac.n)], None, None))
        ctx.finalize()
    return nc



def build_launch2():
    nc = bass.Bass("TRN2", target_bir_lowering=False, debug=False)
    ein = lambda n, sh, dt: nc.dram_tensor(n, sh, dt, kind="ExternalInput").ap()
    eout = lambda n, sh, dt: nc.dram_tensor(n, sh, dt, kind="ExternalOutput").ap()
    xfm = ein("xfm", [DC, 128, TL], F32)
    a_in = ein("a_in", [DC, 128, TL], F32)
    b_in = ein("b_in", [DC, 128, TL], F32)
    r_in = ein("r_in", [DC, 128, TL], BF16)
    g0_in = ein("g0_in", [DC, 128, TL], F32)
    epta = ein("epta", [DC, 128, 64], F32)
    eptb = ein("eptb", [DC, 128, 64], F32)
    offw = ein("offw", [NB, 128, 64], F32)
    tf = ein("tf", [128, DC], F32)
    lng = ein("lng", [128, DC], F32)
    lnb = ein("lnb", [128, DC], F32)
    wo = ein("wo", [DC, 128, D], F32)
    wks = ein("wks", [128, D], F32)
    wkg = ein("wkg", [128, D], F32)
    wv1 = ein("wv1", [2, 128, D], F32)
    swp = ein("swp", [128, 128], F32)
    ident = ein("ident", [128, 128], F32)
    cosk = ein("cosk", [128, TL], F32)
    sink = ein("sink", [128, TL], F32)
    poolm = ein("poolm", [128, 2], F32)
    sw1 = ein("sw1", [2, 11, 128, D], F32)
    sw2 = ein("sw2", [2, DC, 128, DFF1], F32)
    rw1 = ein("rw1", [16, 11, 128, D], F32)
    rw2 = ein("rw2", [16, DC, 128, DFF1], F32)
    b1all = ein("b1all", [18, 128, 11], F32)
    b2t = ein("b2t", [DC, 18, 128], F32)
    cent = ein("cent", [128, DC * 16], F32)
    cbias = ein("cbias", [128, 16], F32)
    x1o = eout("x1o", [DC, 128, TL], F32)
    xn1o = eout("xn1o", [DC, 128, TL], BF16)
    g1o = eout("g1o", [DC, 128, TL], F32)
    ks2o = eout("ks2o", [128, 14], F32)
    kg2o = eout("kg2o", [128, 14], F32)
    v2o = eout("v2o", [14, 256], F32)
    ksro = eout("ksro", [128, 128], F32)
    kgro = eout("kgro", [128, 128], F32)
    vro = eout("vro", [128, 256], F32)
    macc = nc.dram_tensor("macc", [DC, 128, TL], F32).ap()

    GROUPS = [[0, 1, 2], [3, 4, 5], [6, 7, 8], [9, 10, 11], [12, 13, 14], [15, 16, 17]]

    es = contextlib.ExitStack()
    with es:
        ctx = Ctx(nc, es)
        sb = lambda n, sh, dt: es.enter_context(nc.sbuf_tensor(n, sh, dt))
        xn1 = sb("xn1", [128, DC * TL], BF16)
        ones1 = sb("ones1", [128, 1], F32)
        onesrow = sb("onesrow", [1, 128], F32)
        epsc = sb("epsc", [128, 1], F32)
        identb = sb("identb", [128, 128], BF16)
        swpb = sb("swpb", [128, 128], BF16)
        euc = sb("euc", [128, 4 * DC], F32)
        poolt = sb("poolt", [128, 2], BF16)
        cbt = sb("cbt", [128, 16], F32)
        G18 = sb("G18", [18, TL], BF16)
        onesrow_bf = sb("onesrow_bf", [1, 128], BF16)
        ones1b = sb("ones1b", [128, 1], BF16)
        onestl2 = sb("onestl2", [2, TL], BF16)
        identf = sb("identf", [128, 128], F32)
        centb = sb("centb", [128, DC * 16], BF16)
        sq = sb("sq", [128, TL], F32)
        abuf = [sb(f"ab{i}", [128, TL], F32) for i in range(2)]
        bbuf = [sb(f"bb{i}", [128, TL], F32) for i in range(2)]
        P = [es.enter_context(nc.psum_tensor(f"pb{i}", [128, 512], F32)) for i in range(8)]

        s_tn, s_act, s_dve = ctx.sem("s_tn"), ctx.sem("s_act"), ctx.sem("s_dve")
        d_cn = ctx.sem("d_cn", dma=True)
        d_ld = ctx.sem("d_ld", dma=True)
        d_l2 = ctx.sem("d_l2", dma=True)
        d_st = ctx.sem("d_st", dma=True)
        d_g0, d_g1 = ctx.sem("d_g0", dma=True), ctx.sem("d_g1", dma=True)
        d_gs = ctx.sem("d_gs", dma=True)
        d_ac = ctx.sem("d_ac", dma=True)
        dws = [d_g0, d_g1]
        H2 = (slice(0, 512), slice(512, 1024))

        def bcast_row(row_ap, dst, bank, wait_sems):
            for h in range(2):
                sl = H2[h]
                n = ctx.emit("tn", lambda t, sl=sl: t.matmul(
                    P[bank][:, :], onesrow[:], row_ap[:, sl], start=True, stop=True),
                    waits=list(wait_sems) + _now(s_dve), inc=s_tn)
                ctx.emit("dve", lambda v, sl=sl: v.tensor_copy(dst[:, sl], P[bank][:, :]),
                         waits=[(s_tn, n)], inc=s_dve)

        ctx.emit("dve", lambda v: v.memset(ones1[:], 1.0), inc=s_dve)
        ctx.emit("dve", lambda v: v.memset(onesrow[:], 1.0), inc=s_dve)
        ctx.emit("dve", lambda v: v.memset(epsc[:], LN_EPS), inc=s_dve)
        ctx.emit("dve", lambda v: v.memset(onesrow_bf[:], 1.0), inc=s_dve)
        ctx.emit("dve", lambda v: v.memset(ones1b[:], 1.0), inc=s_dve)
        ctx.emit("dve", lambda v: v.memset(onestl2[:], 1.0), inc=s_dve)
        n_const = s_dve.n
        ctx.emit("sy", lambda s: s.dma_start(out=euc[:, 0:DC], in_=tf[:, :]), inc=d_cn, inck=16)
        ctx.emit("sy", lambda s: s.dma_start(out=euc[:, 2 * DC:3 * DC], in_=lng[:, :]), inc=d_cn, inck=16)
        ctx.emit("sy", lambda s: s.dma_start(out=euc[:, 3 * DC:4 * DC], in_=lnb[:, :]), inc=d_cn, inck=16)
        ctx.emit("sy", lambda s: s.dma_start(out=cbt[:], in_=cbias[:, :]), inc=d_cn, inck=16)
        ctx.emit("gp", lambda g: g.dma_start(out=identb[:], in_=ident[:, :]), inc=d_gs, inck=16)
        ctx.emit("gp", lambda g: g.dma_start(out=centb[:], in_=cent[:, :]), inc=d_gs, inck=16)
        ctx.emit("sy", lambda s: s.dma_start(out=identf[:], in_=ident[:, :]), inc=d_cn, inck=16)
        ctx.emit("gp", lambda g: g.dma_start(out=swpb[:], in_=swp[:, :]), inc=d_gs, inck=16)
        ctx.emit("gp", lambda g: g.dma_start(out=poolt[:], in_=poolm[:, :]), inc=d_gs, inck=16)
        ctx.emit("act", lambda a: a.activation(euc[:, 0:DC], euc[:, 0:DC], AF.Exp),
                 waits=[(d_cn, 16)], inc=s_act)
        ctx.emit("dve", lambda v: v.tensor_scalar(euc[:, DC:2 * DC], euc[:, 0:DC], -1.0, None, op0=ALU.mult),
                 waits=_now(s_act), inc=s_dve)
        ctx.emit("dve", lambda v: v.tensor_scalar(euc[:, DC:2 * DC], euc[:, DC:2 * DC], 1.0, None, op0=ALU.add),
                 waits=_now(s_dve), inc=s_dve)

        with contextlib.ExitStack() as ph:
            vr = ph.enter_context(nc.sbuf_tensor("vr", [1, 7 * TL], F32))
            mubc = ph.enter_context(nc.sbuf_tensor("mubc", [128, TL], F32))
            rbc = ph.enter_context(nc.sbuf_tensor("rbc", [128, TL], F32))
            eptA = ph.enter_context(nc.sbuf_tensor("eptA", [128, DC * 64], F32))
            eptB = ph.enter_context(nc.sbuf_tensor("eptB", [128, DC * 64], F32))
            offwt = ph.enter_context(nc.sbuf_tensor("offwt", [128, NB * 64], F32))
            rwkvb = ph.enter_context(nc.sbuf_tensor("rwkvb", [128, DC * TL], BF16))
            wkvs = ph.enter_context(nc.sbuf_tensor("wkvs", [128, DC * TL], BF16))
            rbuf = ph.enter_context(nc.sbuf_tensor("rbuf", [128, TL], BF16))
            numt = ph.enter_context(nc.sbuf_tensor("numt", [128, TL], F32))
            dent = ph.enter_context(nc.sbuf_tensor("dent", [128, TL], F32))
            offc = ph.enter_context(nc.sbuf_tensor("offc", [128, 2 * NB], F32))
            tmpo = ph.enter_context(nc.sbuf_tensor("tmpo", [128, 64], F32))
            V = lambda i: vr[0:1, i * TL:(i + 1) * TL]
            Vh = lambda i, h: vr[0:1, i * TL + H2[h].start:i * TL + H2[h].stop]
            for c in range(DC):
                ctx.emit("sy", lambda s, c=c: s.dma_start(out=eptA[:, c * 64:(c + 1) * 64], in_=epta[c, :, :]), inc=d_cn, inck=16)
                ctx.emit("sy", lambda s, c=c: s.dma_start(out=eptB[:, c * 64:(c + 1) * 64], in_=eptb[c, :, :]), inc=d_cn, inck=16)
            for j in range(NB):
                ctx.emit("sy", lambda s, j=j: s.dma_start(out=offwt[:, j * 64:(j + 1) * 64], in_=offw[j, :, :]), inc=d_cn, inck=16)
            wk_marks = []
            for c in range(DC):
                ba = abuf[c % 2]
                bb = bbuf[c % 2]
                wts = ([(s_dve, wk_marks[c - 2])] if c >= 2 else [])
                na = ctx.emit("sy", lambda s, c=c, ba=ba: s.dma_start(out=ba[:], in_=a_in[c, :, :]),
                              waits=wts, inc=d_ld, inck=16)
                nb = ctx.emit("sy", lambda s, c=c, bb=bb: s.dma_start(out=bb[:], in_=b_in[c, :, :]),
                              inc=d_ld, inck=16)
                for j in range(NB):
                    ctx.emit("dve", lambda v, c=c, j=j: v.tensor_tensor(
                        tmpo[:], eptA[:, c * 64:(c + 1) * 64], offwt[:, j * 64:(j + 1) * 64], op=ALU.mult),
                        waits=[(d_cn, d_cn.n)] + _now(s_dve), inc=s_dve)
                    ctx.emit("dve", lambda v, j=j: v.tensor_reduce(
                        offc[:, j:j + 1], tmpo[:], axis=AX.X, op=ALU.add),
                        waits=_now(s_dve), inc=s_dve)
                    ctx.emit("dve", lambda v, c=c, j=j: v.tensor_tensor(
                        tmpo[:], eptB[:, c * 64:(c + 1) * 64], offwt[:, j * 64:(j + 1) * 64], op=ALU.mult),
                        waits=_now(s_dve), inc=s_dve)
                    ctx.emit("dve", lambda v, j=j: v.tensor_reduce(
                        offc[:, NB + j:NB + j + 1], tmpo[:], axis=AX.X, op=ALU.add),
                        waits=_now(s_dve), inc=s_dve)
                eu_c = euc[:, c:c + 1]
                ieu_c = euc[:, DC + c:DC + c + 1]
                ctx.emit("dve", lambda v, ba=ba, eu_c=eu_c: v.tensor_scalar(
                    numt[:], ba[:], eu_c, None, op0=ALU.mult),
                    waits=[(d_ld, na)] + _now(s_dve, s_act), inc=s_dve)
                ctx.emit("dve", lambda v, ba=ba, ieu_c=ieu_c: v.scalar_tensor_tensor(
                    numt[:, 1:TL], ba[:, 0:TL - 1], ieu_c, numt[:, 1:TL], op0=ALU.mult, op1=ALU.add),
                    waits=_now(s_dve), inc=s_dve)
                ctx.emit("dve", lambda v, bb=bb, eu_c=eu_c: v.tensor_scalar(
                    dent[:], bb[:], eu_c, None, op0=ALU.mult),
                    waits=[(d_ld, nb)] + _now(s_dve), inc=s_dve)
                ctx.emit("dve", lambda v, bb=bb, ieu_c=ieu_c: v.scalar_tensor_tensor(
                    dent[:, 1:TL], bb[:, 0:TL - 1], ieu_c, dent[:, 1:TL], op0=ALU.mult, op1=ALU.add),
                    waits=_now(s_dve), inc=s_dve)
                offa3 = offc[:, 0:NB].unsqueeze(2).broadcast_to([128, NB, 128])
                offb3 = offc[:, NB:2 * NB].unsqueeze(2).broadcast_to([128, NB, 128])
                ctx.emit("dve", lambda v, offa3=offa3: v.tensor_tensor(
                    numt[:].rearrange("p (j r) -> p j r", r=128), numt[:].rearrange("p (j r) -> p j r", r=128),
                    offa3, op=ALU.add), waits=_now(s_dve), inc=s_dve)
                ctx.emit("dve", lambda v, offb3=offb3: v.tensor_tensor(
                    dent[:].rearrange("p (j r) -> p j r", r=128), dent[:].rearrange("p (j r) -> p j r", r=128),
                    offb3, op=ALU.add), waits=_now(s_dve), inc=s_dve)
                ctx.emit("act", lambda a: a.activation(sq[:], dent[:], AF.Ln),
                         waits=_now(s_dve, s_tn), inc=s_act)
                ctx.emit("act", lambda a: a.activation(sq[:], sq[:], AF.Exp, scale=-1.0),
                         waits=_now(s_act), inc=s_act)
                mw = ctx.emit("dve", lambda v, c=c: v.tensor_tensor(
                    wkvs[:, c * TL:(c + 1) * TL], numt[:], sq[:], op=ALU.mult),
                    waits=_now(s_act), inc=s_dve)
                wk_marks.append(mw)
                na2 = ctx.emit("act", lambda a, c=c: a.activation(sq[:], wkvs[:, c * TL:(c + 1) * TL], AF.Square),
                               waits=[(s_dve, mw)] + _now(s_tn), inc=s_act)
                for h in range(2):
                    ctx.emit("tn", lambda t, c=c, h=h: t.matmul(
                        P[h][0:1, :], ones1b[:], wkvs[:, c * TL + H2[h].start:c * TL + H2[h].stop],
                        start=(c == 0), stop=(c == DC - 1)),
                        waits=[(s_act, na2)] + ([(s_dve, n_const)] if c == 0 else []))
                    ctx.emit("tn", lambda t, c=c, h=h: t.matmul(
                        P[2 + h][0:1, :], ones1[:], sq[:, H2[h]],
                        start=(c == 0), stop=(c == DC - 1)),
                        waits=[(s_act, na2)], inc=(s_tn if h == 1 else None))
            for h in range(2):
                ctx.emit("dve", lambda v, h=h: v.tensor_copy(Vh(0, h), P[h][0:1, :]),
                         waits=_now(s_tn), inc=s_dve)
                ctx.emit("dve", lambda v, h=h: v.tensor_copy(Vh(1, h), P[2 + h][0:1, :]),
                         waits=_now(s_tn), inc=s_dve)
            ctx.emit("dve", lambda v: v.tensor_scalar(V(2), V(0), 1.0 / D, None, op0=ALU.mult),
                     waits=_now(s_dve), inc=s_dve)
            ctx.emit("dve", lambda v: v.tensor_scalar(V(1), V(1), 1.0 / D, None, op0=ALU.mult),
                     waits=_now(s_dve), inc=s_dve)
            ctx.emit("dve", lambda v: v.tensor_tensor(V(4), V(2), V(2), op=ALU.mult),
                     waits=_now(s_dve), inc=s_dve)
            ctx.emit("dve", lambda v: v.tensor_tensor(V(5), V(1), V(4), op=ALU.subtract),
                     waits=_now(s_dve), inc=s_dve)
            ctx.emit("act", lambda a: a.activation(V(6), V(5), AF.Ln, bias=epsc[0:1, 0:1]),
                     waits=_now(s_dve), inc=s_act)
            ctx.emit("act", lambda a: a.activation(V(3), V(6), AF.Exp, scale=-0.5),
                     waits=_now(s_act), inc=s_act)
            bcast_row(V(2), mubc, 4, _now(s_act))
            bcast_row(V(3), rbc, 4, _now(s_act))

            for c in range(DC):
                rb = rbuf
                nr = ctx.emit("sy", lambda s, c=c: s.dma_start(out=rbuf[:], in_=r_in[c, :, :]),
                              waits=(_now(s_dve) if c > 0 else _now(s_dve)), inc=d_ld, inck=16)
                ctx.emit("dve", lambda v, c=c: v.tensor_tensor(
                    sq[:], wkvs[:, c * TL:(c + 1) * TL], mubc[:], op=ALU.subtract),
                    waits=_now(s_dve) + [(d_ld, nr)], inc=s_dve)
                ctx.emit("dve", lambda v: v.tensor_tensor(sq[:], sq[:], rbc[:], op=ALU.mult),
                         waits=_now(s_dve), inc=s_dve)
                ctx.emit("dve", lambda v, c=c: v.tensor_scalar(
                    sq[:], sq[:], euc[:, 2 * DC + c:2 * DC + c + 1], euc[:, 3 * DC + c:3 * DC + c + 1],
                    op0=ALU.mult, op1=ALU.add),
                    waits=_now(s_dve), inc=s_dve)
                ctx.emit("dve", lambda v, c=c: v.tensor_tensor(
                    rwkvb[:, c * TL:(c + 1) * TL], sq[:], rbuf[:], op=ALU.mult),
                    waits=_now(s_dve), inc=s_dve)
            ctx.barrier()
            warena = ph.enter_context(nc.sbuf_tensor("warena", [128, 4096], BF16))
            wob = [warena[:, 0:2048], warena[:, 2048:4096]]
            x1t = ph.enter_context(nc.sbuf_tensor("x1t", [128, TL], F32))
            gbuf = ph.enter_context(nc.sbuf_tensor("gbuf", [128, TL], F32))
            s0_marks = []
            for co in range(DC):
                b = co % 2
                wts = [(s_tn, s0_marks[co - 2])] if co >= 2 else []
                nw = ctx.emit("gp", lambda g, co=co, b=b: g.dma_start(out=wob[b][:], in_=wo[co, :, :]),
                              waits=wts, inc=dws[b], inck=16)
                nx = ctx.emit("sy", lambda s, co=co: s.dma_start(out=abuf[0][:], in_=xfm[co, :, :]),
                              waits=(_now(s_dve) if co > 0 else []), inc=d_ld, inck=16)
                ng = ctx.emit("sy", lambda s, co=co: s.dma_start(out=gbuf[:], in_=g0_in[co, :, :]),
                              waits=(_now(s_dve) if co > 0 else []), inc=d_l2, inck=16)
                mlast = None
                for th in range(2):
                    po = P[4 + th]
                    for dc in range(DC):
                        mlast = ctx.emit("tn", lambda t, b=b, dc=dc, th=th, po=po: t.matmul(
                            po[:, :], wob[b][:, dc * 128:(dc + 1) * 128],
                            rwkvb[:, dc * TL + H2[th].start:dc * TL + H2[th].stop],
                            start=(dc == 0), stop=(dc == DC - 1)),
                            waits=([(dws[b], nw)] + _now(s_dve) if dc == 0 else []),
                            inc=(s_tn if dc == DC - 1 else None))
                    ctx.emit("dve", lambda v, th=th, po=po: v.tensor_tensor(
                        x1t[:, H2[th]], po[:, :], gbuf[:, H2[th]], op=ALU.mult),
                        waits=[(s_tn, mlast), (d_l2, ng), (d_ld, nx), (d_st, d_st.n)] + _now(s_act), inc=s_dve)
                    ctx.emit("dve", lambda v, th=th: v.tensor_tensor(
                        x1t[:, H2[th]], x1t[:, H2[th]], abuf[0][:, H2[th]], op=ALU.add),
                        waits=_now(s_dve), inc=s_dve)
                s0_marks.append(mlast)
                ctx.emit("sy", lambda s, co=co: s.dma_start(out=x1o[co, :, :], in_=x1t[:]),
                         waits=_now(s_dve), inc=d_st, inck=16)
                na2 = ctx.emit("act", lambda a: a.activation(sq[:], x1t[:], AF.Square),
                               waits=_now(s_dve, s_tn), inc=s_act)
                for h in range(2):
                    ctx.emit("tn", lambda t, co=co, h=h: t.matmul(
                        P[h][0:1, :], ones1[:], x1t[:, H2[h]], start=(co == 0), stop=(co == DC - 1)),
                        waits=[(s_act, na2)] + ([(s_dve, n_const)] if co == 0 else []))
                    ctx.emit("tn", lambda t, co=co, h=h: t.matmul(
                        P[2 + h][0:1, :], ones1[:], sq[:, H2[h]], start=(co == 0), stop=(co == DC - 1)),
                        waits=[(s_act, na2)], inc=(s_tn if h == 1 else None))
            for h in range(2):
                ctx.emit("dve", lambda v, h=h: v.tensor_copy(Vh(0, h), P[h][0:1, :]),
                         waits=_now(s_tn), inc=s_dve)
                ctx.emit("dve", lambda v, h=h: v.tensor_copy(Vh(1, h), P[2 + h][0:1, :]),
                         waits=_now(s_tn), inc=s_dve)
            ctx.emit("dve", lambda v: v.tensor_scalar(V(2), V(0), 1.0 / D, None, op0=ALU.mult),
                     waits=_now(s_dve), inc=s_dve)
            ctx.emit("dve", lambda v: v.tensor_scalar(V(1), V(1), 1.0 / D, None, op0=ALU.mult),
                     waits=_now(s_dve), inc=s_dve)
            ctx.emit("dve", lambda v: v.tensor_tensor(V(4), V(2), V(2), op=ALU.mult),
                     waits=_now(s_dve), inc=s_dve)
            ctx.emit("dve", lambda v: v.tensor_tensor(V(5), V(1), V(4), op=ALU.subtract),
                     waits=_now(s_dve), inc=s_dve)
            ctx.emit("act", lambda a: a.activation(V(6), V(5), AF.Ln, bias=epsc[0:1, 0:1]),
                     waits=_now(s_dve), inc=s_act)
            ctx.emit("act", lambda a: a.activation(V(3), V(6), AF.Exp, scale=-0.5),
                     waits=_now(s_act), inc=s_act)
            bcast_row(V(2), mubc, 4, _now(s_act))
            bcast_row(V(3), rbc, 4, _now(s_act))
            ctx.barrier()
            xa_marks = []
            for c in range(DC):
                buf = abuf[c % 2]
                wts = [(s_dve, xa_marks[c - 2])] if c >= 2 else []
                n = ctx.emit("sy", lambda s, c=c, buf=buf: s.dma_start(out=buf[:], in_=x1o[c, :, :]),
                             waits=wts, inc=d_ld, inck=16)
                ctx.emit("dve", lambda v, buf=buf: v.tensor_tensor(sq[:], buf[:], mubc[:], op=ALU.subtract),
                         waits=[(d_ld, n)] + _now(s_dve), inc=s_dve)
                m = ctx.emit("dve", lambda v, c=c: v.tensor_tensor(
                    xn1[:, c * TL:(c + 1) * TL], sq[:], rbc[:], op=ALU.mult),
                    waits=_now(s_dve), inc=s_dve)
                xa_marks.append(m)
                ctx.emit("sy", lambda s, c=c: s.dma_start(
                    out=xn1o[c, :, :], in_=xn1[:, c * TL:(c + 1) * TL]),
                    waits=[(s_dve, m)], inc=d_st, inck=16)
            ctx.barrier()
        with contextlib.ExitStack() as ph:
            cost = ph.enter_context(nc.sbuf_tensor("cost", [128, TL], F32))
            sint = ph.enter_context(nc.sbuf_tensor("sint", [128, TL], F32))
            wsl = ph.enter_context(nc.sbuf_tensor("wsl", [128, 2048], BF16))
            ksf = ph.enter_context(nc.sbuf_tensor("ksf", [128, TL], F32))
            kgf = ph.enter_context(nc.sbuf_tensor("kgf", [128, TL], F32))
            kgb = ph.enter_context(nc.sbuf_tensor("kgb", [128, TL], BF16))
            kgr = ph.enter_context(nc.sbuf_tensor("kgr", [128, TL], F32))
            p16 = ph.enter_context(nc.sbuf_tensor("p16", [128, 16], F32))
            vtm = ph.enter_context(nc.sbuf_tensor("vtm", [128, 256], F32))
            vtmb = ph.enter_context(nc.sbuf_tensor("vtmb", [128, 256], BF16))
            v2s = ph.enter_context(nc.sbuf_tensor("v2s", [2, 256], F32))
            ctx.emit("sy", lambda s: s.dma_start(out=cost[:], in_=cosk[:, :]), inc=d_cn, inck=16)
            ctx.emit("sy", lambda s: s.dma_start(out=sint[:], in_=sink[:, :]), inc=d_cn, inck=16)
            nw = ctx.emit("gp", lambda g: g.dma_start(out=wsl[:], in_=wks[:, :]), inc=d_g0, inck=16)
            ml = None
            for th in range(2):
                for dc in range(DC):
                    ml = ctx.emit("tn", lambda t, dc=dc, th=th: t.matmul(
                        P[th][:, :], wsl[:, dc * 128:(dc + 1) * 128],
                        xn1[:, dc * TL + H2[th].start:dc * TL + H2[th].stop],
                        start=(dc == 0), stop=(dc == DC - 1)),
                        waits=([(d_g0, nw)] if dc == 0 else []), inc=(s_tn if dc == DC - 1 else None))
                ctx.emit("dve", lambda v, th=th: v.tensor_copy(ksf[:, H2[th]], P[th][:, :]),
                         waits=[(s_tn, ml)], inc=s_dve)
            ctx.emit("dve", lambda v: v.tensor_reduce(
                p16[:], ksf[:].rearrange("p (j r) -> p j r", r=64), axis=AX.X, op=ALU.add),
                waits=_now(s_dve), inc=s_dve)
            ctx.emit("dve", lambda v: v.tensor_scalar(p16[:], p16[:], 1.0 / 64, None, op0=ALU.mult),
                     waits=_now(s_dve), inc=s_dve)
            ctx.emit("sy", lambda s: s.dma_start(out=ks2o[:, :], in_=p16[:, 0:14]),
                     waits=_now(s_dve), inc=d_st, inck=16)
            ctx.emit("sy", lambda s: s.dma_start(out=ksro[:, :], in_=ksf[:, 896:1024]),
                     waits=_now(s_dve), inc=d_st, inck=16)
            nw = ctx.emit("gp", lambda g: g.dma_start(out=wsl[:], in_=wkg[:, :]),
                          waits=_now(s_tn), inc=d_g0, inck=16)
            for th in range(2):
                for dc in range(DC):
                    ml = ctx.emit("tn", lambda t, dc=dc, th=th: t.matmul(
                        P[th][:, :], wsl[:, dc * 128:(dc + 1) * 128],
                        xn1[:, dc * TL + H2[th].start:dc * TL + H2[th].stop],
                        start=(dc == 0), stop=(dc == DC - 1)),
                        waits=([(d_g0, nw), (d_st, d_st.n)] + _now(s_dve) if dc == 0 else []),
                        inc=(s_tn if dc == DC - 1 else None))
                ctx.emit("dve", lambda v, th=th: v.tensor_copy(kgf[:, H2[th]], P[th][:, :]),
                         waits=[(s_tn, ml)], inc=s_dve)
                ctx.emit("dve", lambda v, th=th: v.tensor_copy(kgb[:, H2[th]], kgf[:, H2[th]]),
                         waits=_now(s_dve), inc=s_dve)
                ns = ctx.emit("tn", lambda t, th=th: t.matmul(
                    P[2 + th][:, :], swpb[:], kgb[:, H2[th]], start=True, stop=True),
                    waits=_now(s_dve) + [(d_gs, d_gs.n)], inc=s_tn)
                ctx.emit("dve", lambda v, th=th: v.tensor_tensor(
                    kgr[:, H2[th]], kgf[:, H2[th]], cost[:, H2[th]], op=ALU.mult),
                    waits=[(s_tn, ns), (d_cn, d_cn.n)], inc=s_dve)
                ctx.emit("dve", lambda v, th=th: v.tensor_tensor(
                    sq[:, H2[th]], P[2 + th][:, :], sint[:, H2[th]], op=ALU.mult),
                    waits=[(s_tn, ns)], inc=s_dve)
                ctx.emit("dve", lambda v, th=th: v.tensor_tensor(
                    kgr[:, H2[th]], kgr[:, H2[th]], sq[:, H2[th]], op=ALU.add),
                    waits=_now(s_dve), inc=s_dve)
            ctx.emit("dve", lambda v: v.tensor_reduce(
                p16[:], kgr[:].rearrange("p (j r) -> p j r", r=64), axis=AX.X, op=ALU.add),
                waits=_now(s_dve), inc=s_dve)
            ctx.emit("dve", lambda v: v.tensor_scalar(p16[:], p16[:], 1.0 / 64, None, op0=ALU.mult),
                     waits=_now(s_dve), inc=s_dve)
            ctx.emit("sy", lambda s: s.dma_start(out=kg2o[:, :], in_=p16[:, 0:14]),
                     waits=_now(s_dve), inc=d_st, inck=16)
            ctx.emit("sy", lambda s: s.dma_start(out=kgro[:, :], in_=kgr[:, 896:1024]),
                     waits=_now(s_dve), inc=d_st, inck=16)
            nw = ctx.emit("gp", lambda g: g.dma_start(out=wsl[:, 0:2048], in_=wv1[0, :, :]),
                          waits=_now(s_tn), inc=d_g0, inck=16)
            nw2 = ctx.emit("gp", lambda g: g.dma_start(out=kgf[:].bitcast(BF16)[:, 0:2048], in_=wv1[1, :, :]),
                           waits=_now(s_dve, s_tn) + [(d_st, d_st.n)], inc=d_g1, inck=16)
            for tc in range(8):
                ml = None
                for oc in range(2):
                    wv_sl = wsl if oc == 0 else kgf.bitcast(BF16)[:, 0:2048]
                    for dc in range(DC):
                        ml = ctx.emit("tn", lambda t, tc=tc, oc=oc, dc=dc, wv_sl=wv_sl: t.matmul(
                            P[4 + oc][:, 0:128], xn1[:, dc * TL + tc * 128:dc * TL + (tc + 1) * 128],
                            wv_sl[:, dc * 128:(dc + 1) * 128],
                            start=(dc == 0), stop=(dc == DC - 1)),
                            waits=([(d_g0, nw), (d_g1, nw2)] + _now(s_dve) if dc == 0 else []),
                            inc=(s_tn if dc == DC - 1 else None))
                for oc in range(2):
                    ctx.emit("dve", lambda v, oc=oc: v.tensor_copy(vtm[:, oc * 128:(oc + 1) * 128], P[4 + oc][:, 0:128]),
                             waits=[(s_tn, ml)] + _now(s_dve), inc=s_dve)
                ctx.emit("dve", lambda v: v.tensor_copy(vtmb[:], vtm[:, 0:256]),
                         waits=_now(s_dve), inc=s_dve)
                if tc < 7:
                    np_ = ctx.emit("tn", lambda t: t.matmul(
                        P[6][0:2, 0:256], poolt[:], vtmb[:], start=True, stop=True),
                        waits=_now(s_dve) + [(d_gs, d_gs.n)], inc=s_tn)
                    ctx.emit("dve", lambda v: v.tensor_copy(v2s[:, :], P[6][0:2, 0:256]),
                             waits=[(s_tn, np_), (d_st, d_st.n)], inc=s_dve)
                    ctx.emit("sy", lambda s, tc=tc: s.dma_start(out=v2o[2 * tc:2 * tc + 2, :], in_=v2s[:, :]),
                             waits=_now(s_dve), inc=d_st, inck=16)
                else:
                    ctx.emit("sy", lambda s: s.dma_start(out=vro[:, :], in_=vtm[:, 0:256]),
                             waits=_now(s_dve), inc=d_st, inck=16)
            ctx.barrier()

        with contextlib.ExitStack() as ph:
            Gflat = ph.enter_context(nc.sbuf_tensor("Gflat", [1, 18 * TL], BF16))
            gm2 = ph.enter_context(nc.sbuf_tensor("gm2", [128, 16], F32))
            hg = ph.enter_context(nc.sbuf_tensor("hg", [128, 3 * 11 * TL], BF16))
            w1b = ph.enter_context(nc.sbuf_tensor("w1b", [128, 4096], BF16))
            w2b = ph.enter_context(nc.sbuf_tensor("w2b", [128, 2816 + 36], BF16))
            b1t = ph.enter_context(nc.sbuf_tensor("b1t", [128, 18 * 11], F32))
            hs1 = ph.enter_context(nc.sbuf_tensor("hs1", [128, 1024], F32))
            gbc = ph.enter_context(nc.sbuf_tensor("gbc", [128, TL], BF16))
            g16 = ph.enter_context(nc.sbuf_tensor("g16", [128, 16], F32))
            gm = ph.enter_context(nc.sbuf_tensor("gm", [128, 16], F32))
            gv = ph.enter_context(nc.sbuf_tensor("gv", [128, 3], F32))
            zst = ph.enter_context(nc.sbuf_tensor("zst2", [128, TL], F32))
            zstB = ph.enter_context(nc.sbuf_tensor("zst2b", [128, TL], F32))
            w1bb = [w1b[:, 0:2048], w1b[:, 2048:4096]]
            w2bb = [w2b[:, 0:1408], w2b[:, 1408:2816]]
            b2sb = ph.enter_context(nc.sbuf_tensor("b2sb", [18, 128], BF16))
            for e in range(18):
                ctx.emit("sy", lambda s, e=e: s.dma_start(out=b1t[:, e * 11:(e + 1) * 11], in_=b1all[e, :, :]),
                         inc=d_cn, inck=16)
            n_b1 = d_cn.n
            for tc in range(8):
                ml = None
                for dc in range(DC):
                    ml = ctx.emit("tn", lambda t, tc=tc, dc=dc: t.matmul(
                        P[6][:, 0:16], xn1[:, dc * TL + tc * 128:dc * TL + (tc + 1) * 128],
                        centb[:, dc * 16:(dc + 1) * 16],
                        start=(dc == 0), stop=(dc == DC - 1)),
                        waits=(_now(s_dve) + [(d_gs, d_gs.n)] if dc == 0 else []),
                        inc=(s_tn if dc == DC - 1 else None))
                na = ctx.emit("act", lambda a: a.activation(g16[:], P[6][:, 0:16], AF.Sigmoid),
                              waits=[(s_tn, ml)] + _now(s_dve), inc=s_act)
                ctx.emit("dve", lambda v: v.tensor_tensor(g16[:], g16[:], cbt[:], op=ALU.add),
                         waits=[(s_act, na), (d_cn, d_cn.n)], inc=s_dve)
                ctx.emit("dve", lambda v: v.tensor_copy(gm[:], g16[:]), waits=_now(s_dve), inc=s_dve)
                for it in range(4):
                    ctx.emit("dve", lambda v: v.reduce_max(gv[:, 0:1], gm[:], axis=AX.X),
                             waits=_now(s_dve), inc=s_dve)
                    if it < 3:
                        ctx.emit("dve", lambda v: v.tensor_scalar(gm2[:], gm[:], gv[:, 0:1], None, op0=ALU.is_ge),
                                 waits=_now(s_dve), inc=s_dve)
                        ctx.emit("dve", lambda v: v.scalar_tensor_tensor(gm[:], gm2[:], -1e30, gm[:], op0=ALU.mult, op1=ALU.add),
                                 waits=_now(s_dve), inc=s_dve)
                ctx.emit("dve", lambda v: v.tensor_scalar(gm[:], g16[:], gv[:, 0:1], None, op0=ALU.is_ge),
                         waits=_now(s_dve), inc=s_dve)
                ctx.emit("dve", lambda v: v.tensor_tensor(gm[:], gm[:], g16[:], op=ALU.mult),
                         waits=_now(s_dve), inc=s_dve)
                ctx.emit("dve", lambda v: v.tensor_reduce(gv[:, 1:2], gm[:], axis=AX.X, op=ALU.add),
                         waits=_now(s_dve), inc=s_dve)
                ctx.emit("dve", lambda v: v.tensor_scalar(gv[:, 1:2], gv[:, 1:2], 1e-8, None, op0=ALU.max),
                         waits=_now(s_dve), inc=s_dve)
                ctx.emit("dve", lambda v: v.reciprocal(gv[:, 2:3], gv[:, 1:2]),
                         waits=_now(s_dve), inc=s_dve)
                mg = ctx.emit("dve", lambda v: v.tensor_scalar(gm[:], gm[:], gv[:, 2:3], None, op0=ALU.mult),
                              waits=_now(s_dve), inc=s_dve)
                nt = ctx.emit("tn", lambda t: t.transpose(
                    P[7][0:16, 0:128], gm[:, 0:16], identf[:]),
                    waits=[(s_dve, mg), (d_cn, d_cn.n)] + _now(s_act), inc=s_tn)
                ctx.emit("dve", lambda v, tc=tc: v.tensor_copy(
                    G18[0:16, tc * 128:(tc + 1) * 128], P[7][0:16, 0:128]),
                    waits=[(s_tn, nt)], inc=s_dve)
            ctx.emit("sy", lambda s: s.dma_start(out=G18[16:18, :], in_=onestl2[:, :]),
                     waits=_now(s_dve), inc=d_cn, inck=16)
            n_g18 = ctx.emit("sy", lambda s: s.dma_start(
                out=Gflat[0:1, :].rearrange("o (e t) -> o e t", t=TL), in_=G18[:, :]),
                waits=[(d_cn, d_cn.n)] + _now(s_dve), inc=d_cn, inck=16)
            ctx.barrier()
            first_grp = True
            zbuf = [zst, zstB]
            h1_marks = []
            cons_h1 = [0, 0]
            cons_h1_d = [0, 0]
            for gi, grp in enumerate(GROUPS):
                for ei, e in enumerate(grp):
                    routed = e < 16
                    ngbc = 0
                    if routed:
                        for h in range(2):
                            nb_ = ctx.emit("tn", lambda t, e=e, h=h: t.matmul(
                                P[6][:, :], onesrow_bf[:], Gflat[0:1, e * TL + H2[h].start:e * TL + H2[h].stop],
                                start=True, stop=True),
                                waits=[(d_cn, n_g18)] + _now(s_dve), inc=s_tn)
                            ngbc = ctx.emit("dve", lambda v, h=h: v.tensor_copy(gbc[:, H2[h]], P[6][:, :]),
                                            waits=[(s_tn, nb_)], inc=s_dve)
                    for fc in range(11):
                        b = (gi * 33 + ei * 11 + fc) % 2
                        wts = [(s_tn, h1_marks[-2])] if len(h1_marks) >= 2 else []
                        nw = ctx.emit("gp", lambda g, e=e, fc=fc, b=b, routed=routed: g.dma_start(
                            out=w1bb[b][:], in_=(rw1[e, fc, :, :] if routed else sw1[e - 16, fc, :, :])),
                            waits=wts, inc=dws[b], inck=16)
                        ml = None
                        for th in range(2):
                            pw = [(dws[b], nw)]
                            if cons_h1[th]:
                                pw.append((s_act, cons_h1[th]))
                            if cons_h1_d[th]:
                                pw.append((s_dve, cons_h1_d[th]))
                            for dc in range(DC):
                                ml = ctx.emit("tn", lambda t, b=b, dc=dc, th=th: t.matmul(
                                    P[th][:, :], w1bb[b][:, dc * 128:(dc + 1) * 128],
                                    xn1[:, dc * TL + H2[th].start:dc * TL + H2[th].stop],
                                    start=(dc == 0), stop=(dc == DC - 1)),
                                    waits=(pw if dc == 0 else []),
                                    inc=(s_tn if dc == DC - 1 else None))
                            if routed:
                                na = ctx.emit("act", lambda a, th=th, e=e, fc=fc: a.activation(
                                    hs1[:, H2[th]], P[th][:, :], AF.Relu, bias=b1t[:, e * 11 + fc:e * 11 + fc + 1]),
                                    waits=[(s_tn, ml), (d_cn, n_b1)] + ([(s_dve, cons_h1_d[th])] if cons_h1_d[th] else []),
                                    inc=s_act)
                                cons_h1[th] = na
                                cons_h1_d[th] = ctx.emit("dve", lambda v, ei=ei, fc=fc, th=th: v.tensor_tensor(
                                    hg[:, (ei * 11 + fc) * TL + H2[th].start:(ei * 11 + fc) * TL + H2[th].stop],
                                    hs1[:, H2[th]], gbc[:, H2[th]], op=ALU.mult),
                                    waits=[(s_act, na), (s_dve, ngbc)], inc=s_dve)
                            else:
                                cons_h1[th] = ctx.emit("act", lambda a, ei=ei, fc=fc, th=th, e=e: a.activation(
                                    hg[:, (ei * 11 + fc) * TL + H2[th].start:(ei * 11 + fc) * TL + H2[th].stop],
                                    P[th][:, :], AF.Relu, bias=b1t[:, e * 11 + fc:e * 11 + fc + 1]),
                                    waits=[(s_tn, ml), (d_cn, n_b1)], inc=s_act)
                        h1_marks.append(ml)
                ctx.barrier()
                cons_h1 = [0, 0]
                cons_h1_d = [0, 0]
                last = (gi == len(GROUPS) - 1)
                y2_marks = []
                cons_y = [0, 0]
                pend = []
                for do in range(DC):
                    while len(pend) >= 2:
                        pend.pop(0)()
                    ml = None
                    for ei, e in enumerate(grp):
                        routed = e < 16
                        b = (do * len(grp) + ei) % 2
                        wts = [(s_tn, y2_marks[-2])] if len(y2_marks) >= 2 else []
                        nw = ctx.emit("gp", lambda g, e=e, do=do, b=b, routed=routed: g.dma_start(
                            out=w2bb[b][:], in_=(rw2[e, do, :, :] if routed else sw2[e - 16, do, :, :])),
                            waits=wts, inc=dws[b], inck=16)
                        for th in range(2):
                            py = P[4 + th]
                            pw = [(dws[b], nw)]
                            if ei == 0 and cons_y[th]:
                                pw.append((s_dve, cons_y[th]))
                            for fc in range(11):
                                is_stop = (not last) and (ei == len(grp) - 1) and (fc == 10)
                                ml = ctx.emit("tn", lambda t, b=b, ei=ei, fc=fc, th=th, py=py, is_stop=is_stop: t.matmul(
                                    py[:, :], w2bb[b][:, fc * 128:(fc + 1) * 128],
                                    hg[:, (ei * 11 + fc) * TL + H2[th].start:(ei * 11 + fc) * TL + H2[th].stop],
                                    start=(ei == 0 and fc == 0), stop=is_stop),
                                    waits=(pw if fc == 0 else []),
                                    inc=(s_tn if (fc == 10 and th == 1) else None))
                        y2_marks.append(ml)
                    if last:
                        nb2 = ctx.emit("gp", lambda g, do=do: g.dma_start(out=b2sb[:, :], in_=b2t[do, :, :]),
                                       waits=(_now(s_tn) if do < 2 else [(s_tn, y2_marks[-2])]), inc=d_gs, inck=16)
                        for th in range(2):
                            py = P[4 + th]
                            ctx.emit("tn", lambda t, th=th, py=py: t.matmul(
                                py[:, :], b2sb[:, :], G18[:, H2[th]], start=False, stop=False),
                                waits=[(d_gs, nb2), (d_cn, n_g18)])
                            ml = ctx.emit("tn", lambda t, do=do, th=th, py=py: t.matmul(
                                py[:, :], identb[:], xn1[:, do * TL + H2[th].start:do * TL + H2[th].stop],
                                start=False, stop=True),
                                waits=[], inc=s_tn)
                    zb = zbuf[do % 2]
                    for th in range(2):
                        py = P[4 + th]
                        cons_y[th] = ctx.emit("dve", lambda v, th=th, py=py, zb=zb: v.tensor_copy(zb[:, H2[th]], py[:, :]),
                                              waits=[(s_tn, s_tn.n), (d_ac, d_ac.n)], inc=s_dve)
                    nd = s_dve.n
                    def mk_acc(do=do, zb=zb, fg=first_grp, nd=nd):
                        def go():
                            ctx.emit("gp", lambda g: (
                                g.dma_start(out=macc[do, :, :], in_=zb[:]) if fg else
                                g.dma_start(out=macc[do, :, :], in_=zb[:], accum_op=ALU.add)),
                                waits=[(s_dve, nd)], inc=d_ac, inck=16)
                        return go
                    pend.append(mk_acc())
                for p_ in pend:
                    p_()
                first_grp = False
                ctx.barrier()
            for do in range(DC):
                buf = abuf[do % 2]
                n = ctx.emit("sy", lambda s, do=do, buf=buf: s.dma_start(out=buf[:], in_=macc[do, :, :]),
                             waits=[(d_ac, d_ac.n)] + (_now(s_act) if do >= 2 else []), inc=d_ld, inck=16)
                na = ctx.emit("act", lambda a, buf=buf: a.activation(zst[:, 0:TL], buf[:], AF.Silu),
                              waits=[(d_ld, n), (d_st, d_st.n)], inc=s_act)
                ctx.emit("sy", lambda s, do=do: s.dma_start(out=g1o[do, :, :], in_=zst[:, 0:TL]),
                         waits=[(s_act, na)], inc=d_st, inck=16)
        for e in ENGINES:
            ctx.ops[e].append(([(d_st.h, d_st.n), (d_ac.h, d_ac.n)], None, None))
        ctx.finalize()
    return nc



def _swap_perm():
    P = np.zeros((128, 128), np.float32)
    for m in range(128):
        blk, d = m // 32, m % 32
        P[blk * 32 + ((d + 16) % 32), m] = 1.0
    return P


def _rope_tables(pos):
    invf = 1.0 / (10000.0 ** (np.arange(0, 32, 2, dtype=np.float64) / 32.0))
    ang = pos[None, :].astype(np.float64) * invf[:, None]
    cos = np.cos(ang).astype(np.float32)
    sin = np.sin(ang).astype(np.float32)
    C = np.zeros((128, len(pos)), np.float32)
    S = np.zeros((128, len(pos)), np.float32)
    for r in range(128):
        d = r % 32
        f = d % 16
        C[r] = cos[f]
        S[r] = sin[f] if d >= 16 else -sin[f]
    return C, S


def _prep_shared_l1(inp):
    f = lambda n: np.ascontiguousarray(np.asarray(inp[n], np.float32))
    out = {}
    out["wk"] = _tile_w(f("L0_attn_Wk").T)
    out["wv"] = _tile_w(f("L0_attn_Wv").T)
    out["wr"] = _tile_w(f("L0_attn_Wr").T)
    gwt = _tile_w(f("L0_ffn_gate_w").T)
    gw = np.zeros((128, 32), np.float32)
    for c in range(DC):
        gw[:, 2 * c:2 * c + 2] = gwt[0][:, c * 128:c * 128 + 2]
    out["gw"] = gw
    out["wg"] = np.stack([_tile_w(f("L0_ffn_w_gate")[e].T) for e in range(2)])
    out["wu"] = np.stack([_tile_w(f("L0_ffn_w_up")[e].T) for e in range(2)])
    out["wd"] = np.stack([_tile_w(f("L0_ffn_w_down")[e].T) for e in range(2)])
    out["td"] = _vec_pc(f("L0_attn_time_decay").ravel())
    out["tmk"] = _vec_pc(f("L0_attn_tm_k").ravel())
    out["tmv"] = _vec_pc(f("L0_attn_tm_v").ravel())
    out["tmr"] = _vec_pc(f("L0_attn_tm_r").ravel())
    out["ident"] = np.eye(128, dtype=np.float32)
    return out


def _launch1_inmaps(inp, shared):
    x = np.ascontiguousarray(np.asarray(inp["L0_attn_x"], np.float32)[0])
    maps = []
    for i in range(NC):
        toks = _core_tokens(i)
        xfm = np.ascontiguousarray(x[toks].T.reshape(DC, 128, TL))
        xbnd = np.zeros((NB, D), np.float32)
        for j in range(NB):
            g = i + 8 * j
            if g > 0:
                xbnd[j] = x[g * 128 - 1]
        m = {"xfm": xfm, "xbnd": xbnd}
        m.update(shared)
        maps.append(m)
    return maps


def _prep_shared_l2(inp):
    f = lambda n: np.ascontiguousarray(np.asarray(inp[n], np.float32))
    out = {}
    out["wo"] = _tile_w(f("L0_attn_Wo").T)
    out["wks"] = _tile_w(f("L1_attn_Wk_sem"))[0]
    out["wkg"] = _tile_w(f("L1_attn_Wk_geo"))[0]
    out["wv1"] = _tile_w(f("L1_attn_Wv"))
    out["swp"] = _swap_perm()
    out["ident"] = np.eye(128, dtype=np.float32)
    pm = np.zeros((128, 2), np.float32)
    pm[0:64, 0] = 1.0 / 64
    pm[64:128, 1] = 1.0 / 64
    out["poolm"] = pm
    out["sw1"] = np.stack([_tile_w(f("L1_ffn_shared_w1")[n]) for n in range(2)])
    out["sw2"] = np.stack([_tile_w(f("L1_ffn_shared_w2")[n]) for n in range(2)])
    out["rw1"] = np.stack([_tile_w(f("L1_ffn_routed_w1")[n]) for n in range(16)])
    out["rw2"] = np.stack([_tile_w(f("L1_ffn_routed_w2")[n]) for n in range(16)])
    b1 = np.concatenate([f("L1_ffn_routed_b1"), f("L1_ffn_shared_b1")], 0)
    out["b1all"] = np.ascontiguousarray(
        b1.reshape(18, 11, 128).transpose(0, 2, 1))
    b2 = np.concatenate([f("L1_ffn_routed_b2"), f("L1_ffn_shared_b2")], 0)
    out["b2t"] = np.ascontiguousarray(b2.reshape(18, DC, 128).transpose(1, 0, 2))
    cen = f("L1_ffn_centroids")
    out["cent"] = np.ascontiguousarray(
        cen.reshape(16, DC, 128).transpose(2, 1, 0).reshape(128, DC * 16))
    out["cbias"] = np.ascontiguousarray(np.broadcast_to(f("L1_ffn_bias"), (128, 16)))
    out["tf"] = _vec_pc(f("L0_attn_time_first").ravel())
    out["lng"] = _vec_pc(f("L0_attn_ln_g").ravel())
    out["lnb"] = _vec_pc(f("L0_attn_ln_b").ravel())
    return out


def _launch2_inmaps(inp, shared2, l1res):
    x = np.ascontiguousarray(np.asarray(inp["L0_attn_x"], np.float32)[0])
    epta = np.zeros((DC, 128, 64), np.float32)
    eptb = np.zeros((DC, 128, 64), np.float32)
    for g in range(64):
        c, j = g % 8, g // 8
        epta[:, :, g] = np.asarray(l1res[c]["a_out"])[:, :, j * 128 + 127]
        eptb[:, :, g] = np.asarray(l1res[c]["b_out"])[:, :, j * 128 + 127]
    maps = []
    for i in range(NC):
        toks = _core_tokens(i)
        xfm = np.ascontiguousarray(x[toks].T.reshape(DC, 128, TL))
        offw = np.zeros((NB, 128, 64), np.float32)
        for j in range(NB):
            gj = i + 8 * j
            for cp in range(8):
                J = (gj - 1 - cp) // 8
                if cp < gj and cp != i:
                    offw[j, :, cp + 8 * J] = 1.0
        C, S = _rope_tables(toks)
        m = {
            "xfm": xfm,
            "a_in": np.asarray(l1res[i]["a_out"]),
            "b_in": np.asarray(l1res[i]["b_out"]),
            "r_in": np.asarray(l1res[i]["r_out"]),
            "g0_in": np.asarray(l1res[i]["g0_out"]),
            "epta": epta, "eptb": eptb, "offw": offw,
            "cosk": C, "sink": S,
        }
        m.update(shared2)
        maps.append(m)
    return maps



def build_launch3():
    nc = bass.Bass("TRN2", target_bir_lowering=False, debug=False)
    ein = lambda n, sh, dt: nc.dram_tensor(n, sh, dt, kind="ExternalInput").ap()
    eout = lambda n, sh, dt: nc.dram_tensor(n, sh, dt, kind="ExternalOutput").ap()
    xn1i = ein("xn1i", [DC, 128, TL], BF16)
    x1i = ein("x1i", [DC, 128, TL], F32)
    g1i = ein("g1i", [DC, 128, TL], F32)
    wqs = ein("wqs", [4, 128, D], F32)
    wqg = ein("wqg", [4, 128, D], F32)
    wo1 = ein("wo1", [DC, 128, 1024], F32)
    lsrep = ein("lsrep", [128, 4], F32)
    swp = ein("swp", [128, 128], F32)
    ident = ein("ident", [128, 128], F32)
    cosq = ein("cosq", [128, TL], F32)
    sinq = ein("sinq", [128, TL], F32)
    kst = ein("kst", [4, 128, 112], F32)
    kgt = ein("kgt", [4, 128, 112], F32)
    ksr = ein("ksr", [4, 128, 1024], F32)
    kgr = ein("kgr", [4, 128, 1024], F32)
    vt = ein("vt", [112, 256], F32)
    vrr = ein("vrr", [8, 128, 256], F32)
    pmask = ein("pmask", [NB, 128, 112], F32)
    rmask = ein("rmask", [128, 1024], F32)
    fixm = ein("fixm", [128, 128], F32)
    outp = eout("outp", [DC, 128, TL], F32)

    es = contextlib.ExitStack()
    with es:
        ctx = Ctx(nc, es)
        sb = lambda n, sh, dt: es.enter_context(nc.sbuf_tensor(n, sh, dt))
        xn1 = sb("xn1", [128, DC * TL], BF16)
        identb = sb("identb", [128, 128], BF16)
        swpb = sb("swpb", [128, 128], BF16)
        cost = sb("cost", [128, TL], F32)
        sint = sb("sint", [128, TL], F32)
        sclc = sb("sclc", [128, 4], F32)
        ones1 = sb("ones1", [128, 1], F32)
        ones1b = sb("ones1b", [128, 1], BF16)
        kstb = sb("kstb", [128, 4 * 112], BF16)
        kgtb = sb("kgtb", [128, 4 * 112], BF16)
        ksrb = sb("ksrb", [128, 4 * TL], BF16)
        kgrb = sb("kgrb", [128, 4 * TL], BF16)
        vtb = sb("vtb", [112, 256], BF16)
        vrb = sb("vrb", [128, 8 * 256], BF16)
        pmt = sb("pmt", [128, NB * 112], F32)
        rmt = sb("rmt", [128, 1024], F32)
        fmt = sb("fmt", [128, 128], F32)
        vbarc = sb("vbarc", [128, 4], F32)
        qs = sb("qs", [128, 4 * TL], BF16)
        qg = sb("qg", [128, 4 * TL], BF16)
        qgf = sb("qgf", [128, TL], F32)
        att = sb("att", [128, 8 * TL], BF16)
        zp = sb("zp", [128, 112], F32)
        zr = sb("zr", [128, 1024], F32)
        ep = sb("ep", [128, 112], BF16)
        er = sb("er", [128, 1024], BF16)
        etp = sb("etp", [112, 128], BF16)
        etr = sb("etr", [128, 8 * 128], BF16)
        rowv = sb("rowv", [128, 8], F32)
        wsl = sb("wsl", [128, 2048], BF16)
        x1t = sb("x1t", [128, TL], F32)
        gbuf = sb("gbuf", [128, TL], F32)
        ot = sb("ot", [128, TL], F32)
        P = [es.enter_context(nc.psum_tensor(f"pb{i}", [128, 512], F32)) for i in range(8)]

        s_tn, s_act, s_dve = ctx.sem("s_tn"), ctx.sem("s_act"), ctx.sem("s_dve")
        d_cn = ctx.sem("d_cn", dma=True)
        d_ld = ctx.sem("d_ld", dma=True)
        d_l2 = ctx.sem("d_l2", dma=True)
        d_st = ctx.sem("d_st", dma=True)
        d_g0, d_g1 = ctx.sem("d_g0", dma=True), ctx.sem("d_g1", dma=True)
        d_gs = ctx.sem("d_gs", dma=True)
        dws = [d_g0, d_g1]
        H2 = (slice(0, 512), slice(512, 1024))

        ctx.emit("dve", lambda v: v.memset(ones1[:], 1.0), inc=s_dve)
        ctx.emit("dve", lambda v: v.memset(ones1b[:], 1.0), inc=s_dve)
        n_const = s_dve.n
        ctx.emit("sy", lambda s: s.dma_start(out=cost[:], in_=cosq[:, :]), inc=d_cn, inck=16)
        ctx.emit("sy", lambda s: s.dma_start(out=sint[:], in_=sinq[:, :]), inc=d_cn, inck=16)
        ctx.emit("sy", lambda s: s.dma_start(out=sclc[:], in_=lsrep[:, :]), inc=d_cn, inck=16)
        for j in range(NB):
            ctx.emit("sy", lambda s, j=j: s.dma_start(out=pmt[:, j * 112:(j + 1) * 112], in_=pmask[j, :, :]), inc=d_cn, inck=16)
        ctx.emit("sy", lambda s: s.dma_start(out=rmt[:], in_=rmask[:, :]), inc=d_cn, inck=16)
        ctx.emit("sy", lambda s: s.dma_start(out=fmt[:], in_=fixm[:, :]), inc=d_cn, inck=16)
        ctx.emit("gp", lambda g: g.dma_start(out=identb[:], in_=ident[:, :]), inc=d_gs, inck=16)
        ctx.emit("gp", lambda g: g.dma_start(out=swpb[:], in_=swp[:, :]), inc=d_gs, inck=16)
        for g_ in range(4):
            ctx.emit("gp", lambda g, g_=g_: g.dma_start(out=kstb[:, g_ * 112:(g_ + 1) * 112], in_=kst[g_, :, :]), inc=d_gs, inck=16)
            ctx.emit("gp", lambda g, g_=g_: g.dma_start(out=kgtb[:, g_ * 112:(g_ + 1) * 112], in_=kgt[g_, :, :]), inc=d_gs, inck=16)
            ctx.emit("gp", lambda g, g_=g_: g.dma_start(out=ksrb[:, g_ * TL:(g_ + 1) * TL], in_=ksr[g_, :, :]), inc=d_gs, inck=16)
            ctx.emit("gp", lambda g, g_=g_: g.dma_start(out=kgrb[:, g_ * TL:(g_ + 1) * TL], in_=kgr[g_, :, :]), inc=d_gs, inck=16)
        ctx.emit("gp", lambda g: g.dma_start(out=vtb[:, :], in_=vt[:, :]), inc=d_gs, inck=16)
        for kc in range(8):
            ctx.emit("gp", lambda g, kc=kc: g.dma_start(out=vrb[:, kc * 256:(kc + 1) * 256], in_=vrr[kc, :, :]), inc=d_gs, inck=16)
        for c in range(DC):
            ctx.emit("sy", lambda s, c=c: s.dma_start(out=xn1[:, c * TL:(c + 1) * TL], in_=xn1i[c, :, :]), inc=d_ld, inck=16)
        n_xn1 = d_ld.n
        ctx.emit("act", lambda a: a.activation(sclc[:], sclc[:], AF.Exp), waits=[(d_cn, 48)], inc=s_act)
        ctx.emit("dve", lambda v: v.tensor_scalar(sclc[:], sclc[:], 1.0 / np.sqrt(32.0), None, op0=ALU.mult),
                 waits=_now(s_act), inc=s_dve)
        nv1 = ctx.emit("tn", lambda t: t.matmul(P[0][0:128, 0:1], vtb[:, 0:128], ones1b[0:112, 0:1], start=True, stop=False),
                       waits=[(d_gs, d_gs.n), (s_dve, n_const)])
        ctx.emit("tn", lambda t: t.matmul(P[1][0:128, 0:1], vtb[:, 128:256], ones1b[0:112, 0:1], start=True, stop=False))
        for kc in range(8):
            ctx.emit("tn", lambda t, kc=kc: t.matmul(P[0][0:128, 0:1], vrb[:, kc * 256:kc * 256 + 128],
                                                     ones1b[:, 0:1], start=False, stop=(kc == 7)))
            ctx.emit("tn", lambda t, kc=kc: t.matmul(P[1][0:128, 0:1], vrb[:, kc * 256 + 128:(kc + 1) * 256],
                                                     ones1b[:, 0:1], start=False, stop=(kc == 7)),
                     inc=(s_tn if kc == 7 else None))
        for g_ in range(4):
            src = P[g_ // 2]
            lo = (g_ % 2) * 64
            for half in range(2):
                ctx.emit("dve", lambda v, g_=g_, src=src, lo=lo, half=half: v.tensor_scalar(
                    vbarc[half * 64:(half + 1) * 64, g_:g_ + 1], src[lo:lo + 64, 0:1], 1.0 / 1136.0, None, op0=ALU.mult),
                    waits=_now(s_tn), inc=s_dve)
        ctx.barrier()

        wq_marks = []
        for qt in range(4):
            b = qt % 2
            wts = [(s_tn, wq_marks[qt - 2][0])] if qt >= 2 else []
            nw = ctx.emit("gp", lambda g, qt=qt: g.dma_start(out=wsl[:], in_=wqs[qt, :, :]),
                          waits=(_now(s_tn) if qt > 0 else []), inc=d_g0, inck=16)
            ml = None
            for th in range(2):
                for dc in range(DC):
                    ml = ctx.emit("tn", lambda t, dc=dc, th=th: t.matmul(
                        P[th][:, :], wsl[:, dc * 128:(dc + 1) * 128],
                        xn1[:, dc * TL + H2[th].start:dc * TL + H2[th].stop],
                        start=(dc == 0), stop=(dc == DC - 1)),
                        waits=([(d_g0, nw), (d_ld, n_xn1)] + _now(s_dve, s_act) if dc == 0 else []),
                        inc=(s_tn if dc == DC - 1 else None))
                ctx.emit("dve", lambda v, qt=qt, th=th: v.tensor_scalar(
                    qs[:, qt * TL + H2[th].start:qt * TL + H2[th].stop], P[th][:, :],
                    sclc[:, qt:qt + 1], None, op0=ALU.mult),
                    waits=[(s_tn, ml)], inc=s_dve)
            nw2 = ctx.emit("gp", lambda g, qt=qt: g.dma_start(out=wsl[:], in_=wqg[qt, :, :]),
                           waits=_now(s_tn), inc=d_g0, inck=16)
            for th in range(2):
                for dc in range(DC):
                    ml = ctx.emit("tn", lambda t, dc=dc, th=th: t.matmul(
                        P[2 + th][:, :], wsl[:, dc * 128:(dc + 1) * 128],
                        xn1[:, dc * TL + H2[th].start:dc * TL + H2[th].stop],
                        start=(dc == 0), stop=(dc == DC - 1)),
                        waits=([(d_g0, nw2)] + _now(s_dve) if dc == 0 else []),
                        inc=(s_tn if dc == DC - 1 else None))
                ctx.emit("dve", lambda v, th=th: v.tensor_copy(qgf[:, H2[th]], P[2 + th][:, :]),
                         waits=[(s_tn, ml)], inc=s_dve)
                ctx.emit("dve", lambda v, qt=qt, th=th: v.tensor_copy(
                    qg[:, qt * TL + H2[th].start:qt * TL + H2[th].stop], qgf[:, H2[th]]),
                    waits=_now(s_dve), inc=s_dve)
                ns = ctx.emit("tn", lambda t, qt=qt, th=th: t.matmul(
                    P[4 + th][:, :], swpb[:], qg[:, qt * TL + H2[th].start:qt * TL + H2[th].stop],
                    start=True, stop=True),
                    waits=_now(s_dve) + [(d_gs, 32)], inc=s_tn)
                ctx.emit("dve", lambda v, th=th: v.tensor_tensor(
                    qgf[:, H2[th]], qgf[:, H2[th]], cost[:, H2[th]], op=ALU.mult),
                    waits=[(s_tn, s_tn.n), (d_cn, d_cn.n)], inc=s_dve)
                ctx.emit("dve", lambda v, th=th: v.tensor_tensor(
                    x1t[:, H2[th]], P[4 + th][:, :], sint[:, H2[th]], op=ALU.mult),
                    waits=_now(s_dve), inc=s_dve)
                ctx.emit("dve", lambda v, th=th: v.tensor_tensor(
                    qgf[:, H2[th]], qgf[:, H2[th]], x1t[:, H2[th]], op=ALU.add),
                    waits=_now(s_dve), inc=s_dve)
                ctx.emit("dve", lambda v, qt=qt, th=th: v.tensor_scalar(
                    qg[:, qt * TL + H2[th].start:qt * TL + H2[th].stop], qgf[:, H2[th]],
                    sclc[:, qt:qt + 1], None, op0=ALU.mult),
                    waits=_now(s_dve), inc=s_dve)
            wq_marks.append((ml,))
        ctx.barrier()

        for qt in range(4):
            for m in range(4):
                base = 32 * m
                for qc in range(NB):
                    is_raw = (qc == NB - 1)
                    qsl = slice(qc * 128, (qc + 1) * 128)
                    n1 = ctx.emit("tn", lambda t, qt=qt, m=m, base=base, qsl=qsl: t.matmul(
                        P[0][:, 0:112], qs[base:base + 32, qt * TL + qsl.start:qt * TL + qsl.stop],
                        kstb[base:base + 32, qt * 112:(qt + 1) * 112],
                        start=True, stop=False, tile_position=(base, 0)),
                        waits=_now(s_dve, s_act))
                    n1 = ctx.emit("tn", lambda t, qt=qt, m=m, base=base, qsl=qsl: t.matmul(
                        P[0][:, 0:112], qg[base:base + 32, qt * TL + qsl.start:qt * TL + qsl.stop],
                        kgtb[base:base + 32, qt * 112:(qt + 1) * 112],
                        start=False, stop=True, tile_position=(base, 0)),
                        inc=s_tn)
                    if is_raw:
                        for rh in range(2):
                            ctx.emit("tn", lambda t, qt=qt, base=base, qsl=qsl, rh=rh: t.matmul(
                                P[1 + rh][:, :], qs[base:base + 32, qt * TL + qsl.start:qt * TL + qsl.stop],
                                ksrb[base:base + 32, qt * TL + H2[rh].start:qt * TL + H2[rh].stop],
                                start=True, stop=False, tile_position=(base, 0)))
                            ctx.emit("tn", lambda t, qt=qt, base=base, qsl=qsl, rh=rh: t.matmul(
                                P[1 + rh][:, :], qg[base:base + 32, qt * TL + qsl.start:qt * TL + qsl.stop],
                                kgrb[base:base + 32, qt * TL + H2[rh].start:qt * TL + H2[rh].stop],
                                start=False, stop=True, tile_position=(base, 0)),
                                inc=(s_tn if rh == 1 else None))
                    ctx.emit("dve", lambda v, qc=qc: v.scalar_tensor_tensor(
                        zp[:], P[0][:, 0:112], 30000.0, pmt[:, qc * 112:(qc + 1) * 112],
                        op0=ALU.add, op1=ALU.mult),
                        waits=_now(s_tn) + [(d_cn, d_cn.n), (s_act, s_act.n)], inc=s_dve)
                    ctx.emit("dve", lambda v: v.reduce_max(rowv[:, 0:1], zp[:], axis=AX.X, negate=True),
                             waits=_now(s_dve), inc=s_dve)
                    if is_raw:
                        for rh in range(2):
                            ctx.emit("dve", lambda v, rh=rh: v.scalar_tensor_tensor(
                                zr[:, H2[rh]], P[1 + rh][:, :], 30000.0, rmt[:, H2[rh]],
                                op0=ALU.add, op1=ALU.mult),
                                waits=_now(s_tn), inc=s_dve)
                        ctx.emit("dve", lambda v: v.reduce_max(rowv[:, 1:2], zr[:], axis=AX.X, negate=True),
                                 waits=_now(s_dve), inc=s_dve)
                        ctx.emit("dve", lambda v: v.tensor_tensor(rowv[:, 0:1], rowv[:, 0:1], rowv[:, 1:2], op=ALU.min),
                                 waits=_now(s_dve), inc=s_dve)
                    na = ctx.emit("act", lambda a: a.activation(
                        ep[:], zp[:], AF.Exp, bias=rowv[:, 0:1], accum_out=rowv[:, 2:3]),
                        waits=_now(s_dve), inc=s_act)
                    if is_raw:
                        na = ctx.emit("act", lambda a: a.activation(
                            er[:], zr[:], AF.Exp, bias=rowv[:, 0:1], accum_out=rowv[:, 3:4]),
                            waits=_now(s_act), inc=s_act)
                        ctx.emit("dve", lambda v: v.tensor_tensor(rowv[:, 2:3], rowv[:, 2:3], rowv[:, 3:4], op=ALU.add),
                                 waits=[(s_act, na)], inc=s_dve)
                    ctx.emit("dve", lambda v: v.reciprocal(rowv[:, 4:5], rowv[:, 2:3]),
                             waits=[(s_act, na)] + _now(s_dve), inc=s_dve)
                    ctx.emit("dve", lambda v: v.tensor_scalar(ep[:], ep[:], rowv[:, 4:5], None, op0=ALU.mult),
                             waits=_now(s_dve), inc=s_dve)
                    nt = ctx.emit("tn", lambda t: t.transpose(P[3].bitcast(BF16)[0:112, 0:128], ep[:], identb[:]),
                                  waits=_now(s_dve) + [(d_gs, 32)], inc=s_tn)
                    ctx.emit("dve", lambda v: v.tensor_copy(etp[:, :], P[3].bitcast(BF16)[0:112, 0:128]),
                             waits=[(s_tn, nt)], inc=s_dve)
                    if is_raw:
                        ctx.emit("dve", lambda v: v.tensor_scalar(er[:], er[:], rowv[:, 4:5], None, op0=ALU.mult),
                                 waits=_now(s_dve), inc=s_dve)
                        for kc in range(8):
                            nt = ctx.emit("tn", lambda t, kc=kc: t.transpose(
                                P[3].bitcast(BF16)[:, 0:128], er[:, kc * 128:(kc + 1) * 128], identb[:]),
                                waits=_now(s_dve), inc=s_tn)
                            ctx.emit("dve", lambda v, kc=kc: v.tensor_copy(
                                etr[:, kc * 128:(kc + 1) * 128], P[3].bitcast(BF16)[:, 0:128]),
                                waits=[(s_tn, nt)], inc=s_dve)
                    npv = ctx.emit("tn", lambda t, qt=qt, is_raw=is_raw: t.matmul(
                        P[6][0:64, 0:128], vtb[:, qt * 64:(qt + 1) * 64], etp[:, :],
                        start=True, stop=(not is_raw)),
                        waits=_now(s_dve), inc=(None if is_raw else s_tn))
                    if is_raw:
                        for kc in range(8):
                            npv = ctx.emit("tn", lambda t, qt=qt, kc=kc: t.matmul(
                                P[6][0:64, 0:128], vrb[:, kc * 256 + qt * 64:kc * 256 + (qt + 1) * 64],
                                etr[:, kc * 128:(kc + 1) * 128],
                                start=False, stop=(kc == 7)),
                                inc=(s_tn if kc == 7 else None))
                    h = 4 * qt + m
                    jt, sub = h // 2, h % 2
                    if qc == 0:
                        ctx.emit("dve", lambda v, jt=jt, sub=sub, qsl=qsl: v.tensor_copy(
                            att[sub * 64:(sub + 1) * 64, jt * TL + qsl.start:jt * TL + qsl.stop],
                            P[6][0:64, 0:128]),
                            waits=[(s_tn, s_tn.n)], inc=s_dve)
                        sl2 = slice(sub * 64, (sub + 1) * 64)
                        ctx.emit("dve", lambda v, jt=jt, sl2=sl2, qt=qt: v.tensor_scalar(
                            x1t[sl2, 0:128], att[sl2, jt * TL:jt * TL + 128],
                            vbarc[sl2, qt:qt + 1], None, op0=ALU.subtract),
                            waits=_now(s_dve), inc=s_dve)
                        ctx.emit("dve", lambda v, sl2=sl2: v.tensor_tensor(
                            x1t[sl2, 0:128], x1t[sl2, 0:128], fmt[sl2, 0:128], op=ALU.mult),
                            waits=_now(s_dve) + [(d_cn, d_cn.n)], inc=s_dve)
                        ctx.emit("dve", lambda v, jt=jt, sl2=sl2: v.tensor_tensor(
                            att[sl2, jt * TL:jt * TL + 128],
                            att[sl2, jt * TL:jt * TL + 128],
                            x1t[sl2, 0:128], op=ALU.subtract),
                            waits=_now(s_dve), inc=s_dve)
                    else:
                        ctx.emit("dve", lambda v, jt=jt, sub=sub, qsl=qsl: v.tensor_copy(
                            att[sub * 64:(sub + 1) * 64, jt * TL + qsl.start:jt * TL + qsl.stop],
                            P[6][0:64, 0:128]),
                            waits=[(s_tn, s_tn.n)], inc=s_dve)
        ctx.barrier()

        s1_marks = []
        for co in range(DC):
            nw = ctx.emit("gp", lambda g, co=co: g.dma_start(out=wsl[:, 0:1024], in_=wo1[co, :, :]),
                          waits=(_now(s_tn) if co > 0 else []), inc=d_g0, inck=16)
            nx = ctx.emit("sy", lambda s, co=co: s.dma_start(out=x1t[:], in_=x1i[co, :, :]),
                          waits=(_now(s_dve) if co > 0 else []), inc=d_ld, inck=16)
            ng = ctx.emit("sy", lambda s, co=co: s.dma_start(out=gbuf[:], in_=g1i[co, :, :]),
                          waits=(_now(s_dve) if co > 0 else []), inc=d_l2, inck=16)
            ml = None
            for th in range(2):
                po = P[4 + th]
                for j in range(8):
                    ml = ctx.emit("tn", lambda t, j=j, th=th, po=po: t.matmul(
                        po[:, :], wsl[:, j * 128:(j + 1) * 128],
                        att[:, j * TL + H2[th].start:j * TL + H2[th].stop],
                        start=(j == 0), stop=(j == 7)),
                        waits=([(d_g0, nw)] + _now(s_dve) if j == 0 else []),
                        inc=(s_tn if j == 7 else None))
                ctx.emit("dve", lambda v, th=th, po=po: v.tensor_tensor(
                    ot[:, H2[th]], po[:, :], gbuf[:, H2[th]], op=ALU.mult),
                    waits=[(s_tn, ml), (d_l2, ng), (d_ld, nx), (d_st, d_st.n)], inc=s_dve)
                ctx.emit("dve", lambda v, th=th: v.tensor_tensor(
                    ot[:, H2[th]], ot[:, H2[th]], x1t[:, H2[th]], op=ALU.add),
                    waits=_now(s_dve), inc=s_dve)
            s1_marks.append(ml)
            ctx.emit("sy", lambda s, co=co: s.dma_start(out=outp[co, :, :], in_=ot[:]),
                     waits=_now(s_dve), inc=d_st, inck=16)
        for e in ENGINES:
            ctx.ops[e].append(([(d_st.h, d_st.n)], None, None))
        ctx.finalize()
    return nc


def _prep_shared_l3(inp):
    f = lambda n: np.ascontiguousarray(np.asarray(inp[n], np.float32))
    out = {}
    out["wqs"] = _tile_w(f("L1_attn_Wq_sem"))
    out["wqg"] = _tile_w(f("L1_attn_Wq_geo"))
    out["wo1"] = _tile_w(f("L1_attn_Wo"))
    ls = f("L1_attn_logit_scale")
    lsrep = np.zeros((128, 4), np.float32)
    for qt in range(4):
        for p in range(128):
            lsrep[p, qt] = ls[4 * qt + p // 32]
    out["lsrep"] = lsrep
    out["swp"] = _swap_perm()
    out["ident"] = np.eye(128, dtype=np.float32)
    return out


def _launch3_inmaps(inp, shared3, l2res):
    ks2g = np.zeros((128, 112), np.float32)
    kg2g = np.zeros((128, 112), np.float32)
    v2g = np.zeros((112, 256), np.float32)
    for g in range(56):
        c, j = g % 8, g // 8
        ks2g[:, 2 * g:2 * g + 2] = np.asarray(l2res[c]["ks2o"])[:, 2 * j:2 * j + 2]
        kg2g[:, 2 * g:2 * g + 2] = np.asarray(l2res[c]["kg2o"])[:, 2 * j:2 * j + 2]
        v2g[2 * g:2 * g + 2, :] = np.asarray(l2res[c]["v2o"])[2 * j:2 * j + 2, :]
    ksrg = np.concatenate([np.asarray(l2res[c]["ksro"]) for c in range(NC)], axis=1)
    kgrg = np.concatenate([np.asarray(l2res[c]["kgro"]) for c in range(NC)], axis=1)
    vrg = np.stack([np.asarray(l2res[c]["vro"]) for c in range(NC)])
    kst = np.stack([np.tile(ks2g[32 * g:32 * (g + 1), :], (4, 1)) for g in range(4)])
    kgt = np.stack([np.tile(kg2g[32 * g:32 * (g + 1), :], (4, 1)) for g in range(4)])
    ksr = np.stack([np.tile(ksrg[32 * g:32 * (g + 1), :], (4, 1)) for g in range(4)])
    kgr = np.stack([np.tile(kgrg[32 * g:32 * (g + 1), :], (4, 1)) for g in range(4)])
    k_pos = np.concatenate([np.arange(112) * 64 + 63, np.arange(7168, T)])
    maps = []
    for i in range(NC):
        toks = _core_tokens(i)
        C, S = _rope_tables(toks)
        pmask = np.zeros((NB, 128, 112), np.float32)
        for j in range(NB):
            qpos = (i + 8 * j) * 128 + np.arange(128)
            pmask[j] = (k_pos[None, 0:112] <= qpos[:, None]).astype(np.float32)
        qpos7 = (56 + i) * 128 + np.arange(128)
        rmask = (k_pos[None, 112:] <= qpos7[:, None]).astype(np.float32)
        fixm = np.zeros((128, 128), np.float32)
        if i == 0:
            for r in range(63):
                fixm[:, r] = 1.0
        m = {
            "xn1i": np.asarray(l2res[i]["xn1o"]),
            "x1i": np.asarray(l2res[i]["x1o"]),
            "g1i": np.asarray(l2res[i]["g1o"]),
            "cosq": C, "sinq": S,
            "kst": np.ascontiguousarray(kst), "kgt": np.ascontiguousarray(kgt),
            "ksr": np.ascontiguousarray(ksr), "kgr": np.ascontiguousarray(kgr),
            "vt": np.ascontiguousarray(v2g), "vrr": np.ascontiguousarray(vrg),
            "pmask": pmask, "rmask": np.ascontiguousarray(rmask), "fixm": fixm,
        }
        m.update(shared3)
        maps.append(m)
    return maps


def kernel(**inputs):
    shared1 = _prep_shared_l1(inputs)
    nc1 = build_launch1()
    res1 = run_bass_kernel_spmd(nc1, _launch1_inmaps(inputs, shared1), core_ids=list(range(NC)))
    shared2 = _prep_shared_l2(inputs)
    nc2 = build_launch2()
    res2 = run_bass_kernel_spmd(nc2, _launch2_inmaps(inputs, shared2, res1.results), core_ids=list(range(NC)))
    shared3 = _prep_shared_l3(inputs)
    nc3 = build_launch3()
    res3 = run_bass_kernel_spmd(nc3, _launch3_inmaps(inputs, shared3, res2.results), core_ids=list(range(NC)))
    out = np.zeros((T, D), np.float32)
    for i in range(NC):
        o = np.asarray(res3.results[i]["outp"])
        for j in range(NB):
            g = i + 8 * j
            out[g * 128:(g + 1) * 128, :] = o[:, :, j * 128:(j + 1) * 128].transpose(2, 0, 1).reshape(128, D)
    return out[None]


# version 09
# speedup vs baseline: 1.5356x; 1.1765x over previous
import contextlib
import numpy as np

import concourse.bass as bass
import concourse.mybir as mybir
from concourse.bass_utils import run_bass_kernel_spmd

F32 = mybir.dt.float32
BF16 = mybir.dt.bfloat16
AF = mybir.ActivationFunctionType
ALU = mybir.AluOpType
AX = mybir.AxisListType

NC = 8
T = 8192
D = 2048
TL = 1024
NB = 8
DC = 16
FF = 8192
FC = 64
DFF1 = 1408
LN_EPS = 1e-5

ENGINES = ("tn", "act", "dve", "gp", "sy")
_ENG_ATTR = {"tn": "tensor", "act": "scalar", "dve": "vector", "gp": "gpsimd", "sy": "sync"}



def _tile_w(w_in_out):
    n_in, n_out = w_in_out.shape
    ic, oc = n_in // 128, (n_out + 127) // 128
    if n_out % 128:
        w_in_out = np.concatenate(
            [w_in_out, np.zeros((n_in, oc * 128 - n_out), w_in_out.dtype)], 1)
    w4 = w_in_out.reshape(ic, 128, oc, 128)
    return np.ascontiguousarray(w4.transpose(2, 1, 0, 3).reshape(oc, 128, ic * 128))


def _vec_pc(v):
    return np.ascontiguousarray(np.asarray(v, np.float32).reshape(-1, 128).T)


def _core_tokens(i):
    idx = []
    for j in range(NB):
        g = i + 8 * j
        idx.extend(range(g * 128, (g + 1) * 128))
    return np.array(idx)



class Sem:
    __slots__ = ("h", "n", "name", "dma")

    def __init__(self, h, name, dma=False):
        self.h, self.n, self.name, self.dma = h, 0, name, dma


class Ctx:
    def __init__(self, nc, es):
        self.nc = nc
        self.es = es
        self.ops = {e: [] for e in ENGINES}
        self.sems = {}

    def sem(self, name, dma=False):
        s = Sem(self.es.enter_context(self.nc.semaphore(name)), name, dma)
        self.sems[name] = s
        return s

    def emit(self, eng, fn, waits=(), inc=None, inck=1):
        w = [(s.h, int(v)) for s, v in waits if v > 0]
        if inc is not None and inc.dma and inc.n > 0:
            w.append((inc.h, inc.n))
        tgt = None
        if inc is not None:
            inc.n += inck
            tgt = (inc.h, inck)
        self.ops[eng].append((w, fn, tgt))
        return inc.n if inc is not None else None

    def barrier(self):
        cur = [(s, s.n) for s in self.sems.values() if s.n > 0]
        for e in ENGINES:
            for s, v in cur:
                self.ops[e].append(([(s.h, v)], None, None))

    def finalize(self):
        block = self.es.enter_context(self.nc.Block())
        for name in ENGINES:
            ops = self.ops[name]

            def make(ops):
                def body(eng):
                    for w, fn, tgt in ops:
                        for h, v in w:
                            eng.wait_ge(h, v)
                        if fn is None:
                            continue
                        ins = fn(eng)
                        if tgt is not None:
                            ins.then_inc(tgt[0], tgt[1])
                return body
            getattr(block, _ENG_ATTR[name])(make(ops))


def _now(*sems):
    return [(s, s.n) for s in sems]



def build_launch1():
    nc = bass.Bass("TRN2", target_bir_lowering=False, debug=False)
    ein = lambda n, sh, dt: nc.dram_tensor(n, sh, dt, kind="ExternalInput").ap()
    eout = lambda n, sh, dt: nc.dram_tensor(n, sh, dt, kind="ExternalOutput").ap()
    xfm = ein("xfm", [DC, 128, TL], F32)
    xbnd = ein("xbnd", [NB, D], F32)
    ident = ein("ident", [128, 128], F32)
    wk = ein("wk", [DC, 128, D], F32)
    wv = ein("wv", [DC, 128, D], F32)
    wr = ein("wr", [DC, 128, D], F32)
    gw = ein("gw", [128, 32], F32)
    wg = ein("wg", [2, FC, 128, D], F32)
    wu = ein("wu", [2, FC, 128, D], F32)
    wd = ein("wd", [2, DC, 128, FF], F32)
    td = ein("td", [128, DC], F32)
    tmk = ein("tmk", [128, DC], F32)
    tmv = ein("tmv", [128, DC], F32)
    tmr = ein("tmr", [128, DC], F32)
    a_out = eout("a_out", [DC, 128, TL], F32)
    b_out = eout("b_out", [DC, 128, TL], F32)
    r_out = eout("r_out", [DC, 128, TL], BF16)
    g0_out = eout("g0_out", [DC, 128, TL], F32)
    zacc = nc.dram_tensor("zacc", [DC, 128, TL], F32).ap()

    es = contextlib.ExitStack()
    with es:
        ctx = Ctx(nc, es)
        sb = lambda n, sh, dt: es.enter_context(nc.sbuf_tensor(n, sh, dt))
        xn = sb("xn", [128, DC * TL], BF16)
        ones1 = sb("ones1", [128, 1], F32)
        epsc = sb("epsc", [128, 1], F32)
        onesrow = sb("onesrow", [1, 128], F32)
        identb = sb("identb", [128, 128], BF16)
        w0bc = sb("w0bc", [128, TL], F32)
        w1bc = sb("w1bc", [128, TL], F32)
        tmm = sb("tmm", [128, 3 * DC], F32)
        gwt = sb("gwt", [128, 32], BF16)
        xnbf = sb("xnbf", [128, DC * NB], BF16)
        xtile = [sb(f"xt{i}", [128, TL], F32) for i in range(2)]
        pha = contextlib.ExitStack()
        vr = pha.enter_context(nc.sbuf_tensor("vr", [1, 8 * TL], F32))
        brow = pha.enter_context(nc.sbuf_tensor("brow", [8, 16], F32))
        g2 = pha.enter_context(nc.sbuf_tensor("g2", [2, TL], F32))
        xbt = pha.enter_context(nc.sbuf_tensor("xbt", [NB, D], F32))
        xbsq = pha.enter_context(nc.sbuf_tensor("xbsq", [NB, D], F32))
        sq = pha.enter_context(nc.sbuf_tensor("sq", [128, TL], F32))
        mubc = pha.enter_context(nc.sbuf_tensor("mubc", [128, TL], F32))
        rbc = pha.enter_context(nc.sbuf_tensor("rbc", [128, TL], F32))
        P = [es.enter_context(nc.psum_tensor(f"pb{i}", [128, 512], F32)) for i in range(8)]

        s_tn, s_act, s_dve = ctx.sem("s_tn"), ctx.sem("s_act"), ctx.sem("s_dve")
        d_cn = ctx.sem("d_cn", dma=True)
        d_ld = ctx.sem("d_ld", dma=True)
        d_st = ctx.sem("d_st", dma=True)
        d_g0, d_g1 = ctx.sem("d_g0", dma=True), ctx.sem("d_g1", dma=True)
        d_gs = ctx.sem("d_gs", dma=True)
        d_ac = ctx.sem("d_ac", dma=True)
        dws = [d_g0, d_g1]
        H2 = (slice(0, 512), slice(512, 1024))

        def bcast_row(row_ap, dst, bank, wait_sems):
            for h in range(2):
                sl = H2[h]
                n = ctx.emit("tn", lambda t, sl=sl: t.matmul(
                    P[bank][:, :], onesrow[:], row_ap[:, sl], start=True, stop=True),
                    waits=list(wait_sems) + _now(s_dve), inc=s_tn)
                ctx.emit("dve", lambda v, sl=sl: v.tensor_copy(dst[:, sl], P[bank][:, :]),
                         waits=[(s_tn, n)], inc=s_dve)

        ctx.emit("dve", lambda v: v.memset(ones1[:], 1.0), inc=s_dve)
        ctx.emit("dve", lambda v: v.memset(epsc[:], LN_EPS), inc=s_dve)
        ctx.emit("dve", lambda v: v.memset(onesrow[:], 1.0), inc=s_dve)
        n_const = s_dve.n
        ctx.emit("sy", lambda s: s.dma_start(out=xbt[:], in_=xbnd[:, :]), inc=d_cn, inck=16)
        ctx.emit("sy", lambda s: s.dma_start(out=tmm[:, 0:DC], in_=td[:, :]), inc=d_cn, inck=16)
        ctx.emit("sy", lambda s: s.dma_start(out=tmm[:, 0:DC], in_=tmk[:, :]), inc=d_cn, inck=16)
        ctx.emit("sy", lambda s: s.dma_start(out=tmm[:, DC:2 * DC], in_=tmv[:, :]), inc=d_cn, inck=16)
        ctx.emit("sy", lambda s: s.dma_start(out=tmm[:, 2 * DC:3 * DC], in_=tmr[:, :]), inc=d_cn, inck=16)
        ctx.emit("gp", lambda g: g.dma_start(out=gwt[:], in_=gw[:, :]), inc=d_gs, inck=16)
        ctx.emit("gp", lambda g: g.dma_start(out=identb[:], in_=ident[:, :]), inc=d_gs, inck=16)
        n_tm = ctx.emit("dve", lambda v: v.tensor_scalar(tmm[:], tmm[:], -1.0, None, op0=ALU.add),
                        waits=[(d_cn, d_cn.n)], inc=s_dve)

        sq_marks = []
        for c in range(DC):
            buf = xtile[c % 2]
            wts = [(s_tn, sq_marks[c - 2][1])] if c >= 2 else []
            n = ctx.emit("sy", lambda s, c=c, buf=buf: s.dma_start(out=buf[:], in_=xfm[c, :, :]),
                         waits=wts, inc=d_ld, inck=16)
            na = ctx.emit("act", lambda a, buf=buf: a.activation(sq[:], buf[:], AF.Square),
                          waits=[(d_ld, n)] + ([(s_tn, sq_marks[c - 1][1])] if c >= 1 else []),
                          inc=s_act)
            last = None
            for h in range(2):
                ctx.emit("tn", lambda t, buf=buf, h=h, c=c: t.matmul(
                    P[h][0:1, :], ones1[:], buf[:, H2[h]], start=(c == 0), stop=(c == DC - 1)),
                    waits=[(s_act, na)] + ([(s_dve, n_const)] if c == 0 else []))
                last = ctx.emit("tn", lambda t, h=h, c=c: t.matmul(
                    P[2 + h][0:1, :], ones1[:], sq[:, H2[h]], start=(c == 0), stop=(c == DC - 1)),
                    waits=[(s_act, na)], inc=s_tn)
            sq_marks.append((n, last))
        V = lambda i: vr[0:1, i * TL:(i + 1) * TL]
        Vh = lambda i, h: vr[0:1, i * TL + H2[h].start:i * TL + H2[h].stop]
        for h in range(2):
            ctx.emit("dve", lambda v, h=h: v.tensor_copy(Vh(0, h), P[h][0:1, :]),
                     waits=_now(s_tn), inc=s_dve)
            ctx.emit("dve", lambda v, h=h: v.tensor_copy(Vh(1, h), P[2 + h][0:1, :]),
                     waits=_now(s_tn), inc=s_dve)
        ctx.emit("dve", lambda v: v.tensor_scalar(V(2), V(0), 1.0 / D, None, op0=ALU.mult),
                 waits=_now(s_dve), inc=s_dve)
        ctx.emit("dve", lambda v: v.tensor_scalar(V(1), V(1), 1.0 / D, None, op0=ALU.mult),
                 waits=_now(s_dve), inc=s_dve)
        ctx.emit("dve", lambda v: v.tensor_tensor(V(4), V(2), V(2), op=ALU.mult),
                 waits=_now(s_dve), inc=s_dve)
        ctx.emit("dve", lambda v: v.tensor_tensor(V(5), V(1), V(4), op=ALU.subtract),
                 waits=_now(s_dve), inc=s_dve)
        ctx.emit("act", lambda a: a.activation(V(6), V(5), AF.Ln, bias=epsc[0:1, 0:1]),
                 waits=_now(s_dve), inc=s_act)
        ctx.emit("act", lambda a: a.activation(V(3), V(6), AF.Exp, scale=-0.5),
                 waits=_now(s_act), inc=s_act)
        bcast_row(V(2), mubc, 4, _now(s_act))
        bcast_row(V(3), rbc, 4, _now(s_act))

        ap_marks = []
        for c in range(DC):
            buf = xtile[c % 2]
            wts = [(s_dve, ap_marks[c - 2])] if c >= 2 else _now(s_act)
            n = ctx.emit("sy", lambda s, c=c, buf=buf: s.dma_start(out=buf[:], in_=xfm[c, :, :]),
                         waits=wts, inc=d_ld, inck=16)
            ctx.emit("dve", lambda v, buf=buf: v.tensor_tensor(sq[:], buf[:], mubc[:], op=ALU.subtract),
                     waits=[(d_ld, n)] + _now(s_dve), inc=s_dve)
            m = ctx.emit("dve", lambda v, c=c: v.tensor_tensor(
                xn[:, c * TL:(c + 1) * TL], sq[:], rbc[:], op=ALU.mult),
                waits=_now(s_dve), inc=s_dve)
            ap_marks.append(m)

        ctx.emit("dve", lambda v: v.tensor_reduce(brow[0:8, 0:1], xbt[:], axis=AX.X, op=ALU.add),
                 waits=[(d_cn, 16)], inc=s_dve)
        ctx.emit("dve", lambda v: v.tensor_scalar(brow[0:8, 1:2], brow[0:8, 0:1], 1.0 / D, None, op0=ALU.mult),
                 waits=_now(s_dve), inc=s_dve)
        ctx.emit("act", lambda a: a.activation(xbsq[:], xbt[:], AF.Square, accum_out=brow[0:8, 2:3]),
                 waits=[(d_cn, 16)], inc=s_act)
        ctx.emit("dve", lambda v: v.tensor_scalar(brow[0:8, 3:4], brow[0:8, 2:3], 1.0 / D, None, op0=ALU.mult),
                 waits=_now(s_act), inc=s_dve)
        ctx.emit("dve", lambda v: v.tensor_tensor(brow[0:8, 4:5], brow[0:8, 1:2], brow[0:8, 1:2], op=ALU.mult),
                 waits=_now(s_dve), inc=s_dve)
        ctx.emit("dve", lambda v: v.tensor_tensor(brow[0:8, 5:6], brow[0:8, 3:4], brow[0:8, 4:5], op=ALU.subtract),
                 waits=_now(s_dve), inc=s_dve)
        ctx.emit("act", lambda a: a.activation(brow[0:8, 6:7], brow[0:8, 5:6], AF.Ln, bias=epsc[0:8, 0:1]),
                 waits=_now(s_dve), inc=s_act)
        ctx.emit("act", lambda a: a.activation(brow[0:8, 7:8], brow[0:8, 6:7], AF.Exp, scale=-0.5),
                 waits=_now(s_act), inc=s_act)
        ctx.emit("dve", lambda v: v.tensor_scalar(xbsq[:], xbt[:], brow[0:8, 1:2], None, op0=ALU.subtract),
                 waits=_now(s_act), inc=s_dve)
        xbt_bf = xbt.bitcast(BF16)
        ctx.emit("dve", lambda v: v.tensor_scalar(xbt_bf[:, 0:D], xbsq[:], brow[0:8, 7:8], None, op0=ALU.mult),
                 waits=_now(s_dve), inc=s_dve)
        for c in range(DC):
            n = ctx.emit("tn", lambda t, c=c: t.transpose(
                P[5].bitcast(BF16)[:, 0:8], xbt_bf[0:8, c * 128:(c + 1) * 128], identb[0:8, 0:8]),
                waits=_now(s_dve, d_gs), inc=s_tn)
            ctx.emit("dve", lambda v, c=c: v.tensor_copy(
                xnbf[:, c * NB:(c + 1) * NB], P[5].bitcast(BF16)[:, 0:8]),
                waits=[(s_tn, n)], inc=s_dve)

        for c in range(DC):
            for h in range(2):
                ctx.emit("tn", lambda t, c=c, h=h: t.matmul(
                    P[6 + h][0:2, :], gwt[:, 2 * c:2 * c + 2], xn[:, c * TL + H2[h].start:c * TL + H2[h].stop],
                    start=(c == 0), stop=(c == DC - 1)),
                    waits=(_now(s_dve, d_gs) if c == 0 else []),
                    inc=(s_tn if c == DC - 1 and h == 1 else None))
        for h in range(2):
            ctx.emit("dve", lambda v, h=h: v.tensor_copy(g2[0:2, H2[h]], P[6 + h][0:2, :]),
                     waits=_now(s_tn), inc=s_dve)
        ctx.emit("sy", lambda s: s.dma_start(out=V(1), in_=g2[1:2, 0:TL]),
                 waits=_now(s_dve), inc=d_cn, inck=16)
        ctx.emit("dve", lambda v: v.tensor_tensor(V(3), g2[0:1, 0:TL], V(1), op=ALU.subtract),
                 waits=[(d_cn, d_cn.n)], inc=s_dve)
        ctx.emit("act", lambda a: a.activation(V(4), V(3), AF.Abs),
                 waits=_now(s_dve), inc=s_act)
        ctx.emit("act", lambda a: a.activation(V(5), V(4), AF.Exp, scale=-1.0),
                 waits=_now(s_act), inc=s_act)
        ctx.emit("dve", lambda v: v.tensor_scalar(V(6), V(5), 1.0, None, op0=ALU.add),
                 waits=_now(s_act), inc=s_dve)
        ctx.emit("dve", lambda v: v.reciprocal(V(5), V(6)),
                 waits=_now(s_dve), inc=s_dve)
        ctx.emit("dve", lambda v: v.tensor_scalar(V(6), V(3), 0.0, None, op0=ALU.is_ge),
                 waits=_now(s_dve), inc=s_dve)
        ctx.emit("dve", lambda v: v.tensor_tensor(V(7), V(5), V(6), op=ALU.mult),
                 waits=_now(s_dve), inc=s_dve)
        ctx.emit("dve", lambda v: v.tensor_tensor(V(6), V(5), V(7), op=ALU.subtract),
                 waits=_now(s_dve), inc=s_dve)
        bcast_row(V(7), w0bc, 6, _now(s_dve))
        bcast_row(V(6), w1bc, 6, _now(s_dve))

        ctx.barrier()
        pha.close()

        with contextlib.ExitStack() as ph:
            hbuf = ph.enter_context(nc.sbuf_tensor("hbuf", [128, FC * TL], BF16))
            warena = ph.enter_context(nc.sbuf_tensor("warena", [128, 8192], BF16))
            hs = ph.enter_context(nc.sbuf_tensor("hs", [128, 1024], F32))
            zst = ph.enter_context(nc.sbuf_tensor("zst", [128, TL], F32))
            zstB = ph.enter_context(nc.sbuf_tensor("zstB", [128, TL], F32))
            zbuf1 = [zst, zstB]
            wgb = [warena[:, 0:2048], warena[:, 2048:4096]]
            wub = [warena[:, 4096:6144], warena[:, 6144:8192]]
            wdb = [warena[:, 0:4096], warena[:, 4096:8192]]
            for e in range(2):
                gu_marks = []
                cons_a = [0, 0]
                cons_d = [0, 0]
                for fc in range(FC):
                    b = fc % 2
                    wts = [(s_tn, gu_marks[fc - 2])] if fc >= 2 else []
                    ctx.emit("gp", lambda g, e=e, fc=fc, b=b: g.dma_start(
                        out=wgb[b][:], in_=wg[e, fc, :, :]), waits=wts, inc=dws[b], inck=16)
                    nu = ctx.emit("gp", lambda g, e=e, fc=fc, b=b: g.dma_start(
                        out=wub[b][:], in_=wu[e, fc, :, :]), inc=dws[b], inck=16)
                    mlast = None
                    for th in range(2):
                        pg, pu = P[2 * th], P[2 * th + 1]
                        pw = [(dws[b], nu)]
                        if cons_a[th]:
                            pw.append((s_act, cons_a[th]))
                        if cons_d[th]:
                            pw.append((s_dve, cons_d[th]))
                        for dc in range(DC):
                            ctx.emit("tn", lambda t, b=b, dc=dc, th=th, pg=pg: t.matmul(
                                pg[:, :], wgb[b][:, dc * 128:(dc + 1) * 128],
                                xn[:, dc * TL + H2[th].start:dc * TL + H2[th].stop],
                                start=(dc == 0), stop=(dc == DC - 1)),
                                waits=(pw if dc == 0 else []))
                        for dc in range(DC):
                            mlast = ctx.emit("tn", lambda t, b=b, dc=dc, th=th, pu=pu: t.matmul(
                                pu[:, :], wub[b][:, dc * 128:(dc + 1) * 128],
                                xn[:, dc * TL + H2[th].start:dc * TL + H2[th].stop],
                                start=(dc == 0), stop=(dc == DC - 1)),
                                inc=(s_tn if dc == DC - 1 else None))
                        na = ctx.emit("act", lambda a, pg=pg, th=th: a.activation(hs[:, H2[th]], pg[:, :], AF.Silu),
                                      waits=[(s_tn, mlast)] + ([(s_dve, cons_d[th])] if cons_d[th] else []),
                                      inc=s_act)
                        cons_a[th] = na
                        cons_d[th] = ctx.emit("dve", lambda v, fc=fc, th=th, pu=pu: v.tensor_tensor(
                            hbuf[:, fc * TL + H2[th].start:fc * TL + H2[th].stop], hs[:, H2[th]], pu[:, :], op=ALU.mult),
                            waits=[(s_act, na)], inc=s_dve)
                    gu_marks.append(mlast)
                ctx.barrier()
                wbc = w0bc if e == 0 else w1bc
                y_marks = []
                cons_y = [0, 0]
                pend = []
                for do in range(DC):
                    while len(pend) >= 2:
                        pend.pop(0)()
                    mlast = None
                    for fh in range(2):
                        b = fh
                        wts = [(s_tn, y_marks[do - 1])] if do >= 1 else []
                        nw = ctx.emit("gp", lambda g, e=e, do=do, fh=fh, b=b: g.dma_start(
                            out=wdb[b][:], in_=wd[e, do, :, fh * 4096:(fh + 1) * 4096]),
                            waits=wts, inc=dws[b], inck=16)
                        for th in range(2):
                            py = P[4 + th]
                            pw = [(dws[b], nw)]
                            if fh == 0 and cons_y[th]:
                                pw.append((s_dve, cons_y[th]))
                            for fi in range(32):
                                fc = fh * 32 + fi
                                mlast = ctx.emit("tn", lambda t, b=b, fi=fi, fc=fc, th=th, py=py: t.matmul(
                                    py[:, :], wdb[b][:, fi * 128:(fi + 1) * 128],
                                    hbuf[:, fc * TL + H2[th].start:fc * TL + H2[th].stop],
                                    start=(fc == 0), stop=(fc == FC - 1)),
                                    waits=(pw if fi == 0 else []),
                                    inc=(s_tn if (fh == 1 and fi == 31) else None))
                    y_marks.append(mlast)
                    zb = zbuf1[do % 2]
                    for th in range(2):
                        py = P[4 + th]
                        cons_y[th] = ctx.emit("dve", lambda v, th=th, py=py, zb=zb, wbc=wbc: v.tensor_tensor(
                            zb[:, H2[th]], py[:, :], wbc[:, H2[th]], op=ALU.mult),
                            waits=[(s_tn, mlast), (d_ac, d_ac.n)], inc=s_dve)
                    nd = s_dve.n
                    def mk_acc(do=do, zb=zb, e=e, nd=nd):
                        def go():
                            ctx.emit("gp", lambda g: (
                                g.dma_start(out=zacc[do, :, :], in_=zb[:]) if e == 0 else
                                g.dma_start(out=zacc[do, :, :], in_=zb[:], accum_op=ALU.add)),
                                waits=[(s_dve, nd)], inc=d_ac, inck=16)
                        return go
                    pend.append(mk_acc())
                for p_ in pend:
                    p_()
                ctx.barrier()
            g0_marks = []
            for do in range(DC):
                buf = xtile[do % 2]
                wts = [(d_ac, d_ac.n)] + ([(s_act, g0_marks[do - 2])] if do >= 2 else [])
                n = ctx.emit("sy", lambda s, do=do, buf=buf: s.dma_start(out=buf[:], in_=zacc[do, :, :]),
                             waits=wts, inc=d_ld, inck=16)
                na = ctx.emit("act", lambda a, buf=buf: a.activation(zst[:, 0:TL], buf[:], AF.Silu),
                              waits=[(d_ld, n), (d_st, d_st.n)], inc=s_act)
                g0_marks.append(na)
                ctx.emit("sy", lambda s, do=do: s.dma_start(out=g0_out[do, :, :], in_=zst[:, 0:TL]),
                         waits=[(s_act, na)], inc=d_st, inck=16)
            ctx.barrier()

        with contextlib.ExitStack() as ph:
            xk = ph.enter_context(nc.sbuf_tensor("xk", [128, DC * TL], BF16))
            xv = ph.enter_context(nc.sbuf_tensor("xv", [128, DC * TL], BF16))
            xr = ph.enter_context(nc.sbuf_tensor("xr", [128, DC * TL], BF16))
            dtile = ph.enter_context(nc.sbuf_tensor("dtile", [128, TL], BF16))
            warena = ph.enter_context(nc.sbuf_tensor("warena2", [128, 8192], BF16))
            ct = ph.enter_context(nc.sbuf_tensor("ct", [128, TL], F32))
            cvt = ph.enter_context(nc.sbuf_tensor("cvt", [128, TL], F32))
            at = ph.enter_context(nc.sbuf_tensor("at", [128, TL], F32))
            bt = ph.enter_context(nc.sbuf_tensor("bt", [128, TL], F32))
            onesT = ph.enter_context(nc.sbuf_tensor("onesT", [128, TL], F32))
            rt = ph.enter_context(nc.sbuf_tensor("rt", [128, TL], BF16))
            ctx.emit("dve", lambda v: v.memset(onesT[:], 1.0), inc=s_dve)

            for c in range(DC):
                xn_c = xn[:, c * TL:(c + 1) * TL]
                ctx.emit("dve", lambda v, xn_c=xn_c: v.tensor_tensor(
                    dtile[:, 1:TL], xn_c[:, 1:TL], xn_c[:, 0:TL - 1], op=ALU.subtract),
                    waits=_now(s_dve), inc=s_dve)
                bsl = dtile[:].rearrange("p (j r) -> p j r", r=128)[:, :, 0:1].squeeze(2)
                xcol = xn_c.rearrange("p (j r) -> p j r", r=128)[:, :, 0:1].squeeze(2)
                ctx.emit("dve", lambda v, c=c, bsl=bsl, xcol=xcol: v.tensor_tensor(
                    bsl, xcol, xnbf[:, c * NB:(c + 1) * NB], op=ALU.subtract),
                    waits=_now(s_dve), inc=s_dve)
                for j, dst in enumerate((xk, xv, xr)):
                    ctx.emit("dve", lambda v, c=c, j=j, dst=dst, xn_c=xn_c: v.scalar_tensor_tensor(
                        dst[:, c * TL:(c + 1) * TL], dtile[:, 0:TL], tmm[:, j * DC + c:j * DC + c + 1],
                        xn_c, op0=ALU.mult, op1=ALU.add),
                        waits=_now(s_dve), inc=s_dve)
            ctx.barrier()

            wrb = [warena[:, 0:2048], warena[:, 2048:4096]]
            r_marks = []
            for c in range(DC):
                b = c % 2
                wts = [(s_tn, r_marks[c - 2])] if c >= 2 else []
                nw = ctx.emit("gp", lambda g, c=c, b=b: g.dma_start(out=wrb[b][:], in_=wr[c, :, :]),
                              waits=wts, inc=dws[b], inck=16)
                mlast = None
                for th in range(2):
                    pr = P[th]
                    for dc in range(DC):
                        mlast = ctx.emit("tn", lambda t, b=b, dc=dc, th=th, pr=pr: t.matmul(
                            pr[:, :], wrb[b][:, dc * 128:(dc + 1) * 128],
                            xr[:, dc * TL + H2[th].start:dc * TL + H2[th].stop],
                            start=(dc == 0), stop=(dc == DC - 1)),
                            waits=([(dws[b], nw)] + _now(s_act) if dc == 0 else []),
                            inc=(s_tn if dc == DC - 1 else None))
                    ctx.emit("act", lambda a, th=th, pr=pr: a.activation(rt[:, H2[th]], pr[:, :], AF.Sigmoid),
                             waits=[(s_tn, mlast), (d_st, d_st.n)], inc=s_act)
                r_marks.append(mlast)
                ctx.emit("sy", lambda s, c=c: s.dma_start(out=r_out[c, :, :], in_=rt[:]),
                         waits=_now(s_act), inc=d_st, inck=16)
            ctx.barrier()

            wkb = [warena[:, 0:2048], warena[:, 2048:4096]]
            wvb = [warena[:, 4096:6144], warena[:, 6144:8192]]
            kv_marks = []
            kv_a = [0, 0]
            kv_d = [0, 0]
            for c in range(DC):
                b = c % 2
                wts = [(s_tn, kv_marks[c - 2])] if c >= 2 else []
                ctx.emit("gp", lambda g, c=c, b=b: g.dma_start(out=wkb[b][:], in_=wk[c, :, :]),
                         waits=wts, inc=dws[b], inck=16)
                nv = ctx.emit("gp", lambda g, c=c, b=b: g.dma_start(out=wvb[b][:], in_=wv[c, :, :]),
                              inc=dws[b], inck=16)
                mlast = None
                for th in range(2):
                    pk, pv = P[2 * th], P[2 * th + 1]
                    pw = [(dws[b], nv)]
                    if kv_a[th]:
                        pw.append((s_act, kv_a[th]))
                    if kv_d[th]:
                        pw.append((s_dve, kv_d[th]))
                    for dc in range(DC):
                        ctx.emit("tn", lambda t, b=b, dc=dc, th=th, pk=pk: t.matmul(
                            pk[:, :], wkb[b][:, dc * 128:(dc + 1) * 128],
                            xk[:, dc * TL + H2[th].start:dc * TL + H2[th].stop],
                            start=(dc == 0), stop=(dc == DC - 1)),
                            waits=(pw if dc == 0 else []))
                    for dc in range(DC):
                        mlast = ctx.emit("tn", lambda t, b=b, dc=dc, th=th, pv=pv: t.matmul(
                            pv[:, :], wvb[b][:, dc * 128:(dc + 1) * 128],
                            xv[:, dc * TL + H2[th].start:dc * TL + H2[th].stop],
                            start=(dc == 0), stop=(dc == DC - 1)),
                            inc=(s_tn if dc == DC - 1 else None))
                    na = ctx.emit("act", lambda a, th=th, pk=pk: a.activation(ct[:, H2[th]], pk[:, :], AF.Exp),
                                  waits=[(s_tn, mlast)] + _now(s_dve), inc=s_act)
                    kv_a[th] = na
                    kv_d[th] = ctx.emit("dve", lambda v, th=th, pv=pv: v.tensor_tensor(
                        cvt[:, H2[th]], ct[:, H2[th]], pv[:, :], op=ALU.mult),
                        waits=[(s_act, na)] + _now(s_dve), inc=s_dve)
                kv_marks.append(mlast)
                ctx.emit("dve", lambda v: v.tensor_tensor_scan(at[:], onesT[:], cvt[:], 0.0, op0=ALU.mult, op1=ALU.add),
                         waits=_now(s_dve) + [(d_st, d_st.n)], inc=s_dve)
                ctx.emit("dve", lambda v: v.tensor_tensor_scan(bt[:], onesT[:], ct[:], 0.0, op0=ALU.mult, op1=ALU.add),
                         waits=_now(s_dve), inc=s_dve)
                ctx.emit("sy", lambda s, c=c: s.dma_start(out=a_out[c, :, :], in_=at[:]),
                         waits=_now(s_dve), inc=d_st, inck=16)
                ctx.emit("sy", lambda s, c=c: s.dma_start(out=b_out[c, :, :], in_=bt[:]),
                         waits=_now(s_dve), inc=d_st, inck=16)

        for e in ENGINES:
            ctx.ops[e].append(([(d_st.h, d_st.n), (d_ac.h, d_ac.n)], None, None))
        ctx.finalize()
    return nc



def build_launch2():
    nc = bass.Bass("TRN2", target_bir_lowering=False, debug=False)
    ein = lambda n, sh, dt: nc.dram_tensor(n, sh, dt, kind="ExternalInput").ap()
    eout = lambda n, sh, dt: nc.dram_tensor(n, sh, dt, kind="ExternalOutput").ap()
    xfm = ein("xfm", [DC, 128, TL], F32)
    a_in = ein("a_in", [DC, 128, TL], F32)
    b_in = ein("b_in", [DC, 128, TL], F32)
    r_in = ein("r_in", [DC, 128, TL], BF16)
    g0_in = ein("g0_in", [DC, 128, TL], F32)
    epta = ein("epta", [DC, 128, 64], F32)
    eptb = ein("eptb", [DC, 128, 64], F32)
    offw = ein("offw", [NB, 128, 64], F32)
    tf = ein("tf", [128, DC], F32)
    lng = ein("lng", [128, DC], F32)
    lnb = ein("lnb", [128, DC], F32)
    wo = ein("wo", [DC, 128, D], F32)
    wks = ein("wks", [128, D], F32)
    wkg = ein("wkg", [128, D], F32)
    wv1 = ein("wv1", [2, 128, D], F32)
    swp = ein("swp", [128, 128], F32)
    ident = ein("ident", [128, 128], F32)
    cosk = ein("cosk", [128, TL], F32)
    sink = ein("sink", [128, TL], F32)
    poolm = ein("poolm", [128, 2], F32)
    sw1 = ein("sw1", [2, 11, 128, D], F32)
    sw2 = ein("sw2", [2, DC, 128, DFF1], F32)
    rw1 = ein("rw1", [16, 11, 128, D], F32)
    rw2 = ein("rw2", [16, DC, 128, DFF1], F32)
    b1all = ein("b1all", [18, 128, 11], F32)
    b2t = ein("b2t", [DC, 18, 128], F32)
    cent = ein("cent", [128, DC * 16], F32)
    cbias = ein("cbias", [128, 16], F32)
    x1o = eout("x1o", [DC, 128, TL], F32)
    xn1o = eout("xn1o", [DC, 128, TL], BF16)
    g1o = eout("g1o", [DC, 128, TL], F32)
    ks2o = eout("ks2o", [128, 14], F32)
    kg2o = eout("kg2o", [128, 14], F32)
    v2o = eout("v2o", [14, 256], F32)
    ksro = eout("ksro", [128, 128], F32)
    kgro = eout("kgro", [128, 128], F32)
    vro = eout("vro", [128, 256], F32)
    macc = nc.dram_tensor("macc", [DC, 128, TL], F32).ap()

    GROUPS = [[0, 1, 2], [3, 4, 5], [6, 7, 8], [9, 10, 11], [12, 13, 14], [15, 16, 17]]

    es = contextlib.ExitStack()
    with es:
        ctx = Ctx(nc, es)
        sb = lambda n, sh, dt: es.enter_context(nc.sbuf_tensor(n, sh, dt))
        xn1 = sb("xn1", [128, DC * TL], BF16)
        ones1 = sb("ones1", [128, 1], F32)
        onesrow = sb("onesrow", [1, 128], F32)
        epsc = sb("epsc", [128, 1], F32)
        identb = sb("identb", [128, 128], BF16)
        swpb = sb("swpb", [128, 128], BF16)
        euc = sb("euc", [128, 4 * DC], F32)
        poolt = sb("poolt", [128, 2], BF16)
        cbt = sb("cbt", [128, 16], F32)
        G18 = sb("G18", [18, TL], BF16)
        onesrow_bf = sb("onesrow_bf", [1, 128], BF16)
        ones1b = sb("ones1b", [128, 1], BF16)
        onestl2 = sb("onestl2", [2, TL], BF16)
        identf = sb("identf", [128, 128], F32)
        centb = sb("centb", [128, DC * 16], BF16)
        sq = sb("sq", [128, TL], F32)
        abuf = [sb(f"ab{i}", [128, TL], F32) for i in range(2)]
        bbuf = [sb(f"bb{i}", [128, TL], F32) for i in range(2)]
        P = [es.enter_context(nc.psum_tensor(f"pb{i}", [128, 512], F32)) for i in range(8)]

        s_tn, s_act, s_dve = ctx.sem("s_tn"), ctx.sem("s_act"), ctx.sem("s_dve")
        d_cn = ctx.sem("d_cn", dma=True)
        d_ld = ctx.sem("d_ld", dma=True)
        d_l2 = ctx.sem("d_l2", dma=True)
        d_st = ctx.sem("d_st", dma=True)
        d_g0, d_g1 = ctx.sem("d_g0", dma=True), ctx.sem("d_g1", dma=True)
        d_gs = ctx.sem("d_gs", dma=True)
        d_ac = ctx.sem("d_ac", dma=True)
        dws = [d_g0, d_g1]
        H2 = (slice(0, 512), slice(512, 1024))

        def bcast_row(row_ap, dst, bank, wait_sems):
            for h in range(2):
                sl = H2[h]
                n = ctx.emit("tn", lambda t, sl=sl: t.matmul(
                    P[bank][:, :], onesrow[:], row_ap[:, sl], start=True, stop=True),
                    waits=list(wait_sems) + _now(s_dve), inc=s_tn)
                ctx.emit("dve", lambda v, sl=sl: v.tensor_copy(dst[:, sl], P[bank][:, :]),
                         waits=[(s_tn, n)], inc=s_dve)

        ctx.emit("dve", lambda v: v.memset(ones1[:], 1.0), inc=s_dve)
        ctx.emit("dve", lambda v: v.memset(onesrow[:], 1.0), inc=s_dve)
        ctx.emit("dve", lambda v: v.memset(epsc[:], LN_EPS), inc=s_dve)
        ctx.emit("dve", lambda v: v.memset(onesrow_bf[:], 1.0), inc=s_dve)
        ctx.emit("dve", lambda v: v.memset(ones1b[:], 1.0), inc=s_dve)
        ctx.emit("dve", lambda v: v.memset(onestl2[:], 1.0), inc=s_dve)
        n_const = s_dve.n
        ctx.emit("sy", lambda s: s.dma_start(out=euc[:, 0:DC], in_=tf[:, :]), inc=d_cn, inck=16)
        ctx.emit("sy", lambda s: s.dma_start(out=euc[:, 2 * DC:3 * DC], in_=lng[:, :]), inc=d_cn, inck=16)
        ctx.emit("sy", lambda s: s.dma_start(out=euc[:, 3 * DC:4 * DC], in_=lnb[:, :]), inc=d_cn, inck=16)
        ctx.emit("sy", lambda s: s.dma_start(out=cbt[:], in_=cbias[:, :]), inc=d_cn, inck=16)
        ctx.emit("gp", lambda g: g.dma_start(out=identb[:], in_=ident[:, :]), inc=d_gs, inck=16)
        ctx.emit("gp", lambda g: g.dma_start(out=centb[:], in_=cent[:, :]), inc=d_gs, inck=16)
        ctx.emit("sy", lambda s: s.dma_start(out=identf[:], in_=ident[:, :]), inc=d_cn, inck=16)
        ctx.emit("gp", lambda g: g.dma_start(out=swpb[:], in_=swp[:, :]), inc=d_gs, inck=16)
        ctx.emit("gp", lambda g: g.dma_start(out=poolt[:], in_=poolm[:, :]), inc=d_gs, inck=16)
        ctx.emit("act", lambda a: a.activation(euc[:, 0:DC], euc[:, 0:DC], AF.Exp),
                 waits=[(d_cn, 16)], inc=s_act)
        ctx.emit("dve", lambda v: v.tensor_scalar(euc[:, DC:2 * DC], euc[:, 0:DC], -1.0, None, op0=ALU.mult),
                 waits=_now(s_act), inc=s_dve)
        ctx.emit("dve", lambda v: v.tensor_scalar(euc[:, DC:2 * DC], euc[:, DC:2 * DC], 1.0, None, op0=ALU.add),
                 waits=_now(s_dve), inc=s_dve)

        with contextlib.ExitStack() as ph:
            vr = ph.enter_context(nc.sbuf_tensor("vr", [1, 7 * TL], F32))
            mubc = ph.enter_context(nc.sbuf_tensor("mubc", [128, TL], F32))
            rbc = ph.enter_context(nc.sbuf_tensor("rbc", [128, TL], F32))
            eptA = ph.enter_context(nc.sbuf_tensor("eptA", [128, DC * 64], F32))
            eptB = ph.enter_context(nc.sbuf_tensor("eptB", [128, DC * 64], F32))
            offwt = ph.enter_context(nc.sbuf_tensor("offwt", [128, NB * 64], F32))
            rwkvb = ph.enter_context(nc.sbuf_tensor("rwkvb", [128, DC * TL], BF16))
            wkvs = ph.enter_context(nc.sbuf_tensor("wkvs", [128, DC * TL], BF16))
            rbuf = ph.enter_context(nc.sbuf_tensor("rbuf", [128, TL], BF16))
            numt = ph.enter_context(nc.sbuf_tensor("numt", [128, TL], F32))
            dent = ph.enter_context(nc.sbuf_tensor("dent", [128, TL], F32))
            offc = ph.enter_context(nc.sbuf_tensor("offc", [128, 2 * NB], F32))
            tmpo = ph.enter_context(nc.sbuf_tensor("tmpo", [128, 64], F32))
            V = lambda i: vr[0:1, i * TL:(i + 1) * TL]
            Vh = lambda i, h: vr[0:1, i * TL + H2[h].start:i * TL + H2[h].stop]
            for c in range(DC):
                ctx.emit("sy", lambda s, c=c: s.dma_start(out=eptA[:, c * 64:(c + 1) * 64], in_=epta[c, :, :]), inc=d_cn, inck=16)
                ctx.emit("sy", lambda s, c=c: s.dma_start(out=eptB[:, c * 64:(c + 1) * 64], in_=eptb[c, :, :]), inc=d_cn, inck=16)
            for j in range(NB):
                ctx.emit("sy", lambda s, j=j: s.dma_start(out=offwt[:, j * 64:(j + 1) * 64], in_=offw[j, :, :]), inc=d_cn, inck=16)
            wk_marks = []
            for c in range(DC):
                ba = abuf[c % 2]
                bb = bbuf[c % 2]
                wts = ([(s_dve, wk_marks[c - 2])] if c >= 2 else [])
                na = ctx.emit("sy", lambda s, c=c, ba=ba: s.dma_start(out=ba[:], in_=a_in[c, :, :]),
                              waits=wts, inc=d_ld, inck=16)
                nb = ctx.emit("sy", lambda s, c=c, bb=bb: s.dma_start(out=bb[:], in_=b_in[c, :, :]),
                              inc=d_ld, inck=16)
                for j in range(NB):
                    ctx.emit("dve", lambda v, c=c, j=j: v.tensor_tensor(
                        tmpo[:], eptA[:, c * 64:(c + 1) * 64], offwt[:, j * 64:(j + 1) * 64], op=ALU.mult),
                        waits=[(d_cn, d_cn.n)] + _now(s_dve), inc=s_dve)
                    ctx.emit("dve", lambda v, j=j: v.tensor_reduce(
                        offc[:, j:j + 1], tmpo[:], axis=AX.X, op=ALU.add),
                        waits=_now(s_dve), inc=s_dve)
                    ctx.emit("dve", lambda v, c=c, j=j: v.tensor_tensor(
                        tmpo[:], eptB[:, c * 64:(c + 1) * 64], offwt[:, j * 64:(j + 1) * 64], op=ALU.mult),
                        waits=_now(s_dve), inc=s_dve)
                    ctx.emit("dve", lambda v, j=j: v.tensor_reduce(
                        offc[:, NB + j:NB + j + 1], tmpo[:], axis=AX.X, op=ALU.add),
                        waits=_now(s_dve), inc=s_dve)
                eu_c = euc[:, c:c + 1]
                ieu_c = euc[:, DC + c:DC + c + 1]
                ctx.emit("dve", lambda v, ba=ba, eu_c=eu_c: v.tensor_scalar(
                    numt[:], ba[:], eu_c, None, op0=ALU.mult),
                    waits=[(d_ld, na)] + _now(s_dve, s_act), inc=s_dve)
                ctx.emit("dve", lambda v, ba=ba, ieu_c=ieu_c: v.scalar_tensor_tensor(
                    numt[:, 1:TL], ba[:, 0:TL - 1], ieu_c, numt[:, 1:TL], op0=ALU.mult, op1=ALU.add),
                    waits=_now(s_dve), inc=s_dve)
                ctx.emit("dve", lambda v, bb=bb, eu_c=eu_c: v.tensor_scalar(
                    dent[:], bb[:], eu_c, None, op0=ALU.mult),
                    waits=[(d_ld, nb)] + _now(s_dve), inc=s_dve)
                ctx.emit("dve", lambda v, bb=bb, ieu_c=ieu_c: v.scalar_tensor_tensor(
                    dent[:, 1:TL], bb[:, 0:TL - 1], ieu_c, dent[:, 1:TL], op0=ALU.mult, op1=ALU.add),
                    waits=_now(s_dve), inc=s_dve)
                offa3 = offc[:, 0:NB].unsqueeze(2).broadcast_to([128, NB, 128])
                offb3 = offc[:, NB:2 * NB].unsqueeze(2).broadcast_to([128, NB, 128])
                ctx.emit("dve", lambda v, offa3=offa3: v.tensor_tensor(
                    numt[:].rearrange("p (j r) -> p j r", r=128), numt[:].rearrange("p (j r) -> p j r", r=128),
                    offa3, op=ALU.add), waits=_now(s_dve), inc=s_dve)
                ctx.emit("dve", lambda v, offb3=offb3: v.tensor_tensor(
                    dent[:].rearrange("p (j r) -> p j r", r=128), dent[:].rearrange("p (j r) -> p j r", r=128),
                    offb3, op=ALU.add), waits=_now(s_dve), inc=s_dve)
                ctx.emit("act", lambda a: a.activation(sq[:], dent[:], AF.Ln),
                         waits=_now(s_dve, s_tn), inc=s_act)
                ctx.emit("act", lambda a: a.activation(sq[:], sq[:], AF.Exp, scale=-1.0),
                         waits=_now(s_act), inc=s_act)
                mw = ctx.emit("dve", lambda v, c=c: v.tensor_tensor(
                    wkvs[:, c * TL:(c + 1) * TL], numt[:], sq[:], op=ALU.mult),
                    waits=_now(s_act), inc=s_dve)
                wk_marks.append(mw)
                na2 = ctx.emit("act", lambda a, c=c: a.activation(sq[:], wkvs[:, c * TL:(c + 1) * TL], AF.Square),
                               waits=[(s_dve, mw)] + _now(s_tn), inc=s_act)
                for h in range(2):
                    ctx.emit("tn", lambda t, c=c, h=h: t.matmul(
                        P[h][0:1, :], ones1b[:], wkvs[:, c * TL + H2[h].start:c * TL + H2[h].stop],
                        start=(c == 0), stop=(c == DC - 1)),
                        waits=[(s_act, na2)] + ([(s_dve, n_const)] if c == 0 else []))
                    ctx.emit("tn", lambda t, c=c, h=h: t.matmul(
                        P[2 + h][0:1, :], ones1[:], sq[:, H2[h]],
                        start=(c == 0), stop=(c == DC - 1)),
                        waits=[(s_act, na2)], inc=(s_tn if h == 1 else None))
            for h in range(2):
                ctx.emit("dve", lambda v, h=h: v.tensor_copy(Vh(0, h), P[h][0:1, :]),
                         waits=_now(s_tn), inc=s_dve)
                ctx.emit("dve", lambda v, h=h: v.tensor_copy(Vh(1, h), P[2 + h][0:1, :]),
                         waits=_now(s_tn), inc=s_dve)
            ctx.emit("dve", lambda v: v.tensor_scalar(V(2), V(0), 1.0 / D, None, op0=ALU.mult),
                     waits=_now(s_dve), inc=s_dve)
            ctx.emit("dve", lambda v: v.tensor_scalar(V(1), V(1), 1.0 / D, None, op0=ALU.mult),
                     waits=_now(s_dve), inc=s_dve)
            ctx.emit("dve", lambda v: v.tensor_tensor(V(4), V(2), V(2), op=ALU.mult),
                     waits=_now(s_dve), inc=s_dve)
            ctx.emit("dve", lambda v: v.tensor_tensor(V(5), V(1), V(4), op=ALU.subtract),
                     waits=_now(s_dve), inc=s_dve)
            ctx.emit("act", lambda a: a.activation(V(6), V(5), AF.Ln, bias=epsc[0:1, 0:1]),
                     waits=_now(s_dve), inc=s_act)
            ctx.emit("act", lambda a: a.activation(V(3), V(6), AF.Exp, scale=-0.5),
                     waits=_now(s_act), inc=s_act)
            bcast_row(V(2), mubc, 4, _now(s_act))
            bcast_row(V(3), rbc, 4, _now(s_act))

            for c in range(DC):
                rb = rbuf
                nr = ctx.emit("sy", lambda s, c=c: s.dma_start(out=rbuf[:], in_=r_in[c, :, :]),
                              waits=(_now(s_dve) if c > 0 else _now(s_dve)), inc=d_ld, inck=16)
                ctx.emit("dve", lambda v, c=c: v.tensor_tensor(
                    sq[:], wkvs[:, c * TL:(c + 1) * TL], mubc[:], op=ALU.subtract),
                    waits=_now(s_dve) + [(d_ld, nr)], inc=s_dve)
                ctx.emit("dve", lambda v: v.tensor_tensor(sq[:], sq[:], rbc[:], op=ALU.mult),
                         waits=_now(s_dve), inc=s_dve)
                ctx.emit("dve", lambda v, c=c: v.tensor_scalar(
                    sq[:], sq[:], euc[:, 2 * DC + c:2 * DC + c + 1], euc[:, 3 * DC + c:3 * DC + c + 1],
                    op0=ALU.mult, op1=ALU.add),
                    waits=_now(s_dve), inc=s_dve)
                ctx.emit("dve", lambda v, c=c: v.tensor_tensor(
                    rwkvb[:, c * TL:(c + 1) * TL], sq[:], rbuf[:], op=ALU.mult),
                    waits=_now(s_dve), inc=s_dve)
            ctx.barrier()
            warena = ph.enter_context(nc.sbuf_tensor("warena", [128, 4096], BF16))
            wob = [warena[:, 0:2048], warena[:, 2048:4096]]
            x1t = ph.enter_context(nc.sbuf_tensor("x1t", [128, TL], F32))
            gbuf = ph.enter_context(nc.sbuf_tensor("gbuf", [128, TL], F32))
            s0_marks = []
            for co in range(DC):
                b = co % 2
                wts = [(s_tn, s0_marks[co - 2])] if co >= 2 else []
                nw = ctx.emit("gp", lambda g, co=co, b=b: g.dma_start(out=wob[b][:], in_=wo[co, :, :]),
                              waits=wts, inc=dws[b], inck=16)
                nx = ctx.emit("sy", lambda s, co=co: s.dma_start(out=abuf[0][:], in_=xfm[co, :, :]),
                              waits=(_now(s_dve) if co > 0 else []), inc=d_ld, inck=16)
                ng = ctx.emit("sy", lambda s, co=co: s.dma_start(out=gbuf[:], in_=g0_in[co, :, :]),
                              waits=(_now(s_dve) if co > 0 else []), inc=d_l2, inck=16)
                mlast = None
                for th in range(2):
                    po = P[4 + th]
                    for dc in range(DC):
                        mlast = ctx.emit("tn", lambda t, b=b, dc=dc, th=th, po=po: t.matmul(
                            po[:, :], wob[b][:, dc * 128:(dc + 1) * 128],
                            rwkvb[:, dc * TL + H2[th].start:dc * TL + H2[th].stop],
                            start=(dc == 0), stop=(dc == DC - 1)),
                            waits=([(dws[b], nw)] + _now(s_dve) if dc == 0 else []),
                            inc=(s_tn if dc == DC - 1 else None))
                    ctx.emit("dve", lambda v, th=th, po=po: v.tensor_tensor(
                        x1t[:, H2[th]], po[:, :], gbuf[:, H2[th]], op=ALU.mult),
                        waits=[(s_tn, mlast), (d_l2, ng), (d_ld, nx), (d_st, d_st.n)] + _now(s_act), inc=s_dve)
                    ctx.emit("dve", lambda v, th=th: v.tensor_tensor(
                        x1t[:, H2[th]], x1t[:, H2[th]], abuf[0][:, H2[th]], op=ALU.add),
                        waits=_now(s_dve), inc=s_dve)
                s0_marks.append(mlast)
                ctx.emit("sy", lambda s, co=co: s.dma_start(out=x1o[co, :, :], in_=x1t[:]),
                         waits=_now(s_dve), inc=d_st, inck=16)
                na2 = ctx.emit("act", lambda a: a.activation(sq[:], x1t[:], AF.Square),
                               waits=_now(s_dve, s_tn), inc=s_act)
                for h in range(2):
                    ctx.emit("tn", lambda t, co=co, h=h: t.matmul(
                        P[h][0:1, :], ones1[:], x1t[:, H2[h]], start=(co == 0), stop=(co == DC - 1)),
                        waits=[(s_act, na2)] + ([(s_dve, n_const)] if co == 0 else []))
                    ctx.emit("tn", lambda t, co=co, h=h: t.matmul(
                        P[2 + h][0:1, :], ones1[:], sq[:, H2[h]], start=(co == 0), stop=(co == DC - 1)),
                        waits=[(s_act, na2)], inc=(s_tn if h == 1 else None))
            for h in range(2):
                ctx.emit("dve", lambda v, h=h: v.tensor_copy(Vh(0, h), P[h][0:1, :]),
                         waits=_now(s_tn), inc=s_dve)
                ctx.emit("dve", lambda v, h=h: v.tensor_copy(Vh(1, h), P[2 + h][0:1, :]),
                         waits=_now(s_tn), inc=s_dve)
            ctx.emit("dve", lambda v: v.tensor_scalar(V(2), V(0), 1.0 / D, None, op0=ALU.mult),
                     waits=_now(s_dve), inc=s_dve)
            ctx.emit("dve", lambda v: v.tensor_scalar(V(1), V(1), 1.0 / D, None, op0=ALU.mult),
                     waits=_now(s_dve), inc=s_dve)
            ctx.emit("dve", lambda v: v.tensor_tensor(V(4), V(2), V(2), op=ALU.mult),
                     waits=_now(s_dve), inc=s_dve)
            ctx.emit("dve", lambda v: v.tensor_tensor(V(5), V(1), V(4), op=ALU.subtract),
                     waits=_now(s_dve), inc=s_dve)
            ctx.emit("act", lambda a: a.activation(V(6), V(5), AF.Ln, bias=epsc[0:1, 0:1]),
                     waits=_now(s_dve), inc=s_act)
            ctx.emit("act", lambda a: a.activation(V(3), V(6), AF.Exp, scale=-0.5),
                     waits=_now(s_act), inc=s_act)
            bcast_row(V(2), mubc, 4, _now(s_act))
            bcast_row(V(3), rbc, 4, _now(s_act))
            ctx.barrier()
            xa_marks = []
            for c in range(DC):
                buf = abuf[c % 2]
                wts = [(s_dve, xa_marks[c - 2])] if c >= 2 else []
                n = ctx.emit("sy", lambda s, c=c, buf=buf: s.dma_start(out=buf[:], in_=x1o[c, :, :]),
                             waits=wts, inc=d_ld, inck=16)
                ctx.emit("dve", lambda v, buf=buf: v.tensor_tensor(sq[:], buf[:], mubc[:], op=ALU.subtract),
                         waits=[(d_ld, n)] + _now(s_dve), inc=s_dve)
                m = ctx.emit("dve", lambda v, c=c: v.tensor_tensor(
                    xn1[:, c * TL:(c + 1) * TL], sq[:], rbc[:], op=ALU.mult),
                    waits=_now(s_dve), inc=s_dve)
                xa_marks.append(m)
                ctx.emit("sy", lambda s, c=c: s.dma_start(
                    out=xn1o[c, :, :], in_=xn1[:, c * TL:(c + 1) * TL]),
                    waits=[(s_dve, m)], inc=d_st, inck=16)
            ctx.barrier()
        with contextlib.ExitStack() as ph:
            cost = ph.enter_context(nc.sbuf_tensor("cost", [128, TL], F32))
            sint = ph.enter_context(nc.sbuf_tensor("sint", [128, TL], F32))
            wsl = ph.enter_context(nc.sbuf_tensor("wsl", [128, 2048], BF16))
            ksf = ph.enter_context(nc.sbuf_tensor("ksf", [128, TL], F32))
            kgf = ph.enter_context(nc.sbuf_tensor("kgf", [128, TL], F32))
            kgb = ph.enter_context(nc.sbuf_tensor("kgb", [128, TL], BF16))
            kgr = ph.enter_context(nc.sbuf_tensor("kgr", [128, TL], F32))
            p16 = ph.enter_context(nc.sbuf_tensor("p16", [128, 16], F32))
            vtm = ph.enter_context(nc.sbuf_tensor("vtm", [128, 256], F32))
            vtmb = ph.enter_context(nc.sbuf_tensor("vtmb", [128, 256], BF16))
            v2s = ph.enter_context(nc.sbuf_tensor("v2s", [2, 256], F32))
            ctx.emit("sy", lambda s: s.dma_start(out=cost[:], in_=cosk[:, :]), inc=d_cn, inck=16)
            ctx.emit("sy", lambda s: s.dma_start(out=sint[:], in_=sink[:, :]), inc=d_cn, inck=16)
            nw = ctx.emit("gp", lambda g: g.dma_start(out=wsl[:], in_=wks[:, :]), inc=d_g0, inck=16)
            ml = None
            for th in range(2):
                for dc in range(DC):
                    ml = ctx.emit("tn", lambda t, dc=dc, th=th: t.matmul(
                        P[th][:, :], wsl[:, dc * 128:(dc + 1) * 128],
                        xn1[:, dc * TL + H2[th].start:dc * TL + H2[th].stop],
                        start=(dc == 0), stop=(dc == DC - 1)),
                        waits=([(d_g0, nw)] if dc == 0 else []), inc=(s_tn if dc == DC - 1 else None))
                ctx.emit("dve", lambda v, th=th: v.tensor_copy(ksf[:, H2[th]], P[th][:, :]),
                         waits=[(s_tn, ml)], inc=s_dve)
            ctx.emit("dve", lambda v: v.tensor_reduce(
                p16[:], ksf[:].rearrange("p (j r) -> p j r", r=64), axis=AX.X, op=ALU.add),
                waits=_now(s_dve), inc=s_dve)
            ctx.emit("dve", lambda v: v.tensor_scalar(p16[:], p16[:], 1.0 / 64, None, op0=ALU.mult),
                     waits=_now(s_dve), inc=s_dve)
            ctx.emit("sy", lambda s: s.dma_start(out=ks2o[:, :], in_=p16[:, 0:14]),
                     waits=_now(s_dve), inc=d_st, inck=16)
            ctx.emit("sy", lambda s: s.dma_start(out=ksro[:, :], in_=ksf[:, 896:1024]),
                     waits=_now(s_dve), inc=d_st, inck=16)
            nw = ctx.emit("gp", lambda g: g.dma_start(out=wsl[:], in_=wkg[:, :]),
                          waits=_now(s_tn), inc=d_g0, inck=16)
            for th in range(2):
                for dc in range(DC):
                    ml = ctx.emit("tn", lambda t, dc=dc, th=th: t.matmul(
                        P[th][:, :], wsl[:, dc * 128:(dc + 1) * 128],
                        xn1[:, dc * TL + H2[th].start:dc * TL + H2[th].stop],
                        start=(dc == 0), stop=(dc == DC - 1)),
                        waits=([(d_g0, nw), (d_st, d_st.n)] + _now(s_dve) if dc == 0 else []),
                        inc=(s_tn if dc == DC - 1 else None))
                ctx.emit("dve", lambda v, th=th: v.tensor_copy(kgf[:, H2[th]], P[th][:, :]),
                         waits=[(s_tn, ml)], inc=s_dve)
                ctx.emit("dve", lambda v, th=th: v.tensor_copy(kgb[:, H2[th]], kgf[:, H2[th]]),
                         waits=_now(s_dve), inc=s_dve)
                ns = ctx.emit("tn", lambda t, th=th: t.matmul(
                    P[2 + th][:, :], swpb[:], kgb[:, H2[th]], start=True, stop=True),
                    waits=_now(s_dve) + [(d_gs, d_gs.n)], inc=s_tn)
                ctx.emit("dve", lambda v, th=th: v.tensor_tensor(
                    kgr[:, H2[th]], kgf[:, H2[th]], cost[:, H2[th]], op=ALU.mult),
                    waits=[(s_tn, ns), (d_cn, d_cn.n)], inc=s_dve)
                ctx.emit("dve", lambda v, th=th: v.tensor_tensor(
                    sq[:, H2[th]], P[2 + th][:, :], sint[:, H2[th]], op=ALU.mult),
                    waits=[(s_tn, ns)], inc=s_dve)
                ctx.emit("dve", lambda v, th=th: v.tensor_tensor(
                    kgr[:, H2[th]], kgr[:, H2[th]], sq[:, H2[th]], op=ALU.add),
                    waits=_now(s_dve), inc=s_dve)
            ctx.emit("dve", lambda v: v.tensor_reduce(
                p16[:], kgr[:].rearrange("p (j r) -> p j r", r=64), axis=AX.X, op=ALU.add),
                waits=_now(s_dve), inc=s_dve)
            ctx.emit("dve", lambda v: v.tensor_scalar(p16[:], p16[:], 1.0 / 64, None, op0=ALU.mult),
                     waits=_now(s_dve), inc=s_dve)
            ctx.emit("sy", lambda s: s.dma_start(out=kg2o[:, :], in_=p16[:, 0:14]),
                     waits=_now(s_dve), inc=d_st, inck=16)
            ctx.emit("sy", lambda s: s.dma_start(out=kgro[:, :], in_=kgr[:, 896:1024]),
                     waits=_now(s_dve), inc=d_st, inck=16)
            nw = ctx.emit("gp", lambda g: g.dma_start(out=wsl[:, 0:2048], in_=wv1[0, :, :]),
                          waits=_now(s_tn), inc=d_g0, inck=16)
            nw2 = ctx.emit("gp", lambda g: g.dma_start(out=kgf[:].bitcast(BF16)[:, 0:2048], in_=wv1[1, :, :]),
                           waits=_now(s_dve, s_tn) + [(d_st, d_st.n)], inc=d_g1, inck=16)
            for tc in range(8):
                ml = None
                for oc in range(2):
                    wv_sl = wsl if oc == 0 else kgf.bitcast(BF16)[:, 0:2048]
                    for dc in range(DC):
                        ml = ctx.emit("tn", lambda t, tc=tc, oc=oc, dc=dc, wv_sl=wv_sl: t.matmul(
                            P[4 + oc][:, 0:128], xn1[:, dc * TL + tc * 128:dc * TL + (tc + 1) * 128],
                            wv_sl[:, dc * 128:(dc + 1) * 128],
                            start=(dc == 0), stop=(dc == DC - 1)),
                            waits=([(d_g0, nw), (d_g1, nw2)] + _now(s_dve) if dc == 0 else []),
                            inc=(s_tn if dc == DC - 1 else None))
                for oc in range(2):
                    ctx.emit("dve", lambda v, oc=oc: v.tensor_copy(vtm[:, oc * 128:(oc + 1) * 128], P[4 + oc][:, 0:128]),
                             waits=[(s_tn, ml)] + _now(s_dve), inc=s_dve)
                ctx.emit("dve", lambda v: v.tensor_copy(vtmb[:], vtm[:, 0:256]),
                         waits=_now(s_dve), inc=s_dve)
                if tc < 7:
                    np_ = ctx.emit("tn", lambda t: t.matmul(
                        P[6][0:2, 0:256], poolt[:], vtmb[:], start=True, stop=True),
                        waits=_now(s_dve) + [(d_gs, d_gs.n)], inc=s_tn)
                    ctx.emit("dve", lambda v: v.tensor_copy(v2s[:, :], P[6][0:2, 0:256]),
                             waits=[(s_tn, np_), (d_st, d_st.n)], inc=s_dve)
                    ctx.emit("sy", lambda s, tc=tc: s.dma_start(out=v2o[2 * tc:2 * tc + 2, :], in_=v2s[:, :]),
                             waits=_now(s_dve), inc=d_st, inck=16)
                else:
                    ctx.emit("sy", lambda s: s.dma_start(out=vro[:, :], in_=vtm[:, 0:256]),
                             waits=_now(s_dve), inc=d_st, inck=16)
            ctx.barrier()

        with contextlib.ExitStack() as ph:
            Gflat = ph.enter_context(nc.sbuf_tensor("Gflat", [1, 18 * TL], BF16))
            gm2 = ph.enter_context(nc.sbuf_tensor("gm2", [128, 16], F32))
            hg = ph.enter_context(nc.sbuf_tensor("hg", [128, 3 * 11 * TL], BF16))
            w1b = ph.enter_context(nc.sbuf_tensor("w1b", [128, 4096], BF16))
            w2b = ph.enter_context(nc.sbuf_tensor("w2b", [128, 2816 + 36], BF16))
            b1t = ph.enter_context(nc.sbuf_tensor("b1t", [128, 18 * 11], F32))
            hs1 = ph.enter_context(nc.sbuf_tensor("hs1", [128, 1024], F32))
            gbc = ph.enter_context(nc.sbuf_tensor("gbc", [128, TL], BF16))
            g16 = ph.enter_context(nc.sbuf_tensor("g16", [128, 16], F32))
            gm = ph.enter_context(nc.sbuf_tensor("gm", [128, 16], F32))
            gv = ph.enter_context(nc.sbuf_tensor("gv", [128, 3], F32))
            zst = ph.enter_context(nc.sbuf_tensor("zst2", [128, TL], F32))
            zstB = ph.enter_context(nc.sbuf_tensor("zst2b", [128, TL], F32))
            w1bb = [w1b[:, 0:2048], w1b[:, 2048:4096]]
            w2bb = [w2b[:, 0:1408], w2b[:, 1408:2816]]
            b2sb = ph.enter_context(nc.sbuf_tensor("b2sb", [18, 128], BF16))
            for e in range(18):
                ctx.emit("sy", lambda s, e=e: s.dma_start(out=b1t[:, e * 11:(e + 1) * 11], in_=b1all[e, :, :]),
                         inc=d_cn, inck=16)
            n_b1 = d_cn.n
            for tc in range(8):
                ml = None
                for dc in range(DC):
                    ml = ctx.emit("tn", lambda t, tc=tc, dc=dc: t.matmul(
                        P[6][:, 0:16], xn1[:, dc * TL + tc * 128:dc * TL + (tc + 1) * 128],
                        centb[:, dc * 16:(dc + 1) * 16],
                        start=(dc == 0), stop=(dc == DC - 1)),
                        waits=(_now(s_dve) + [(d_gs, d_gs.n)] if dc == 0 else []),
                        inc=(s_tn if dc == DC - 1 else None))
                na = ctx.emit("act", lambda a: a.activation(g16[:], P[6][:, 0:16], AF.Sigmoid),
                              waits=[(s_tn, ml)] + _now(s_dve), inc=s_act)
                ctx.emit("dve", lambda v: v.tensor_tensor(g16[:], g16[:], cbt[:], op=ALU.add),
                         waits=[(s_act, na), (d_cn, d_cn.n)], inc=s_dve)
                ctx.emit("dve", lambda v: v.tensor_copy(gm[:], g16[:]), waits=_now(s_dve), inc=s_dve)
                for it in range(4):
                    ctx.emit("dve", lambda v: v.reduce_max(gv[:, 0:1], gm[:], axis=AX.X),
                             waits=_now(s_dve), inc=s_dve)
                    if it < 3:
                        ctx.emit("dve", lambda v: v.tensor_scalar(gm2[:], gm[:], gv[:, 0:1], None, op0=ALU.is_ge),
                                 waits=_now(s_dve), inc=s_dve)
                        ctx.emit("dve", lambda v: v.scalar_tensor_tensor(gm[:], gm2[:], -1e30, gm[:], op0=ALU.mult, op1=ALU.add),
                                 waits=_now(s_dve), inc=s_dve)
                ctx.emit("dve", lambda v: v.tensor_scalar(gm[:], g16[:], gv[:, 0:1], None, op0=ALU.is_ge),
                         waits=_now(s_dve), inc=s_dve)
                ctx.emit("dve", lambda v: v.tensor_tensor(gm[:], gm[:], g16[:], op=ALU.mult),
                         waits=_now(s_dve), inc=s_dve)
                ctx.emit("dve", lambda v: v.tensor_reduce(gv[:, 1:2], gm[:], axis=AX.X, op=ALU.add),
                         waits=_now(s_dve), inc=s_dve)
                ctx.emit("dve", lambda v: v.tensor_scalar(gv[:, 1:2], gv[:, 1:2], 1e-8, None, op0=ALU.max),
                         waits=_now(s_dve), inc=s_dve)
                ctx.emit("dve", lambda v: v.reciprocal(gv[:, 2:3], gv[:, 1:2]),
                         waits=_now(s_dve), inc=s_dve)
                mg = ctx.emit("dve", lambda v: v.tensor_scalar(gm[:], gm[:], gv[:, 2:3], None, op0=ALU.mult),
                              waits=_now(s_dve), inc=s_dve)
                nt = ctx.emit("tn", lambda t: t.transpose(
                    P[7][0:16, 0:128], gm[:, 0:16], identf[:]),
                    waits=[(s_dve, mg), (d_cn, d_cn.n)] + _now(s_act), inc=s_tn)
                ctx.emit("dve", lambda v, tc=tc: v.tensor_copy(
                    G18[0:16, tc * 128:(tc + 1) * 128], P[7][0:16, 0:128]),
                    waits=[(s_tn, nt)], inc=s_dve)
            ctx.emit("sy", lambda s: s.dma_start(out=G18[16:18, :], in_=onestl2[:, :]),
                     waits=_now(s_dve), inc=d_cn, inck=16)
            n_g18 = ctx.emit("sy", lambda s: s.dma_start(
                out=Gflat[0:1, :].rearrange("o (e t) -> o e t", t=TL), in_=G18[:, :]),
                waits=[(d_cn, d_cn.n)] + _now(s_dve), inc=d_cn, inck=16)
            ctx.barrier()
            first_grp = True
            zbuf = [zst, zstB]
            h1_marks = []
            cons_h1 = [0, 0]
            cons_h1_d = [0, 0]
            for gi, grp in enumerate(GROUPS):
                for ei, e in enumerate(grp):
                    routed = e < 16
                    ngbc = 0
                    if routed:
                        for h in range(2):
                            nb_ = ctx.emit("tn", lambda t, e=e, h=h: t.matmul(
                                P[6][:, :], onesrow_bf[:], Gflat[0:1, e * TL + H2[h].start:e * TL + H2[h].stop],
                                start=True, stop=True),
                                waits=[(d_cn, n_g18)] + _now(s_dve), inc=s_tn)
                            ngbc = ctx.emit("dve", lambda v, h=h: v.tensor_copy(gbc[:, H2[h]], P[6][:, :]),
                                            waits=[(s_tn, nb_)], inc=s_dve)
                    for fc in range(11):
                        b = (gi * 33 + ei * 11 + fc) % 2
                        wts = [(s_tn, h1_marks[-2])] if len(h1_marks) >= 2 else []
                        nw = ctx.emit("gp", lambda g, e=e, fc=fc, b=b, routed=routed: g.dma_start(
                            out=w1bb[b][:], in_=(rw1[e, fc, :, :] if routed else sw1[e - 16, fc, :, :])),
                            waits=wts, inc=dws[b], inck=16)
                        ml = None
                        for th in range(2):
                            pw = [(dws[b], nw)]
                            if cons_h1[th]:
                                pw.append((s_act, cons_h1[th]))
                            if cons_h1_d[th]:
                                pw.append((s_dve, cons_h1_d[th]))
                            for dc in range(DC):
                                ml = ctx.emit("tn", lambda t, b=b, dc=dc, th=th: t.matmul(
                                    P[th][:, :], w1bb[b][:, dc * 128:(dc + 1) * 128],
                                    xn1[:, dc * TL + H2[th].start:dc * TL + H2[th].stop],
                                    start=(dc == 0), stop=(dc == DC - 1)),
                                    waits=(pw if dc == 0 else []),
                                    inc=(s_tn if dc == DC - 1 else None))
                            if routed:
                                na = ctx.emit("act", lambda a, th=th, e=e, fc=fc: a.activation(
                                    hs1[:, H2[th]], P[th][:, :], AF.Relu, bias=b1t[:, e * 11 + fc:e * 11 + fc + 1]),
                                    waits=[(s_tn, ml), (d_cn, n_b1)] + ([(s_dve, cons_h1_d[th])] if cons_h1_d[th] else []),
                                    inc=s_act)
                                cons_h1[th] = na
                                cons_h1_d[th] = ctx.emit("dve", lambda v, ei=ei, fc=fc, th=th: v.tensor_tensor(
                                    hg[:, (ei * 11 + fc) * TL + H2[th].start:(ei * 11 + fc) * TL + H2[th].stop],
                                    hs1[:, H2[th]], gbc[:, H2[th]], op=ALU.mult),
                                    waits=[(s_act, na), (s_dve, ngbc)], inc=s_dve)
                            else:
                                cons_h1[th] = ctx.emit("act", lambda a, ei=ei, fc=fc, th=th, e=e: a.activation(
                                    hg[:, (ei * 11 + fc) * TL + H2[th].start:(ei * 11 + fc) * TL + H2[th].stop],
                                    P[th][:, :], AF.Relu, bias=b1t[:, e * 11 + fc:e * 11 + fc + 1]),
                                    waits=[(s_tn, ml), (d_cn, n_b1)], inc=s_act)
                        h1_marks.append(ml)
                ctx.barrier()
                cons_h1 = [0, 0]
                cons_h1_d = [0, 0]
                last = (gi == len(GROUPS) - 1)
                y2_marks = []
                cons_y = [0, 0]
                pend = []
                for do in range(DC):
                    while len(pend) >= 2:
                        pend.pop(0)()
                    ml = None
                    for ei, e in enumerate(grp):
                        routed = e < 16
                        b = (do * len(grp) + ei) % 2
                        wts = [(s_tn, y2_marks[-2])] if len(y2_marks) >= 2 else []
                        nw = ctx.emit("gp", lambda g, e=e, do=do, b=b, routed=routed: g.dma_start(
                            out=w2bb[b][:], in_=(rw2[e, do, :, :] if routed else sw2[e - 16, do, :, :])),
                            waits=wts, inc=dws[b], inck=16)
                        for th in range(2):
                            py = P[4 + th]
                            pw = [(dws[b], nw)]
                            if ei == 0 and cons_y[th]:
                                pw.append((s_dve, cons_y[th]))
                            for fc in range(11):
                                is_stop = (not last) and (ei == len(grp) - 1) and (fc == 10)
                                ml = ctx.emit("tn", lambda t, b=b, ei=ei, fc=fc, th=th, py=py, is_stop=is_stop: t.matmul(
                                    py[:, :], w2bb[b][:, fc * 128:(fc + 1) * 128],
                                    hg[:, (ei * 11 + fc) * TL + H2[th].start:(ei * 11 + fc) * TL + H2[th].stop],
                                    start=(ei == 0 and fc == 0), stop=is_stop),
                                    waits=(pw if fc == 0 else []),
                                    inc=(s_tn if (fc == 10 and th == 1) else None))
                        y2_marks.append(ml)
                    if last:
                        nb2 = ctx.emit("gp", lambda g, do=do: g.dma_start(out=b2sb[:, :], in_=b2t[do, :, :]),
                                       waits=(_now(s_tn) if do < 2 else [(s_tn, y2_marks[-2])]), inc=d_gs, inck=16)
                        for th in range(2):
                            py = P[4 + th]
                            ctx.emit("tn", lambda t, th=th, py=py: t.matmul(
                                py[:, :], b2sb[:, :], G18[:, H2[th]], start=False, stop=False),
                                waits=[(d_gs, nb2), (d_cn, n_g18)])
                            ml = ctx.emit("tn", lambda t, do=do, th=th, py=py: t.matmul(
                                py[:, :], identb[:], xn1[:, do * TL + H2[th].start:do * TL + H2[th].stop],
                                start=False, stop=True),
                                waits=[], inc=s_tn)
                    zb = zbuf[do % 2]
                    for th in range(2):
                        py = P[4 + th]
                        cons_y[th] = ctx.emit("dve", lambda v, th=th, py=py, zb=zb: v.tensor_copy(zb[:, H2[th]], py[:, :]),
                                              waits=[(s_tn, s_tn.n), (d_ac, d_ac.n)], inc=s_dve)
                    nd = s_dve.n
                    def mk_acc(do=do, zb=zb, fg=first_grp, nd=nd):
                        def go():
                            ctx.emit("gp", lambda g: (
                                g.dma_start(out=macc[do, :, :], in_=zb[:]) if fg else
                                g.dma_start(out=macc[do, :, :], in_=zb[:], accum_op=ALU.add)),
                                waits=[(s_dve, nd)], inc=d_ac, inck=16)
                        return go
                    pend.append(mk_acc())
                for p_ in pend:
                    p_()
                first_grp = False
                ctx.barrier()
            for do in range(DC):
                buf = abuf[do % 2]
                n = ctx.emit("sy", lambda s, do=do, buf=buf: s.dma_start(out=buf[:], in_=macc[do, :, :]),
                             waits=[(d_ac, d_ac.n)] + (_now(s_act) if do >= 2 else []), inc=d_ld, inck=16)
                na = ctx.emit("act", lambda a, buf=buf: a.activation(zst[:, 0:TL], buf[:], AF.Silu),
                              waits=[(d_ld, n), (d_st, d_st.n)], inc=s_act)
                ctx.emit("sy", lambda s, do=do: s.dma_start(out=g1o[do, :, :], in_=zst[:, 0:TL]),
                         waits=[(s_act, na)], inc=d_st, inck=16)
        for e in ENGINES:
            ctx.ops[e].append(([(d_st.h, d_st.n), (d_ac.h, d_ac.n)], None, None))
        ctx.finalize()
    return nc



def _swap_perm():
    P = np.zeros((128, 128), np.float32)
    for m in range(128):
        blk, d = m // 32, m % 32
        P[blk * 32 + ((d + 16) % 32), m] = 1.0
    return P


def _rope_tables(pos):
    invf = 1.0 / (10000.0 ** (np.arange(0, 32, 2, dtype=np.float64) / 32.0))
    ang = pos[None, :].astype(np.float64) * invf[:, None]
    cos = np.cos(ang).astype(np.float32)
    sin = np.sin(ang).astype(np.float32)
    C = np.zeros((128, len(pos)), np.float32)
    S = np.zeros((128, len(pos)), np.float32)
    for r in range(128):
        d = r % 32
        f = d % 16
        C[r] = cos[f]
        S[r] = sin[f] if d >= 16 else -sin[f]
    return C, S


def _prep_shared_l1(inp):
    f = lambda n: np.ascontiguousarray(np.asarray(inp[n], np.float32))
    out = {}
    out["wk"] = _tile_w(f("L0_attn_Wk").T)
    out["wv"] = _tile_w(f("L0_attn_Wv").T)
    out["wr"] = _tile_w(f("L0_attn_Wr").T)
    gwt = _tile_w(f("L0_ffn_gate_w").T)
    gw = np.zeros((128, 32), np.float32)
    for c in range(DC):
        gw[:, 2 * c:2 * c + 2] = gwt[0][:, c * 128:c * 128 + 2]
    out["gw"] = gw
    out["wg"] = np.stack([_tile_w(f("L0_ffn_w_gate")[e].T) for e in range(2)])
    out["wu"] = np.stack([_tile_w(f("L0_ffn_w_up")[e].T) for e in range(2)])
    out["wd"] = np.stack([_tile_w(f("L0_ffn_w_down")[e].T) for e in range(2)])
    out["td"] = _vec_pc(f("L0_attn_time_decay").ravel())
    out["tmk"] = _vec_pc(f("L0_attn_tm_k").ravel())
    out["tmv"] = _vec_pc(f("L0_attn_tm_v").ravel())
    out["tmr"] = _vec_pc(f("L0_attn_tm_r").ravel())
    out["ident"] = np.eye(128, dtype=np.float32)
    return out


def _launch1_inmaps(inp, shared):
    x = np.ascontiguousarray(np.asarray(inp["L0_attn_x"], np.float32)[0])
    maps = []
    for i in range(NC):
        toks = _core_tokens(i)
        xfm = np.ascontiguousarray(x[toks].T.reshape(DC, 128, TL))
        xbnd = np.zeros((NB, D), np.float32)
        for j in range(NB):
            g = i + 8 * j
            if g > 0:
                xbnd[j] = x[g * 128 - 1]
        m = {"xfm": xfm, "xbnd": xbnd}
        m.update(shared)
        maps.append(m)
    return maps


def _prep_shared_l2(inp):
    f = lambda n: np.ascontiguousarray(np.asarray(inp[n], np.float32))
    out = {}
    out["wo"] = _tile_w(f("L0_attn_Wo").T)
    out["wks"] = _tile_w(f("L1_attn_Wk_sem"))[0]
    out["wkg"] = _tile_w(f("L1_attn_Wk_geo"))[0]
    out["wv1"] = _tile_w(f("L1_attn_Wv"))
    out["swp"] = _swap_perm()
    out["ident"] = np.eye(128, dtype=np.float32)
    pm = np.zeros((128, 2), np.float32)
    pm[0:64, 0] = 1.0 / 64
    pm[64:128, 1] = 1.0 / 64
    out["poolm"] = pm
    out["sw1"] = np.stack([_tile_w(f("L1_ffn_shared_w1")[n]) for n in range(2)])
    out["sw2"] = np.stack([_tile_w(f("L1_ffn_shared_w2")[n]) for n in range(2)])
    out["rw1"] = np.stack([_tile_w(f("L1_ffn_routed_w1")[n]) for n in range(16)])
    out["rw2"] = np.stack([_tile_w(f("L1_ffn_routed_w2")[n]) for n in range(16)])
    b1 = np.concatenate([f("L1_ffn_routed_b1"), f("L1_ffn_shared_b1")], 0)
    out["b1all"] = np.ascontiguousarray(
        b1.reshape(18, 11, 128).transpose(0, 2, 1))
    b2 = np.concatenate([f("L1_ffn_routed_b2"), f("L1_ffn_shared_b2")], 0)
    out["b2t"] = np.ascontiguousarray(b2.reshape(18, DC, 128).transpose(1, 0, 2))
    cen = f("L1_ffn_centroids")
    out["cent"] = np.ascontiguousarray(
        cen.reshape(16, DC, 128).transpose(2, 1, 0).reshape(128, DC * 16))
    out["cbias"] = np.ascontiguousarray(np.broadcast_to(f("L1_ffn_bias"), (128, 16)))
    out["tf"] = _vec_pc(f("L0_attn_time_first").ravel())
    out["lng"] = _vec_pc(f("L0_attn_ln_g").ravel())
    out["lnb"] = _vec_pc(f("L0_attn_ln_b").ravel())
    return out


def _launch2_inmaps(inp, shared2, l1res):
    x = np.ascontiguousarray(np.asarray(inp["L0_attn_x"], np.float32)[0])
    epta = np.zeros((DC, 128, 64), np.float32)
    eptb = np.zeros((DC, 128, 64), np.float32)
    for g in range(64):
        c, j = g % 8, g // 8
        epta[:, :, g] = np.asarray(l1res[c]["a_out"])[:, :, j * 128 + 127]
        eptb[:, :, g] = np.asarray(l1res[c]["b_out"])[:, :, j * 128 + 127]
    maps = []
    for i in range(NC):
        toks = _core_tokens(i)
        xfm = np.ascontiguousarray(x[toks].T.reshape(DC, 128, TL))
        offw = np.zeros((NB, 128, 64), np.float32)
        for j in range(NB):
            gj = i + 8 * j
            for cp in range(8):
                J = (gj - 1 - cp) // 8
                if cp < gj and cp != i:
                    offw[j, :, cp + 8 * J] = 1.0
        C, S = _rope_tables(toks)
        m = {
            "xfm": xfm,
            "a_in": np.asarray(l1res[i]["a_out"]),
            "b_in": np.asarray(l1res[i]["b_out"]),
            "r_in": np.asarray(l1res[i]["r_out"]),
            "g0_in": np.asarray(l1res[i]["g0_out"]),
            "epta": epta, "eptb": eptb, "offw": offw,
            "cosk": C, "sink": S,
        }
        m.update(shared2)
        maps.append(m)
    return maps



def build_launch3():
    nc = bass.Bass("TRN2", target_bir_lowering=False, debug=False)
    ein = lambda n, sh, dt: nc.dram_tensor(n, sh, dt, kind="ExternalInput").ap()
    eout = lambda n, sh, dt: nc.dram_tensor(n, sh, dt, kind="ExternalOutput").ap()
    xn1i = ein("xn1i", [DC, 128, TL], BF16)
    x1i = ein("x1i", [DC, 128, TL], F32)
    g1i = ein("g1i", [DC, 128, TL], F32)
    wqs = ein("wqs", [4, 128, D], F32)
    wqg = ein("wqg", [4, 128, D], F32)
    wo1 = ein("wo1", [DC, 128, 1024], F32)
    lsrep = ein("lsrep", [128, 4], F32)
    swp = ein("swp", [128, 128], F32)
    ident = ein("ident", [128, 128], F32)
    cosq = ein("cosq", [128, TL], F32)
    sinq = ein("sinq", [128, TL], F32)
    kst = ein("kst", [4, 128, 112], F32)
    kgt = ein("kgt", [4, 128, 112], F32)
    ksr = ein("ksr", [4, 128, 1024], F32)
    kgr = ein("kgr", [4, 128, 1024], F32)
    vt = ein("vt", [112, 256], F32)
    vrr = ein("vrr", [8, 128, 256], F32)
    pmask = ein("pmask", [NB, 128, 112], F32)
    rmask = ein("rmask", [128, 1024], F32)
    fixm = ein("fixm", [128, 128], F32)
    outp = eout("outp", [DC, 128, TL], F32)

    es = contextlib.ExitStack()
    with es:
        ctx = Ctx(nc, es)
        sb = lambda n, sh, dt: es.enter_context(nc.sbuf_tensor(n, sh, dt))
        xn1 = sb("xn1", [128, DC * TL], BF16)
        identb = sb("identb", [128, 128], BF16)
        swpb = sb("swpb", [128, 128], BF16)
        cost = sb("cost", [128, TL], F32)
        sint = sb("sint", [128, TL], F32)
        sclc = sb("sclc", [128, 4], F32)
        ones1 = sb("ones1", [128, 1], F32)
        ones1b = sb("ones1b", [128, 1], BF16)
        kstb = sb("kstb", [128, 4 * 112], BF16)
        kgtb = sb("kgtb", [128, 4 * 112], BF16)
        ksrb = sb("ksrb", [128, 4 * TL], BF16)
        kgrb = sb("kgrb", [128, 4 * TL], BF16)
        vtb = sb("vtb", [112, 256], BF16)
        vrb = sb("vrb", [128, 8 * 256], BF16)
        pmt = sb("pmt", [128, NB * 112], F32)
        rmt = sb("rmt", [128, 1024], F32)
        fmt = sb("fmt", [128, 128], F32)
        vbarc = sb("vbarc", [128, 4], F32)
        qs = sb("qs", [128, 4 * TL], BF16)
        qg = sb("qg", [128, 4 * TL], BF16)
        qgf = sb("qgf", [128, TL], F32)
        att = sb("att", [128, 8 * TL], BF16)
        zp = sb("zp", [128, 112], F32)
        zr = sb("zr", [128, 1024], F32)
        ep = sb("ep", [128, 112], BF16)
        er = sb("er", [128, 1024], BF16)
        etp = sb("etp", [112, 128], BF16)
        etr = sb("etr", [128, 8 * 128], BF16)
        rowv = sb("rowv", [128, 8], F32)
        wsl = sb("wsl", [128, 2048], BF16)
        x1t = sb("x1t", [128, TL], F32)
        gbuf = sb("gbuf", [128, TL], F32)
        ot = sb("ot", [128, TL], F32)
        P = [es.enter_context(nc.psum_tensor(f"pb{i}", [128, 512], F32)) for i in range(8)]

        s_tn, s_act, s_dve = ctx.sem("s_tn"), ctx.sem("s_act"), ctx.sem("s_dve")
        d_cn = ctx.sem("d_cn", dma=True)
        d_ld = ctx.sem("d_ld", dma=True)
        d_l2 = ctx.sem("d_l2", dma=True)
        d_st = ctx.sem("d_st", dma=True)
        d_g0, d_g1 = ctx.sem("d_g0", dma=True), ctx.sem("d_g1", dma=True)
        d_gs = ctx.sem("d_gs", dma=True)
        dws = [d_g0, d_g1]
        H2 = (slice(0, 512), slice(512, 1024))

        ctx.emit("dve", lambda v: v.memset(ones1[:], 1.0), inc=s_dve)
        ctx.emit("dve", lambda v: v.memset(ones1b[:], 1.0), inc=s_dve)
        n_const = s_dve.n
        ctx.emit("sy", lambda s: s.dma_start(out=cost[:], in_=cosq[:, :]), inc=d_cn, inck=16)
        ctx.emit("sy", lambda s: s.dma_start(out=sint[:], in_=sinq[:, :]), inc=d_cn, inck=16)
        ctx.emit("sy", lambda s: s.dma_start(out=sclc[:], in_=lsrep[:, :]), inc=d_cn, inck=16)
        for j in range(NB):
            ctx.emit("sy", lambda s, j=j: s.dma_start(out=pmt[:, j * 112:(j + 1) * 112], in_=pmask[j, :, :]), inc=d_cn, inck=16)
        ctx.emit("sy", lambda s: s.dma_start(out=rmt[:], in_=rmask[:, :]), inc=d_cn, inck=16)
        ctx.emit("sy", lambda s: s.dma_start(out=fmt[:], in_=fixm[:, :]), inc=d_cn, inck=16)
        ctx.emit("gp", lambda g: g.dma_start(out=identb[:], in_=ident[:, :]), inc=d_gs, inck=16)
        ctx.emit("gp", lambda g: g.dma_start(out=swpb[:], in_=swp[:, :]), inc=d_gs, inck=16)
        for g_ in range(4):
            ctx.emit("gp", lambda g, g_=g_: g.dma_start(out=kstb[:, g_ * 112:(g_ + 1) * 112], in_=kst[g_, :, :]), inc=d_gs, inck=16)
            ctx.emit("gp", lambda g, g_=g_: g.dma_start(out=kgtb[:, g_ * 112:(g_ + 1) * 112], in_=kgt[g_, :, :]), inc=d_gs, inck=16)
            ctx.emit("gp", lambda g, g_=g_: g.dma_start(out=ksrb[:, g_ * TL:(g_ + 1) * TL], in_=ksr[g_, :, :]), inc=d_gs, inck=16)
            ctx.emit("gp", lambda g, g_=g_: g.dma_start(out=kgrb[:, g_ * TL:(g_ + 1) * TL], in_=kgr[g_, :, :]), inc=d_gs, inck=16)
        ctx.emit("gp", lambda g: g.dma_start(out=vtb[:, :], in_=vt[:, :]), inc=d_gs, inck=16)
        for kc in range(8):
            ctx.emit("gp", lambda g, kc=kc: g.dma_start(out=vrb[:, kc * 256:(kc + 1) * 256], in_=vrr[kc, :, :]), inc=d_gs, inck=16)
        for c in range(DC):
            ctx.emit("sy", lambda s, c=c: s.dma_start(out=xn1[:, c * TL:(c + 1) * TL], in_=xn1i[c, :, :]), inc=d_ld, inck=16)
        n_xn1 = d_ld.n
        ctx.emit("act", lambda a: a.activation(sclc[:], sclc[:], AF.Exp), waits=[(d_cn, 48)], inc=s_act)
        ctx.emit("dve", lambda v: v.tensor_scalar(sclc[:], sclc[:], 1.0 / np.sqrt(32.0), None, op0=ALU.mult),
                 waits=_now(s_act), inc=s_dve)
        nv1 = ctx.emit("tn", lambda t: t.matmul(P[0][0:128, 0:1], vtb[:, 0:128], ones1b[0:112, 0:1], start=True, stop=False),
                       waits=[(d_gs, d_gs.n), (s_dve, n_const)])
        ctx.emit("tn", lambda t: t.matmul(P[1][0:128, 0:1], vtb[:, 128:256], ones1b[0:112, 0:1], start=True, stop=False))
        for kc in range(8):
            ctx.emit("tn", lambda t, kc=kc: t.matmul(P[0][0:128, 0:1], vrb[:, kc * 256:kc * 256 + 128],
                                                     ones1b[:, 0:1], start=False, stop=(kc == 7)))
            ctx.emit("tn", lambda t, kc=kc: t.matmul(P[1][0:128, 0:1], vrb[:, kc * 256 + 128:(kc + 1) * 256],
                                                     ones1b[:, 0:1], start=False, stop=(kc == 7)),
                     inc=(s_tn if kc == 7 else None))
        for g_ in range(4):
            src = P[g_ // 2]
            lo = (g_ % 2) * 64
            for half in range(2):
                ctx.emit("dve", lambda v, g_=g_, src=src, lo=lo, half=half: v.tensor_scalar(
                    vbarc[half * 64:(half + 1) * 64, g_:g_ + 1], src[lo:lo + 64, 0:1], 1.0 / 1136.0, None, op0=ALU.mult),
                    waits=_now(s_tn), inc=s_dve)
        ctx.barrier()

        wq_marks = []
        for qt in range(4):
            b = qt % 2
            wts = [(s_tn, wq_marks[qt - 2][0])] if qt >= 2 else []
            nw = ctx.emit("gp", lambda g, qt=qt: g.dma_start(out=wsl[:], in_=wqs[qt, :, :]),
                          waits=(_now(s_tn) if qt > 0 else []), inc=d_g0, inck=16)
            ml = None
            for th in range(2):
                for dc in range(DC):
                    ml = ctx.emit("tn", lambda t, dc=dc, th=th: t.matmul(
                        P[th][:, :], wsl[:, dc * 128:(dc + 1) * 128],
                        xn1[:, dc * TL + H2[th].start:dc * TL + H2[th].stop],
                        start=(dc == 0), stop=(dc == DC - 1)),
                        waits=([(d_g0, nw), (d_ld, n_xn1)] + _now(s_dve, s_act) if dc == 0 else []),
                        inc=(s_tn if dc == DC - 1 else None))
                ctx.emit("dve", lambda v, qt=qt, th=th: v.tensor_scalar(
                    qs[:, qt * TL + H2[th].start:qt * TL + H2[th].stop], P[th][:, :],
                    sclc[:, qt:qt + 1], None, op0=ALU.mult),
                    waits=[(s_tn, ml)], inc=s_dve)
            nw2 = ctx.emit("gp", lambda g, qt=qt: g.dma_start(out=wsl[:], in_=wqg[qt, :, :]),
                           waits=_now(s_tn), inc=d_g0, inck=16)
            for th in range(2):
                for dc in range(DC):
                    ml = ctx.emit("tn", lambda t, dc=dc, th=th: t.matmul(
                        P[2 + th][:, :], wsl[:, dc * 128:(dc + 1) * 128],
                        xn1[:, dc * TL + H2[th].start:dc * TL + H2[th].stop],
                        start=(dc == 0), stop=(dc == DC - 1)),
                        waits=([(d_g0, nw2)] + _now(s_dve) if dc == 0 else []),
                        inc=(s_tn if dc == DC - 1 else None))
                ctx.emit("dve", lambda v, th=th: v.tensor_copy(qgf[:, H2[th]], P[2 + th][:, :]),
                         waits=[(s_tn, ml)], inc=s_dve)
                ctx.emit("dve", lambda v, qt=qt, th=th: v.tensor_copy(
                    qg[:, qt * TL + H2[th].start:qt * TL + H2[th].stop], qgf[:, H2[th]]),
                    waits=_now(s_dve), inc=s_dve)
                ns = ctx.emit("tn", lambda t, qt=qt, th=th: t.matmul(
                    P[4 + th][:, :], swpb[:], qg[:, qt * TL + H2[th].start:qt * TL + H2[th].stop],
                    start=True, stop=True),
                    waits=_now(s_dve) + [(d_gs, 32)], inc=s_tn)
                ctx.emit("dve", lambda v, th=th: v.tensor_tensor(
                    qgf[:, H2[th]], qgf[:, H2[th]], cost[:, H2[th]], op=ALU.mult),
                    waits=[(s_tn, s_tn.n), (d_cn, d_cn.n)], inc=s_dve)
                ctx.emit("dve", lambda v, th=th: v.tensor_tensor(
                    x1t[:, H2[th]], P[4 + th][:, :], sint[:, H2[th]], op=ALU.mult),
                    waits=_now(s_dve), inc=s_dve)
                ctx.emit("dve", lambda v, th=th: v.tensor_tensor(
                    qgf[:, H2[th]], qgf[:, H2[th]], x1t[:, H2[th]], op=ALU.add),
                    waits=_now(s_dve), inc=s_dve)
                ctx.emit("dve", lambda v, qt=qt, th=th: v.tensor_scalar(
                    qg[:, qt * TL + H2[th].start:qt * TL + H2[th].stop], qgf[:, H2[th]],
                    sclc[:, qt:qt + 1], None, op0=ALU.mult),
                    waits=_now(s_dve), inc=s_dve)
            wq_marks.append((ml,))
        ctx.barrier()

        zp4 = [sb(f"zp4_{i}", [128, 448], F32) for i in range(2)]
        ep4 = [sb(f"ep4_{i}", [128, 448], BF16) for i in range(2)]
        etp2 = [sb(f"etp2_{i}", [112, 512], BF16) for i in range(2)]
        rv2 = [sb(f"rv2_{i}", [128, 8], F32) for i in range(2)]
        pooled_it = []
        for qt in range(4):
            for qc in range(NB - 1):
                pooled_it.append((qt, qc))
        cons_sc = 0
        cons_tp = 0
        cons_pv = 0
        prev_ep = 0
        prev_et = 0
        bu = 0
        for it, (qt, qc) in enumerate(pooled_it):
            bu = it % 2
            qsl = slice(qc * 128, (qc + 1) * 128)
            n1 = None
            for m in range(4):
                base = 32 * m
                ctx.emit("tn", lambda t, qt=qt, base=base, qsl=qsl, m=m: t.matmul(
                    P[m][:, 0:112],
                    qs[base:base + 32, qt * TL + qsl.start:qt * TL + qsl.stop],
                    kstb[base:base + 32, qt * 112:(qt + 1) * 112],
                    start=True, stop=False, tile_position=(base, 0)),
                    waits=(([(s_dve, cons_sc)] if cons_sc else []) if m == 0 else []))
                n1 = ctx.emit("tn", lambda t, qt=qt, base=base, qsl=qsl, m=m: t.matmul(
                    P[m][:, 0:112],
                    qg[base:base + 32, qt * TL + qsl.start:qt * TL + qsl.stop],
                    kgtb[base:base + 32, qt * 112:(qt + 1) * 112],
                    start=False, stop=True, tile_position=(base, 0)),
                    inc=s_tn)
            nz = None
            for m in range(4):
                nz = ctx.emit("dve", lambda v, bu=bu, qc=qc, m=m: v.scalar_tensor_tensor(
                    zp4[bu][:, m * 112:(m + 1) * 112], P[m][:, 0:112],
                    30000.0, pmt[:, qc * 112:(qc + 1) * 112], op0=ALU.add, op1=ALU.mult),
                    waits=([(s_tn, n1), (d_cn, d_cn.n)] if m == 0 else []), inc=(s_dve if m == 3 else None))
            cons_sc = nz
            ctx.emit("dve", lambda v, bu=bu: v.reduce_max(rv2[bu][:, 0:1], zp4[bu][:], axis=AX.X, negate=True),
                     waits=[(s_dve, nz)], inc=s_dve)
            na = ctx.emit("act", lambda a, bu=bu: a.activation(
                ep4[bu][:], zp4[bu][:], AF.Exp, bias=rv2[bu][:, 0:1]),
                waits=_now(s_dve) + ([(s_tn, prev_ep)] if prev_ep else []), inc=s_act)
            ctx.emit("dve", lambda v, bu=bu: v.tensor_reduce(
                rv2[bu][:, 4:8], ep4[bu][:].rearrange("p (m k) -> p m k", k=112), axis=AX.X, op=ALU.add),
                waits=[(s_act, na)], inc=s_dve)
            ctx.emit("dve", lambda v, bu=bu: v.reciprocal(rv2[bu][:, 4:8], rv2[bu][:, 4:8]),
                     waits=_now(s_dve), inc=s_dve)
            rb4 = rv2[bu][:, 4:8].unsqueeze(2).broadcast_to([128, 4, 112])
            ne = ctx.emit("dve", lambda v, bu=bu, rb4=rb4: v.tensor_tensor(
                ep4[bu][:].rearrange("p (m k) -> p m k", k=112),
                ep4[bu][:].rearrange("p (m k) -> p m k", k=112), rb4, op=ALU.mult),
                waits=_now(s_dve), inc=s_dve)
            ntl = 0
            for m in range(4):
                tw = [(s_dve, ne), (d_gs, 32)]
                if m == 0 and cons_tp:
                    tw.append((s_dve, cons_tp))
                if ntl:
                    tw.append((s_dve, ntl))
                nt = ctx.emit("tn", lambda t, bu=bu, m=m: t.transpose(
                    P[4].bitcast(BF16)[0:112, 0:128], ep4[bu][:, m * 112:(m + 1) * 112], identb[:]),
                    waits=tw, inc=s_tn)
                prev_ep = nt
                ntl = ctx.emit("dve", lambda v, bu=bu, m=m: v.tensor_copy(
                    etp2[bu][:, m * 128:(m + 1) * 128], P[4].bitcast(BF16)[0:112, 0:128]),
                    waits=[(s_tn, nt)] + ([(s_tn, prev_et)] if (m == 0 and prev_et) else []),
                    inc=s_dve)
            cons_tp = ntl
            ncp = 0
            for m in range(4):
                pvw = [(s_dve, cons_tp)] if m == 0 else []
                if m == 0 and cons_pv:
                    pvw.append((s_dve, cons_pv))
                if ncp:
                    pvw.append((s_dve, ncp))
                npv = ctx.emit("tn", lambda t, qt=qt, m=m, bu=bu: t.matmul(
                    P[5][0:64, 0:128], vtb[:, qt * 64:(qt + 1) * 64],
                    etp2[bu][:, m * 128:(m + 1) * 128], start=True, stop=True),
                    waits=pvw, inc=s_tn)
                prev_et = npv
                h = 4 * qt + m
                jt, sub = h // 2, h % 2
                ncp = ctx.emit("dve", lambda v, jt=jt, sub=sub, qsl=qsl: v.tensor_copy(
                    att[sub * 64:(sub + 1) * 64, jt * TL + qsl.start:jt * TL + qsl.stop],
                    P[5][0:64, 0:128]),
                    waits=[(s_tn, npv)], inc=s_dve)
                if qc == 0:
                    sl2 = slice(sub * 64, (sub + 1) * 64)
                    ctx.emit("dve", lambda v, jt=jt, sl2=sl2, qt=qt, bu=bu: v.tensor_scalar(
                        zp4[bu][sl2, 0:128], att[sl2, jt * TL:jt * TL + 128],
                        vbarc[sl2, qt:qt + 1], None, op0=ALU.subtract),
                        waits=_now(s_dve), inc=s_dve)
                    ctx.emit("dve", lambda v, sl2=sl2, bu=bu: v.tensor_tensor(
                        zp4[bu][sl2, 0:128], zp4[bu][sl2, 0:128], fmt[sl2, 0:128], op=ALU.mult),
                        waits=_now(s_dve) + [(d_cn, d_cn.n)], inc=s_dve)
                    ncp = ctx.emit("dve", lambda v, jt=jt, sl2=sl2, bu=bu: v.tensor_tensor(
                        att[sl2, jt * TL:jt * TL + 128], att[sl2, jt * TL:jt * TL + 128],
                        zp4[bu][sl2, 0:128], op=ALU.subtract),
                        waits=_now(s_dve), inc=s_dve)
                    cons_sc = ncp
            cons_pv = ncp
        for qt in range(4):
            for m in range(4):
                base = 32 * m
                qc = NB - 1
                qsl = slice(qc * 128, (qc + 1) * 128)
                n1 = ctx.emit("tn", lambda t, qt=qt, base=base, qsl=qsl: t.matmul(
                    P[0][:, 0:112], qs[base:base + 32, qt * TL + qsl.start:qt * TL + qsl.stop],
                    kstb[base:base + 32, qt * 112:(qt + 1) * 112],
                    start=True, stop=False, tile_position=(base, 0)),
                    waits=_now(s_dve, s_act))
                n1 = ctx.emit("tn", lambda t, qt=qt, base=base, qsl=qsl: t.matmul(
                    P[0][:, 0:112], qg[base:base + 32, qt * TL + qsl.start:qt * TL + qsl.stop],
                    kgtb[base:base + 32, qt * 112:(qt + 1) * 112],
                    start=False, stop=True, tile_position=(base, 0)),
                    inc=s_tn)
                for rh in range(2):
                    ctx.emit("tn", lambda t, qt=qt, base=base, qsl=qsl, rh=rh: t.matmul(
                        P[1 + rh][:, :], qs[base:base + 32, qt * TL + qsl.start:qt * TL + qsl.stop],
                        ksrb[base:base + 32, qt * TL + H2[rh].start:qt * TL + H2[rh].stop],
                        start=True, stop=False, tile_position=(base, 0)))
                    ctx.emit("tn", lambda t, qt=qt, base=base, qsl=qsl, rh=rh: t.matmul(
                        P[1 + rh][:, :], qg[base:base + 32, qt * TL + qsl.start:qt * TL + qsl.stop],
                        kgrb[base:base + 32, qt * TL + H2[rh].start:qt * TL + H2[rh].stop],
                        start=False, stop=True, tile_position=(base, 0)),
                        inc=(s_tn if rh == 1 else None))
                ctx.emit("dve", lambda v, qc=qc: v.scalar_tensor_tensor(
                    zp[:], P[0][:, 0:112], 30000.0, pmt[:, qc * 112:(qc + 1) * 112],
                    op0=ALU.add, op1=ALU.mult),
                    waits=_now(s_tn) + [(d_cn, d_cn.n), (s_act, s_act.n)], inc=s_dve)
                ctx.emit("dve", lambda v: v.reduce_max(rowv[:, 0:1], zp[:], axis=AX.X, negate=True),
                         waits=_now(s_dve), inc=s_dve)
                for rh in range(2):
                    ctx.emit("dve", lambda v, rh=rh: v.scalar_tensor_tensor(
                        zr[:, H2[rh]], P[1 + rh][:, :], 30000.0, rmt[:, H2[rh]],
                        op0=ALU.add, op1=ALU.mult),
                        waits=_now(s_tn), inc=s_dve)
                ctx.emit("dve", lambda v: v.reduce_max(rowv[:, 1:2], zr[:], axis=AX.X, negate=True),
                         waits=_now(s_dve), inc=s_dve)
                ctx.emit("dve", lambda v: v.tensor_tensor(rowv[:, 0:1], rowv[:, 0:1], rowv[:, 1:2], op=ALU.min),
                         waits=_now(s_dve), inc=s_dve)
                na = ctx.emit("act", lambda a: a.activation(
                    ep[:], zp[:], AF.Exp, bias=rowv[:, 0:1], accum_out=rowv[:, 2:3]),
                    waits=_now(s_dve), inc=s_act)
                na = ctx.emit("act", lambda a: a.activation(
                    er[:], zr[:], AF.Exp, bias=rowv[:, 0:1], accum_out=rowv[:, 3:4]),
                    waits=_now(s_act), inc=s_act)
                ctx.emit("dve", lambda v: v.tensor_tensor(rowv[:, 2:3], rowv[:, 2:3], rowv[:, 3:4], op=ALU.add),
                         waits=[(s_act, na)], inc=s_dve)
                ctx.emit("dve", lambda v: v.reciprocal(rowv[:, 4:5], rowv[:, 2:3]),
                         waits=_now(s_dve), inc=s_dve)
                ctx.emit("dve", lambda v: v.tensor_scalar(ep[:], ep[:], rowv[:, 4:5], None, op0=ALU.mult),
                         waits=_now(s_dve), inc=s_dve)
                nt = ctx.emit("tn", lambda t: t.transpose(P[3].bitcast(BF16)[0:112, 0:128], ep[:], identb[:]),
                              waits=_now(s_dve) + [(d_gs, 32)], inc=s_tn)
                ctx.emit("dve", lambda v: v.tensor_copy(etp[:, :], P[3].bitcast(BF16)[0:112, 0:128]),
                         waits=[(s_tn, nt)], inc=s_dve)
                ctx.emit("dve", lambda v: v.tensor_scalar(er[:], er[:], rowv[:, 4:5], None, op0=ALU.mult),
                         waits=_now(s_dve), inc=s_dve)
                for kc in range(8):
                    nt = ctx.emit("tn", lambda t, kc=kc: t.transpose(
                        P[3].bitcast(BF16)[:, 0:128], er[:, kc * 128:(kc + 1) * 128], identb[:]),
                        waits=_now(s_dve), inc=s_tn)
                    ctx.emit("dve", lambda v, kc=kc: v.tensor_copy(
                        etr[:, kc * 128:(kc + 1) * 128], P[3].bitcast(BF16)[:, 0:128]),
                        waits=[(s_tn, nt)], inc=s_dve)
                npv = ctx.emit("tn", lambda t, qt=qt: t.matmul(
                    P[6][0:64, 0:128], vtb[:, qt * 64:(qt + 1) * 64], etp[:, :],
                    start=True, stop=False),
                    waits=_now(s_dve))
                for kc in range(8):
                    npv = ctx.emit("tn", lambda t, qt=qt, kc=kc: t.matmul(
                        P[6][0:64, 0:128], vrb[:, kc * 256 + qt * 64:kc * 256 + (qt + 1) * 64],
                        etr[:, kc * 128:(kc + 1) * 128],
                        start=False, stop=(kc == 7)),
                        inc=(s_tn if kc == 7 else None))
                h = 4 * qt + m
                jt, sub = h // 2, h % 2
                ctx.emit("dve", lambda v, jt=jt, sub=sub, qsl=qsl: v.tensor_copy(
                    att[sub * 64:(sub + 1) * 64, jt * TL + qsl.start:jt * TL + qsl.stop],
                    P[6][0:64, 0:128]),
                    waits=[(s_tn, s_tn.n)], inc=s_dve)
        ctx.barrier()

        s1_marks = []
        cons_o = [0, 0]
        for co in range(DC):
            b = co % 2
            wts = [(s_tn, s1_marks[co - 2])] if co >= 2 else []
            nw = ctx.emit("gp", lambda g, co=co, b=b: g.dma_start(
                out=wsl[:, b * 1024:(b + 1) * 1024], in_=wo1[co, :, :]),
                waits=wts, inc=dws[b], inck=16)
            nx = ctx.emit("sy", lambda s, co=co: s.dma_start(out=x1t[:], in_=x1i[co, :, :]),
                          waits=(_now(s_dve) if co > 0 else []), inc=d_ld, inck=16)
            ng = ctx.emit("sy", lambda s, co=co: s.dma_start(out=gbuf[:], in_=g1i[co, :, :]),
                          waits=(_now(s_dve) if co > 0 else []), inc=d_l2, inck=16)
            ml = None
            for th in range(2):
                po = P[6] if th == 0 else P[7]
                pw = [(dws[b], nw)]
                if cons_o[th]:
                    pw.append((s_dve, cons_o[th]))
                for j in range(8):
                    ml = ctx.emit("tn", lambda t, j=j, th=th, po=po, b=b: t.matmul(
                        po[:, :], wsl[:, b * 1024 + j * 128:b * 1024 + (j + 1) * 128],
                        att[:, j * TL + H2[th].start:j * TL + H2[th].stop],
                        start=(j == 0), stop=(j == 7)),
                        waits=(pw if j == 0 else []),
                        inc=(s_tn if j == 7 else None))
                ctx.emit("dve", lambda v, th=th, po=po: v.tensor_tensor(
                    ot[:, H2[th]], po[:, :], gbuf[:, H2[th]], op=ALU.mult),
                    waits=[(s_tn, ml), (d_l2, ng), (d_ld, nx), (d_st, d_st.n)], inc=s_dve)
                cons_o[th] = ctx.emit("dve", lambda v, th=th: v.tensor_tensor(
                    ot[:, H2[th]], ot[:, H2[th]], x1t[:, H2[th]], op=ALU.add),
                    waits=_now(s_dve), inc=s_dve)
            s1_marks.append(ml)
            ctx.emit("sy", lambda s, co=co: s.dma_start(out=outp[co, :, :], in_=ot[:]),
                     waits=_now(s_dve), inc=d_st, inck=16)
        for e in ENGINES:
            ctx.ops[e].append(([(d_st.h, d_st.n)], None, None))
        ctx.finalize()
    return nc


def _prep_shared_l3(inp):
    f = lambda n: np.ascontiguousarray(np.asarray(inp[n], np.float32))
    out = {}
    out["wqs"] = _tile_w(f("L1_attn_Wq_sem"))
    out["wqg"] = _tile_w(f("L1_attn_Wq_geo"))
    out["wo1"] = _tile_w(f("L1_attn_Wo"))
    ls = f("L1_attn_logit_scale")
    lsrep = np.zeros((128, 4), np.float32)
    for qt in range(4):
        for p in range(128):
            lsrep[p, qt] = ls[4 * qt + p // 32]
    out["lsrep"] = lsrep
    out["swp"] = _swap_perm()
    out["ident"] = np.eye(128, dtype=np.float32)
    return out


def _launch3_inmaps(inp, shared3, l2res):
    ks2g = np.zeros((128, 112), np.float32)
    kg2g = np.zeros((128, 112), np.float32)
    v2g = np.zeros((112, 256), np.float32)
    for g in range(56):
        c, j = g % 8, g // 8
        ks2g[:, 2 * g:2 * g + 2] = np.asarray(l2res[c]["ks2o"])[:, 2 * j:2 * j + 2]
        kg2g[:, 2 * g:2 * g + 2] = np.asarray(l2res[c]["kg2o"])[:, 2 * j:2 * j + 2]
        v2g[2 * g:2 * g + 2, :] = np.asarray(l2res[c]["v2o"])[2 * j:2 * j + 2, :]
    ksrg = np.concatenate([np.asarray(l2res[c]["ksro"]) for c in range(NC)], axis=1)
    kgrg = np.concatenate([np.asarray(l2res[c]["kgro"]) for c in range(NC)], axis=1)
    vrg = np.stack([np.asarray(l2res[c]["vro"]) for c in range(NC)])
    kst = np.stack([np.tile(ks2g[32 * g:32 * (g + 1), :], (4, 1)) for g in range(4)])
    kgt = np.stack([np.tile(kg2g[32 * g:32 * (g + 1), :], (4, 1)) for g in range(4)])
    ksr = np.stack([np.tile(ksrg[32 * g:32 * (g + 1), :], (4, 1)) for g in range(4)])
    kgr = np.stack([np.tile(kgrg[32 * g:32 * (g + 1), :], (4, 1)) for g in range(4)])
    k_pos = np.concatenate([np.arange(112) * 64 + 63, np.arange(7168, T)])
    maps = []
    for i in range(NC):
        toks = _core_tokens(i)
        C, S = _rope_tables(toks)
        pmask = np.zeros((NB, 128, 112), np.float32)
        for j in range(NB):
            qpos = (i + 8 * j) * 128 + np.arange(128)
            pmask[j] = (k_pos[None, 0:112] <= qpos[:, None]).astype(np.float32)
        qpos7 = (56 + i) * 128 + np.arange(128)
        rmask = (k_pos[None, 112:] <= qpos7[:, None]).astype(np.float32)
        fixm = np.zeros((128, 128), np.float32)
        if i == 0:
            for r in range(63):
                fixm[:, r] = 1.0
        m = {
            "xn1i": np.asarray(l2res[i]["xn1o"]),
            "x1i": np.asarray(l2res[i]["x1o"]),
            "g1i": np.asarray(l2res[i]["g1o"]),
            "cosq": C, "sinq": S,
            "kst": np.ascontiguousarray(kst), "kgt": np.ascontiguousarray(kgt),
            "ksr": np.ascontiguousarray(ksr), "kgr": np.ascontiguousarray(kgr),
            "vt": np.ascontiguousarray(v2g), "vrr": np.ascontiguousarray(vrg),
            "pmask": pmask, "rmask": np.ascontiguousarray(rmask), "fixm": fixm,
        }
        m.update(shared3)
        maps.append(m)
    return maps


def kernel(**inputs):
    shared1 = _prep_shared_l1(inputs)
    nc1 = build_launch1()
    res1 = run_bass_kernel_spmd(nc1, _launch1_inmaps(inputs, shared1), core_ids=list(range(NC)))
    shared2 = _prep_shared_l2(inputs)
    nc2 = build_launch2()
    res2 = run_bass_kernel_spmd(nc2, _launch2_inmaps(inputs, shared2, res1.results), core_ids=list(range(NC)))
    shared3 = _prep_shared_l3(inputs)
    nc3 = build_launch3()
    res3 = run_bass_kernel_spmd(nc3, _launch3_inmaps(inputs, shared3, res2.results), core_ids=list(range(NC)))
    out = np.zeros((T, D), np.float32)
    for i in range(NC):
        o = np.asarray(res3.results[i]["outp"])
        for j in range(NB):
            g = i + 8 * j
            out[g * 128:(g + 1) * 128, :] = o[:, :, j * 128:(j + 1) * 128].transpose(2, 0, 1).reshape(128, D)
    return out[None]


# version 11
# speedup vs baseline: 1.5731x; 1.0244x over previous
import contextlib
import numpy as np

import concourse.bass as bass
import concourse.mybir as mybir
from concourse.bass_utils import run_bass_kernel_spmd

F32 = mybir.dt.float32
BF16 = mybir.dt.bfloat16
AF = mybir.ActivationFunctionType
ALU = mybir.AluOpType
AX = mybir.AxisListType

NC = 8
T = 8192
D = 2048
TL = 1024
NB = 8
DC = 16
FF = 8192
FC = 64
DFF1 = 1408
LN_EPS = 1e-5

ENGINES = ("tn", "act", "dve", "gp", "sy")
_ENG_ATTR = {"tn": "tensor", "act": "scalar", "dve": "vector", "gp": "gpsimd", "sy": "sync"}



def _tile_w(w_in_out):
    n_in, n_out = w_in_out.shape
    ic, oc = n_in // 128, (n_out + 127) // 128
    if n_out % 128:
        w_in_out = np.concatenate(
            [w_in_out, np.zeros((n_in, oc * 128 - n_out), w_in_out.dtype)], 1)
    w4 = w_in_out.reshape(ic, 128, oc, 128)
    return np.ascontiguousarray(w4.transpose(2, 1, 0, 3).reshape(oc, 128, ic * 128))


def _vec_pc(v):
    return np.ascontiguousarray(np.asarray(v, np.float32).reshape(-1, 128).T)


def _core_tokens(i):
    idx = []
    for j in range(NB):
        g = i + 8 * j
        idx.extend(range(g * 128, (g + 1) * 128))
    return np.array(idx)



class Sem:
    __slots__ = ("h", "n", "name", "dma")

    def __init__(self, h, name, dma=False):
        self.h, self.n, self.name, self.dma = h, 0, name, dma


class Ctx:
    def __init__(self, nc, es):
        self.nc = nc
        self.es = es
        self.ops = {e: [] for e in ENGINES}
        self.sems = {}

    def sem(self, name, dma=False):
        s = Sem(self.es.enter_context(self.nc.semaphore(name)), name, dma)
        self.sems[name] = s
        return s

    def emit(self, eng, fn, waits=(), inc=None, inck=1):
        w = [(s.h, int(v)) for s, v in waits if v > 0]
        if inc is not None and inc.dma and inc.n > 0:
            w.append((inc.h, inc.n))
        tgt = None
        if inc is not None:
            inc.n += inck
            tgt = (inc.h, inck)
        self.ops[eng].append((w, fn, tgt))
        return inc.n if inc is not None else None

    def barrier(self):
        cur = [(s, s.n) for s in self.sems.values() if s.n > 0]
        for e in ENGINES:
            for s, v in cur:
                self.ops[e].append(([(s.h, v)], None, None))

    def finalize(self):
        block = self.es.enter_context(self.nc.Block())
        for name in ENGINES:
            ops = self.ops[name]

            def make(ops):
                def body(eng):
                    for w, fn, tgt in ops:
                        for h, v in w:
                            eng.wait_ge(h, v)
                        if fn is None:
                            continue
                        ins = fn(eng)
                        if tgt is not None:
                            ins.then_inc(tgt[0], tgt[1])
                return body
            getattr(block, _ENG_ATTR[name])(make(ops))


def _now(*sems):
    return [(s, s.n) for s in sems]



def build_launch1():
    nc = bass.Bass("TRN2", target_bir_lowering=False, debug=False)
    ein = lambda n, sh, dt: nc.dram_tensor(n, sh, dt, kind="ExternalInput").ap()
    eout = lambda n, sh, dt: nc.dram_tensor(n, sh, dt, kind="ExternalOutput").ap()
    xfm = ein("xfm", [DC, 128, TL], F32)
    xbnd = ein("xbnd", [NB, D], F32)
    ident = ein("ident", [128, 128], F32)
    wk = ein("wk", [DC, 128, D], F32)
    wv = ein("wv", [DC, 128, D], F32)
    wr = ein("wr", [DC, 128, D], F32)
    gw = ein("gw", [128, 32], F32)
    wg = ein("wg", [2, FC, 128, D], F32)
    wu = ein("wu", [2, FC, 128, D], F32)
    wd = ein("wd", [2, DC, 128, FF], F32)
    td = ein("td", [128, DC], F32)
    tmk = ein("tmk", [128, DC], F32)
    tmv = ein("tmv", [128, DC], F32)
    tmr = ein("tmr", [128, DC], F32)
    a_out = eout("a_out", [DC, 128, TL], F32)
    b_out = eout("b_out", [DC, 128, TL], F32)
    r_out = eout("r_out", [DC, 128, TL], BF16)
    g0_out = eout("g0_out", [DC, 128, TL], F32)
    zacc = nc.dram_tensor("zacc", [DC, 128, TL], F32).ap()

    es = contextlib.ExitStack()
    with es:
        ctx = Ctx(nc, es)
        sb = lambda n, sh, dt: es.enter_context(nc.sbuf_tensor(n, sh, dt))
        xn = sb("xn", [128, DC * TL], BF16)
        ones1 = sb("ones1", [128, 1], F32)
        epsc = sb("epsc", [128, 1], F32)
        onesrow = sb("onesrow", [1, 128], F32)
        identb = sb("identb", [128, 128], BF16)
        w0bc = sb("w0bc", [128, TL], F32)
        w1bc = sb("w1bc", [128, TL], F32)
        tmm = sb("tmm", [128, 3 * DC], F32)
        gwt = sb("gwt", [128, 32], BF16)
        xnbf = sb("xnbf", [128, DC * NB], BF16)
        xtile = [sb(f"xt{i}", [128, TL], F32) for i in range(2)]
        pha = contextlib.ExitStack()
        vr = pha.enter_context(nc.sbuf_tensor("vr", [1, 8 * TL], F32))
        brow = pha.enter_context(nc.sbuf_tensor("brow", [8, 16], F32))
        g2 = pha.enter_context(nc.sbuf_tensor("g2", [2, TL], F32))
        xbt = pha.enter_context(nc.sbuf_tensor("xbt", [NB, D], F32))
        xbsq = pha.enter_context(nc.sbuf_tensor("xbsq", [NB, D], F32))
        sq = pha.enter_context(nc.sbuf_tensor("sq", [128, TL], F32))
        mubc = pha.enter_context(nc.sbuf_tensor("mubc", [128, TL], F32))
        rbc = pha.enter_context(nc.sbuf_tensor("rbc", [128, TL], F32))
        P = [es.enter_context(nc.psum_tensor(f"pb{i}", [128, 512], F32)) for i in range(8)]

        s_tn, s_act, s_dve = ctx.sem("s_tn"), ctx.sem("s_act"), ctx.sem("s_dve")
        d_cn = ctx.sem("d_cn", dma=True)
        d_ld = ctx.sem("d_ld", dma=True)
        d_st = ctx.sem("d_st", dma=True)
        d_g0, d_g1 = ctx.sem("d_g0", dma=True), ctx.sem("d_g1", dma=True)
        d_gs = ctx.sem("d_gs", dma=True)
        d_ac = ctx.sem("d_ac", dma=True)
        dws = [d_g0, d_g1]
        H2 = (slice(0, 512), slice(512, 1024))

        def bcast_row(row_ap, dst, bank, wait_sems):
            for h in range(2):
                sl = H2[h]
                n = ctx.emit("tn", lambda t, sl=sl: t.matmul(
                    P[bank][:, :], onesrow[:], row_ap[:, sl], start=True, stop=True),
                    waits=list(wait_sems) + _now(s_dve), inc=s_tn)
                ctx.emit("dve", lambda v, sl=sl: v.tensor_copy(dst[:, sl], P[bank][:, :]),
                         waits=[(s_tn, n)], inc=s_dve)

        ctx.emit("dve", lambda v: v.memset(ones1[:], 1.0), inc=s_dve)
        ctx.emit("dve", lambda v: v.memset(epsc[:], LN_EPS), inc=s_dve)
        ctx.emit("dve", lambda v: v.memset(onesrow[:], 1.0), inc=s_dve)
        n_const = s_dve.n
        ctx.emit("sy", lambda s: s.dma_start(out=xbt[:], in_=xbnd[:, :]), inc=d_cn, inck=16)
        ctx.emit("sy", lambda s: s.dma_start(out=tmm[:, 0:DC], in_=td[:, :]), inc=d_cn, inck=16)
        ctx.emit("sy", lambda s: s.dma_start(out=tmm[:, 0:DC], in_=tmk[:, :]), inc=d_cn, inck=16)
        ctx.emit("sy", lambda s: s.dma_start(out=tmm[:, DC:2 * DC], in_=tmv[:, :]), inc=d_cn, inck=16)
        ctx.emit("sy", lambda s: s.dma_start(out=tmm[:, 2 * DC:3 * DC], in_=tmr[:, :]), inc=d_cn, inck=16)
        ctx.emit("gp", lambda g: g.dma_start(out=gwt[:], in_=gw[:, :]), inc=d_gs, inck=16)
        ctx.emit("gp", lambda g: g.dma_start(out=identb[:], in_=ident[:, :]), inc=d_gs, inck=16)
        n_tm = ctx.emit("dve", lambda v: v.tensor_scalar(tmm[:], tmm[:], -1.0, None, op0=ALU.add),
                        waits=[(d_cn, d_cn.n)], inc=s_dve)

        sq_marks = []
        for c in range(DC):
            buf = xtile[c % 2]
            wts = [(s_tn, sq_marks[c - 2][1])] if c >= 2 else []
            n = ctx.emit("sy", lambda s, c=c, buf=buf: s.dma_start(out=buf[:], in_=xfm[c, :, :]),
                         waits=wts, inc=d_ld, inck=16)
            na = ctx.emit("act", lambda a, buf=buf: a.activation(sq[:], buf[:], AF.Square),
                          waits=[(d_ld, n)] + ([(s_tn, sq_marks[c - 1][1])] if c >= 1 else []),
                          inc=s_act)
            last = None
            for h in range(2):
                ctx.emit("tn", lambda t, buf=buf, h=h, c=c: t.matmul(
                    P[h][0:1, :], ones1[:], buf[:, H2[h]], start=(c == 0), stop=(c == DC - 1)),
                    waits=[(s_act, na)] + ([(s_dve, n_const)] if c == 0 else []))
                last = ctx.emit("tn", lambda t, h=h, c=c: t.matmul(
                    P[2 + h][0:1, :], ones1[:], sq[:, H2[h]], start=(c == 0), stop=(c == DC - 1)),
                    waits=[(s_act, na)], inc=s_tn)
            sq_marks.append((n, last))
        V = lambda i: vr[0:1, i * TL:(i + 1) * TL]
        Vh = lambda i, h: vr[0:1, i * TL + H2[h].start:i * TL + H2[h].stop]
        for h in range(2):
            ctx.emit("dve", lambda v, h=h: v.tensor_copy(Vh(0, h), P[h][0:1, :]),
                     waits=_now(s_tn), inc=s_dve)
            ctx.emit("dve", lambda v, h=h: v.tensor_copy(Vh(1, h), P[2 + h][0:1, :]),
                     waits=_now(s_tn), inc=s_dve)
        ctx.emit("dve", lambda v: v.tensor_scalar(V(2), V(0), 1.0 / D, None, op0=ALU.mult),
                 waits=_now(s_dve), inc=s_dve)
        ctx.emit("dve", lambda v: v.tensor_scalar(V(1), V(1), 1.0 / D, None, op0=ALU.mult),
                 waits=_now(s_dve), inc=s_dve)
        ctx.emit("dve", lambda v: v.tensor_tensor(V(4), V(2), V(2), op=ALU.mult),
                 waits=_now(s_dve), inc=s_dve)
        ctx.emit("dve", lambda v: v.tensor_tensor(V(5), V(1), V(4), op=ALU.subtract),
                 waits=_now(s_dve), inc=s_dve)
        ctx.emit("act", lambda a: a.activation(V(6), V(5), AF.Ln, bias=epsc[0:1, 0:1]),
                 waits=_now(s_dve), inc=s_act)
        ctx.emit("act", lambda a: a.activation(V(3), V(6), AF.Exp, scale=-0.5),
                 waits=_now(s_act), inc=s_act)
        bcast_row(V(2), mubc, 4, _now(s_act))
        bcast_row(V(3), rbc, 4, _now(s_act))

        ap_marks = []
        for c in range(DC):
            buf = xtile[c % 2]
            wts = [(s_dve, ap_marks[c - 2])] if c >= 2 else _now(s_act)
            n = ctx.emit("sy", lambda s, c=c, buf=buf: s.dma_start(out=buf[:], in_=xfm[c, :, :]),
                         waits=wts, inc=d_ld, inck=16)
            ctx.emit("dve", lambda v, buf=buf: v.tensor_tensor(sq[:], buf[:], mubc[:], op=ALU.subtract),
                     waits=[(d_ld, n)] + _now(s_dve), inc=s_dve)
            m = ctx.emit("dve", lambda v, c=c: v.tensor_tensor(
                xn[:, c * TL:(c + 1) * TL], sq[:], rbc[:], op=ALU.mult),
                waits=_now(s_dve), inc=s_dve)
            ap_marks.append(m)

        ctx.emit("dve", lambda v: v.tensor_reduce(brow[0:8, 0:1], xbt[:], axis=AX.X, op=ALU.add),
                 waits=[(d_cn, 16)], inc=s_dve)
        ctx.emit("dve", lambda v: v.tensor_scalar(brow[0:8, 1:2], brow[0:8, 0:1], 1.0 / D, None, op0=ALU.mult),
                 waits=_now(s_dve), inc=s_dve)
        ctx.emit("act", lambda a: a.activation(xbsq[:], xbt[:], AF.Square, accum_out=brow[0:8, 2:3]),
                 waits=[(d_cn, 16)], inc=s_act)
        ctx.emit("dve", lambda v: v.tensor_scalar(brow[0:8, 3:4], brow[0:8, 2:3], 1.0 / D, None, op0=ALU.mult),
                 waits=_now(s_act), inc=s_dve)
        ctx.emit("dve", lambda v: v.tensor_tensor(brow[0:8, 4:5], brow[0:8, 1:2], brow[0:8, 1:2], op=ALU.mult),
                 waits=_now(s_dve), inc=s_dve)
        ctx.emit("dve", lambda v: v.tensor_tensor(brow[0:8, 5:6], brow[0:8, 3:4], brow[0:8, 4:5], op=ALU.subtract),
                 waits=_now(s_dve), inc=s_dve)
        ctx.emit("act", lambda a: a.activation(brow[0:8, 6:7], brow[0:8, 5:6], AF.Ln, bias=epsc[0:8, 0:1]),
                 waits=_now(s_dve), inc=s_act)
        ctx.emit("act", lambda a: a.activation(brow[0:8, 7:8], brow[0:8, 6:7], AF.Exp, scale=-0.5),
                 waits=_now(s_act), inc=s_act)
        ctx.emit("dve", lambda v: v.tensor_scalar(xbsq[:], xbt[:], brow[0:8, 1:2], None, op0=ALU.subtract),
                 waits=_now(s_act), inc=s_dve)
        xbt_bf = xbt.bitcast(BF16)
        ctx.emit("dve", lambda v: v.tensor_scalar(xbt_bf[:, 0:D], xbsq[:], brow[0:8, 7:8], None, op0=ALU.mult),
                 waits=_now(s_dve), inc=s_dve)
        for c in range(DC):
            n = ctx.emit("tn", lambda t, c=c: t.transpose(
                P[5].bitcast(BF16)[:, 0:8], xbt_bf[0:8, c * 128:(c + 1) * 128], identb[0:8, 0:8]),
                waits=_now(s_dve, d_gs), inc=s_tn)
            ctx.emit("dve", lambda v, c=c: v.tensor_copy(
                xnbf[:, c * NB:(c + 1) * NB], P[5].bitcast(BF16)[:, 0:8]),
                waits=[(s_tn, n)], inc=s_dve)

        for c in range(DC):
            for h in range(2):
                ctx.emit("tn", lambda t, c=c, h=h: t.matmul(
                    P[6 + h][0:2, :], gwt[:, 2 * c:2 * c + 2], xn[:, c * TL + H2[h].start:c * TL + H2[h].stop],
                    start=(c == 0), stop=(c == DC - 1)),
                    waits=(_now(s_dve, d_gs) if c == 0 else []),
                    inc=(s_tn if c == DC - 1 and h == 1 else None))
        for h in range(2):
            ctx.emit("dve", lambda v, h=h: v.tensor_copy(g2[0:2, H2[h]], P[6 + h][0:2, :]),
                     waits=_now(s_tn), inc=s_dve)
        ctx.emit("sy", lambda s: s.dma_start(out=V(1), in_=g2[1:2, 0:TL]),
                 waits=_now(s_dve), inc=d_cn, inck=16)
        ctx.emit("dve", lambda v: v.tensor_tensor(V(3), g2[0:1, 0:TL], V(1), op=ALU.subtract),
                 waits=[(d_cn, d_cn.n)], inc=s_dve)
        ctx.emit("act", lambda a: a.activation(V(4), V(3), AF.Abs),
                 waits=_now(s_dve), inc=s_act)
        ctx.emit("act", lambda a: a.activation(V(5), V(4), AF.Exp, scale=-1.0),
                 waits=_now(s_act), inc=s_act)
        ctx.emit("dve", lambda v: v.tensor_scalar(V(6), V(5), 1.0, None, op0=ALU.add),
                 waits=_now(s_act), inc=s_dve)
        ctx.emit("dve", lambda v: v.reciprocal(V(5), V(6)),
                 waits=_now(s_dve), inc=s_dve)
        ctx.emit("dve", lambda v: v.tensor_scalar(V(6), V(3), 0.0, None, op0=ALU.is_ge),
                 waits=_now(s_dve), inc=s_dve)
        ctx.emit("dve", lambda v: v.tensor_tensor(V(7), V(5), V(6), op=ALU.mult),
                 waits=_now(s_dve), inc=s_dve)
        ctx.emit("dve", lambda v: v.tensor_tensor(V(6), V(5), V(7), op=ALU.subtract),
                 waits=_now(s_dve), inc=s_dve)
        bcast_row(V(7), w0bc, 6, _now(s_dve))
        bcast_row(V(6), w1bc, 6, _now(s_dve))

        ctx.barrier()
        pha.close()

        with contextlib.ExitStack() as ph:
            hbuf = ph.enter_context(nc.sbuf_tensor("hbuf", [128, FC * TL], BF16))
            warena = ph.enter_context(nc.sbuf_tensor("warena", [128, 8192], BF16))
            hs = ph.enter_context(nc.sbuf_tensor("hs", [128, 1024], F32))
            zst = ph.enter_context(nc.sbuf_tensor("zst", [128, TL], F32))
            zstB = ph.enter_context(nc.sbuf_tensor("zstB", [128, TL], F32))
            zbuf1 = [zst, zstB]
            wgb = [warena[:, 0:2048], warena[:, 2048:4096]]
            wub = [warena[:, 4096:6144], warena[:, 6144:8192]]
            wdb = [warena[:, 0:4096], warena[:, 4096:8192]]
            for e in range(2):
                gu_marks = []
                cons_a = [0, 0]
                cons_d = [0, 0]
                for fc in range(FC):
                    b = fc % 2
                    wts = [(s_tn, gu_marks[fc - 2])] if fc >= 2 else []
                    ctx.emit("gp", lambda g, e=e, fc=fc, b=b: g.dma_start(
                        out=wgb[b][:], in_=wg[e, fc, :, :]), waits=wts, inc=dws[b], inck=16)
                    nu = ctx.emit("gp", lambda g, e=e, fc=fc, b=b: g.dma_start(
                        out=wub[b][:], in_=wu[e, fc, :, :]), inc=dws[b], inck=16)
                    mlast = None
                    for th in range(2):
                        pg, pu = P[2 * th], P[2 * th + 1]
                        pw = [(dws[b], nu)]
                        if cons_a[th]:
                            pw.append((s_act, cons_a[th]))
                        if cons_d[th]:
                            pw.append((s_dve, cons_d[th]))
                        for dc in range(DC):
                            ctx.emit("tn", lambda t, b=b, dc=dc, th=th, pg=pg: t.matmul(
                                pg[:, :], wgb[b][:, dc * 128:(dc + 1) * 128],
                                xn[:, dc * TL + H2[th].start:dc * TL + H2[th].stop],
                                start=(dc == 0), stop=(dc == DC - 1)),
                                waits=(pw if dc == 0 else []))
                        for dc in range(DC):
                            mlast = ctx.emit("tn", lambda t, b=b, dc=dc, th=th, pu=pu: t.matmul(
                                pu[:, :], wub[b][:, dc * 128:(dc + 1) * 128],
                                xn[:, dc * TL + H2[th].start:dc * TL + H2[th].stop],
                                start=(dc == 0), stop=(dc == DC - 1)),
                                inc=(s_tn if dc == DC - 1 else None))
                        na = ctx.emit("act", lambda a, pg=pg, th=th: a.activation(hs[:, H2[th]], pg[:, :], AF.Silu),
                                      waits=[(s_tn, mlast)] + ([(s_dve, cons_d[th])] if cons_d[th] else []),
                                      inc=s_act)
                        cons_a[th] = na
                        cons_d[th] = ctx.emit("dve", lambda v, fc=fc, th=th, pu=pu: v.tensor_tensor(
                            hbuf[:, fc * TL + H2[th].start:fc * TL + H2[th].stop], hs[:, H2[th]], pu[:, :], op=ALU.mult),
                            waits=[(s_act, na)], inc=s_dve)
                    gu_marks.append(mlast)
                ctx.barrier()
                wbc = w0bc if e == 0 else w1bc
                y_marks = []
                cons_y = [0, 0]
                pend = []
                for do in range(DC):
                    while len(pend) >= 2:
                        pend.pop(0)()
                    mlast = None
                    for fh in range(2):
                        b = fh
                        wts = [(s_tn, y_marks[do - 1])] if do >= 1 else []
                        nw = ctx.emit("gp", lambda g, e=e, do=do, fh=fh, b=b: g.dma_start(
                            out=wdb[b][:], in_=wd[e, do, :, fh * 4096:(fh + 1) * 4096]),
                            waits=wts, inc=dws[b], inck=16)
                        for th in range(2):
                            py = P[4 + th]
                            pw = [(dws[b], nw)]
                            if fh == 0 and cons_y[th]:
                                pw.append((s_dve, cons_y[th]))
                            for fi in range(32):
                                fc = fh * 32 + fi
                                mlast = ctx.emit("tn", lambda t, b=b, fi=fi, fc=fc, th=th, py=py: t.matmul(
                                    py[:, :], wdb[b][:, fi * 128:(fi + 1) * 128],
                                    hbuf[:, fc * TL + H2[th].start:fc * TL + H2[th].stop],
                                    start=(fc == 0), stop=(fc == FC - 1)),
                                    waits=(pw if fi == 0 else []),
                                    inc=(s_tn if (fh == 1 and fi == 31) else None))
                    y_marks.append(mlast)
                    zb = zbuf1[do % 2]
                    for th in range(2):
                        py = P[4 + th]
                        cons_y[th] = ctx.emit("dve", lambda v, th=th, py=py, zb=zb, wbc=wbc: v.tensor_tensor(
                            zb[:, H2[th]], py[:, :], wbc[:, H2[th]], op=ALU.mult),
                            waits=[(s_tn, mlast), (d_ac, d_ac.n)], inc=s_dve)
                    nd = s_dve.n
                    def mk_acc(do=do, zb=zb, e=e, nd=nd):
                        def go():
                            ctx.emit("gp", lambda g: (
                                g.dma_start(out=zacc[do, :, :], in_=zb[:]) if e == 0 else
                                g.dma_start(out=zacc[do, :, :], in_=zb[:], accum_op=ALU.add)),
                                waits=[(s_dve, nd)], inc=d_ac, inck=16)
                        return go
                    pend.append(mk_acc())
                for p_ in pend:
                    p_()
                ctx.barrier()
            g0_marks = []
            for do in range(DC):
                buf = xtile[do % 2]
                wts = [(d_ac, d_ac.n)] + ([(s_act, g0_marks[do - 2])] if do >= 2 else [])
                n = ctx.emit("sy", lambda s, do=do, buf=buf: s.dma_start(out=buf[:], in_=zacc[do, :, :]),
                             waits=wts, inc=d_ld, inck=16)
                na = ctx.emit("act", lambda a, buf=buf: a.activation(zst[:, 0:TL], buf[:], AF.Silu),
                              waits=[(d_ld, n), (d_st, d_st.n)], inc=s_act)
                g0_marks.append(na)
                ctx.emit("sy", lambda s, do=do: s.dma_start(out=g0_out[do, :, :], in_=zst[:, 0:TL]),
                         waits=[(s_act, na)], inc=d_st, inck=16)
            ctx.barrier()

        with contextlib.ExitStack() as ph:
            xk = ph.enter_context(nc.sbuf_tensor("xk", [128, DC * TL], BF16))
            xv = ph.enter_context(nc.sbuf_tensor("xv", [128, DC * TL], BF16))
            xr = ph.enter_context(nc.sbuf_tensor("xr", [128, DC * TL], BF16))
            dtile = ph.enter_context(nc.sbuf_tensor("dtile", [128, TL], BF16))
            warena = ph.enter_context(nc.sbuf_tensor("warena2", [128, 8192], BF16))
            ct = ph.enter_context(nc.sbuf_tensor("ct", [128, TL], F32))
            cvt = ph.enter_context(nc.sbuf_tensor("cvt", [128, TL], F32))
            at = ph.enter_context(nc.sbuf_tensor("at", [128, TL], F32))
            bt = ph.enter_context(nc.sbuf_tensor("bt", [128, TL], F32))
            onesT = ph.enter_context(nc.sbuf_tensor("onesT", [128, TL], F32))
            rt = ph.enter_context(nc.sbuf_tensor("rt", [128, TL], BF16))
            ctx.emit("dve", lambda v: v.memset(onesT[:], 1.0), inc=s_dve)

            s_gp1 = ctx.sem("s_gp1")
            gscr = ph.enter_context(nc.sbuf_tensor("gscr", [128, TL], F32))
            for c in range(DC):
                xn_c = xn[:, c * TL:(c + 1) * TL]
                nd0 = ctx.emit("dve", lambda v, xn_c=xn_c: v.tensor_tensor(
                    dtile[:, 1:TL], xn_c[:, 1:TL], xn_c[:, 0:TL - 1], op=ALU.subtract),
                    waits=_now(s_dve) + [(s_gp1, s_gp1.n)], inc=s_dve)
                bsl = dtile[:].rearrange("p (j r) -> p j r", r=128)[:, :, 0:1].squeeze(2)
                xcol = xn_c.rearrange("p (j r) -> p j r", r=128)[:, :, 0:1].squeeze(2)
                nd1 = ctx.emit("dve", lambda v, c=c, bsl=bsl, xcol=xcol: v.tensor_tensor(
                    bsl, xcol, xnbf[:, c * NB:(c + 1) * NB], op=ALU.subtract),
                    waits=_now(s_dve), inc=s_dve)
                for j, dst in enumerate((xk, xv, xr)):
                    ctx.emit("dve", lambda v, c=c, j=j, dst=dst, xn_c=xn_c: v.scalar_tensor_tensor(
                        dst[:, c * TL:(c + 1) * TL], dtile[:, 0:TL], tmm[:, j * DC + c:j * DC + c + 1],
                        xn_c, op0=ALU.mult, op1=ALU.add),
                        waits=_now(s_dve), inc=s_dve)
            ctx.barrier()

            wrb = [warena[:, 0:2048], warena[:, 2048:4096]]
            r_marks = []
            for c in range(DC):
                b = c % 2
                wts = [(s_tn, r_marks[c - 2])] if c >= 2 else []
                nw = ctx.emit("gp", lambda g, c=c, b=b: g.dma_start(out=wrb[b][:], in_=wr[c, :, :]),
                              waits=wts, inc=dws[b], inck=16)
                mlast = None
                for th in range(2):
                    pr = P[th]
                    for dc in range(DC):
                        mlast = ctx.emit("tn", lambda t, b=b, dc=dc, th=th, pr=pr: t.matmul(
                            pr[:, :], wrb[b][:, dc * 128:(dc + 1) * 128],
                            xr[:, dc * TL + H2[th].start:dc * TL + H2[th].stop],
                            start=(dc == 0), stop=(dc == DC - 1)),
                            waits=([(dws[b], nw)] + _now(s_act) if dc == 0 else []),
                            inc=(s_tn if dc == DC - 1 else None))
                    ctx.emit("act", lambda a, th=th, pr=pr: a.activation(rt[:, H2[th]], pr[:, :], AF.Sigmoid),
                             waits=[(s_tn, mlast), (d_st, d_st.n)], inc=s_act)
                r_marks.append(mlast)
                ctx.emit("sy", lambda s, c=c: s.dma_start(out=r_out[c, :, :], in_=rt[:]),
                         waits=_now(s_act), inc=d_st, inck=16)
            ctx.barrier()

            wkb = [warena[:, 0:2048], warena[:, 2048:4096]]
            wvb = [warena[:, 4096:6144], warena[:, 6144:8192]]
            kv_marks = []
            kv_a = [0, 0]
            kv_d = [0, 0]
            for c in range(DC):
                b = c % 2
                wts = [(s_tn, kv_marks[c - 2])] if c >= 2 else []
                ctx.emit("gp", lambda g, c=c, b=b: g.dma_start(out=wkb[b][:], in_=wk[c, :, :]),
                         waits=wts, inc=dws[b], inck=16)
                nv = ctx.emit("gp", lambda g, c=c, b=b: g.dma_start(out=wvb[b][:], in_=wv[c, :, :]),
                              inc=dws[b], inck=16)
                mlast = None
                for th in range(2):
                    pk, pv = P[2 * th], P[2 * th + 1]
                    pw = [(dws[b], nv)]
                    if kv_a[th]:
                        pw.append((s_act, kv_a[th]))
                    if kv_d[th]:
                        pw.append((s_dve, kv_d[th]))
                    for dc in range(DC):
                        ctx.emit("tn", lambda t, b=b, dc=dc, th=th, pk=pk: t.matmul(
                            pk[:, :], wkb[b][:, dc * 128:(dc + 1) * 128],
                            xk[:, dc * TL + H2[th].start:dc * TL + H2[th].stop],
                            start=(dc == 0), stop=(dc == DC - 1)),
                            waits=(pw if dc == 0 else []))
                    for dc in range(DC):
                        mlast = ctx.emit("tn", lambda t, b=b, dc=dc, th=th, pv=pv: t.matmul(
                            pv[:, :], wvb[b][:, dc * 128:(dc + 1) * 128],
                            xv[:, dc * TL + H2[th].start:dc * TL + H2[th].stop],
                            start=(dc == 0), stop=(dc == DC - 1)),
                            inc=(s_tn if dc == DC - 1 else None))
                    na = ctx.emit("act", lambda a, th=th, pk=pk: a.activation(ct[:, H2[th]], pk[:, :], AF.Exp),
                                  waits=[(s_tn, mlast)] + _now(s_dve), inc=s_act)
                    kv_a[th] = na
                    kv_d[th] = ctx.emit("dve", lambda v, th=th, pv=pv: v.tensor_tensor(
                        cvt[:, H2[th]], ct[:, H2[th]], pv[:, :], op=ALU.mult),
                        waits=[(s_act, na)] + _now(s_dve), inc=s_dve)
                kv_marks.append(mlast)
                ctx.emit("dve", lambda v: v.tensor_tensor_scan(at[:], onesT[:], cvt[:], 0.0, op0=ALU.mult, op1=ALU.add),
                         waits=_now(s_dve) + [(d_st, d_st.n)], inc=s_dve)
                ctx.emit("dve", lambda v: v.tensor_tensor_scan(bt[:], onesT[:], ct[:], 0.0, op0=ALU.mult, op1=ALU.add),
                         waits=_now(s_dve), inc=s_dve)
                ctx.emit("sy", lambda s, c=c: s.dma_start(out=a_out[c, :, :], in_=at[:]),
                         waits=_now(s_dve), inc=d_st, inck=16)
                ctx.emit("sy", lambda s, c=c: s.dma_start(out=b_out[c, :, :], in_=bt[:]),
                         waits=_now(s_dve), inc=d_st, inck=16)

        for e in ENGINES:
            ctx.ops[e].append(([(d_st.h, d_st.n), (d_ac.h, d_ac.n)], None, None))
        ctx.finalize()
    return nc



def build_launch2():
    nc = bass.Bass("TRN2", target_bir_lowering=False, debug=False)
    ein = lambda n, sh, dt: nc.dram_tensor(n, sh, dt, kind="ExternalInput").ap()
    eout = lambda n, sh, dt: nc.dram_tensor(n, sh, dt, kind="ExternalOutput").ap()
    xfm = ein("xfm", [DC, 128, TL], F32)
    a_in = ein("a_in", [DC, 128, TL], F32)
    b_in = ein("b_in", [DC, 128, TL], F32)
    r_in = ein("r_in", [DC, 128, TL], BF16)
    g0_in = ein("g0_in", [DC, 128, TL], F32)
    epta = ein("epta", [DC, 128, 64], F32)
    eptb = ein("eptb", [DC, 128, 64], F32)
    offw = ein("offw", [NB, 128, 64], F32)
    tf = ein("tf", [128, DC], F32)
    lng = ein("lng", [128, DC], F32)
    lnb = ein("lnb", [128, DC], F32)
    wo = ein("wo", [DC, 128, D], F32)
    wks = ein("wks", [128, D], F32)
    wkg = ein("wkg", [128, D], F32)
    wv1 = ein("wv1", [2, 128, D], F32)
    swp = ein("swp", [128, 128], F32)
    ident = ein("ident", [128, 128], F32)
    cosk = ein("cosk", [128, TL], F32)
    sink = ein("sink", [128, TL], F32)
    poolm = ein("poolm", [128, 2], F32)
    sw1 = ein("sw1", [2, 11, 128, D], F32)
    sw2 = ein("sw2", [2, DC, 128, DFF1], F32)
    rw1 = ein("rw1", [16, 11, 128, D], F32)
    rw2 = ein("rw2", [16, DC, 128, DFF1], F32)
    b1all = ein("b1all", [18, 128, 11], F32)
    b2t = ein("b2t", [DC, 18, 128], F32)
    cent = ein("cent", [128, DC * 16], F32)
    cbias = ein("cbias", [128, 16], F32)
    x1o = eout("x1o", [DC, 128, TL], F32)
    xn1o = eout("xn1o", [DC, 128, TL], BF16)
    g1o = eout("g1o", [DC, 128, TL], F32)
    ks2o = eout("ks2o", [128, 14], F32)
    kg2o = eout("kg2o", [128, 14], F32)
    v2o = eout("v2o", [14, 256], F32)
    ksro = eout("ksro", [128, 128], F32)
    kgro = eout("kgro", [128, 128], F32)
    vro = eout("vro", [128, 256], F32)
    macc = nc.dram_tensor("macc", [DC, 128, TL], F32).ap()

    GROUPS = [[0, 1, 2], [3, 4, 5], [6, 7, 8], [9, 10, 11], [12, 13, 14], [15, 16, 17]]

    es = contextlib.ExitStack()
    with es:
        ctx = Ctx(nc, es)
        sb = lambda n, sh, dt: es.enter_context(nc.sbuf_tensor(n, sh, dt))
        xn1 = sb("xn1", [128, DC * TL], BF16)
        ones1 = sb("ones1", [128, 1], F32)
        onesrow = sb("onesrow", [1, 128], F32)
        epsc = sb("epsc", [128, 1], F32)
        identb = sb("identb", [128, 128], BF16)
        swpb = sb("swpb", [128, 128], BF16)
        euc = sb("euc", [128, 4 * DC], F32)
        poolt = sb("poolt", [128, 2], BF16)
        cbt = sb("cbt", [128, 16], F32)
        G18 = sb("G18", [18, TL], BF16)
        onesrow_bf = sb("onesrow_bf", [1, 128], BF16)
        ones1b = sb("ones1b", [128, 1], BF16)
        onestl2 = sb("onestl2", [2, TL], BF16)
        identf = sb("identf", [128, 128], F32)
        centb = sb("centb", [128, DC * 16], BF16)
        sq = sb("sq", [128, TL], F32)
        abuf = [sb(f"ab{i}", [128, TL], F32) for i in range(2)]
        bbuf = [sb(f"bb{i}", [128, TL], F32) for i in range(2)]
        P = [es.enter_context(nc.psum_tensor(f"pb{i}", [128, 512], F32)) for i in range(8)]

        s_tn, s_act, s_dve = ctx.sem("s_tn"), ctx.sem("s_act"), ctx.sem("s_dve")
        d_cn = ctx.sem("d_cn", dma=True)
        d_ld = ctx.sem("d_ld", dma=True)
        d_l2 = ctx.sem("d_l2", dma=True)
        d_st = ctx.sem("d_st", dma=True)
        d_g0, d_g1 = ctx.sem("d_g0", dma=True), ctx.sem("d_g1", dma=True)
        d_gs = ctx.sem("d_gs", dma=True)
        d_ac = ctx.sem("d_ac", dma=True)
        dws = [d_g0, d_g1]
        H2 = (slice(0, 512), slice(512, 1024))

        def bcast_row(row_ap, dst, bank, wait_sems):
            for h in range(2):
                sl = H2[h]
                n = ctx.emit("tn", lambda t, sl=sl: t.matmul(
                    P[bank][:, :], onesrow[:], row_ap[:, sl], start=True, stop=True),
                    waits=list(wait_sems) + _now(s_dve), inc=s_tn)
                ctx.emit("dve", lambda v, sl=sl: v.tensor_copy(dst[:, sl], P[bank][:, :]),
                         waits=[(s_tn, n)], inc=s_dve)

        ctx.emit("dve", lambda v: v.memset(ones1[:], 1.0), inc=s_dve)
        ctx.emit("dve", lambda v: v.memset(onesrow[:], 1.0), inc=s_dve)
        ctx.emit("dve", lambda v: v.memset(epsc[:], LN_EPS), inc=s_dve)
        ctx.emit("dve", lambda v: v.memset(onesrow_bf[:], 1.0), inc=s_dve)
        ctx.emit("dve", lambda v: v.memset(ones1b[:], 1.0), inc=s_dve)
        ctx.emit("dve", lambda v: v.memset(onestl2[:], 1.0), inc=s_dve)
        n_const = s_dve.n
        ctx.emit("sy", lambda s: s.dma_start(out=euc[:, 0:DC], in_=tf[:, :]), inc=d_cn, inck=16)
        ctx.emit("sy", lambda s: s.dma_start(out=euc[:, 2 * DC:3 * DC], in_=lng[:, :]), inc=d_cn, inck=16)
        ctx.emit("sy", lambda s: s.dma_start(out=euc[:, 3 * DC:4 * DC], in_=lnb[:, :]), inc=d_cn, inck=16)
        ctx.emit("sy", lambda s: s.dma_start(out=cbt[:], in_=cbias[:, :]), inc=d_cn, inck=16)
        ctx.emit("gp", lambda g: g.dma_start(out=identb[:], in_=ident[:, :]), inc=d_gs, inck=16)
        ctx.emit("gp", lambda g: g.dma_start(out=centb[:], in_=cent[:, :]), inc=d_gs, inck=16)
        ctx.emit("sy", lambda s: s.dma_start(out=identf[:], in_=ident[:, :]), inc=d_cn, inck=16)
        ctx.emit("gp", lambda g: g.dma_start(out=swpb[:], in_=swp[:, :]), inc=d_gs, inck=16)
        ctx.emit("gp", lambda g: g.dma_start(out=poolt[:], in_=poolm[:, :]), inc=d_gs, inck=16)
        ctx.emit("act", lambda a: a.activation(euc[:, 0:DC], euc[:, 0:DC], AF.Exp),
                 waits=[(d_cn, 16)], inc=s_act)
        ctx.emit("dve", lambda v: v.tensor_scalar(euc[:, DC:2 * DC], euc[:, 0:DC], -1.0, None, op0=ALU.mult),
                 waits=_now(s_act), inc=s_dve)
        ctx.emit("dve", lambda v: v.tensor_scalar(euc[:, DC:2 * DC], euc[:, DC:2 * DC], 1.0, None, op0=ALU.add),
                 waits=_now(s_dve), inc=s_dve)

        with contextlib.ExitStack() as ph:
            vr = ph.enter_context(nc.sbuf_tensor("vr", [1, 7 * TL], F32))
            mubc = ph.enter_context(nc.sbuf_tensor("mubc", [128, TL], F32))
            rbc = ph.enter_context(nc.sbuf_tensor("rbc", [128, TL], F32))
            eptA = ph.enter_context(nc.sbuf_tensor("eptA", [128, DC * 64], F32))
            eptB = ph.enter_context(nc.sbuf_tensor("eptB", [128, DC * 64], F32))
            offwt = ph.enter_context(nc.sbuf_tensor("offwt", [128, NB * 64], F32))
            rwkvb = ph.enter_context(nc.sbuf_tensor("rwkvb", [128, DC * TL], BF16))
            wkvs = ph.enter_context(nc.sbuf_tensor("wkvs", [128, DC * TL], BF16))
            rbuf = ph.enter_context(nc.sbuf_tensor("rbuf", [128, TL], BF16))
            numt = ph.enter_context(nc.sbuf_tensor("numt", [128, TL], F32))
            dent = ph.enter_context(nc.sbuf_tensor("dent", [128, TL], F32))
            offc = ph.enter_context(nc.sbuf_tensor("offc", [128, 2 * NB], F32))
            tmpo = ph.enter_context(nc.sbuf_tensor("tmpo", [128, 512], F32))
            V = lambda i: vr[0:1, i * TL:(i + 1) * TL]
            Vh = lambda i, h: vr[0:1, i * TL + H2[h].start:i * TL + H2[h].stop]
            for c in range(DC):
                ctx.emit("sy", lambda s, c=c: s.dma_start(out=eptA[:, c * 64:(c + 1) * 64], in_=epta[c, :, :]), inc=d_cn, inck=16)
                ctx.emit("sy", lambda s, c=c: s.dma_start(out=eptB[:, c * 64:(c + 1) * 64], in_=eptb[c, :, :]), inc=d_cn, inck=16)
            for j in range(NB):
                ctx.emit("sy", lambda s, j=j: s.dma_start(out=offwt[:, j * 64:(j + 1) * 64], in_=offw[j, :, :]), inc=d_cn, inck=16)
            wk_marks = []
            for c in range(DC):
                ba = abuf[c % 2]
                bb = bbuf[c % 2]
                wts = ([(s_dve, wk_marks[c - 2])] if c >= 2 else [])
                na = ctx.emit("sy", lambda s, c=c, ba=ba: s.dma_start(out=ba[:], in_=a_in[c, :, :]),
                              waits=wts, inc=d_ld, inck=16)
                nb = ctx.emit("sy", lambda s, c=c, bb=bb: s.dma_start(out=bb[:], in_=b_in[c, :, :]),
                              inc=d_ld, inck=16)
                ow3 = offwt[:].rearrange("p (j s) -> p j s", s=64)
                ctx.emit("dve", lambda v, c=c, ow3=ow3: v.tensor_tensor(
                    tmpo[:].rearrange("p (j s) -> p j s", s=64),
                    eptA[:, c * 64:(c + 1) * 64].unsqueeze(1).broadcast_to([128, NB, 64]),
                    ow3, op=ALU.mult),
                    waits=[(d_cn, d_cn.n)] + _now(s_dve), inc=s_dve)
                ctx.emit("dve", lambda v: v.tensor_reduce(
                    offc[:, 0:NB], tmpo[:].rearrange("p (j s) -> p j s", s=64), axis=AX.X, op=ALU.add),
                    waits=_now(s_dve), inc=s_dve)
                ctx.emit("dve", lambda v, c=c, ow3=ow3: v.tensor_tensor(
                    tmpo[:].rearrange("p (j s) -> p j s", s=64),
                    eptB[:, c * 64:(c + 1) * 64].unsqueeze(1).broadcast_to([128, NB, 64]),
                    ow3, op=ALU.mult),
                    waits=_now(s_dve), inc=s_dve)
                ctx.emit("dve", lambda v: v.tensor_reduce(
                    offc[:, NB:2 * NB], tmpo[:].rearrange("p (j s) -> p j s", s=64), axis=AX.X, op=ALU.add),
                    waits=_now(s_dve), inc=s_dve)
                eu_c = euc[:, c:c + 1]
                ieu_c = euc[:, DC + c:DC + c + 1]
                ctx.emit("dve", lambda v, ba=ba, eu_c=eu_c: v.tensor_scalar(
                    numt[:], ba[:], eu_c, None, op0=ALU.mult),
                    waits=[(d_ld, na)] + _now(s_dve, s_act), inc=s_dve)
                ctx.emit("dve", lambda v, ba=ba, ieu_c=ieu_c: v.scalar_tensor_tensor(
                    numt[:, 1:TL], ba[:, 0:TL - 1], ieu_c, numt[:, 1:TL], op0=ALU.mult, op1=ALU.add),
                    waits=_now(s_dve), inc=s_dve)
                ctx.emit("dve", lambda v, bb=bb, eu_c=eu_c: v.tensor_scalar(
                    dent[:], bb[:], eu_c, None, op0=ALU.mult),
                    waits=[(d_ld, nb)] + _now(s_dve), inc=s_dve)
                ctx.emit("dve", lambda v, bb=bb, ieu_c=ieu_c: v.scalar_tensor_tensor(
                    dent[:, 1:TL], bb[:, 0:TL - 1], ieu_c, dent[:, 1:TL], op0=ALU.mult, op1=ALU.add),
                    waits=_now(s_dve), inc=s_dve)
                offa3 = offc[:, 0:NB].unsqueeze(2).broadcast_to([128, NB, 128])
                offb3 = offc[:, NB:2 * NB].unsqueeze(2).broadcast_to([128, NB, 128])
                ctx.emit("dve", lambda v, offa3=offa3: v.tensor_tensor(
                    numt[:].rearrange("p (j r) -> p j r", r=128), numt[:].rearrange("p (j r) -> p j r", r=128),
                    offa3, op=ALU.add), waits=_now(s_dve), inc=s_dve)
                ctx.emit("dve", lambda v, offb3=offb3: v.tensor_tensor(
                    dent[:].rearrange("p (j r) -> p j r", r=128), dent[:].rearrange("p (j r) -> p j r", r=128),
                    offb3, op=ALU.add), waits=_now(s_dve), inc=s_dve)
                ctx.emit("act", lambda a: a.activation(sq[:], dent[:], AF.Ln),
                         waits=_now(s_dve, s_tn), inc=s_act)
                ctx.emit("act", lambda a: a.activation(sq[:], sq[:], AF.Exp, scale=-1.0),
                         waits=_now(s_act), inc=s_act)
                mw = ctx.emit("dve", lambda v, c=c: v.tensor_tensor(
                    wkvs[:, c * TL:(c + 1) * TL], numt[:], sq[:], op=ALU.mult),
                    waits=_now(s_act), inc=s_dve)
                wk_marks.append(mw)
                na2 = ctx.emit("act", lambda a, c=c: a.activation(sq[:], wkvs[:, c * TL:(c + 1) * TL], AF.Square),
                               waits=[(s_dve, mw)] + _now(s_tn), inc=s_act)
                for h in range(2):
                    ctx.emit("tn", lambda t, c=c, h=h: t.matmul(
                        P[h][0:1, :], ones1b[:], wkvs[:, c * TL + H2[h].start:c * TL + H2[h].stop],
                        start=(c == 0), stop=(c == DC - 1)),
                        waits=[(s_act, na2)] + ([(s_dve, n_const)] if c == 0 else []))
                    ctx.emit("tn", lambda t, c=c, h=h: t.matmul(
                        P[2 + h][0:1, :], ones1[:], sq[:, H2[h]],
                        start=(c == 0), stop=(c == DC - 1)),
                        waits=[(s_act, na2)], inc=(s_tn if h == 1 else None))
            for h in range(2):
                ctx.emit("dve", lambda v, h=h: v.tensor_copy(Vh(0, h), P[h][0:1, :]),
                         waits=_now(s_tn), inc=s_dve)
                ctx.emit("dve", lambda v, h=h: v.tensor_copy(Vh(1, h), P[2 + h][0:1, :]),
                         waits=_now(s_tn), inc=s_dve)
            ctx.emit("dve", lambda v: v.tensor_scalar(V(2), V(0), 1.0 / D, None, op0=ALU.mult),
                     waits=_now(s_dve), inc=s_dve)
            ctx.emit("dve", lambda v: v.tensor_scalar(V(1), V(1), 1.0 / D, None, op0=ALU.mult),
                     waits=_now(s_dve), inc=s_dve)
            ctx.emit("dve", lambda v: v.tensor_tensor(V(4), V(2), V(2), op=ALU.mult),
                     waits=_now(s_dve), inc=s_dve)
            ctx.emit("dve", lambda v: v.tensor_tensor(V(5), V(1), V(4), op=ALU.subtract),
                     waits=_now(s_dve), inc=s_dve)
            ctx.emit("act", lambda a: a.activation(V(6), V(5), AF.Ln, bias=epsc[0:1, 0:1]),
                     waits=_now(s_dve), inc=s_act)
            ctx.emit("act", lambda a: a.activation(V(3), V(6), AF.Exp, scale=-0.5),
                     waits=_now(s_act), inc=s_act)
            bcast_row(V(2), mubc, 4, _now(s_act))
            bcast_row(V(3), rbc, 4, _now(s_act))

            rbgf = ph.enter_context(nc.sbuf_tensor("rbgf", [128, TL], BF16))
            for c in range(DC):
                rbq = rbuf if c % 2 == 0 else rbgf
                dsem = d_ld if c % 2 == 0 else d_l2
                nr = ctx.emit("sy", lambda s, c=c, rbq=rbq: s.dma_start(out=rbq[:], in_=r_in[c, :, :]),
                              waits=(_now(s_dve) if c >= 2 else []), inc=dsem, inck=16)
                ctx.emit("dve", lambda v, c=c: v.tensor_tensor(
                    sq[:], wkvs[:, c * TL:(c + 1) * TL], mubc[:], op=ALU.subtract),
                    waits=_now(s_dve), inc=s_dve)
                ctx.emit("dve", lambda v: v.tensor_tensor(sq[:], sq[:], rbc[:], op=ALU.mult),
                         waits=_now(s_dve), inc=s_dve)
                ctx.emit("dve", lambda v, c=c: v.tensor_scalar(
                    sq[:], sq[:], euc[:, 2 * DC + c:2 * DC + c + 1], euc[:, 3 * DC + c:3 * DC + c + 1],
                    op0=ALU.mult, op1=ALU.add),
                    waits=_now(s_dve), inc=s_dve)
                ctx.emit("dve", lambda v, c=c, rbq=rbq: v.tensor_tensor(
                    rwkvb[:, c * TL:(c + 1) * TL], sq[:], rbq[:], op=ALU.mult),
                    waits=_now(s_dve) + [(dsem, nr)], inc=s_dve)
            ctx.barrier()
            warena = ph.enter_context(nc.sbuf_tensor("warena", [128, 4096], BF16))
            wob = [warena[:, 0:2048], warena[:, 2048:4096]]
            x1t = ph.enter_context(nc.sbuf_tensor("x1t", [128, TL], F32))
            gbuf = ph.enter_context(nc.sbuf_tensor("gbuf", [128, TL], F32))
            s0_marks = []
            for co in range(DC):
                b = co % 2
                wts = [(s_tn, s0_marks[co - 2])] if co >= 2 else []
                nw = ctx.emit("gp", lambda g, co=co, b=b: g.dma_start(out=wob[b][:], in_=wo[co, :, :]),
                              waits=wts, inc=dws[b], inck=16)
                nx = ctx.emit("sy", lambda s, co=co: s.dma_start(out=abuf[0][:], in_=xfm[co, :, :]),
                              waits=(_now(s_dve) if co > 0 else []), inc=d_ld, inck=16)
                ng = ctx.emit("sy", lambda s, co=co: s.dma_start(out=gbuf[:], in_=g0_in[co, :, :]),
                              waits=(_now(s_dve) if co > 0 else []), inc=d_l2, inck=16)
                mlast = None
                for th in range(2):
                    po = P[4 + th]
                    for dc in range(DC):
                        mlast = ctx.emit("tn", lambda t, b=b, dc=dc, th=th, po=po: t.matmul(
                            po[:, :], wob[b][:, dc * 128:(dc + 1) * 128],
                            rwkvb[:, dc * TL + H2[th].start:dc * TL + H2[th].stop],
                            start=(dc == 0), stop=(dc == DC - 1)),
                            waits=([(dws[b], nw)] + _now(s_dve) if dc == 0 else []),
                            inc=(s_tn if dc == DC - 1 else None))
                    ctx.emit("dve", lambda v, th=th, po=po: v.tensor_tensor(
                        x1t[:, H2[th]], po[:, :], gbuf[:, H2[th]], op=ALU.mult),
                        waits=[(s_tn, mlast), (d_l2, ng), (d_ld, nx), (d_st, d_st.n)] + _now(s_act), inc=s_dve)
                    ctx.emit("dve", lambda v, th=th: v.tensor_tensor(
                        x1t[:, H2[th]], x1t[:, H2[th]], abuf[0][:, H2[th]], op=ALU.add),
                        waits=_now(s_dve), inc=s_dve)
                s0_marks.append(mlast)
                ctx.emit("sy", lambda s, co=co: s.dma_start(out=x1o[co, :, :], in_=x1t[:]),
                         waits=_now(s_dve), inc=d_st, inck=16)
                na2 = ctx.emit("act", lambda a: a.activation(sq[:], x1t[:], AF.Square),
                               waits=_now(s_dve, s_tn), inc=s_act)
                for h in range(2):
                    ctx.emit("tn", lambda t, co=co, h=h: t.matmul(
                        P[h][0:1, :], ones1[:], x1t[:, H2[h]], start=(co == 0), stop=(co == DC - 1)),
                        waits=[(s_act, na2)] + ([(s_dve, n_const)] if co == 0 else []))
                    ctx.emit("tn", lambda t, co=co, h=h: t.matmul(
                        P[2 + h][0:1, :], ones1[:], sq[:, H2[h]], start=(co == 0), stop=(co == DC - 1)),
                        waits=[(s_act, na2)], inc=(s_tn if h == 1 else None))
            for h in range(2):
                ctx.emit("dve", lambda v, h=h: v.tensor_copy(Vh(0, h), P[h][0:1, :]),
                         waits=_now(s_tn), inc=s_dve)
                ctx.emit("dve", lambda v, h=h: v.tensor_copy(Vh(1, h), P[2 + h][0:1, :]),
                         waits=_now(s_tn), inc=s_dve)
            ctx.emit("dve", lambda v: v.tensor_scalar(V(2), V(0), 1.0 / D, None, op0=ALU.mult),
                     waits=_now(s_dve), inc=s_dve)
            ctx.emit("dve", lambda v: v.tensor_scalar(V(1), V(1), 1.0 / D, None, op0=ALU.mult),
                     waits=_now(s_dve), inc=s_dve)
            ctx.emit("dve", lambda v: v.tensor_tensor(V(4), V(2), V(2), op=ALU.mult),
                     waits=_now(s_dve), inc=s_dve)
            ctx.emit("dve", lambda v: v.tensor_tensor(V(5), V(1), V(4), op=ALU.subtract),
                     waits=_now(s_dve), inc=s_dve)
            ctx.emit("act", lambda a: a.activation(V(6), V(5), AF.Ln, bias=epsc[0:1, 0:1]),
                     waits=_now(s_dve), inc=s_act)
            ctx.emit("act", lambda a: a.activation(V(3), V(6), AF.Exp, scale=-0.5),
                     waits=_now(s_act), inc=s_act)
            bcast_row(V(2), mubc, 4, _now(s_act))
            bcast_row(V(3), rbc, 4, _now(s_act))
            ctx.barrier()
            xa_marks = {}
            for c in range(DC):
                buf = abuf[c % 2]
                wts = [(s_dve, xa_marks[c - 2])] if c >= 2 else []
                n = ctx.emit("sy", lambda s, c=c, buf=buf: s.dma_start(out=buf[:], in_=x1o[c, :, :]),
                             waits=wts, inc=d_ld, inck=16)
                ctx.emit("dve", lambda v, buf=buf: v.tensor_tensor(sq[:], buf[:], mubc[:], op=ALU.subtract),
                         waits=[(d_ld, n)] + _now(s_dve), inc=s_dve)
                m = ctx.emit("dve", lambda v, c=c: v.tensor_tensor(
                    xn1[:, c * TL:(c + 1) * TL], sq[:], rbc[:], op=ALU.mult),
                    waits=_now(s_dve), inc=s_dve)
                xa_marks[c] = m
                ctx.emit("sy", lambda s, c=c: s.dma_start(
                    out=xn1o[c, :, :], in_=xn1[:, c * TL:(c + 1) * TL]),
                    waits=[(s_dve, m)], inc=d_st, inck=16)
            ctx.barrier()
        with contextlib.ExitStack() as ph:
            cost = ph.enter_context(nc.sbuf_tensor("cost", [128, TL], F32))
            sint = ph.enter_context(nc.sbuf_tensor("sint", [128, TL], F32))
            wsl = ph.enter_context(nc.sbuf_tensor("wsl", [128, 2048], BF16))
            ksf = ph.enter_context(nc.sbuf_tensor("ksf", [128, TL], F32))
            kgf = ph.enter_context(nc.sbuf_tensor("kgf", [128, TL], F32))
            kgb = ph.enter_context(nc.sbuf_tensor("kgb", [128, TL], BF16))
            kgr = ph.enter_context(nc.sbuf_tensor("kgr", [128, TL], F32))
            p16 = ph.enter_context(nc.sbuf_tensor("p16", [128, 16], F32))
            vtm = ph.enter_context(nc.sbuf_tensor("vtm", [128, 256], F32))
            vtmb = ph.enter_context(nc.sbuf_tensor("vtmb", [128, 256], BF16))
            v2s = ph.enter_context(nc.sbuf_tensor("v2s", [2, 256], F32))
            ctx.emit("sy", lambda s: s.dma_start(out=cost[:], in_=cosk[:, :]), inc=d_cn, inck=16)
            ctx.emit("sy", lambda s: s.dma_start(out=sint[:], in_=sink[:, :]), inc=d_cn, inck=16)
            nw = ctx.emit("gp", lambda g: g.dma_start(out=wsl[:], in_=wks[:, :]), inc=d_g0, inck=16)
            ml = None
            for th in range(2):
                for dc in range(DC):
                    ml = ctx.emit("tn", lambda t, dc=dc, th=th: t.matmul(
                        P[th][:, :], wsl[:, dc * 128:(dc + 1) * 128],
                        xn1[:, dc * TL + H2[th].start:dc * TL + H2[th].stop],
                        start=(dc == 0), stop=(dc == DC - 1)),
                        waits=([(d_g0, nw)] if dc == 0 else []), inc=(s_tn if dc == DC - 1 else None))
                ctx.emit("dve", lambda v, th=th: v.tensor_copy(ksf[:, H2[th]], P[th][:, :]),
                         waits=[(s_tn, ml)], inc=s_dve)
            ctx.emit("dve", lambda v: v.tensor_reduce(
                p16[:], ksf[:].rearrange("p (j r) -> p j r", r=64), axis=AX.X, op=ALU.add),
                waits=_now(s_dve), inc=s_dve)
            ctx.emit("dve", lambda v: v.tensor_scalar(p16[:], p16[:], 1.0 / 64, None, op0=ALU.mult),
                     waits=_now(s_dve), inc=s_dve)
            ctx.emit("sy", lambda s: s.dma_start(out=ks2o[:, :], in_=p16[:, 0:14]),
                     waits=_now(s_dve), inc=d_st, inck=16)
            ctx.emit("sy", lambda s: s.dma_start(out=ksro[:, :], in_=ksf[:, 896:1024]),
                     waits=_now(s_dve), inc=d_st, inck=16)
            nw = ctx.emit("gp", lambda g: g.dma_start(out=wsl[:], in_=wkg[:, :]),
                          waits=_now(s_tn), inc=d_g0, inck=16)
            for th in range(2):
                for dc in range(DC):
                    ml = ctx.emit("tn", lambda t, dc=dc, th=th: t.matmul(
                        P[th][:, :], wsl[:, dc * 128:(dc + 1) * 128],
                        xn1[:, dc * TL + H2[th].start:dc * TL + H2[th].stop],
                        start=(dc == 0), stop=(dc == DC - 1)),
                        waits=([(d_g0, nw), (d_st, d_st.n)] + _now(s_dve) if dc == 0 else []),
                        inc=(s_tn if dc == DC - 1 else None))
                ctx.emit("dve", lambda v, th=th: v.tensor_copy(kgf[:, H2[th]], P[th][:, :]),
                         waits=[(s_tn, ml)], inc=s_dve)
                ctx.emit("dve", lambda v, th=th: v.tensor_copy(kgb[:, H2[th]], kgf[:, H2[th]]),
                         waits=_now(s_dve), inc=s_dve)
                ns = ctx.emit("tn", lambda t, th=th: t.matmul(
                    P[2 + th][:, :], swpb[:], kgb[:, H2[th]], start=True, stop=True),
                    waits=_now(s_dve) + [(d_gs, d_gs.n)], inc=s_tn)
                ctx.emit("dve", lambda v, th=th: v.tensor_tensor(
                    kgr[:, H2[th]], kgf[:, H2[th]], cost[:, H2[th]], op=ALU.mult),
                    waits=[(s_tn, ns), (d_cn, d_cn.n)], inc=s_dve)
                ctx.emit("dve", lambda v, th=th: v.tensor_tensor(
                    sq[:, H2[th]], P[2 + th][:, :], sint[:, H2[th]], op=ALU.mult),
                    waits=[(s_tn, ns)], inc=s_dve)
                ctx.emit("dve", lambda v, th=th: v.tensor_tensor(
                    kgr[:, H2[th]], kgr[:, H2[th]], sq[:, H2[th]], op=ALU.add),
                    waits=_now(s_dve), inc=s_dve)
            ctx.emit("dve", lambda v: v.tensor_reduce(
                p16[:], kgr[:].rearrange("p (j r) -> p j r", r=64), axis=AX.X, op=ALU.add),
                waits=_now(s_dve), inc=s_dve)
            ctx.emit("dve", lambda v: v.tensor_scalar(p16[:], p16[:], 1.0 / 64, None, op0=ALU.mult),
                     waits=_now(s_dve), inc=s_dve)
            ctx.emit("sy", lambda s: s.dma_start(out=kg2o[:, :], in_=p16[:, 0:14]),
                     waits=_now(s_dve), inc=d_st, inck=16)
            ctx.emit("sy", lambda s: s.dma_start(out=kgro[:, :], in_=kgr[:, 896:1024]),
                     waits=_now(s_dve), inc=d_st, inck=16)
            nw = ctx.emit("gp", lambda g: g.dma_start(out=wsl[:, 0:2048], in_=wv1[0, :, :]),
                          waits=_now(s_tn), inc=d_g0, inck=16)
            nw2 = ctx.emit("gp", lambda g: g.dma_start(out=kgf[:].bitcast(BF16)[:, 0:2048], in_=wv1[1, :, :]),
                           waits=_now(s_dve, s_tn) + [(d_st, d_st.n)], inc=d_g1, inck=16)
            for tc in range(8):
                ml = None
                for oc in range(2):
                    wv_sl = wsl if oc == 0 else kgf.bitcast(BF16)[:, 0:2048]
                    for dc in range(DC):
                        ml = ctx.emit("tn", lambda t, tc=tc, oc=oc, dc=dc, wv_sl=wv_sl: t.matmul(
                            P[4 + oc][:, 0:128], xn1[:, dc * TL + tc * 128:dc * TL + (tc + 1) * 128],
                            wv_sl[:, dc * 128:(dc + 1) * 128],
                            start=(dc == 0), stop=(dc == DC - 1)),
                            waits=([(d_g0, nw), (d_g1, nw2)] + _now(s_dve) if dc == 0 else []),
                            inc=(s_tn if dc == DC - 1 else None))
                for oc in range(2):
                    ctx.emit("dve", lambda v, oc=oc: v.tensor_copy(vtm[:, oc * 128:(oc + 1) * 128], P[4 + oc][:, 0:128]),
                             waits=[(s_tn, ml)] + _now(s_dve), inc=s_dve)
                ctx.emit("dve", lambda v: v.tensor_copy(vtmb[:], vtm[:, 0:256]),
                         waits=_now(s_dve), inc=s_dve)
                if tc < 7:
                    np_ = ctx.emit("tn", lambda t: t.matmul(
                        P[6][0:2, 0:256], poolt[:], vtmb[:], start=True, stop=True),
                        waits=_now(s_dve) + [(d_gs, d_gs.n)], inc=s_tn)
                    ctx.emit("dve", lambda v: v.tensor_copy(v2s[:, :], P[6][0:2, 0:256]),
                             waits=[(s_tn, np_), (d_st, d_st.n)], inc=s_dve)
                    ctx.emit("sy", lambda s, tc=tc: s.dma_start(out=v2o[2 * tc:2 * tc + 2, :], in_=v2s[:, :]),
                             waits=_now(s_dve), inc=d_st, inck=16)
                else:
                    ctx.emit("sy", lambda s: s.dma_start(out=vro[:, :], in_=vtm[:, 0:256]),
                             waits=_now(s_dve), inc=d_st, inck=16)
            ctx.barrier()

        with contextlib.ExitStack() as ph:
            Gflat = ph.enter_context(nc.sbuf_tensor("Gflat", [1, 18 * TL], BF16))
            gm2 = ph.enter_context(nc.sbuf_tensor("gm2", [128, 16], F32))
            hg = ph.enter_context(nc.sbuf_tensor("hg", [128, 3 * 11 * TL], BF16))
            w1b = ph.enter_context(nc.sbuf_tensor("w1b", [128, 4096], BF16))
            w2b = ph.enter_context(nc.sbuf_tensor("w2b", [128, 2816 + 36], BF16))
            b1t = ph.enter_context(nc.sbuf_tensor("b1t", [128, 18 * 11], F32))
            hs1 = ph.enter_context(nc.sbuf_tensor("hs1", [128, 1024], F32))
            gbc = ph.enter_context(nc.sbuf_tensor("gbc", [128, TL], BF16))
            g16 = ph.enter_context(nc.sbuf_tensor("g16", [128, 16], F32))
            gm = ph.enter_context(nc.sbuf_tensor("gm", [128, 16], F32))
            gv = ph.enter_context(nc.sbuf_tensor("gv", [128, 3], F32))
            zst = ph.enter_context(nc.sbuf_tensor("zst2", [128, TL], F32))
            zstB = ph.enter_context(nc.sbuf_tensor("zst2b", [128, TL], F32))
            w1bb = [w1b[:, 0:2048], w1b[:, 2048:4096]]
            w2bb = [w2b[:, 0:1408], w2b[:, 1408:2816]]
            b2sb = ph.enter_context(nc.sbuf_tensor("b2sb", [18, 128], BF16))
            for e in range(18):
                ctx.emit("sy", lambda s, e=e: s.dma_start(out=b1t[:, e * 11:(e + 1) * 11], in_=b1all[e, :, :]),
                         inc=d_cn, inck=16)
            n_b1 = d_cn.n
            for tc in range(8):
                ml = None
                for dc in range(DC):
                    ml = ctx.emit("tn", lambda t, tc=tc, dc=dc: t.matmul(
                        P[6][:, 0:16], xn1[:, dc * TL + tc * 128:dc * TL + (tc + 1) * 128],
                        centb[:, dc * 16:(dc + 1) * 16],
                        start=(dc == 0), stop=(dc == DC - 1)),
                        waits=(_now(s_dve) + [(d_gs, d_gs.n)] if dc == 0 else []),
                        inc=(s_tn if dc == DC - 1 else None))
                na = ctx.emit("act", lambda a: a.activation(g16[:], P[6][:, 0:16], AF.Sigmoid),
                              waits=[(s_tn, ml)] + _now(s_dve), inc=s_act)
                ctx.emit("dve", lambda v: v.tensor_tensor(g16[:], g16[:], cbt[:], op=ALU.add),
                         waits=[(s_act, na), (d_cn, d_cn.n)], inc=s_dve)
                ctx.emit("dve", lambda v: v.tensor_copy(gm[:], g16[:]), waits=_now(s_dve), inc=s_dve)
                for it in range(4):
                    ctx.emit("dve", lambda v: v.reduce_max(gv[:, 0:1], gm[:], axis=AX.X),
                             waits=_now(s_dve), inc=s_dve)
                    if it < 3:
                        ctx.emit("dve", lambda v: v.tensor_scalar(gm2[:], gm[:], gv[:, 0:1], None, op0=ALU.is_ge),
                                 waits=_now(s_dve), inc=s_dve)
                        ctx.emit("dve", lambda v: v.scalar_tensor_tensor(gm[:], gm2[:], -1e30, gm[:], op0=ALU.mult, op1=ALU.add),
                                 waits=_now(s_dve), inc=s_dve)
                ctx.emit("dve", lambda v: v.tensor_scalar(gm[:], g16[:], gv[:, 0:1], None, op0=ALU.is_ge),
                         waits=_now(s_dve), inc=s_dve)
                ctx.emit("dve", lambda v: v.tensor_tensor(gm[:], gm[:], g16[:], op=ALU.mult),
                         waits=_now(s_dve), inc=s_dve)
                ctx.emit("dve", lambda v: v.tensor_reduce(gv[:, 1:2], gm[:], axis=AX.X, op=ALU.add),
                         waits=_now(s_dve), inc=s_dve)
                ctx.emit("dve", lambda v: v.tensor_scalar(gv[:, 1:2], gv[:, 1:2], 1e-8, None, op0=ALU.max),
                         waits=_now(s_dve), inc=s_dve)
                ctx.emit("dve", lambda v: v.reciprocal(gv[:, 2:3], gv[:, 1:2]),
                         waits=_now(s_dve), inc=s_dve)
                mg = ctx.emit("dve", lambda v: v.tensor_scalar(gm[:], gm[:], gv[:, 2:3], None, op0=ALU.mult),
                              waits=_now(s_dve), inc=s_dve)
                nt = ctx.emit("tn", lambda t: t.transpose(
                    P[7][0:16, 0:128], gm[:, 0:16], identf[:]),
                    waits=[(s_dve, mg), (d_cn, d_cn.n)] + _now(s_act), inc=s_tn)
                ctx.emit("dve", lambda v, tc=tc: v.tensor_copy(
                    G18[0:16, tc * 128:(tc + 1) * 128], P[7][0:16, 0:128]),
                    waits=[(s_tn, nt)], inc=s_dve)
            ctx.emit("sy", lambda s: s.dma_start(out=G18[16:18, :], in_=onestl2[:, :]),
                     waits=_now(s_dve), inc=d_cn, inck=16)
            n_g18 = ctx.emit("sy", lambda s: s.dma_start(
                out=Gflat[0:1, :].rearrange("o (e t) -> o e t", t=TL), in_=G18[:, :]),
                waits=[(d_cn, d_cn.n)] + _now(s_dve), inc=d_cn, inck=16)
            ctx.barrier()
            first_grp = True
            zbuf = [zst, zstB]
            h1_marks = []
            cons_h1 = [0, 0]
            cons_h1_d = [0, 0]
            for gi, grp in enumerate(GROUPS):
                for ei, e in enumerate(grp):
                    routed = e < 16
                    ngbc = 0
                    if routed:
                        for h in range(2):
                            nb_ = ctx.emit("tn", lambda t, e=e, h=h: t.matmul(
                                P[6][:, :], onesrow_bf[:], Gflat[0:1, e * TL + H2[h].start:e * TL + H2[h].stop],
                                start=True, stop=True),
                                waits=[(d_cn, n_g18)] + _now(s_dve), inc=s_tn)
                            ngbc = ctx.emit("dve", lambda v, h=h: v.tensor_copy(gbc[:, H2[h]], P[6][:, :]),
                                            waits=[(s_tn, nb_)], inc=s_dve)
                    for fc in range(11):
                        b = (gi * 33 + ei * 11 + fc) % 2
                        wts = [(s_tn, h1_marks[-2])] if len(h1_marks) >= 2 else []
                        nw = ctx.emit("gp", lambda g, e=e, fc=fc, b=b, routed=routed: g.dma_start(
                            out=w1bb[b][:], in_=(rw1[e, fc, :, :] if routed else sw1[e - 16, fc, :, :])),
                            waits=wts, inc=dws[b], inck=16)
                        ml = None
                        for th in range(2):
                            pw = [(dws[b], nw)]
                            if cons_h1[th]:
                                pw.append((s_act, cons_h1[th]))
                            if cons_h1_d[th]:
                                pw.append((s_dve, cons_h1_d[th]))
                            for dc in range(DC):
                                ml = ctx.emit("tn", lambda t, b=b, dc=dc, th=th: t.matmul(
                                    P[th][:, :], w1bb[b][:, dc * 128:(dc + 1) * 128],
                                    xn1[:, dc * TL + H2[th].start:dc * TL + H2[th].stop],
                                    start=(dc == 0), stop=(dc == DC - 1)),
                                    waits=(pw if dc == 0 else []),
                                    inc=(s_tn if dc == DC - 1 else None))
                            if routed:
                                na = ctx.emit("act", lambda a, th=th, e=e, fc=fc: a.activation(
                                    hs1[:, H2[th]], P[th][:, :], AF.Relu, bias=b1t[:, e * 11 + fc:e * 11 + fc + 1]),
                                    waits=[(s_tn, ml), (d_cn, n_b1)] + ([(s_dve, cons_h1_d[th])] if cons_h1_d[th] else []),
                                    inc=s_act)
                                cons_h1[th] = na
                                cons_h1_d[th] = ctx.emit("dve", lambda v, ei=ei, fc=fc, th=th: v.tensor_tensor(
                                    hg[:, (ei * 11 + fc) * TL + H2[th].start:(ei * 11 + fc) * TL + H2[th].stop],
                                    hs1[:, H2[th]], gbc[:, H2[th]], op=ALU.mult),
                                    waits=[(s_act, na), (s_dve, ngbc)], inc=s_dve)
                            else:
                                cons_h1[th] = ctx.emit("act", lambda a, ei=ei, fc=fc, th=th, e=e: a.activation(
                                    hg[:, (ei * 11 + fc) * TL + H2[th].start:(ei * 11 + fc) * TL + H2[th].stop],
                                    P[th][:, :], AF.Relu, bias=b1t[:, e * 11 + fc:e * 11 + fc + 1]),
                                    waits=[(s_tn, ml), (d_cn, n_b1)], inc=s_act)
                        h1_marks.append(ml)
                ctx.barrier()
                cons_h1 = [0, 0]
                cons_h1_d = [0, 0]
                last = (gi == len(GROUPS) - 1)
                y2_marks = []
                cons_y = [0, 0]
                pend = []
                for do in range(DC):
                    while len(pend) >= 2:
                        pend.pop(0)()
                    ml = None
                    for ei, e in enumerate(grp):
                        routed = e < 16
                        b = (do * len(grp) + ei) % 2
                        wts = [(s_tn, y2_marks[-2])] if len(y2_marks) >= 2 else []
                        nw = ctx.emit("gp", lambda g, e=e, do=do, b=b, routed=routed: g.dma_start(
                            out=w2bb[b][:], in_=(rw2[e, do, :, :] if routed else sw2[e - 16, do, :, :])),
                            waits=wts, inc=dws[b], inck=16)
                        for th in range(2):
                            py = P[4 + th]
                            pw = [(dws[b], nw)]
                            if ei == 0 and cons_y[th]:
                                pw.append((s_dve, cons_y[th]))
                            for fc in range(11):
                                is_stop = (not last) and (ei == len(grp) - 1) and (fc == 10)
                                ml = ctx.emit("tn", lambda t, b=b, ei=ei, fc=fc, th=th, py=py, is_stop=is_stop: t.matmul(
                                    py[:, :], w2bb[b][:, fc * 128:(fc + 1) * 128],
                                    hg[:, (ei * 11 + fc) * TL + H2[th].start:(ei * 11 + fc) * TL + H2[th].stop],
                                    start=(ei == 0 and fc == 0), stop=is_stop),
                                    waits=(pw if fc == 0 else []),
                                    inc=(s_tn if (fc == 10 and th == 1) else None))
                        y2_marks.append(ml)
                    if last:
                        nb2 = ctx.emit("gp", lambda g, do=do: g.dma_start(out=b2sb[:, :], in_=b2t[do, :, :]),
                                       waits=(_now(s_tn) if do < 2 else [(s_tn, y2_marks[-2])]), inc=d_gs, inck=16)
                        for th in range(2):
                            py = P[4 + th]
                            ctx.emit("tn", lambda t, th=th, py=py: t.matmul(
                                py[:, :], b2sb[:, :], G18[:, H2[th]], start=False, stop=False),
                                waits=[(d_gs, nb2), (d_cn, n_g18)])
                            ml = ctx.emit("tn", lambda t, do=do, th=th, py=py: t.matmul(
                                py[:, :], identb[:], xn1[:, do * TL + H2[th].start:do * TL + H2[th].stop],
                                start=False, stop=True),
                                waits=[], inc=s_tn)
                    zb = zbuf[do % 2]
                    for th in range(2):
                        py = P[4 + th]
                        cons_y[th] = ctx.emit("dve", lambda v, th=th, py=py, zb=zb: v.tensor_copy(zb[:, H2[th]], py[:, :]),
                                              waits=[(s_tn, s_tn.n), (d_ac, d_ac.n)], inc=s_dve)
                    nd = s_dve.n
                    def mk_acc(do=do, zb=zb, fg=first_grp, nd=nd):
                        def go():
                            ctx.emit("gp", lambda g: (
                                g.dma_start(out=macc[do, :, :], in_=zb[:]) if fg else
                                g.dma_start(out=macc[do, :, :], in_=zb[:], accum_op=ALU.add)),
                                waits=[(s_dve, nd)], inc=d_ac, inck=16)
                        return go
                    pend.append(mk_acc())
                for p_ in pend:
                    p_()
                first_grp = False
                ctx.barrier()
            for do in range(DC):
                buf = abuf[do % 2]
                n = ctx.emit("sy", lambda s, do=do, buf=buf: s.dma_start(out=buf[:], in_=macc[do, :, :]),
                             waits=[(d_ac, d_ac.n)] + (_now(s_act) if do >= 2 else []), inc=d_ld, inck=16)
                na = ctx.emit("act", lambda a, buf=buf: a.activation(zst[:, 0:TL], buf[:], AF.Silu),
                              waits=[(d_ld, n), (d_st, d_st.n)], inc=s_act)
                ctx.emit("sy", lambda s, do=do: s.dma_start(out=g1o[do, :, :], in_=zst[:, 0:TL]),
                         waits=[(s_act, na)], inc=d_st, inck=16)
        for e in ENGINES:
            ctx.ops[e].append(([(d_st.h, d_st.n), (d_ac.h, d_ac.n)], None, None))
        ctx.finalize()
    return nc



def _swap_perm():
    P = np.zeros((128, 128), np.float32)
    for m in range(128):
        blk, d = m // 32, m % 32
        P[blk * 32 + ((d + 16) % 32), m] = 1.0
    return P


def _rope_tables(pos):
    invf = 1.0 / (10000.0 ** (np.arange(0, 32, 2, dtype=np.float64) / 32.0))
    ang = pos[None, :].astype(np.float64) * invf[:, None]
    cos = np.cos(ang).astype(np.float32)
    sin = np.sin(ang).astype(np.float32)
    C = np.zeros((128, len(pos)), np.float32)
    S = np.zeros((128, len(pos)), np.float32)
    for r in range(128):
        d = r % 32
        f = d % 16
        C[r] = cos[f]
        S[r] = sin[f] if d >= 16 else -sin[f]
    return C, S


def _prep_shared_l1(inp):
    f = lambda n: np.ascontiguousarray(np.asarray(inp[n], np.float32))
    out = {}
    out["wk"] = _tile_w(f("L0_attn_Wk").T)
    out["wv"] = _tile_w(f("L0_attn_Wv").T)
    out["wr"] = _tile_w(f("L0_attn_Wr").T)
    gwt = _tile_w(f("L0_ffn_gate_w").T)
    gw = np.zeros((128, 32), np.float32)
    for c in range(DC):
        gw[:, 2 * c:2 * c + 2] = gwt[0][:, c * 128:c * 128 + 2]
    out["gw"] = gw
    out["wg"] = np.stack([_tile_w(f("L0_ffn_w_gate")[e].T) for e in range(2)])
    out["wu"] = np.stack([_tile_w(f("L0_ffn_w_up")[e].T) for e in range(2)])
    out["wd"] = np.stack([_tile_w(f("L0_ffn_w_down")[e].T) for e in range(2)])
    out["td"] = _vec_pc(f("L0_attn_time_decay").ravel())
    out["tmk"] = _vec_pc(f("L0_attn_tm_k").ravel())
    out["tmv"] = _vec_pc(f("L0_attn_tm_v").ravel())
    out["tmr"] = _vec_pc(f("L0_attn_tm_r").ravel())
    out["ident"] = np.eye(128, dtype=np.float32)
    return out


def _launch1_inmaps(inp, shared):
    x = np.ascontiguousarray(np.asarray(inp["L0_attn_x"], np.float32)[0])
    maps = []
    for i in range(NC):
        toks = _core_tokens(i)
        xfm = np.ascontiguousarray(x[toks].T.reshape(DC, 128, TL))
        xbnd = np.zeros((NB, D), np.float32)
        for j in range(NB):
            g = i + 8 * j
            if g > 0:
                xbnd[j] = x[g * 128 - 1]
        m = {"xfm": xfm, "xbnd": xbnd}
        m.update(shared)
        maps.append(m)
    return maps


def _prep_shared_l2(inp):
    f = lambda n: np.ascontiguousarray(np.asarray(inp[n], np.float32))
    out = {}
    out["wo"] = _tile_w(f("L0_attn_Wo").T)
    out["wks"] = _tile_w(f("L1_attn_Wk_sem"))[0]
    out["wkg"] = _tile_w(f("L1_attn_Wk_geo"))[0]
    out["wv1"] = _tile_w(f("L1_attn_Wv"))
    out["swp"] = _swap_perm()
    out["ident"] = np.eye(128, dtype=np.float32)
    pm = np.zeros((128, 2), np.float32)
    pm[0:64, 0] = 1.0 / 64
    pm[64:128, 1] = 1.0 / 64
    out["poolm"] = pm
    out["sw1"] = np.stack([_tile_w(f("L1_ffn_shared_w1")[n]) for n in range(2)])
    out["sw2"] = np.stack([_tile_w(f("L1_ffn_shared_w2")[n]) for n in range(2)])
    out["rw1"] = np.stack([_tile_w(f("L1_ffn_routed_w1")[n]) for n in range(16)])
    out["rw2"] = np.stack([_tile_w(f("L1_ffn_routed_w2")[n]) for n in range(16)])
    b1 = np.concatenate([f("L1_ffn_routed_b1"), f("L1_ffn_shared_b1")], 0)
    out["b1all"] = np.ascontiguousarray(
        b1.reshape(18, 11, 128).transpose(0, 2, 1))
    b2 = np.concatenate([f("L1_ffn_routed_b2"), f("L1_ffn_shared_b2")], 0)
    out["b2t"] = np.ascontiguousarray(b2.reshape(18, DC, 128).transpose(1, 0, 2))
    cen = f("L1_ffn_centroids")
    out["cent"] = np.ascontiguousarray(
        cen.reshape(16, DC, 128).transpose(2, 1, 0).reshape(128, DC * 16))
    out["cbias"] = np.ascontiguousarray(np.broadcast_to(f("L1_ffn_bias"), (128, 16)))
    out["tf"] = _vec_pc(f("L0_attn_time_first").ravel())
    out["lng"] = _vec_pc(f("L0_attn_ln_g").ravel())
    out["lnb"] = _vec_pc(f("L0_attn_ln_b").ravel())
    return out


def _launch2_inmaps(inp, shared2, l1res):
    x = np.ascontiguousarray(np.asarray(inp["L0_attn_x"], np.float32)[0])
    epta = np.zeros((DC, 128, 64), np.float32)
    eptb = np.zeros((DC, 128, 64), np.float32)
    for g in range(64):
        c, j = g % 8, g // 8
        epta[:, :, g] = np.asarray(l1res[c]["a_out"])[:, :, j * 128 + 127]
        eptb[:, :, g] = np.asarray(l1res[c]["b_out"])[:, :, j * 128 + 127]
    maps = []
    for i in range(NC):
        toks = _core_tokens(i)
        xfm = np.ascontiguousarray(x[toks].T.reshape(DC, 128, TL))
        offw = np.zeros((NB, 128, 64), np.float32)
        for j in range(NB):
            gj = i + 8 * j
            for cp in range(8):
                J = (gj - 1 - cp) // 8
                if cp < gj and cp != i:
                    offw[j, :, cp + 8 * J] = 1.0
        C, S = _rope_tables(toks)
        m = {
            "xfm": xfm,
            "a_in": np.asarray(l1res[i]["a_out"]),
            "b_in": np.asarray(l1res[i]["b_out"]),
            "r_in": np.asarray(l1res[i]["r_out"]),
            "g0_in": np.asarray(l1res[i]["g0_out"]),
            "epta": epta, "eptb": eptb, "offw": offw,
            "cosk": C, "sink": S,
        }
        m.update(shared2)
        maps.append(m)
    return maps



def build_launch3():
    nc = bass.Bass("TRN2", target_bir_lowering=False, debug=False)
    ein = lambda n, sh, dt: nc.dram_tensor(n, sh, dt, kind="ExternalInput").ap()
    eout = lambda n, sh, dt: nc.dram_tensor(n, sh, dt, kind="ExternalOutput").ap()
    xn1i = ein("xn1i", [DC, 128, TL], BF16)
    x1i = ein("x1i", [DC, 128, TL], F32)
    g1i = ein("g1i", [DC, 128, TL], F32)
    wqs = ein("wqs", [4, 128, D], F32)
    wqg = ein("wqg", [4, 128, D], F32)
    wo1 = ein("wo1", [DC, 128, 1024], F32)
    lsrep = ein("lsrep", [128, 4], F32)
    swp = ein("swp", [128, 128], F32)
    ident = ein("ident", [128, 128], F32)
    cosq = ein("cosq", [128, TL], F32)
    sinq = ein("sinq", [128, TL], F32)
    kst = ein("kst", [4, 128, 112], F32)
    kgt = ein("kgt", [4, 128, 112], F32)
    ksr = ein("ksr", [4, 128, 1024], F32)
    kgr = ein("kgr", [4, 128, 1024], F32)
    vt = ein("vt", [112, 256], F32)
    vrr = ein("vrr", [8, 128, 256], F32)
    pmask = ein("pmask", [NB, 128, 112], F32)
    rmask = ein("rmask", [128, 1024], F32)
    fixm = ein("fixm", [128, 128], F32)
    outp = eout("outp", [DC, 128, TL], F32)

    es = contextlib.ExitStack()
    with es:
        ctx = Ctx(nc, es)
        sb = lambda n, sh, dt: es.enter_context(nc.sbuf_tensor(n, sh, dt))
        xn1 = sb("xn1", [128, DC * TL], BF16)
        identb = sb("identb", [128, 128], BF16)
        swpb = sb("swpb", [128, 128], BF16)
        cost = sb("cost", [128, TL], F32)
        sint = sb("sint", [128, TL], F32)
        sclc = sb("sclc", [128, 4], F32)
        ones1 = sb("ones1", [128, 1], F32)
        ones1b = sb("ones1b", [128, 1], BF16)
        kstb = sb("kstb", [128, 4 * 112], BF16)
        kgtb = sb("kgtb", [128, 4 * 112], BF16)
        ksrb = sb("ksrb", [128, 4 * TL], BF16)
        kgrb = sb("kgrb", [128, 4 * TL], BF16)
        vtb = sb("vtb", [112, 256], BF16)
        vrb = sb("vrb", [128, 8 * 256], BF16)
        pmt = sb("pmt", [128, NB * 112], F32)
        rmt = sb("rmt", [128, 1024], F32)
        fmt = sb("fmt", [128, 128], F32)
        vbarc = sb("vbarc", [128, 4], F32)
        qs = sb("qs", [128, 4 * TL], BF16)
        qg = sb("qg", [128, 4 * TL], BF16)
        qgf = sb("qgf", [128, TL], F32)
        att = sb("att", [128, 8 * TL], BF16)
        zp = sb("zp", [128, 112], F32)
        zr = sb("zr", [128, 1024], F32)
        ep = sb("ep", [128, 112], BF16)
        er = sb("er", [128, 1024], BF16)
        etp = sb("etp", [112, 128], BF16)
        etr = sb("etr", [128, 8 * 128], BF16)
        rowv = sb("rowv", [128, 8], F32)
        wsl = sb("wsl", [128, 2048], BF16)
        x1t = sb("x1t", [128, TL], F32)
        gbuf = sb("gbuf", [128, TL], F32)
        ot = sb("ot", [128, TL], F32)
        P = [es.enter_context(nc.psum_tensor(f"pb{i}", [128, 512], F32)) for i in range(8)]

        s_tn, s_act, s_dve = ctx.sem("s_tn"), ctx.sem("s_act"), ctx.sem("s_dve")
        d_cn = ctx.sem("d_cn", dma=True)
        d_ld = ctx.sem("d_ld", dma=True)
        d_l2 = ctx.sem("d_l2", dma=True)
        d_st = ctx.sem("d_st", dma=True)
        d_g0, d_g1 = ctx.sem("d_g0", dma=True), ctx.sem("d_g1", dma=True)
        d_gs = ctx.sem("d_gs", dma=True)
        dws = [d_g0, d_g1]
        H2 = (slice(0, 512), slice(512, 1024))

        ctx.emit("dve", lambda v: v.memset(ones1[:], 1.0), inc=s_dve)
        ctx.emit("dve", lambda v: v.memset(ones1b[:], 1.0), inc=s_dve)
        n_const = s_dve.n
        ctx.emit("sy", lambda s: s.dma_start(out=cost[:], in_=cosq[:, :]), inc=d_cn, inck=16)
        ctx.emit("sy", lambda s: s.dma_start(out=sint[:], in_=sinq[:, :]), inc=d_cn, inck=16)
        ctx.emit("sy", lambda s: s.dma_start(out=sclc[:], in_=lsrep[:, :]), inc=d_cn, inck=16)
        for j in range(NB):
            ctx.emit("sy", lambda s, j=j: s.dma_start(out=pmt[:, j * 112:(j + 1) * 112], in_=pmask[j, :, :]), inc=d_cn, inck=16)
        ctx.emit("sy", lambda s: s.dma_start(out=rmt[:], in_=rmask[:, :]), inc=d_cn, inck=16)
        ctx.emit("sy", lambda s: s.dma_start(out=fmt[:], in_=fixm[:, :]), inc=d_cn, inck=16)
        ctx.emit("gp", lambda g: g.dma_start(out=identb[:], in_=ident[:, :]), inc=d_gs, inck=16)
        ctx.emit("gp", lambda g: g.dma_start(out=swpb[:], in_=swp[:, :]), inc=d_gs, inck=16)
        for g_ in range(4):
            ctx.emit("gp", lambda g, g_=g_: g.dma_start(out=kstb[:, g_ * 112:(g_ + 1) * 112], in_=kst[g_, :, :]), inc=d_gs, inck=16)
            ctx.emit("gp", lambda g, g_=g_: g.dma_start(out=kgtb[:, g_ * 112:(g_ + 1) * 112], in_=kgt[g_, :, :]), inc=d_gs, inck=16)
            ctx.emit("gp", lambda g, g_=g_: g.dma_start(out=ksrb[:, g_ * TL:(g_ + 1) * TL], in_=ksr[g_, :, :]), inc=d_gs, inck=16)
            ctx.emit("gp", lambda g, g_=g_: g.dma_start(out=kgrb[:, g_ * TL:(g_ + 1) * TL], in_=kgr[g_, :, :]), inc=d_gs, inck=16)
        ctx.emit("gp", lambda g: g.dma_start(out=vtb[:, :], in_=vt[:, :]), inc=d_gs, inck=16)
        for kc in range(8):
            ctx.emit("gp", lambda g, kc=kc: g.dma_start(out=vrb[:, kc * 256:(kc + 1) * 256], in_=vrr[kc, :, :]), inc=d_gs, inck=16)
        for c in range(DC):
            ctx.emit("sy", lambda s, c=c: s.dma_start(out=xn1[:, c * TL:(c + 1) * TL], in_=xn1i[c, :, :]), inc=d_ld, inck=16)
        n_xn1 = d_ld.n
        ctx.emit("act", lambda a: a.activation(sclc[:], sclc[:], AF.Exp), waits=[(d_cn, 48)], inc=s_act)
        ctx.emit("dve", lambda v: v.tensor_scalar(sclc[:], sclc[:], 1.0 / np.sqrt(32.0), None, op0=ALU.mult),
                 waits=_now(s_act), inc=s_dve)
        nv1 = ctx.emit("tn", lambda t: t.matmul(P[0][0:128, 0:1], vtb[:, 0:128], ones1b[0:112, 0:1], start=True, stop=False),
                       waits=[(d_gs, d_gs.n), (s_dve, n_const)])
        ctx.emit("tn", lambda t: t.matmul(P[1][0:128, 0:1], vtb[:, 128:256], ones1b[0:112, 0:1], start=True, stop=False))
        for kc in range(8):
            ctx.emit("tn", lambda t, kc=kc: t.matmul(P[0][0:128, 0:1], vrb[:, kc * 256:kc * 256 + 128],
                                                     ones1b[:, 0:1], start=False, stop=(kc == 7)))
            ctx.emit("tn", lambda t, kc=kc: t.matmul(P[1][0:128, 0:1], vrb[:, kc * 256 + 128:(kc + 1) * 256],
                                                     ones1b[:, 0:1], start=False, stop=(kc == 7)),
                     inc=(s_tn if kc == 7 else None))
        for g_ in range(4):
            src = P[g_ // 2]
            lo = (g_ % 2) * 64
            for half in range(2):
                ctx.emit("dve", lambda v, g_=g_, src=src, lo=lo, half=half: v.tensor_scalar(
                    vbarc[half * 64:(half + 1) * 64, g_:g_ + 1], src[lo:lo + 64, 0:1], 1.0 / 1136.0, None, op0=ALU.mult),
                    waits=_now(s_tn), inc=s_dve)
        ctx.barrier()

        wq_marks = []
        for qt in range(4):
            b = qt % 2
            wts = [(s_tn, wq_marks[qt - 2][0])] if qt >= 2 else []
            nw = ctx.emit("gp", lambda g, qt=qt: g.dma_start(out=wsl[:], in_=wqs[qt, :, :]),
                          waits=(_now(s_tn) if qt > 0 else []), inc=d_g0, inck=16)
            ml = None
            for th in range(2):
                for dc in range(DC):
                    ml = ctx.emit("tn", lambda t, dc=dc, th=th: t.matmul(
                        P[th][:, :], wsl[:, dc * 128:(dc + 1) * 128],
                        xn1[:, dc * TL + H2[th].start:dc * TL + H2[th].stop],
                        start=(dc == 0), stop=(dc == DC - 1)),
                        waits=([(d_g0, nw), (d_ld, n_xn1)] + _now(s_dve, s_act) if dc == 0 else []),
                        inc=(s_tn if dc == DC - 1 else None))
                ctx.emit("dve", lambda v, qt=qt, th=th: v.tensor_scalar(
                    qs[:, qt * TL + H2[th].start:qt * TL + H2[th].stop], P[th][:, :],
                    sclc[:, qt:qt + 1], None, op0=ALU.mult),
                    waits=[(s_tn, ml)], inc=s_dve)
            nw2 = ctx.emit("gp", lambda g, qt=qt: g.dma_start(out=wsl[:], in_=wqg[qt, :, :]),
                           waits=_now(s_tn), inc=d_g0, inck=16)
            for th in range(2):
                for dc in range(DC):
                    ml = ctx.emit("tn", lambda t, dc=dc, th=th: t.matmul(
                        P[2 + th][:, :], wsl[:, dc * 128:(dc + 1) * 128],
                        xn1[:, dc * TL + H2[th].start:dc * TL + H2[th].stop],
                        start=(dc == 0), stop=(dc == DC - 1)),
                        waits=([(d_g0, nw2)] + _now(s_dve) if dc == 0 else []),
                        inc=(s_tn if dc == DC - 1 else None))
                ctx.emit("dve", lambda v, th=th: v.tensor_copy(qgf[:, H2[th]], P[2 + th][:, :]),
                         waits=[(s_tn, ml)], inc=s_dve)
                ctx.emit("dve", lambda v, qt=qt, th=th: v.tensor_copy(
                    qg[:, qt * TL + H2[th].start:qt * TL + H2[th].stop], qgf[:, H2[th]]),
                    waits=_now(s_dve), inc=s_dve)
                ns = ctx.emit("tn", lambda t, qt=qt, th=th: t.matmul(
                    P[4 + th][:, :], swpb[:], qg[:, qt * TL + H2[th].start:qt * TL + H2[th].stop],
                    start=True, stop=True),
                    waits=_now(s_dve) + [(d_gs, 32)], inc=s_tn)
                ctx.emit("dve", lambda v, th=th: v.tensor_tensor(
                    qgf[:, H2[th]], qgf[:, H2[th]], cost[:, H2[th]], op=ALU.mult),
                    waits=[(s_tn, s_tn.n), (d_cn, d_cn.n)], inc=s_dve)
                ctx.emit("dve", lambda v, th=th: v.tensor_tensor(
                    x1t[:, H2[th]], P[4 + th][:, :], sint[:, H2[th]], op=ALU.mult),
                    waits=_now(s_dve), inc=s_dve)
                ctx.emit("dve", lambda v, th=th: v.tensor_tensor(
                    qgf[:, H2[th]], qgf[:, H2[th]], x1t[:, H2[th]], op=ALU.add),
                    waits=_now(s_dve), inc=s_dve)
                ctx.emit("dve", lambda v, qt=qt, th=th: v.tensor_scalar(
                    qg[:, qt * TL + H2[th].start:qt * TL + H2[th].stop], qgf[:, H2[th]],
                    sclc[:, qt:qt + 1], None, op0=ALU.mult),
                    waits=_now(s_dve), inc=s_dve)
            wq_marks.append((ml,))
        ctx.barrier()

        zp4 = [sb(f"zp4_{i}", [128, 448], F32) for i in range(2)]
        ep4 = [sb(f"ep4_{i}", [128, 448], BF16) for i in range(2)]
        etp2 = [sb(f"etp2_{i}", [112, 512], BF16) for i in range(2)]
        rv2 = [sb(f"rv2_{i}", [128, 8], F32) for i in range(2)]
        pooled_it = []
        for qt in range(4):
            for qc in range(NB - 1):
                pooled_it.append((qt, qc))
        cons_sc = 0
        cons_tp = 0
        cons_pv = 0
        prev_ep = 0
        prev_et = 0
        bu = 0
        for it, (qt, qc) in enumerate(pooled_it):
            bu = it % 2
            qsl = slice(qc * 128, (qc + 1) * 128)
            n1 = None
            for m in range(4):
                base = 32 * m
                ctx.emit("tn", lambda t, qt=qt, base=base, qsl=qsl, m=m: t.matmul(
                    P[m][:, 0:112],
                    qs[base:base + 32, qt * TL + qsl.start:qt * TL + qsl.stop],
                    kstb[base:base + 32, qt * 112:(qt + 1) * 112],
                    start=True, stop=False, tile_position=(base, 0)),
                    waits=(([(s_dve, cons_sc)] if cons_sc else []) if m == 0 else []))
                n1 = ctx.emit("tn", lambda t, qt=qt, base=base, qsl=qsl, m=m: t.matmul(
                    P[m][:, 0:112],
                    qg[base:base + 32, qt * TL + qsl.start:qt * TL + qsl.stop],
                    kgtb[base:base + 32, qt * 112:(qt + 1) * 112],
                    start=False, stop=True, tile_position=(base, 0)),
                    inc=s_tn)
            nz = None
            for m in range(4):
                nz = ctx.emit("dve", lambda v, bu=bu, qc=qc, m=m: v.scalar_tensor_tensor(
                    zp4[bu][:, m * 112:(m + 1) * 112], P[m][:, 0:112],
                    30000.0, pmt[:, qc * 112:(qc + 1) * 112], op0=ALU.add, op1=ALU.mult),
                    waits=([(s_tn, n1), (d_cn, d_cn.n)] if m == 0 else []), inc=(s_dve if m == 3 else None))
            cons_sc = nz
            ctx.emit("dve", lambda v, bu=bu: v.reduce_max(rv2[bu][:, 0:1], zp4[bu][:], axis=AX.X, negate=True),
                     waits=[(s_dve, nz)], inc=s_dve)
            na = ctx.emit("act", lambda a, bu=bu: a.activation(
                ep4[bu][:], zp4[bu][:], AF.Exp, bias=rv2[bu][:, 0:1]),
                waits=_now(s_dve) + ([(s_tn, prev_ep)] if prev_ep else []), inc=s_act)
            ctx.emit("dve", lambda v, bu=bu: v.tensor_reduce(
                rv2[bu][:, 4:8], ep4[bu][:].rearrange("p (m k) -> p m k", k=112), axis=AX.X, op=ALU.add),
                waits=[(s_act, na)], inc=s_dve)
            ctx.emit("dve", lambda v, bu=bu: v.reciprocal(rv2[bu][:, 4:8], rv2[bu][:, 4:8]),
                     waits=_now(s_dve), inc=s_dve)
            rb4 = rv2[bu][:, 4:8].unsqueeze(2).broadcast_to([128, 4, 112])
            ne = ctx.emit("dve", lambda v, bu=bu, rb4=rb4: v.tensor_tensor(
                ep4[bu][:].rearrange("p (m k) -> p m k", k=112),
                ep4[bu][:].rearrange("p (m k) -> p m k", k=112), rb4, op=ALU.mult),
                waits=_now(s_dve), inc=s_dve)
            ntl = 0
            for m in range(4):
                tw = [(s_dve, ne), (d_gs, 32)]
                if m == 0 and cons_tp:
                    tw.append((s_dve, cons_tp))
                if ntl:
                    tw.append((s_dve, ntl))
                nt = ctx.emit("tn", lambda t, bu=bu, m=m: t.transpose(
                    P[4].bitcast(BF16)[0:112, 0:128], ep4[bu][:, m * 112:(m + 1) * 112], identb[:]),
                    waits=tw, inc=s_tn)
                prev_ep = nt
                ntl = ctx.emit("dve", lambda v, bu=bu, m=m: v.tensor_copy(
                    etp2[bu][:, m * 128:(m + 1) * 128], P[4].bitcast(BF16)[0:112, 0:128]),
                    waits=[(s_tn, nt)] + ([(s_tn, prev_et)] if (m == 0 and prev_et) else []),
                    inc=s_dve)
            cons_tp = ntl
            ncp = 0
            for m in range(4):
                pvw = [(s_dve, cons_tp)] if m == 0 else []
                if m == 0 and cons_pv:
                    pvw.append((s_dve, cons_pv))
                if ncp:
                    pvw.append((s_dve, ncp))
                npv = ctx.emit("tn", lambda t, qt=qt, m=m, bu=bu: t.matmul(
                    P[5][0:64, 0:128], vtb[:, qt * 64:(qt + 1) * 64],
                    etp2[bu][:, m * 128:(m + 1) * 128], start=True, stop=True),
                    waits=pvw, inc=s_tn)
                prev_et = npv
                h = 4 * qt + m
                jt, sub = h // 2, h % 2
                ncp = ctx.emit("dve", lambda v, jt=jt, sub=sub, qsl=qsl: v.tensor_copy(
                    att[sub * 64:(sub + 1) * 64, jt * TL + qsl.start:jt * TL + qsl.stop],
                    P[5][0:64, 0:128]),
                    waits=[(s_tn, npv)], inc=s_dve)
                if qc == 0:
                    sl2 = slice(sub * 64, (sub + 1) * 64)
                    ctx.emit("dve", lambda v, jt=jt, sl2=sl2, qt=qt, bu=bu: v.tensor_scalar(
                        zp4[bu][sl2, 0:128], att[sl2, jt * TL:jt * TL + 128],
                        vbarc[sl2, qt:qt + 1], None, op0=ALU.subtract),
                        waits=_now(s_dve), inc=s_dve)
                    ctx.emit("dve", lambda v, sl2=sl2, bu=bu: v.tensor_tensor(
                        zp4[bu][sl2, 0:128], zp4[bu][sl2, 0:128], fmt[sl2, 0:128], op=ALU.mult),
                        waits=_now(s_dve) + [(d_cn, d_cn.n)], inc=s_dve)
                    ncp = ctx.emit("dve", lambda v, jt=jt, sl2=sl2, bu=bu: v.tensor_tensor(
                        att[sl2, jt * TL:jt * TL + 128], att[sl2, jt * TL:jt * TL + 128],
                        zp4[bu][sl2, 0:128], op=ALU.subtract),
                        waits=_now(s_dve), inc=s_dve)
                    cons_sc = ncp
            cons_pv = ncp
        for qt in range(4):
            for m in range(4):
                base = 32 * m
                qc = NB - 1
                qsl = slice(qc * 128, (qc + 1) * 128)
                n1 = ctx.emit("tn", lambda t, qt=qt, base=base, qsl=qsl: t.matmul(
                    P[0][:, 0:112], qs[base:base + 32, qt * TL + qsl.start:qt * TL + qsl.stop],
                    kstb[base:base + 32, qt * 112:(qt + 1) * 112],
                    start=True, stop=False, tile_position=(base, 0)),
                    waits=_now(s_dve, s_act))
                n1 = ctx.emit("tn", lambda t, qt=qt, base=base, qsl=qsl: t.matmul(
                    P[0][:, 0:112], qg[base:base + 32, qt * TL + qsl.start:qt * TL + qsl.stop],
                    kgtb[base:base + 32, qt * 112:(qt + 1) * 112],
                    start=False, stop=True, tile_position=(base, 0)),
                    inc=s_tn)
                for rh in range(2):
                    ctx.emit("tn", lambda t, qt=qt, base=base, qsl=qsl, rh=rh: t.matmul(
                        P[1 + rh][:, :], qs[base:base + 32, qt * TL + qsl.start:qt * TL + qsl.stop],
                        ksrb[base:base + 32, qt * TL + H2[rh].start:qt * TL + H2[rh].stop],
                        start=True, stop=False, tile_position=(base, 0)))
                    ctx.emit("tn", lambda t, qt=qt, base=base, qsl=qsl, rh=rh: t.matmul(
                        P[1 + rh][:, :], qg[base:base + 32, qt * TL + qsl.start:qt * TL + qsl.stop],
                        kgrb[base:base + 32, qt * TL + H2[rh].start:qt * TL + H2[rh].stop],
                        start=False, stop=True, tile_position=(base, 0)),
                        inc=(s_tn if rh == 1 else None))
                ctx.emit("dve", lambda v, qc=qc: v.scalar_tensor_tensor(
                    zp[:], P[0][:, 0:112], 30000.0, pmt[:, qc * 112:(qc + 1) * 112],
                    op0=ALU.add, op1=ALU.mult),
                    waits=_now(s_tn) + [(d_cn, d_cn.n), (s_act, s_act.n)], inc=s_dve)
                ctx.emit("dve", lambda v: v.reduce_max(rowv[:, 0:1], zp[:], axis=AX.X, negate=True),
                         waits=_now(s_dve), inc=s_dve)
                for rh in range(2):
                    ctx.emit("dve", lambda v, rh=rh: v.scalar_tensor_tensor(
                        zr[:, H2[rh]], P[1 + rh][:, :], 30000.0, rmt[:, H2[rh]],
                        op0=ALU.add, op1=ALU.mult),
                        waits=_now(s_tn), inc=s_dve)
                ctx.emit("dve", lambda v: v.reduce_max(rowv[:, 1:2], zr[:], axis=AX.X, negate=True),
                         waits=_now(s_dve), inc=s_dve)
                ctx.emit("dve", lambda v: v.tensor_tensor(rowv[:, 0:1], rowv[:, 0:1], rowv[:, 1:2], op=ALU.min),
                         waits=_now(s_dve), inc=s_dve)
                na = ctx.emit("act", lambda a: a.activation(
                    ep[:], zp[:], AF.Exp, bias=rowv[:, 0:1], accum_out=rowv[:, 2:3]),
                    waits=_now(s_dve), inc=s_act)
                na = ctx.emit("act", lambda a: a.activation(
                    er[:], zr[:], AF.Exp, bias=rowv[:, 0:1], accum_out=rowv[:, 3:4]),
                    waits=_now(s_act), inc=s_act)
                ctx.emit("dve", lambda v: v.tensor_tensor(rowv[:, 2:3], rowv[:, 2:3], rowv[:, 3:4], op=ALU.add),
                         waits=[(s_act, na)], inc=s_dve)
                ctx.emit("dve", lambda v: v.reciprocal(rowv[:, 4:5], rowv[:, 2:3]),
                         waits=_now(s_dve), inc=s_dve)
                ctx.emit("dve", lambda v: v.tensor_scalar(ep[:], ep[:], rowv[:, 4:5], None, op0=ALU.mult),
                         waits=_now(s_dve), inc=s_dve)
                nt = ctx.emit("tn", lambda t: t.transpose(P[3].bitcast(BF16)[0:112, 0:128], ep[:], identb[:]),
                              waits=_now(s_dve) + [(d_gs, 32)], inc=s_tn)
                ctx.emit("dve", lambda v: v.tensor_copy(etp[:, :], P[3].bitcast(BF16)[0:112, 0:128]),
                         waits=[(s_tn, nt)], inc=s_dve)
                ctx.emit("dve", lambda v: v.tensor_scalar(er[:], er[:], rowv[:, 4:5], None, op0=ALU.mult),
                         waits=_now(s_dve), inc=s_dve)
                for kc in range(8):
                    nt = ctx.emit("tn", lambda t, kc=kc: t.transpose(
                        P[3].bitcast(BF16)[:, 0:128], er[:, kc * 128:(kc + 1) * 128], identb[:]),
                        waits=_now(s_dve), inc=s_tn)
                    ctx.emit("dve", lambda v, kc=kc: v.tensor_copy(
                        etr[:, kc * 128:(kc + 1) * 128], P[3].bitcast(BF16)[:, 0:128]),
                        waits=[(s_tn, nt)], inc=s_dve)
                npv = ctx.emit("tn", lambda t, qt=qt: t.matmul(
                    P[6][0:64, 0:128], vtb[:, qt * 64:(qt + 1) * 64], etp[:, :],
                    start=True, stop=False),
                    waits=_now(s_dve))
                for kc in range(8):
                    npv = ctx.emit("tn", lambda t, qt=qt, kc=kc: t.matmul(
                        P[6][0:64, 0:128], vrb[:, kc * 256 + qt * 64:kc * 256 + (qt + 1) * 64],
                        etr[:, kc * 128:(kc + 1) * 128],
                        start=False, stop=(kc == 7)),
                        inc=(s_tn if kc == 7 else None))
                h = 4 * qt + m
                jt, sub = h // 2, h % 2
                ctx.emit("dve", lambda v, jt=jt, sub=sub, qsl=qsl: v.tensor_copy(
                    att[sub * 64:(sub + 1) * 64, jt * TL + qsl.start:jt * TL + qsl.stop],
                    P[6][0:64, 0:128]),
                    waits=[(s_tn, s_tn.n)], inc=s_dve)
        ctx.barrier()

        s1_marks = []
        cons_o = [0, 0]
        for co in range(DC):
            b = co % 2
            wts = [(s_tn, s1_marks[co - 2])] if co >= 2 else []
            nw = ctx.emit("gp", lambda g, co=co, b=b: g.dma_start(
                out=wsl[:, b * 1024:(b + 1) * 1024], in_=wo1[co, :, :]),
                waits=wts, inc=dws[b], inck=16)
            nx = ctx.emit("sy", lambda s, co=co: s.dma_start(out=x1t[:], in_=x1i[co, :, :]),
                          waits=(_now(s_dve) if co > 0 else []), inc=d_ld, inck=16)
            ng = ctx.emit("sy", lambda s, co=co: s.dma_start(out=gbuf[:], in_=g1i[co, :, :]),
                          waits=(_now(s_dve) if co > 0 else []), inc=d_l2, inck=16)
            ml = None
            for th in range(2):
                po = P[6] if th == 0 else P[7]
                pw = [(dws[b], nw)]
                if cons_o[th]:
                    pw.append((s_dve, cons_o[th]))
                for j in range(8):
                    ml = ctx.emit("tn", lambda t, j=j, th=th, po=po, b=b: t.matmul(
                        po[:, :], wsl[:, b * 1024 + j * 128:b * 1024 + (j + 1) * 128],
                        att[:, j * TL + H2[th].start:j * TL + H2[th].stop],
                        start=(j == 0), stop=(j == 7)),
                        waits=(pw if j == 0 else []),
                        inc=(s_tn if j == 7 else None))
                ctx.emit("dve", lambda v, th=th, po=po: v.tensor_tensor(
                    ot[:, H2[th]], po[:, :], gbuf[:, H2[th]], op=ALU.mult),
                    waits=[(s_tn, ml), (d_l2, ng), (d_ld, nx), (d_st, d_st.n)], inc=s_dve)
                cons_o[th] = ctx.emit("dve", lambda v, th=th: v.tensor_tensor(
                    ot[:, H2[th]], ot[:, H2[th]], x1t[:, H2[th]], op=ALU.add),
                    waits=_now(s_dve), inc=s_dve)
            s1_marks.append(ml)
            ctx.emit("sy", lambda s, co=co: s.dma_start(out=outp[co, :, :], in_=ot[:]),
                     waits=_now(s_dve), inc=d_st, inck=16)
        for e in ENGINES:
            ctx.ops[e].append(([(d_st.h, d_st.n)], None, None))
        ctx.finalize()
    return nc


def _prep_shared_l3(inp):
    f = lambda n: np.ascontiguousarray(np.asarray(inp[n], np.float32))
    out = {}
    out["wqs"] = _tile_w(f("L1_attn_Wq_sem"))
    out["wqg"] = _tile_w(f("L1_attn_Wq_geo"))
    out["wo1"] = _tile_w(f("L1_attn_Wo"))
    ls = f("L1_attn_logit_scale")
    lsrep = np.zeros((128, 4), np.float32)
    for qt in range(4):
        for p in range(128):
            lsrep[p, qt] = ls[4 * qt + p // 32]
    out["lsrep"] = lsrep
    out["swp"] = _swap_perm()
    out["ident"] = np.eye(128, dtype=np.float32)
    return out


def _launch3_inmaps(inp, shared3, l2res):
    ks2g = np.zeros((128, 112), np.float32)
    kg2g = np.zeros((128, 112), np.float32)
    v2g = np.zeros((112, 256), np.float32)
    for g in range(56):
        c, j = g % 8, g // 8
        ks2g[:, 2 * g:2 * g + 2] = np.asarray(l2res[c]["ks2o"])[:, 2 * j:2 * j + 2]
        kg2g[:, 2 * g:2 * g + 2] = np.asarray(l2res[c]["kg2o"])[:, 2 * j:2 * j + 2]
        v2g[2 * g:2 * g + 2, :] = np.asarray(l2res[c]["v2o"])[2 * j:2 * j + 2, :]
    ksrg = np.concatenate([np.asarray(l2res[c]["ksro"]) for c in range(NC)], axis=1)
    kgrg = np.concatenate([np.asarray(l2res[c]["kgro"]) for c in range(NC)], axis=1)
    vrg = np.stack([np.asarray(l2res[c]["vro"]) for c in range(NC)])
    kst = np.stack([np.tile(ks2g[32 * g:32 * (g + 1), :], (4, 1)) for g in range(4)])
    kgt = np.stack([np.tile(kg2g[32 * g:32 * (g + 1), :], (4, 1)) for g in range(4)])
    ksr = np.stack([np.tile(ksrg[32 * g:32 * (g + 1), :], (4, 1)) for g in range(4)])
    kgr = np.stack([np.tile(kgrg[32 * g:32 * (g + 1), :], (4, 1)) for g in range(4)])
    k_pos = np.concatenate([np.arange(112) * 64 + 63, np.arange(7168, T)])
    maps = []
    for i in range(NC):
        toks = _core_tokens(i)
        C, S = _rope_tables(toks)
        pmask = np.zeros((NB, 128, 112), np.float32)
        for j in range(NB):
            qpos = (i + 8 * j) * 128 + np.arange(128)
            pmask[j] = (k_pos[None, 0:112] <= qpos[:, None]).astype(np.float32)
        qpos7 = (56 + i) * 128 + np.arange(128)
        rmask = (k_pos[None, 112:] <= qpos7[:, None]).astype(np.float32)
        fixm = np.zeros((128, 128), np.float32)
        if i == 0:
            for r in range(63):
                fixm[:, r] = 1.0
        m = {
            "xn1i": np.asarray(l2res[i]["xn1o"]),
            "x1i": np.asarray(l2res[i]["x1o"]),
            "g1i": np.asarray(l2res[i]["g1o"]),
            "cosq": C, "sinq": S,
            "kst": np.ascontiguousarray(kst), "kgt": np.ascontiguousarray(kgt),
            "ksr": np.ascontiguousarray(ksr), "kgr": np.ascontiguousarray(kgr),
            "vt": np.ascontiguousarray(v2g), "vrr": np.ascontiguousarray(vrg),
            "pmask": pmask, "rmask": np.ascontiguousarray(rmask), "fixm": fixm,
        }
        m.update(shared3)
        maps.append(m)
    return maps


def kernel(**inputs):
    shared1 = _prep_shared_l1(inputs)
    nc1 = build_launch1()
    res1 = run_bass_kernel_spmd(nc1, _launch1_inmaps(inputs, shared1), core_ids=list(range(NC)))
    shared2 = _prep_shared_l2(inputs)
    nc2 = build_launch2()
    res2 = run_bass_kernel_spmd(nc2, _launch2_inmaps(inputs, shared2, res1.results), core_ids=list(range(NC)))
    shared3 = _prep_shared_l3(inputs)
    nc3 = build_launch3()
    res3 = run_bass_kernel_spmd(nc3, _launch3_inmaps(inputs, shared3, res2.results), core_ids=list(range(NC)))
    out = np.zeros((T, D), np.float32)
    for i in range(NC):
        o = np.asarray(res3.results[i]["outp"])
        for j in range(NB):
            g = i + 8 * j
            out[g * 128:(g + 1) * 128, :] = o[:, :, j * 128:(j + 1) * 128].transpose(2, 0, 1).reshape(128, D)
    return out[None]
